# version 25
# baseline (speedup 1.0000x reference)
import numpy as np
import ml_dtypes
import concourse.bass as bass
import concourse.mybir as mybir
from concourse.bass_utils import run_bass_kernel_spmd

F32 = mybir.dt.float32
BF16 = mybir.dt.bfloat16
AF = mybir.ActivationFunctionType
ALU = mybir.AluOpType
AX = mybir.AxisListType

NCORE = 8
NSEQ = 4
T = 2048
D = 1024
NH = 16
DH = 64
NCH = 8
NT = 16
EPS = 1e-5


class Ctr:
    def __init__(self, nc, name):
        self.h = nc.alloc_semaphore(name)
        self.n = 0

    def tick(self, k=1):
        self.n += k
        return (self.h, k)

    def at(self):
        return (self.h, self.n)


class Eng:
    def __init__(self):
        self.ops = []

    def op(self, fn, waits=(), inc=None):
        ws = []
        for w in waits:
            if w is None:
                continue
            if isinstance(w, list):
                ws.extend(t for t in w if t is not None)
            else:
                ws.append(w)
        self.ops.append((tuple(ws), fn, inc))

    def retro_inc(self, inc):
        w, f, old = self.ops[-1]
        assert old is None
        self.ops[-1] = (w, f, inc)


def run_ops(engine, ops):
    waited = {}
    for waits, fn, inc in ops:
        for sem, val in waits:
            if val > 0 and waited.get(id(sem), -1) < val:
                engine.wait_ge(sem, val)
                waited[id(sem)] = val
        ins = fn(engine)
        if inc is not None:
            sem, k = inc
            ins.then_inc(sem, k)


def build_program(nc=None):
    if nc is None:
        nc = bass.Bass("TRN2", target_bir_lowering=False)

    x_d = nc.dram_tensor("x", [NSEQ, D, T], F32, kind="ExternalInput").ap()
    y_d = nc.dram_tensor("y", [NSEQ, D, T], F32, kind="ExternalOutput").ap()
    wqkvt_d = nc.dram_tensor("wqkvt", [D, 3 * D], F32, kind="ExternalInput").ap()
    wot_d = nc.dram_tensor("wot", [D, D], F32, kind="ExternalInput").ap()
    win_d = nc.dram_tensor("win", [D, 64], F32, kind="ExternalInput").ap()
    wout_d = nc.dram_tensor("wout", [64, D], F32, kind="ExternalInput").ap()
    gcol_d = nc.dram_tensor("gcol", [128, 8], F32, kind="ExternalInput").ap()
    bcol_d = nc.dram_tensor("bcol", [128, 8], F32, kind="ExternalInput").ap()
    bincol_d = nc.dram_tensor("bincol", [64, 1], F32, kind="ExternalInput").ap()
    w1_d = nc.dram_tensor("w1", [64, 16], F32, kind="ExternalInput").ap()
    b1_d = nc.dram_tensor("b1", [64, 16], F32, kind="ExternalInput").ap()
    w2_d = nc.dram_tensor("w2", [64, 16], F32, kind="ExternalInput").ap()
    b2col_d = nc.dram_tensor("b2col", [64, 1], F32, kind="ExternalInput").ap()
    boutcol_d = nc.dram_tensor("boutcol", [128, 8], F32, kind="ExternalInput").ap()
    ident_d = nc.dram_tensor("ident", [128, 128], F32, kind="ExternalInput").ap()
    maskt_d = nc.dram_tensor("maskt", [128, 128], F32, kind="ExternalInput").ap()
    maskn_d = nc.dram_tensor("maskn", [128, 128], F32, kind="ExternalInput").ap()
    wqs_d = nc.dram_tensor("wqs", [24, 8, 128, 128], BF16, kind="Internal").ap()
    wos_d = nc.dram_tensor("wos", [8, 8, 128, 128], BF16, kind="Internal").ap()
    wvs_d = nc.dram_tensor("wvs", [2, 8, 4, 128, 128], BF16, kind="Internal").ap()

    BASE = 18464
    def at(name, shape, dt, off):
        return nc.alloc_sbuf_tensor_at(name, shape, dt, offset=BASE + off).ap()

    slotA = at("slotA", [128, NCH * T], BF16, 0)
    slotB = at("slotB", [128, NCH * T], BF16, 32768)
    slotC = at("slotC", [128, NCH * T], BF16, 65536)
    VOFF = 98304
    vbuf = at("vbuf", [128, NT * NH * 65], BF16, VOFF)
    wstage2 = [at("wstage0", [128, 3 * D], F32, VOFF),
               at("wstage1", [128, 3 * D], F32, VOFF + 12288)]
    wbstage = at("wbstage", [128, 3 * D], BF16, VOFF + 24576)
    xstage = at("xstage", [128, 2 * T], F32, VOFF)
    FOFF = 131584
    s_sb = at("s_sb", [64, T], F32, FOFF)
    tfn = at("tfn", [64, T], F32, FOFF + 8192)
    u_fn = at("u_fn", [64, T], F32, FOFF + 16384)
    g_sb = at("g_sb", [64, T], F32, FOFF + 24576)
    gb_sb = at("gb_sb", [64, T], BF16, FOFF + 32768)
    kt = [at("kt0", [65, T], BF16, FOFF), at("kt1", [65, T], BF16, FOFF + 4096)]
    qt = [at("qt0", [65, T], BF16, FOFF + 8192), at("qt1", [65, T], BF16, FOFF + 12288)]
    aT = [at("aT0", [128, 1024], BF16, FOFF + 16384), at("aT1", [128, 1024], BF16, FOFF + 18432)]
    rball = at("rball", [128, 1024], F32, FOFF + 20480)
    ctxstage = at("ctxstage", [64, 1024], BF16, FOFF + 24576)
    wqk_st = at("wqk_st", [128, 2048], BF16, FOFF + 26624)
    wv_st = at("wv_st", [128, 4096], BF16, FOFF + 26624)
    wo_st = at("wo_st", [128, 2048], BF16, FOFF + 26624)
    tmpblk = at("tmpblk", [128, 128], F32, FOFF + 34816)
    mneg = at("mneg", [128, 8], BF16, FOFF + 35328)
    mtmp = at("mtmp", [128, 8], F32, FOFF + 35360)
    R_row = at("R_row", [1, 2048], BF16, 168448)
    B_row = at("B_row", [1, 2048], BF16, 172544)
    Rb_b = at("Rb_b", [128, T], BF16, 176640)
    Bb_b = at("Bb_b", [128, T], BF16, 180736)
    sq_b = at("sq_b", [128, 2 * T], BF16, 184832)
    tmp_b = at("tmp_b", [128, 2 * T], BF16, 193024)
    CO = 201216
    win_sb = at("win_sb", [128, 8 * 64], BF16, CO)
    wout_sb = at("wout_sb", [64, D], BF16, CO + 1024)
    ident_sb = at("ident_sb", [128, 128], F32, CO + 3072)
    maskt_f = at("maskt_f", [128, 128], F32, CO + 3584)
    maskn_sb = at("maskn_sb", [128, 128], F32, CO + 4096)
    maskt_sb = at("maskt_sb", [128, 128], BF16, CO + 4608)
    onesrow_bf = at("onesrow_bf", [1, 128], BF16, CO + 4864)
    ones64 = at("ones64", [65, 64], BF16, CO + 5152)
    ident_bf = at("ident_bf", [128, 128], BF16, CO + 5920)
    rrow_bf = at("rrow_bf", [65, 1024], BF16, CO + 6208)
    w1_sb = at("w1_sb", [64, 16], F32, CO + 5408)
    b1_sb = at("b1_sb", [64, 16], F32, CO + 5472)
    w2_sb = at("w2_sb", [64, 16], F32, CO + 5536)
    c2_col = at("c2_col", [128, 24], F32, CO + 5600)
    c3_col = at("c3_col", [128, 8], F32, CO + 5696)
    g_col = at("g_col", [128, 8], F32, CO + 5728)
    be_col = at("be_col", [128, 8], F32, CO + 5760)
    bout_col = at("bout_col", [128, 8], F32, CO + 5792)
    bin_sb = at("bin_sb", [64, 1], F32, CO + 5824)
    b2_sb = at("b2_sb", [64, 1], F32, CO + 5856)
    onescol_bf = at("onescol_bf", [128, 1], BF16, CO + 5888)

    pA = nc.alloc_psum_tensor("pA", [128, 1024], F32).ap()
    pB = nc.alloc_psum_tensor("pB", [128, 1024], F32).ap()
    pC = nc.alloc_psum_tensor("pC", [128, 1024], F32).ap()
    pD = nc.alloc_psum_tensor("pD", [128, 1024], F32).ap()
    pAll = [pA, pB, pC, pD]

    Tq, Vq, Aq, Gq, Sq = Eng(), Eng(), Eng(), Eng(), Eng()
    t_s = Ctr(nc, "t_s")
    v_s = Ctr(nc, "v_s")
    a_s = Ctr(nc, "a_s")
    g_s = Ctr(nc, "g_s")
    d_c = Ctr(nc, "d_c")
    d_ws = Ctr(nc, "d_ws")
    d_x = [Ctr(nc, "d_x0"), Ctr(nc, "d_x1")]
    d_w = [Ctr(nc, "d_w0"), Ctr(nc, "d_w1")]
    d_wv = Ctr(nc, "d_wv")
    d_kq = [Ctr(nc, "d_kq0"), Ctr(nc, "d_kq1")]
    d_cs = Ctr(nc, "d_cs")
    d_out = Ctr(nc, "d_out")

    def dma(dst, src, ctr, waits=()):
        Sq.op(lambda e, d=dst, s=src: e.dma_start(d, s), waits=waits, inc=ctr.tick(16))
        return ctr.at()

    for dst, src in [
        (w1_sb[:], w1_d), (b1_sb[:], b1_d), (w2_sb[:], w2_d),
        (bin_sb[:], bincol_d), (b2_sb[:], b2col_d), (bout_col[:], boutcol_d),
        (g_col[:], gcol_d), (be_col[:], bcol_d), (ident_sb[:], ident_d),
        (maskt_f[:], maskt_d), (maskn_sb[:], maskn_d),
    ]:
        dma(dst, src, d_c)
    t_consts = d_c.at()

    Gq.op(lambda e: e.memset(onesrow_bf[0:1, :], 1.0), inc=g_s.tick())
    Gq.op(lambda e: e.memset(onescol_bf[:], 1.0), inc=g_s.tick())
    Gq.op(lambda e: e.memset(ones64[64:65, :], 1.0), inc=g_s.tick())
    Gq.op(lambda e: e.tensor_copy(maskt_sb[:], maskt_f[:]), waits=[t_consts], inc=g_s.tick())
    Gq.op(lambda e: e.tensor_copy(ident_bf[:], ident_sb[:]), waits=[t_consts], inc=g_s.tick())
    g_const = g_s.at()

    tok = dma(xstage[:, 0:512], win_d.rearrange("(kc p) m -> p kc m", p=128), d_ws)
    Vq.op(lambda e: e.tensor_copy(win_sb[:], xstage[:, 0:512]), waits=[tok], inc=v_s.tick())
    tok = dma(xstage[0:64, 1024:2048], wout_d, d_ws, waits=[v_s.at()])
    Vq.op(lambda e: e.tensor_copy(wout_sb[:], xstage[0:64, 1024:2048]), waits=[tok], inc=v_s.tick())
    v_wsmall = v_s.at()

    last_store = None
    last_wread_t = {0: None, 1: None}
    last_wread_v = {0: None, 1: None}
    for kc in range(NCH):
        sb_ = kc % 2
        wstage = wstage2[sb_]
        tok = dma(wstage[:], wqkvt_d[kc * 128:(kc + 1) * 128, :], d_ws,
                  waits=[last_wread_t[sb_], last_wread_v[sb_], v_wsmall])
        Vq.op(lambda e, kc=kc, wstage=wstage: e.tensor_scalar(
            wbstage[:], wstage[:], g_col[:, kc:kc + 1], None, ALU.mult),
            waits=[tok, t_consts, last_store], inc=v_s.tick())
        last_wread_v[sb_] = v_s.at()
        for mc in range(24):
            Tq.op(lambda e, kc=kc, mc=mc, wstage=wstage: e.matmul(
                pA[:, mc * 8 + kc:mc * 8 + kc + 1], wstage[:, mc * 128:(mc + 1) * 128],
                be_col[:, kc:kc + 1], start=True, stop=True),
                waits=([tok, t_consts] if mc == 0 else []))
        Tq.retro_inc(t_s.tick())
        last_wread_t[sb_] = t_s.at()
        dma(wqs_d[0:16, kc, :, :].rearrange("mc p f -> p mc f"),
            wbstage[:, 0:2048].rearrange("p (mc f) -> p mc f", mc=16),
            d_out, waits=[last_wread_v[sb_]])
        for vh_ in range(2):
            dma(wvs_d[vh_, kc, :, :, :].rearrange("mc p f -> p mc f"),
                wbstage[:, 2048 + vh_ * 512:2048 + vh_ * 512 + 512]
                .rearrange("p (mc f) -> p mc f", mc=4),
                d_out, waits=[])
        last_store = d_out.at()
    for mc in range(24):
        Vq.op(lambda e, mc=mc: e.reduce_sum(
            c2_col[:, mc:mc + 1], pA[:, mc * 8:mc * 8 + 8], axis=AX.X),
            waits=[last_wread_t[0], last_wread_t[1]], inc=v_s.tick())
    v_c2 = v_s.at()

    for kc in range(NCH):
        sb_ = kc % 2
        wstage = wstage2[sb_]
        tok = dma(wstage[:, 0:1024], wot_d[kc * 128:(kc + 1) * 128, :], d_ws,
                  waits=[last_wread_t[sb_], last_wread_v[sb_]])
        Vq.op(lambda e, wstage=wstage: e.tensor_copy(
            wbstage[:, 0:1024], wstage[:, 0:1024]),
            waits=[tok, v_c2, last_store], inc=v_s.tick())
        last_wread_v[sb_] = v_s.at()
        for mc in range(8):
            Tq.op(lambda e, kc=kc, mc=mc, wstage=wstage: e.matmul(
                pB[:, mc * 8 + kc:mc * 8 + kc + 1], wstage[:, mc * 128:(mc + 1) * 128],
                c2_col[:, 16 + kc:17 + kc], start=True, stop=True),
                waits=([tok, v_c2] if mc == 0 else []))
        Tq.retro_inc(t_s.tick())
        last_wread_t[sb_] = t_s.at()
        last_store = dma(wos_d[:, kc, :, :].rearrange("mc p f -> p mc f"),
                         wbstage[:, 0:1024].rearrange("p (mc f) -> p mc f", mc=8),
                         d_out, waits=[last_wread_v[sb_]])
    for mc in range(8):
        Vq.op(lambda e, mc=mc: e.reduce_sum(
            c3_col[:, mc:mc + 1], pB[:, mc * 8:mc * 8 + 8], axis=AX.X),
            waits=[last_wread_t[0], last_wread_t[1]], inc=v_s.tick())
    vput = vbuf[:].rearrange("p (t h f) -> p t h f", t=NT, h=NH, f=65)
    setup_v = v_s.at()
    setup_t = max(last_wread_t[0], last_wread_t[1], key=lambda t: t[1])
    setup_d = d_out.at()
    setup_g = g_s.at()

    def emit_ln(src_sb, src_tok_fn, eps_mul, eps_add, guards):
        sq_read_t = {}
        sqtoks = {}
        for c in range(NCH):
            w = [src_tok_fn(c)] + guards
            if c - 2 in sq_read_t:
                w.append(sq_read_t[c - 2])
            Vq.op(lambda e, c=c: e.tensor_tensor(
                sq_b[:, (c % 2) * T:(c % 2) * T + T],
                src_sb[:, c * T:(c + 1) * T], src_sb[:, c * T:(c + 1) * T], ALU.mult),
                waits=w, inc=v_s.tick())
            sqtoks[c] = v_s.at()
            for th in range(2):
                psum_s = (pA, pB)[th]
                psum_q = (pC, pD)[th]
                for h5 in range(2):
                    cs = th * 1024 + h5 * 512
                    w = [src_tok_fn(c)] + (guards if c == 0 else [])
                    Tq.op(lambda e, c=c, h5=h5, cs=cs, psum_s=psum_s: e.matmul(
                        psum_s[0:1, h5 * 512:h5 * 512 + 512], onescol_bf[:],
                        src_sb[:, c * T + cs:c * T + cs + 512],
                        start=(c == 0), stop=(c == NCH - 1)), waits=w)
                for h5 in range(2):
                    cs = th * 1024 + h5 * 512
                    Tq.op(lambda e, c=c, h5=h5, cs=cs, psum_q=psum_q: e.matmul(
                        psum_q[0:1, h5 * 512:h5 * 512 + 512], onescol_bf[:],
                        sq_b[:, (c % 2) * T + cs:(c % 2) * T + cs + 512],
                        start=(c == 0), stop=(c == NCH - 1)), waits=[sqtoks[c]])
            Tq.retro_inc(t_s.tick())
            sq_read_t[c] = t_s.at()
        t_stats = t_s.at()
        out = {}
        for th in range(2):
            psum_s = (pA, pB)[th]
            psum_q = (pC, pD)[th]
            seg = slice(th * 1024, th * 1024 + 1024)
            Vq.op(lambda e, psum_s=psum_s, seg=seg: e.tensor_scalar(
                B_row[0:1, seg], psum_s[0:1, 0:1024], 1.0 / D, None, ALU.mult),
                waits=[t_stats] + guards, inc=v_s.tick())
            Vq.op(lambda e, seg=seg: e.tensor_tensor(
                R_row[0:1, seg], B_row[0:1, seg], B_row[0:1, seg],
                ALU.mult), inc=v_s.tick())
            Vq.op(lambda e, psum_q=psum_q, seg=seg: e.scalar_tensor_tensor(
                psum_q[0:1, 0:1024], psum_q[0:1, 0:1024], 1.0 / D,
                R_row[0:1, seg], ALU.mult, ALU.subtract), inc=v_s.tick())
            Vq.op(lambda e, psum_q=psum_q: e.tensor_scalar(
                psum_q[0:1, 0:1024], psum_q[0:1, 0:1024], eps_mul, eps_add,
                ALU.mult, ALU.add), inc=v_s.tick())
            v_rows = v_s.at()
            Aq.op(lambda e, psum_s=psum_s, psum_q=psum_q: e.activation(
                psum_s[0:1, 0:1024], psum_q[0:1, 0:1024], AF.Ln),
                waits=[v_rows] + guards, inc=a_s.tick())
            Aq.op(lambda e, psum_s=psum_s, seg=seg: e.activation(
                R_row[0:1, seg], psum_s[0:1, 0:1024], AF.Exp, scale=-0.5),
                inc=a_s.tick())
            a_R = a_s.at()
            Vq.op(lambda e, seg=seg: e.scalar_tensor_tensor(
                B_row[0:1, seg], B_row[0:1, seg], -1.0,
                R_row[0:1, seg], ALU.mult, ALU.mult),
                waits=[a_R], inc=v_s.tick())
        v_rowsdone = v_s.at()
        bc_evac = {0: None, 1: None}
        for row, dstb in [(R_row, Rb_b), (B_row, Bb_b)]:
            for th in range(2):
                for h5 in range(2):
                    pdst = (pC, pD)[h5]
                    cs = th * 1024 + h5 * 512
                    Tq.op(lambda e, row=row, cs=cs, pdst=pdst: e.matmul(
                        pdst[:, 0:512], onesrow_bf[0:1, :],
                        row[0:1, cs:cs + 512],
                        start=True, stop=True),
                        waits=[v_rowsdone, bc_evac[h5]] + guards, inc=t_s.tick())
                    tb = t_s.at()
                    Vq.op(lambda e, dstb=dstb, cs=cs, pdst=pdst: e.tensor_copy(
                        dstb[:, cs:cs + 512],
                        pdst[:, 0:512]), waits=[tb], inc=v_s.tick())
                    bc_evac[h5] = v_s.at()
        out["Rb"] = v_s.at()
        out["t_end"] = t_s.at()
        return out

    prevs = {
        "out_dma": [setup_d, None],
        "slotA_free": [setup_t],
        "slotB_free": [],
        "slotC_free": [],
        "slotA_q_free": [],
        "vbuf_free": [],
        "stream_free": [],
        "ln_guards": [setup_v],
    }
    asm_guard = [None, None]
    ctxstage_guard = [None]
    x_first_guard = [setup_t, setup_d, setup_v, setup_g]

    for s in range(NSEQ):
        cast_toks = []
        xload = []
        for c in range(NCH):
            w = list(prevs["out_dma"]) + (x_first_guard if s == 0 else [])
            if c >= 2:
                w.append(cast_toks[c - 2])
            tok = dma(xstage[:, (c % 2) * T:(c % 2) * T + T],
                      x_d[s, c * 128:(c + 1) * 128, :], d_x[c % 2], waits=w)
            xload.append(tok)
            Aq.op(lambda e, c=c: e.activation(
                slotA[:, c * T:(c + 1) * T], xstage[:, (c % 2) * T:(c % 2) * T + T],
                AF.Copy),
                waits=[tok] + prevs["slotA_free"], inc=a_s.tick())
            cast_toks.append(a_s.at())
        ones_w = [cast_toks[-1]] + prevs["vbuf_free"] + prevs["stream_free"]
        if s == 0:
            ones_w += [setup_d, setup_t]
        Gq.op(lambda e: e.memset(vput[:, :, :, 64:65], 1.0), waits=ones_w,
              inc=g_s.tick())
        Gq.op(lambda e: e.memset(kt[0][0:1, :], 1.0), inc=g_s.tick())
        Gq.op(lambda e: e.memset(kt[1][0:1, :], 1.0), inc=g_s.tick())
        ones_tok = g_s.at()
        ln1 = emit_ln(slotA, lambda c: cast_toks[c], 1.0 + EPS, EPS * EPS,
                      prevs["ln_guards"])
        xn_toks = []
        for c in range(NCH):
            Vq.op(lambda e, c=c: e.tensor_tensor(
                tmp_b[:, (c % 2) * T:(c % 2) * T + T],
                slotA[:, c * T:(c + 1) * T], Rb_b[:], ALU.mult),
                waits=[ln1["Rb"], cast_toks[c]], inc=v_s.tick())
            Vq.op(lambda e, c=c: e.tensor_tensor(
                slotC[:, c * T:(c + 1) * T],
                tmp_b[:, (c % 2) * T:(c % 2) * T + T], Bb_b[:], ALU.add),
                waits=prevs["slotC_free"], inc=v_s.tick())
            xn_toks.append(v_s.at())
        xn_done = v_s.at()

        qk_evac = {}
        last_st_read = [None, None]
        first_stream = True
        for th in range(2):
            for mc in range(16):
                g = th * 16 + mc
                sbh = g % 2
                w = [setup_d, last_st_read[sbh]]
                if first_stream:
                    w += prevs["stream_free"]
                    first_stream = False
                wtok = dma(wqk_st[:, sbh * 1024:sbh * 1024 + 1024]
                           .rearrange("p (kc f) -> p kc f", kc=8),
                           wqs_d[mc, :, :, :].rearrange("kc p f -> p kc f"),
                           d_w[sbh], waits=w)
                pt = pAll[g % 4]
                wfirst = [wtok, xn_done, ln1["t_end"]]
                if g - 4 >= 0:
                    wfirst.append(qk_evac[g - 4])
                for kc in range(NCH):
                    for h5 in range(2):
                        Tq.op(lambda e, mc=mc, kc=kc, th=th, h5=h5, pt=pt, sbh=sbh: e.matmul(
                            pt[:, h5 * 512:h5 * 512 + 512],
                            wqk_st[:, sbh * 1024 + kc * 128:sbh * 1024 + kc * 128 + 128],
                            slotC[:, kc * T + th * 1024 + h5 * 512:
                                  kc * T + th * 1024 + h5 * 512 + 512],
                            start=(kc == 0), stop=(kc == NCH - 1)),
                            waits=(wfirst if (kc == 0 and h5 == 0) else []))
                Tq.retro_inc(t_s.tick())
                t_grp = t_s.at()
                last_st_read[sbh] = t_grp
                dst = slotA if mc < 8 else slotB
                dmc = mc if mc < 8 else mc - 8
                wev = [t_grp]
                wev += prevs["slotA_q_free"] if mc < 8 else prevs["slotB_free"]
                Aq.op(lambda e, dst=dst, dmc=dmc, th=th, pt=pt, mc=mc: e.activation(
                    dst[:, dmc * T + th * 1024:dmc * T + th * 1024 + 1024], pt[:],
                    AF.Identity, bias=c2_col[:, mc:mc + 1]),
                    waits=wev, inc=a_s.tick())
                qk_evac[g] = a_s.at()
        qk_done = a_s.at()
        t_qk = t_s.at()


        asm_tok = {}

        def emit_asm(h, extra):
            wk = [qk_done, asm_guard[h % 2], extra]
            dma(kt[h % 2][1:65, :],
                slotB[(h % 2) * 64:(h % 2) * 64 + 64, (h // 2) * T:(h // 2) * T + T],
                d_kq[h % 2], waits=wk)
            dma(qt[h % 2][1:65, :],
                slotA[(h % 2) * 64:(h % 2) * 64 + 64, (h // 2) * T:(h // 2) * T + T],
                d_kq[h % 2], waits=[])
            asm_tok[h] = d_kq[h % 2].at()

        emit_asm(0, None)
        emit_asm(1, None)

        st = {"sprobe": None, "mrow": None,
              "aT": [None, None], "pT": [None, None], "ctx": None,
              "qrow": {}, "t_av": None}

        def gen_A(h, half):
            q0 = half * 1024
            mcol_toks = []
            for idx, i in enumerate(range(8 * half, 8 * half + 8)):
                row_w = (i + 1) * 128
                nchunks = 0
                for ck0 in range(0, row_w, 1024):
                    ck1 = min(ck0 + 1024, row_w)
                    cw = ck1 - ck0
                    w = [qk_done, st["sprobe"], ln1["Rb"]]
                    if cw > 512:
                        w.append(st["mrow"])
                    subs = [(s0, min(s0 + 512, ck1)) for s0 in range(ck0, ck1, 512)]
                    for si, (s0, s1) in enumerate(subs):
                        Tq.op(lambda e, h=h, i=i, s0=s0, s1=s1, b0=s0 - ck0: e.matmul(
                            pA[:, b0:b0 + s1 - s0],
                            slotA[(h % 2) * 64:(h % 2) * 64 + 64,
                                  (h // 2) * T + i * 128:(h // 2) * T + i * 128 + 128],
                            slotB[(h % 2) * 64:(h % 2) * 64 + 64,
                                  (h // 2) * T + s0:(h // 2) * T + s1],
                            start=True, stop=True),
                            waits=(w if si == 0 else []),
                            inc=(t_s.tick() if si == len(subs) - 1 else None))
                    tmm = t_s.at()
                    Vq.op(lambda e, nch=nchunks, cw=cw: e.reduce_max(
                        mtmp[:, nch:nch + 1], pA[:, 0:cw], axis=AX.X),
                        waits=[tmm], inc=v_s.tick())
                    st["sprobe"] = v_s.at()
                    nchunks += 1
                Vq.op(lambda e, idx=idx, nch=nchunks: e.reduce_max(
                    mneg[:, idx:idx + 1], mtmp[:, 0:nch], axis=AX.X, negate=True),
                    inc=v_s.tick())
                mcol_toks.append(v_s.at())
                yield
            for qtr in range(2):
                for r in range(4):
                    w = [mcol_toks[qtr * 4 + r]]
                    if r == 0:
                        w += [st["mrow"], st["sprobe"]]
                    Tq.op(lambda e, qtr=qtr, r=r: e.matmul(
                        pA[0:1, 512 + r * 128:512 + r * 128 + 128],
                        mneg[:, qtr * 4 + r:qtr * 4 + r + 1], ident_bf[:],
                        start=True, stop=True),
                        waits=w, inc=(t_s.tick() if r == 3 else None))
                t_mrow = t_s.at()
                Aq.op(lambda e, h=h, q0=q0, qtr=qtr: e.activation(
                    qt[h % 2][0:1, q0 + qtr * 512:q0 + qtr * 512 + 512],
                    pA[0:1, 512:1024], AF.Copy),
                    waits=[t_mrow, asm_tok[h]], inc=a_s.tick())
                st["mrow"] = a_s.at()
                yield
            st["qrow"][(h, half)] = a_s.at()

        def gen_B(h, half):
            q0 = half * 1024
            qrow_done = st["qrow"][(h, half)]
            strips = list(range(0, 8 * (half + 1)))
            first_j, last_j = strips[0], strips[-1]
            pend = None

            def emit_av(pend):
                cks, bi, a_exp, diag_tok, j = pend
                wv = [a_exp, diag_tok, ones_tok]
                if j == first_j:
                    wv.append(st["ctx"])
                for ci, (c0, c1) in enumerate(cks):
                    lastj_chunk = min(last_j, (q0 + c1 - 1) // 128)
                    Tq.op(lambda e, h=h, j=j, c0=c0, c1=c1, bi=bi,
                          sp=(j == lastj_chunk): e.matmul(
                        pD[0:65, c0:c1],
                        vbuf[:, j * NH * 65 + h * 65:j * NH * 65 + h * 65 + 65],
                        aT[bi][:, c0:c1],
                        start=(j == first_j), stop=sp),
                        waits=(wv if ci == 0 else []),
                        inc=(t_s.tick() if ci == len(cks) - 1 else None))
                st["aT"][bi] = t_s.at()

            for j in strips:
                rel0 = max(j * 128 - q0, 0)
                bi = j % 2
                buf = (pB, pC)[bi]
                cks = []
                c0 = rel0
                while c0 < 1024:
                    c1 = min((c0 // 512 + 1) * 512, 1024)
                    cks.append((c0, c1))
                    c0 = c1
                wv = [qrow_done, asm_tok[h], st["pT"][bi], ones_tok]
                for ci, (c0, c1) in enumerate(cks):
                    Tq.op(lambda e, h=h, j=j, c0=c0, c1=c1, buf=buf, q0=q0: e.matmul(
                        buf[:, c0:c1],
                        kt[h % 2][0:65, j * 128:j * 128 + 128],
                        qt[h % 2][0:65, q0 + c0:q0 + c1],
                        start=True, stop=True),
                        waits=(wv if ci == 0 else []),
                        inc=(t_s.tick() if ci == len(cks) - 1 else None))
                t_st = t_s.at()
                Aq.op(lambda e, bi=bi, rel0=rel0, buf=buf: e.activation(
                    aT[bi][:, rel0:1024], buf[:, rel0:1024], AF.Exp, scale=0.125),
                    waits=[t_st, st["aT"][bi]], inc=a_s.tick())
                st["pT"][bi] = a_s.at()
                a_exp = a_s.at()
                diag_tok = None
                if j >= 8 * half:
                    Gq.op(lambda e, bi=bi, rel0=rel0: e.tensor_tensor(
                        aT[bi][:, rel0:rel0 + 128], aT[bi][:, rel0:rel0 + 128],
                        maskt_sb[:], ALU.mult),
                        waits=[a_exp], inc=g_s.tick())
                    diag_tok = g_s.at()
                if pend is not None:
                    emit_av(pend)
                pend = (cks, bi, a_exp, diag_tok, j)
                yield
            emit_av(pend)
            t_av = t_s.at()
            st["t_av"] = t_av

            Aq.op(lambda e: e.activation(rball[64:65, :], pD[64:65, :], AF.Ln),
                  waits=[t_av], inc=a_s.tick())
            Aq.op(lambda e: e.activation(rrow_bf[64:65, :], rball[64:65, :],
                                         AF.Exp, scale=-1.0), inc=a_s.tick())
            a_r = a_s.at()
            for h5 in range(2):
                Tq.op(lambda e, h5=h5: e.matmul(
                    pB[0:64, h5 * 512:h5 * 512 + 512], ones64[64:65, :],
                    rrow_bf[64:65, h5 * 512:h5 * 512 + 512], start=True, stop=True),
                    waits=([a_r, st["pT"][0]] if h5 == 0 else []),
                    inc=(t_s.tick() if h5 == 1 else None))
            t_rb = t_s.at()
            Aq.op(lambda e: e.activation(rball[0:64, :], pB[0:64, :], AF.Copy),
                  waits=[t_rb], inc=a_s.tick())
            v_rb = a_s.at()
            st["pT"][0] = v_rb
            if h % 2 == 0:
                Vq.op(lambda e, h=h, q0=q0: e.tensor_tensor(
                    slotC[0:64, (h // 2) * T + q0:(h // 2) * T + q0 + 1024],
                    pD[0:64, :], rball[0:64, :], ALU.mult),
                    waits=[v_rb], inc=v_s.tick())
                st["ctx"] = v_s.at()
            else:
                Vq.op(lambda e: e.tensor_tensor(
                    ctxstage[:], pD[0:64, :], rball[0:64, :], ALU.mult),
                    waits=[v_rb, ctxstage_guard[0]], inc=v_s.tick())
                st["ctx"] = v_s.at()
                tok = dma(slotC[64:128, (h // 2) * T + q0:(h // 2) * T + q0 + 1024],
                          ctxstage[:], d_cs, waits=[st["ctx"]])
                ctxstage_guard[0] = tok
            if half == 1:
                asm_guard[h % 2] = t_av
                if h + 2 < NH:
                    emit_asm(h + 2, t_av)
            yield

        pairs = [(h, hf) for h in range(NH) for hf in range(2)]
        gA0 = gen_A(*pairs[0])
        gA0_done = [False]

        def pump():
            if not gA0_done[0]:
                try:
                    next(gA0)
                except StopIteration:
                    gA0_done[0] = True
        v_ev = {}
        last_v_read = None
        for vh in range(2):
            w = [setup_d, t_qk if vh == 0 else last_v_read] + list(prevs["out_dma"])
            wtok = dma(wv_st[:].rearrange("p (km f) -> p km f", km=32),
                       wvs_d[vh, :, :, :, :].rearrange("kc mc p f -> p (kc mc) f"),
                       d_wv, waits=w)
            for tt in range(NT):
                gidx = vh * NT + tt
                pt = (pB, pC, pD)[gidx % 3]
                wfirst = [wtok, xn_done]
                if gidx - 3 >= 0:
                    wfirst.append(v_ev[gidx - 3])
                else:
                    wfirst.append(qk_done)
                for kc in range(NCH):
                    Tq.op(lambda e, tt=tt, kc=kc, pt=pt: e.matmul(
                        pt[:, 0:512],
                        slotC[:, kc * T + tt * 128:kc * T + tt * 128 + 128],
                        wv_st[:, kc * 512:kc * 512 + 512],
                        start=(kc == 0), stop=(kc == NCH - 1)),
                        waits=(wfirst if kc == 0 else []))
                Tq.retro_inc(t_s.tick())
                t_grp = t_s.at()
                last_v_read = t_grp
                dstv = vbuf[:, tt * NH * 65:(tt + 1) * NH * 65] \
                    .rearrange("p (h f) -> p h f", h=NH)[:, vh * 8:vh * 8 + 8, 0:64]
                srcv = pt[:, 0:512].rearrange("p (h f) -> p h f", h=8)
                Aq.op(lambda e, dstv=dstv, srcv=srcv: e.activation(dstv, srcv, AF.Copy),
                      waits=[t_grp, setup_g] + prevs["vbuf_free"] + list(prevs["out_dma"]),
                      inc=a_s.tick())
                v_ev[gidx] = a_s.at()
                pump()
                pump()
        v_done = a_s.at()
        t_v = t_s.at()
        st["pT"] = [v_done, v_done]
        st["ctx"] = v_done
        st["aT"] = [st["aT"][0], st["aT"][1]]
        for _ in gA0:
            pass
        gAn = None
        for k, pr in enumerate(pairs):
            gB = gen_B(*pr)
            gAn = gen_A(*pairs[k + 1]) if k + 1 < len(pairs) else None
            nstrips = 8 * (pr[1] + 1)
            if nstrips >= 10:
                sched = [0] * (nstrips - 10) + [1] * 10
            else:
                sched = [1] * nstrips
                for i in range(10 - nstrips):
                    sched[-1 - (i % 2)] += 1
            yi = 0
            for _ in gB:
                nsteps = sched[yi] if yi < len(sched) else 1
                yi += 1
                for _ in range(nsteps):
                    if gAn is not None:
                        try:
                            next(gAn)
                        except StopIteration:
                            gAn = None
            if gAn is not None:
                for _ in gAn:
                    pass
                gAn = None
        t_av_last = st["t_av"]
        v_p3 = v_s.at()
        a_p3 = a_s.at()
        d_p3 = [d_kq[0].at(), d_kq[1].at(), d_cs.at()]
        t_p3 = t_s.at()

        ao_evac = {}
        last_wo_read = [None, None]
        for mc in range(8):
            sbh = mc % 2
            w = [setup_d, last_wo_read[sbh]]
            if mc == 0:
                w += [t_p3]
            wtok = dma(wo_st[:, sbh * 1024:sbh * 1024 + 1024]
                       .rearrange("p (kc f) -> p kc f", kc=8),
                       wos_d[mc, :, :, :].rearrange("kc p f -> p kc f"),
                       d_w[sbh], waits=w)
            pts = (pA, pB) if mc % 2 == 0 else (pC, pD)
            wfirst = [wtok, v_p3, a_p3, d_p3]
            if mc >= 2:
                wfirst.append(ao_evac[mc - 2])
            for kc in range(NCH):
                for sl in range(4):
                    pt = pts[sl // 2]
                    Tq.op(lambda e, mc=mc, kc=kc, sl=sl, pt=pt, sbh=sbh: e.matmul(
                        pt[:, (sl % 2) * 512:(sl % 2) * 512 + 512],
                        wo_st[:, sbh * 1024 + kc * 128:sbh * 1024 + kc * 128 + 128],
                        slotC[:, kc * T + sl * 512:kc * T + sl * 512 + 512],
                        start=(kc == 0), stop=(kc == NCH - 1)),
                        waits=(wfirst if (kc == 0 and sl == 0) else []))
            Tq.retro_inc(t_s.tick())
            t_grp = t_s.at()
            last_wo_read[sbh] = t_grp
            for halfp in range(2):
                Aq.op(lambda e, mc=mc, halfp=halfp, pts=pts: e.activation(
                    slotA[:, mc * T + halfp * 1024:mc * T + halfp * 1024 + 1024],
                    pts[halfp][:], AF.Identity, bias=c3_col[:, mc:mc + 1]),
                    waits=[t_grp, d_p3], inc=a_s.tick())
            ao_evac[mc] = a_s.at()
        ao_done = a_s.at()
        t_wo = t_s.at()

        ln3 = emit_ln(slotA, lambda c: ao_evac[c], 1.0, EPS, [ao_done, v_p3])
        x2_toks = []
        for c in range(NCH):
            w = [cast_toks[-1], t_av_last] + ([x2_toks[c - 2]] if c >= 2 else [ln3["Rb"]])
            xtok = dma(xstage[:, (c % 2) * T:(c % 2) * T + T],
                       x_d[s, c * 128:(c + 1) * 128, :], d_x[c % 2], waits=w)
            Vq.op(lambda e, c=c: e.tensor_tensor(
                tmp_b[:, (c % 2) * T:(c % 2) * T + T],
                slotA[:, c * T:(c + 1) * T], Rb_b[:], ALU.mult),
                waits=[ln3["Rb"]] + ([x2_toks[c - 2]] if c >= 2 else []),
                inc=v_s.tick())
            Vq.op(lambda e, c=c: e.tensor_tensor(
                tmp_b[:, (c % 2) * T:(c % 2) * T + T],
                tmp_b[:, (c % 2) * T:(c % 2) * T + T], Bb_b[:], ALU.add),
                inc=v_s.tick())
            Vq.op(lambda e, c=c: e.tensor_tensor(
                slotB[:, c * T:(c + 1) * T],
                tmp_b[:, (c % 2) * T:(c % 2) * T + T],
                xstage[:, (c % 2) * T:(c % 2) * T + T], ALU.add),
                waits=[xtok, t_wo, d_p3], inc=v_s.tick())
            x2_toks.append(v_s.at())
        x2_done = v_s.at()

        ln4 = emit_ln(slotB, lambda c: x2_toks[c], 1.0, EPS, [x2_done, ao_done])
        xn2_toks = []
        for c in range(NCH):
            Vq.op(lambda e, c=c: e.tensor_tensor(
                tmp_b[:, (c % 2) * T:(c % 2) * T + T],
                slotB[:, c * T:(c + 1) * T], Rb_b[:], ALU.mult),
                waits=[ln4["Rb"]], inc=v_s.tick())
            Vq.op(lambda e, c=c: e.tensor_tensor(
                slotA[:, c * T:(c + 1) * T],
                tmp_b[:, (c % 2) * T:(c % 2) * T + T], Bb_b[:], ALU.add),
                inc=v_s.tick())
            xn2_toks.append(v_s.at())
        xn2_done = v_s.at()

        s_ev = []
        for sl in range(4):
            pt = (pA, pA, pB, pB)[sl]
            off = (sl % 2) * 512
            for kc in range(NCH):
                Tq.op(lambda e, sl=sl, kc=kc, pt=pt, off=off: e.matmul(
                    pt[0:64, off:off + 512],
                    win_sb[:, kc * 64:kc * 64 + 64],
                    slotA[:, kc * T + sl * 512:kc * T + sl * 512 + 512],
                    start=(kc == 0), stop=(kc == NCH - 1)),
                    waits=([xn2_done, ln4["t_end"]] if kc == 0 else []))
            Tq.retro_inc(t_s.tick())
            t_grp = t_s.at()
            if sl % 2 == 1:
                pth = (pA, pB)[sl // 2]
                Aq.op(lambda e, sl=sl, pth=pth: e.activation(
                    s_sb[:, (sl - 1) * 512:(sl + 1) * 512], pth[0:64, :],
                    AF.Identity, bias=bin_sb[:]),
                    waits=[t_grp, d_p3], inc=a_s.tick())
                s_ev.append(a_s.at())
        s_done = a_s.at()
        g_q = []
        for q4 in range(4):
            qs = slice(q4 * 512, q4 * 512 + 512)
            for i in range(16):
                Vq.op(lambda e, i=i, qs=qs: e.tensor_scalar(
                    tfn[0:64, qs], s_sb[0:64, qs], w1_sb[:, i:i + 1],
                    b1_sb[:, i:i + 1], ALU.mult, ALU.add),
                    waits=([s_done] if i == 0 else []), inc=v_s.tick())
                dst = g_sb if i == 0 else u_fn
                Vq.op(lambda e, dst=dst, i=i, qs=qs: e.tensor_scalar(
                    dst[0:64, qs], tfn[0:64, qs], 0.0, w2_sb[:, i:i + 1],
                    ALU.max, ALU.mult), inc=v_s.tick())
                if i > 0:
                    Vq.op(lambda e, qs=qs: e.tensor_tensor(
                        g_sb[0:64, qs], g_sb[0:64, qs], u_fn[0:64, qs], ALU.add),
                        inc=v_s.tick())
            Vq.op(lambda e, qs=qs: e.tensor_scalar(
                gb_sb[0:64, qs], g_sb[0:64, qs], b2_sb[:], None, ALU.add),
                inc=v_s.tick())
            g_q.append(v_s.at())
        g_done = v_s.at()
        fo_ev = {}
        for mc in range(8):
            pts = (pA, pB) if mc % 2 == 0 else (pC, pD)
            for sl in range(4):
                pt = pts[sl // 2]
                w = [g_q[sl]]
                if sl == 0 and mc >= 2:
                    w.append(fo_ev[mc - 2])
                Tq.op(lambda e, mc=mc, sl=sl, pt=pt: e.matmul(
                    pt[:, (sl % 2) * 512:(sl % 2) * 512 + 512],
                    wout_sb[0:64, mc * 128:mc * 128 + 128],
                    gb_sb[:, sl * 512:sl * 512 + 512],
                    start=True, stop=True),
                    waits=w)
            Tq.retro_inc(t_s.tick())
            t_grp = t_s.at()
            for halfp in range(2):
                Aq.op(lambda e, mc=mc, halfp=halfp, pts=pts: e.activation(
                    slotA[:, mc * T + halfp * 1024:mc * T + halfp * 1024 + 1024],
                    pts[halfp][:], AF.Identity, bias=bout_col[:, mc:mc + 1]),
                    waits=[t_grp, xn2_done], inc=a_s.tick())
            fo_ev[mc] = a_s.at()
        fo_done = a_s.at()

        ln5 = emit_ln(slotA, lambda c: fo_ev[c], 1.0, EPS, [fo_done, x2_done])
        out_toks = []
        out_g = []
        for c in range(NCH):
            Vq.op(lambda e, c=c: e.tensor_tensor(
                tmp_b[:, (c % 2) * T:(c % 2) * T + T],
                slotA[:, c * T:(c + 1) * T], Rb_b[:], ALU.mult),
                waits=[ln5["Rb"]] + ([out_g[c - 2]] if c >= 2 else []),
                inc=v_s.tick())
            Vq.op(lambda e, c=c: e.tensor_tensor(
                tmp_b[:, (c % 2) * T:(c % 2) * T + T],
                tmp_b[:, (c % 2) * T:(c % 2) * T + T], Bb_b[:], ALU.add),
                inc=v_s.tick())
            w = [out_toks[c - 2]] if c >= 2 else [xn2_done]
            Vq.op(lambda e, c=c: e.tensor_tensor(
                xstage[:, (c % 2) * T:(c % 2) * T + T],
                tmp_b[:, (c % 2) * T:(c % 2) * T + T],
                slotB[:, c * T:(c + 1) * T], ALU.add),
                waits=w, inc=v_s.tick())
            v_out = v_s.at()
            out_g.append(v_out)
            tok = dma(y_d[s, c * 128:(c + 1) * 128, :],
                      xstage[:, (c % 2) * T:(c % 2) * T + T], d_out, waits=[v_out])
            out_toks.append(tok)
        v_p6 = v_s.at()
        t_ln5 = ln5["t_end"]

        g_p6 = g_s.at()
        prevs = {
            "out_dma": [out_toks[-1]],
            "slotA_free": [v_p6, t_ln5],
            "slotB_free": [v_p6, g_p6],
            "slotC_free": [t_wo],
            "slotA_q_free": [t_p3, d_p3],
            "vbuf_free": [t_av_last],
            "stream_free": [v_p6, fo_done],
            "ln_guards": [v_p6, fo_done],
        }

    nc._dbg = dict(slotA=slotA, slotB=slotB, slotC=slotC, vbuf=vbuf, kt0=kt[0],
                   kt1=kt[1], qt0=qt[0], qt1=qt[1], c2_col=c2_col, c3_col=c3_col,
                   Rb_b=Rb_b, Bb_b=Bb_b, R_row=R_row, B_row=B_row, s_sb=s_sb,
                   g_sb=g_sb, gb_sb=gb_sb, rball=rball, mneg=mneg, aT0=aT[0],
                   aT1=aT[1], win_sb=win_sb, wout_sb=wout_sb, tfn=tfn, u_fn=u_fn, g=g_sb, w1=w1_sb, b1=b1_sb, w2=w2_sb)

    Sq.op(lambda e: None, waits=[d_out.at()])

    with nc.Block() as block:
        @block.tensor
        def _(e):
            run_ops(e, Tq.ops)

        @block.vector
        def _(e):
            run_ops(e, Vq.ops)

        @block.scalar
        def _(e):
            run_ops(e, Aq.ops)

        @block.gpsimd
        def _(e):
            run_ops(e, Gq.ops)

        @block.sync
        def _(e):
            run_ops(e, Sq.ops)

    return nc


def kernel(L0_attn_x, L0_attn_ln_gamma, L0_attn_ln_beta, L0_attn_W_qkv, L0_attn_W_o,
           L0_ffn_Win, L0_ffn_bin_, L0_ffn_W1, L0_ffn_b1, L0_ffn_W2, L0_ffn_b2,
           L0_ffn_Wout, L0_ffn_bout):
    x = np.asarray(L0_attn_x, dtype=np.float32)
    gamma = np.asarray(L0_attn_ln_gamma, dtype=np.float32)
    beta = np.asarray(L0_attn_ln_beta, dtype=np.float32)
    wqkv = np.asarray(L0_attn_W_qkv, dtype=np.float32)
    wo = np.asarray(L0_attn_W_o, dtype=np.float32)
    win = np.asarray(L0_ffn_Win, dtype=np.float32)
    bin_ = np.asarray(L0_ffn_bin_, dtype=np.float32)
    w1 = np.asarray(L0_ffn_W1, dtype=np.float32)
    b1 = np.asarray(L0_ffn_b1, dtype=np.float32)
    w2 = np.asarray(L0_ffn_W2, dtype=np.float32)
    b2 = np.asarray(L0_ffn_b2, dtype=np.float32)
    wout = np.asarray(L0_ffn_Wout, dtype=np.float32)
    bout = np.asarray(L0_ffn_bout, dtype=np.float32)

    idx = []
    for blk in range(3):
        for h in range(NH):
            base = h * 192 + blk * 64
            idx.extend(range(base, base + 64))
    wqkvt = np.ascontiguousarray(wqkv[np.array(idx)].T)
    wot = np.ascontiguousarray(wo.T)
    gcol = np.ascontiguousarray(gamma.reshape(8, 128).T)
    bcol = np.ascontiguousarray(beta.reshape(8, 128).T)
    boutcol = np.ascontiguousarray(bout.reshape(8, 128).T)
    bincol = np.ascontiguousarray(bin_.reshape(64, 1))
    b2col = np.ascontiguousarray(b2.reshape(64, 1))
    ident = np.eye(128, dtype=np.float32)
    kq = np.arange(128)
    maskt = np.ascontiguousarray((kq[:, None] <= kq[None, :]).astype(np.float32))
    maskn = np.ascontiguousarray(
        np.where(kq[None, :] > kq[:, None], np.float32(-1e30), np.float32(0.0)))

    nc = build_program()
    in_maps = []
    for c in range(NCORE):
        xs = np.ascontiguousarray(np.transpose(x[c * NSEQ:(c + 1) * NSEQ], (0, 2, 1)))
        in_maps.append(dict(
            x=xs, wqkvt=wqkvt, wot=wot, win=win, wout=wout,
            gcol=gcol, bcol=bcol, bincol=bincol, w1=w1, b1=b1, w2=w2,
            b2col=b2col, boutcol=boutcol, ident=ident, maskt=maskt, maskn=maskn,
        ))
    res = run_bass_kernel_spmd(nc, in_maps, core_ids=list(range(NCORE)))
    outs = []
    for c in range(NCORE):
        yc = res.results[c]["y"]
        outs.append(np.transpose(yc, (0, 2, 1)))
    return np.ascontiguousarray(np.concatenate(outs, axis=0))


# version 26
# speedup vs baseline: 1.0959x; 1.0959x over previous
import numpy as np
import ml_dtypes
import concourse.bass as bass
import concourse.mybir as mybir
from concourse.bass_utils import run_bass_kernel_spmd

F32 = mybir.dt.float32
BF16 = mybir.dt.bfloat16
AF = mybir.ActivationFunctionType
ALU = mybir.AluOpType
AX = mybir.AxisListType

NCORE = 8
NSEQ = 4
T = 2048
D = 1024
NH = 16
DH = 64
NCH = 8
NT = 16
EPS = 1e-5


class Ctr:
    def __init__(self, nc, name):
        self.h = nc.alloc_semaphore(name)
        self.n = 0

    def tick(self, k=1):
        self.n += k
        return (self.h, k)

    def at(self):
        return (self.h, self.n)


class Eng:
    def __init__(self):
        self.ops = []

    def op(self, fn, waits=(), inc=None):
        ws = []
        for w in waits:
            if w is None:
                continue
            if isinstance(w, list):
                ws.extend(t for t in w if t is not None)
            else:
                ws.append(w)
        self.ops.append((tuple(ws), fn, inc))

    def retro_inc(self, inc):
        w, f, old = self.ops[-1]
        assert old is None
        self.ops[-1] = (w, f, inc)


def run_ops(engine, ops):
    waited = {}
    for waits, fn, inc in ops:
        for sem, val in waits:
            if val > 0 and waited.get(id(sem), -1) < val:
                engine.wait_ge(sem, val)
                waited[id(sem)] = val
        ins = fn(engine)
        if inc is not None:
            sem, k = inc
            ins.then_inc(sem, k)


def build_program(nc=None):
    if nc is None:
        nc = bass.Bass("TRN2", target_bir_lowering=False)

    x_d = nc.dram_tensor("x", [NSEQ, D, T], F32, kind="ExternalInput").ap()
    y_d = nc.dram_tensor("y", [NSEQ, D, T], F32, kind="ExternalOutput").ap()
    wqkvt_d = nc.dram_tensor("wqkvt", [D, 3 * D], F32, kind="ExternalInput").ap()
    wot_d = nc.dram_tensor("wot", [D, D], F32, kind="ExternalInput").ap()
    win_d = nc.dram_tensor("win", [D, 64], F32, kind="ExternalInput").ap()
    wout_d = nc.dram_tensor("wout", [64, D], F32, kind="ExternalInput").ap()
    gcol_d = nc.dram_tensor("gcol", [128, 8], F32, kind="ExternalInput").ap()
    bcol_d = nc.dram_tensor("bcol", [128, 8], F32, kind="ExternalInput").ap()
    bincol_d = nc.dram_tensor("bincol", [64, 1], F32, kind="ExternalInput").ap()
    w1_d = nc.dram_tensor("w1", [64, 16], F32, kind="ExternalInput").ap()
    b1_d = nc.dram_tensor("b1", [64, 16], F32, kind="ExternalInput").ap()
    w2_d = nc.dram_tensor("w2", [64, 16], F32, kind="ExternalInput").ap()
    b2col_d = nc.dram_tensor("b2col", [64, 1], F32, kind="ExternalInput").ap()
    boutcol_d = nc.dram_tensor("boutcol", [128, 8], F32, kind="ExternalInput").ap()
    ident_d = nc.dram_tensor("ident", [128, 128], F32, kind="ExternalInput").ap()
    maskt_d = nc.dram_tensor("maskt", [128, 128], F32, kind="ExternalInput").ap()
    maskn_d = nc.dram_tensor("maskn", [128, 128], F32, kind="ExternalInput").ap()
    wqs_d = nc.dram_tensor("wqs", [24, 8, 128, 128], BF16, kind="Internal").ap()
    wos_d = nc.dram_tensor("wos", [8, 8, 128, 128], BF16, kind="Internal").ap()
    wvs_d = nc.dram_tensor("wvs", [2, 8, 4, 128, 128], BF16, kind="Internal").ap()

    BASE = 18464
    def at(name, shape, dt, off):
        return nc.alloc_sbuf_tensor_at(name, shape, dt, offset=BASE + off).ap()

    slotA = at("slotA", [128, NCH * T], BF16, 0)
    slotB = at("slotB", [128, NCH * T], BF16, 32768)
    slotC = at("slotC", [128, NCH * T], BF16, 65536)
    VOFF = 98304
    vbuf = at("vbuf", [128, NT * NH * 65], BF16, VOFF)
    wstage2 = [at("wstage0", [128, 3 * D], F32, VOFF),
               at("wstage1", [128, 3 * D], F32, VOFF + 12288)]
    wbstage = at("wbstage", [128, 3 * D], BF16, VOFF + 24576)
    xstage = at("xstage", [128, 2 * T], F32, VOFF)
    FOFF = 131584
    s_sb = at("s_sb", [64, T], F32, FOFF)
    tfn = at("tfn", [64, T], F32, FOFF + 8192)
    u_fn = at("u_fn", [64, T], F32, FOFF + 16384)
    g_sb = at("g_sb", [64, T], F32, FOFF + 24576)
    gb_sb = at("gb_sb", [64, T], BF16, FOFF + 32768)
    kt = [at("kt0", [65, T], BF16, FOFF), at("kt1", [65, T], BF16, FOFF + 4096)]
    qt = [at("qt0", [65, T], BF16, FOFF + 8192), at("qt1", [65, T], BF16, FOFF + 12288)]
    aT = [at("aT0", [128, 1024], BF16, FOFF + 16384), at("aT1", [128, 1024], BF16, FOFF + 18432)]
    rball = at("rball", [128, 1024], F32, FOFF + 20480)
    ctxstage = at("ctxstage", [64, 1024], BF16, FOFF + 24576)
    wqk_st = at("wqk_st", [128, 2048], BF16, FOFF + 26624)
    wv_st = at("wv_st", [128, 4096], BF16, FOFF + 26624)
    wo_st = at("wo_st", [128, 2048], BF16, FOFF + 26624)
    tmpblk = at("tmpblk", [128, 128], F32, FOFF + 34816)
    mneg = at("mneg", [128, 8], BF16, FOFF + 35328)
    mtmp = at("mtmp", [128, 8], F32, FOFF + 35360)
    R_row = at("R_row", [1, 2048], BF16, 168448)
    B_row = at("B_row", [1, 2048], BF16, 172544)
    Rb_b = at("Rb_b", [128, T], BF16, 176640)
    Bb_b = at("Bb_b", [128, T], BF16, 180736)
    sq_b = at("sq_b", [128, 2 * T], BF16, 184832)
    tmp_b = at("tmp_b", [128, 2 * T], BF16, 193024)
    CO = 201216
    win_sb = at("win_sb", [128, 8 * 64], BF16, CO)
    wout_sb = at("wout_sb", [64, D], BF16, CO + 1024)
    ident_sb = at("ident_sb", [128, 128], F32, CO + 3072)
    maskt_f = at("maskt_f", [128, 128], F32, CO + 3584)
    maskn_sb = at("maskn_sb", [128, 128], F32, CO + 4096)
    maskt_sb = at("maskt_sb", [128, 128], BF16, CO + 4608)
    onesrow_bf = at("onesrow_bf", [1, 128], BF16, CO + 4864)
    ones64 = at("ones64", [65, 64], BF16, CO + 5152)
    ident_bf = at("ident_bf", [128, 128], BF16, CO + 5920)
    rrow_bf = at("rrow_bf", [65, 1024], BF16, CO + 6208)
    w1_sb = at("w1_sb", [64, 16], F32, CO + 5408)
    b1_sb = at("b1_sb", [64, 16], F32, CO + 5472)
    w2_sb = at("w2_sb", [64, 16], F32, CO + 5536)
    c2_col = at("c2_col", [128, 24], F32, CO + 5600)
    c3_col = at("c3_col", [128, 8], F32, CO + 5696)
    g_col = at("g_col", [128, 8], F32, CO + 5728)
    be_col = at("be_col", [128, 8], F32, CO + 5760)
    bout_col = at("bout_col", [128, 8], F32, CO + 5792)
    bin_sb = at("bin_sb", [64, 1], F32, CO + 5824)
    b2_sb = at("b2_sb", [64, 1], F32, CO + 5856)
    onescol_bf = at("onescol_bf", [128, 1], BF16, CO + 5888)

    pA = nc.alloc_psum_tensor("pA", [128, 1024], F32).ap()
    pB = nc.alloc_psum_tensor("pB", [128, 1024], F32).ap()
    pC = nc.alloc_psum_tensor("pC", [128, 1024], F32).ap()
    pD = nc.alloc_psum_tensor("pD", [128, 1024], F32).ap()
    pAll = [pA, pB, pC, pD]

    Tq, Vq, Aq, Gq, Sq = Eng(), Eng(), Eng(), Eng(), Eng()
    t_s = Ctr(nc, "t_s")
    v_s = Ctr(nc, "v_s")
    a_s = Ctr(nc, "a_s")
    g_s = Ctr(nc, "g_s")
    d_c = Ctr(nc, "d_c")
    d_ws = Ctr(nc, "d_ws")
    d_x = [Ctr(nc, "d_x0"), Ctr(nc, "d_x1")]
    d_w = [Ctr(nc, "d_w0"), Ctr(nc, "d_w1")]
    d_wv = Ctr(nc, "d_wv")
    d_kq = [Ctr(nc, "d_kq0"), Ctr(nc, "d_kq1")]
    d_cs = Ctr(nc, "d_cs")
    d_out = Ctr(nc, "d_out")

    def dma(dst, src, ctr, waits=()):
        Sq.op(lambda e, d=dst, s=src: e.dma_start(d, s), waits=waits, inc=ctr.tick(16))
        return ctr.at()

    for dst, src in [
        (w1_sb[:], w1_d), (b1_sb[:], b1_d), (w2_sb[:], w2_d),
        (bin_sb[:], bincol_d), (b2_sb[:], b2col_d), (bout_col[:], boutcol_d),
        (g_col[:], gcol_d), (be_col[:], bcol_d), (ident_sb[:], ident_d),
        (maskt_f[:], maskt_d), (maskn_sb[:], maskn_d),
    ]:
        dma(dst, src, d_c)
    t_consts = d_c.at()

    Gq.op(lambda e: e.memset(onesrow_bf[0:1, :], 1.0), inc=g_s.tick())
    Gq.op(lambda e: e.memset(onescol_bf[:], 1.0), inc=g_s.tick())
    Gq.op(lambda e: e.memset(ones64[64:65, :], 1.0), inc=g_s.tick())
    Gq.op(lambda e: e.tensor_copy(maskt_sb[:], maskt_f[:]), waits=[t_consts], inc=g_s.tick())
    Gq.op(lambda e: e.tensor_copy(ident_bf[:], ident_sb[:]), waits=[t_consts], inc=g_s.tick())
    g_const = g_s.at()

    tok = dma(xstage[:, 0:512], win_d.rearrange("(kc p) m -> p kc m", p=128), d_ws)
    Vq.op(lambda e: e.tensor_copy(win_sb[:], xstage[:, 0:512]), waits=[tok], inc=v_s.tick())
    tok = dma(xstage[0:64, 1024:2048], wout_d, d_ws, waits=[v_s.at()])
    Vq.op(lambda e: e.tensor_copy(wout_sb[:], xstage[0:64, 1024:2048]), waits=[tok], inc=v_s.tick())
    v_wsmall = v_s.at()

    last_store = None
    last_wread_t = {0: None, 1: None}
    last_wread_v = {0: None, 1: None}
    for kc in range(NCH):
        sb_ = kc % 2
        wstage = wstage2[sb_]
        tok = dma(wstage[:], wqkvt_d[kc * 128:(kc + 1) * 128, :], d_ws,
                  waits=[last_wread_t[sb_], last_wread_v[sb_], v_wsmall])
        Vq.op(lambda e, kc=kc, wstage=wstage: e.tensor_scalar(
            wbstage[:], wstage[:], g_col[:, kc:kc + 1], None, ALU.mult),
            waits=[tok, t_consts, last_store], inc=v_s.tick())
        last_wread_v[sb_] = v_s.at()
        for mc in range(24):
            Tq.op(lambda e, kc=kc, mc=mc, wstage=wstage: e.matmul(
                pA[:, mc * 8 + kc:mc * 8 + kc + 1], wstage[:, mc * 128:(mc + 1) * 128],
                be_col[:, kc:kc + 1], start=True, stop=True),
                waits=([tok, t_consts] if mc == 0 else []))
        Tq.retro_inc(t_s.tick())
        last_wread_t[sb_] = t_s.at()
        dma(wqs_d[0:16, kc, :, :].rearrange("mc p f -> p mc f"),
            wbstage[:, 0:2048].rearrange("p (mc f) -> p mc f", mc=16),
            d_out, waits=[last_wread_v[sb_]])
        for vh_ in range(2):
            dma(wvs_d[vh_, kc, :, :, :].rearrange("mc p f -> p mc f"),
                wbstage[:, 2048 + vh_ * 512:2048 + vh_ * 512 + 512]
                .rearrange("p (mc f) -> p mc f", mc=4),
                d_out, waits=[])
        last_store = d_out.at()
    for mc in range(24):
        Vq.op(lambda e, mc=mc: e.reduce_sum(
            c2_col[:, mc:mc + 1], pA[:, mc * 8:mc * 8 + 8], axis=AX.X),
            waits=[last_wread_t[0], last_wread_t[1]], inc=v_s.tick())
    v_c2 = v_s.at()

    for kc in range(NCH):
        sb_ = kc % 2
        wstage = wstage2[sb_]
        tok = dma(wstage[:, 0:1024], wot_d[kc * 128:(kc + 1) * 128, :], d_ws,
                  waits=[last_wread_t[sb_], last_wread_v[sb_]])
        Vq.op(lambda e, wstage=wstage: e.tensor_copy(
            wbstage[:, 0:1024], wstage[:, 0:1024]),
            waits=[tok, v_c2, last_store], inc=v_s.tick())
        last_wread_v[sb_] = v_s.at()
        for mc in range(8):
            Tq.op(lambda e, kc=kc, mc=mc, wstage=wstage: e.matmul(
                pB[:, mc * 8 + kc:mc * 8 + kc + 1], wstage[:, mc * 128:(mc + 1) * 128],
                c2_col[:, 16 + kc:17 + kc], start=True, stop=True),
                waits=([tok, v_c2] if mc == 0 else []))
        Tq.retro_inc(t_s.tick())
        last_wread_t[sb_] = t_s.at()
        last_store = dma(wos_d[:, kc, :, :].rearrange("mc p f -> p mc f"),
                         wbstage[:, 0:1024].rearrange("p (mc f) -> p mc f", mc=8),
                         d_out, waits=[last_wread_v[sb_]])
    for mc in range(8):
        Vq.op(lambda e, mc=mc: e.reduce_sum(
            c3_col[:, mc:mc + 1], pB[:, mc * 8:mc * 8 + 8], axis=AX.X),
            waits=[last_wread_t[0], last_wread_t[1]], inc=v_s.tick())
    vput = vbuf[:].rearrange("p (t h f) -> p t h f", t=NT, h=NH, f=65)
    setup_v = v_s.at()
    setup_t = max(last_wread_t[0], last_wread_t[1], key=lambda t: t[1])
    setup_d = d_out.at()
    setup_g = g_s.at()

    def emit_ln(src_sb, src_tok_fn, eps_mul, eps_add, guards):
        sq_read_t = {}
        sqtoks = {}
        for c in range(NCH):
            w = [src_tok_fn(c)] + guards
            if c - 2 in sq_read_t:
                w.append(sq_read_t[c - 2])
            Vq.op(lambda e, c=c: e.tensor_tensor(
                sq_b[:, (c % 2) * T:(c % 2) * T + T],
                src_sb[:, c * T:(c + 1) * T], src_sb[:, c * T:(c + 1) * T], ALU.mult),
                waits=w, inc=v_s.tick())
            sqtoks[c] = v_s.at()
            for th in range(2):
                psum_s = (pA, pB)[th]
                psum_q = (pC, pD)[th]
                for h5 in range(2):
                    cs = th * 1024 + h5 * 512
                    w = [src_tok_fn(c)] + (guards if c == 0 else [])
                    Tq.op(lambda e, c=c, h5=h5, cs=cs, psum_s=psum_s: e.matmul(
                        psum_s[0:1, h5 * 512:h5 * 512 + 512], onescol_bf[:],
                        src_sb[:, c * T + cs:c * T + cs + 512],
                        start=(c == 0), stop=(c == NCH - 1)), waits=w)
                for h5 in range(2):
                    cs = th * 1024 + h5 * 512
                    Tq.op(lambda e, c=c, h5=h5, cs=cs, psum_q=psum_q: e.matmul(
                        psum_q[0:1, h5 * 512:h5 * 512 + 512], onescol_bf[:],
                        sq_b[:, (c % 2) * T + cs:(c % 2) * T + cs + 512],
                        start=(c == 0), stop=(c == NCH - 1)), waits=[sqtoks[c]])
            Tq.retro_inc(t_s.tick())
            sq_read_t[c] = t_s.at()
        t_stats = t_s.at()
        out = {}
        for th in range(2):
            psum_s = (pA, pB)[th]
            psum_q = (pC, pD)[th]
            seg = slice(th * 1024, th * 1024 + 1024)
            Vq.op(lambda e, psum_s=psum_s, seg=seg: e.tensor_scalar(
                B_row[0:1, seg], psum_s[0:1, 0:1024], 1.0 / D, None, ALU.mult),
                waits=[t_stats] + guards, inc=v_s.tick())
            Vq.op(lambda e, seg=seg: e.tensor_tensor(
                R_row[0:1, seg], B_row[0:1, seg], B_row[0:1, seg],
                ALU.mult), inc=v_s.tick())
            Vq.op(lambda e, psum_q=psum_q, seg=seg: e.scalar_tensor_tensor(
                psum_q[0:1, 0:1024], psum_q[0:1, 0:1024], 1.0 / D,
                R_row[0:1, seg], ALU.mult, ALU.subtract), inc=v_s.tick())
            Vq.op(lambda e, psum_q=psum_q: e.tensor_scalar(
                psum_q[0:1, 0:1024], psum_q[0:1, 0:1024], eps_mul, eps_add,
                ALU.mult, ALU.add), inc=v_s.tick())
            v_rows = v_s.at()
            Aq.op(lambda e, psum_s=psum_s, psum_q=psum_q: e.activation(
                psum_s[0:1, 0:1024], psum_q[0:1, 0:1024], AF.Ln),
                waits=[v_rows] + guards, inc=a_s.tick())
            Aq.op(lambda e, psum_s=psum_s, seg=seg: e.activation(
                R_row[0:1, seg], psum_s[0:1, 0:1024], AF.Exp, scale=-0.5),
                inc=a_s.tick())
            a_R = a_s.at()
            Vq.op(lambda e, seg=seg: e.scalar_tensor_tensor(
                B_row[0:1, seg], B_row[0:1, seg], -1.0,
                R_row[0:1, seg], ALU.mult, ALU.mult),
                waits=[a_R], inc=v_s.tick())
        v_rowsdone = v_s.at()
        bc_evac = {0: None, 1: None}
        for row, dstb in [(R_row, Rb_b), (B_row, Bb_b)]:
            for th in range(2):
                for h5 in range(2):
                    pdst = (pC, pD)[h5]
                    cs = th * 1024 + h5 * 512
                    Tq.op(lambda e, row=row, cs=cs, pdst=pdst: e.matmul(
                        pdst[:, 0:512], onesrow_bf[0:1, :],
                        row[0:1, cs:cs + 512],
                        start=True, stop=True),
                        waits=[v_rowsdone, bc_evac[h5]] + guards, inc=t_s.tick())
                    tb = t_s.at()
                    Vq.op(lambda e, dstb=dstb, cs=cs, pdst=pdst: e.tensor_copy(
                        dstb[:, cs:cs + 512],
                        pdst[:, 0:512]), waits=[tb], inc=v_s.tick())
                    bc_evac[h5] = v_s.at()
        out["Rb"] = v_s.at()
        out["t_end"] = t_s.at()
        return out

    prevs = {
        "out_dma": [setup_d, None],
        "slotA_free": [setup_t],
        "slotB_free": [],
        "slotC_free": [],
        "slotA_q_free": [],
        "vbuf_free": [],
        "stream_free": [],
        "ln_guards": [setup_v],
    }
    asm_guard = [None, None]
    ctxstage_guard = [None]
    x_first_guard = [setup_t, setup_d, setup_v, setup_g]

    for s in range(NSEQ):
        cast_toks = []
        xload = []
        for c in range(NCH):
            w = list(prevs["out_dma"]) + (x_first_guard if s == 0 else [])
            if c >= 2:
                w.append(cast_toks[c - 2])
            tok = dma(xstage[:, (c % 2) * T:(c % 2) * T + T],
                      x_d[s, c * 128:(c + 1) * 128, :], d_x[c % 2], waits=w)
            xload.append(tok)
            Aq.op(lambda e, c=c: e.activation(
                slotA[:, c * T:(c + 1) * T], xstage[:, (c % 2) * T:(c % 2) * T + T],
                AF.Copy),
                waits=[tok] + prevs["slotA_free"], inc=a_s.tick())
            cast_toks.append(a_s.at())
        ones_w = [cast_toks[-1]] + prevs["vbuf_free"] + prevs["stream_free"]
        if s == 0:
            ones_w += [setup_d, setup_t]
        Gq.op(lambda e: e.memset(vput[:, :, :, 64:65], 1.0), waits=ones_w,
              inc=g_s.tick())
        Gq.op(lambda e: e.memset(kt[0][0:1, :], 1.0), inc=g_s.tick())
        Gq.op(lambda e: e.memset(kt[1][0:1, :], 1.0), inc=g_s.tick())
        ones_tok = g_s.at()
        ln1 = emit_ln(slotA, lambda c: cast_toks[c], 1.0 + EPS, EPS * EPS,
                      prevs["ln_guards"])
        xn_toks = []
        for c in range(NCH):
            Vq.op(lambda e, c=c: e.tensor_tensor(
                tmp_b[:, (c % 2) * T:(c % 2) * T + T],
                slotA[:, c * T:(c + 1) * T], Rb_b[:], ALU.mult),
                waits=[ln1["Rb"], cast_toks[c]], inc=v_s.tick())
            Vq.op(lambda e, c=c: e.tensor_tensor(
                slotC[:, c * T:(c + 1) * T],
                tmp_b[:, (c % 2) * T:(c % 2) * T + T], Bb_b[:], ALU.add),
                waits=prevs["slotC_free"], inc=v_s.tick())
            xn_toks.append(v_s.at())
        xn_done = v_s.at()

        qk_evac = {}
        last_st_read = [None, None]
        first_stream = True
        for th in range(2):
            for mc in range(16):
                g = th * 16 + mc
                sbh = g % 2
                w = [setup_d, last_st_read[sbh]]
                if first_stream:
                    w += prevs["stream_free"]
                    first_stream = False
                wtok = dma(wqk_st[:, sbh * 1024:sbh * 1024 + 1024]
                           .rearrange("p (kc f) -> p kc f", kc=8),
                           wqs_d[mc, :, :, :].rearrange("kc p f -> p kc f"),
                           d_w[sbh], waits=w)
                pt = pAll[g % 4]
                wfirst = [wtok, xn_done, ln1["t_end"]]
                if g - 4 >= 0:
                    wfirst.append(qk_evac[g - 4])
                for kc in range(NCH):
                    for h5 in range(2):
                        Tq.op(lambda e, mc=mc, kc=kc, th=th, h5=h5, pt=pt, sbh=sbh: e.matmul(
                            pt[:, h5 * 512:h5 * 512 + 512],
                            wqk_st[:, sbh * 1024 + kc * 128:sbh * 1024 + kc * 128 + 128],
                            slotC[:, kc * T + th * 1024 + h5 * 512:
                                  kc * T + th * 1024 + h5 * 512 + 512],
                            start=(kc == 0), stop=(kc == NCH - 1)),
                            waits=(wfirst if (kc == 0 and h5 == 0) else []))
                Tq.retro_inc(t_s.tick())
                t_grp = t_s.at()
                last_st_read[sbh] = t_grp
                dst = slotA if mc < 8 else slotB
                dmc = mc if mc < 8 else mc - 8
                wev = [t_grp]
                wev += prevs["slotA_q_free"] if mc < 8 else prevs["slotB_free"]
                Aq.op(lambda e, dst=dst, dmc=dmc, th=th, pt=pt, mc=mc: e.activation(
                    dst[:, dmc * T + th * 1024:dmc * T + th * 1024 + 1024], pt[:],
                    AF.Identity, bias=c2_col[:, mc:mc + 1]),
                    waits=wev, inc=a_s.tick())
                qk_evac[g] = a_s.at()
        qk_done = a_s.at()
        t_qk = t_s.at()


        asm_tok = {}

        def emit_asm(h, extra):
            wk = [qk_done, asm_guard[h % 2], extra]
            dma(kt[h % 2][1:65, :],
                slotB[(h % 2) * 64:(h % 2) * 64 + 64, (h // 2) * T:(h // 2) * T + T],
                d_kq[h % 2], waits=wk)
            dma(qt[h % 2][1:65, :],
                slotA[(h % 2) * 64:(h % 2) * 64 + 64, (h // 2) * T:(h // 2) * T + T],
                d_kq[h % 2], waits=[])
            asm_tok[h] = d_kq[h % 2].at()

        emit_asm(0, None)
        emit_asm(1, None)

        st = {"sprobe_b": [None, None], "probe_par": 0, "mrow": None,
              "aT": [None, None], "pT": [None, None], "ctx": None,
              "qrow": {}, "t_av": None}

        def gen_A(h, half):
            q0 = half * 1024
            mcol_toks = []
            for idx, i in enumerate(range(8 * half, 8 * half + 8)):
                row_w = (i + 1) * 128
                nchunks = 0
                for ck0 in range(0, row_w, 512):
                    ck1 = min(ck0 + 512, row_w)
                    pb_ = st["probe_par"]
                    st["probe_par"] ^= 1
                    w = [qk_done, st["sprobe_b"][pb_], ln1["Rb"]]
                    if pb_ == 1:
                        w.append(st["mrow"])
                    Tq.op(lambda e, h=h, i=i, ck0=ck0, ck1=ck1, pb_=pb_: e.matmul(
                        pA[:, pb_ * 512:pb_ * 512 + ck1 - ck0],
                        slotA[(h % 2) * 64:(h % 2) * 64 + 64,
                              (h // 2) * T + i * 128:(h // 2) * T + i * 128 + 128],
                        slotB[(h % 2) * 64:(h % 2) * 64 + 64,
                              (h // 2) * T + ck0:(h // 2) * T + ck1],
                        start=True, stop=True),
                        waits=w, inc=t_s.tick())
                    tmm = t_s.at()
                    Vq.op(lambda e, nch=nchunks, w=ck1 - ck0, pb_=pb_: e.reduce_max(
                        mtmp[:, nch:nch + 1], pA[:, pb_ * 512:pb_ * 512 + w], axis=AX.X),
                        waits=[tmm], inc=v_s.tick())
                    st["sprobe_b"][pb_] = v_s.at()
                    nchunks += 1
                Vq.op(lambda e, idx=idx, nch=nchunks: e.reduce_max(
                    mneg[:, idx:idx + 1], mtmp[:, 0:nch], axis=AX.X, negate=True),
                    inc=v_s.tick())
                mcol_toks.append(v_s.at())
                yield
            for qtr in range(2):
                for r in range(4):
                    w = [mcol_toks[qtr * 4 + r]]
                    if r == 0:
                        w += [st["mrow"], st["sprobe_b"][1]]
                    Tq.op(lambda e, qtr=qtr, r=r: e.matmul(
                        pA[0:1, 512 + r * 128:512 + r * 128 + 128],
                        mneg[:, qtr * 4 + r:qtr * 4 + r + 1], ident_bf[:],
                        start=True, stop=True),
                        waits=w, inc=(t_s.tick() if r == 3 else None))
                t_mrow = t_s.at()
                st["sprobe_b"][1] = None
                Aq.op(lambda e, h=h, q0=q0, qtr=qtr: e.activation(
                    qt[h % 2][0:1, q0 + qtr * 512:q0 + qtr * 512 + 512],
                    pA[0:1, 512:1024], AF.Copy),
                    waits=[t_mrow, asm_tok[h]], inc=a_s.tick())
                st["mrow"] = a_s.at()
                yield
            st["qrow"][(h, half)] = a_s.at()

        def gen_B(h, half):
            q0 = half * 1024
            qrow_done = st["qrow"][(h, half)]
            strips = list(range(0, 8 * (half + 1)))
            first_j, last_j = strips[0], strips[-1]
            pend = None

            def emit_av(pend):
                cks, bi, a_exp, diag_tok, j = pend
                wv = [a_exp, diag_tok, ones_tok]
                if j == first_j:
                    wv.append(st["ctx"])
                for ci, (c0, c1) in enumerate(cks):
                    lastj_chunk = min(last_j, (q0 + c1 - 1) // 128)
                    Tq.op(lambda e, h=h, j=j, c0=c0, c1=c1, bi=bi,
                          sp=(j == lastj_chunk): e.matmul(
                        pD[0:65, c0:c1],
                        vbuf[:, j * NH * 65 + h * 65:j * NH * 65 + h * 65 + 65],
                        aT[bi][:, c0:c1],
                        start=(j == first_j), stop=sp),
                        waits=(wv if ci == 0 else []),
                        inc=(t_s.tick() if ci == len(cks) - 1 else None))
                st["aT"][bi] = t_s.at()

            for j in strips:
                rel0 = max(j * 128 - q0, 0)
                bi = j % 2
                buf = (pB, pC)[bi]
                cks = []
                c0 = rel0
                while c0 < 1024:
                    c1 = min((c0 // 512 + 1) * 512, 1024)
                    cks.append((c0, c1))
                    c0 = c1
                wv = [qrow_done, asm_tok[h], st["pT"][bi], ones_tok]
                for ci, (c0, c1) in enumerate(cks):
                    Tq.op(lambda e, h=h, j=j, c0=c0, c1=c1, buf=buf, q0=q0: e.matmul(
                        buf[:, c0:c1],
                        kt[h % 2][0:65, j * 128:j * 128 + 128],
                        qt[h % 2][0:65, q0 + c0:q0 + c1],
                        start=True, stop=True),
                        waits=(wv if ci == 0 else []),
                        inc=(t_s.tick() if ci == len(cks) - 1 else None))
                t_st = t_s.at()
                Aq.op(lambda e, bi=bi, rel0=rel0, buf=buf: e.activation(
                    aT[bi][:, rel0:1024], buf[:, rel0:1024], AF.Exp, scale=0.125),
                    waits=[t_st, st["aT"][bi]], inc=a_s.tick())
                st["pT"][bi] = a_s.at()
                a_exp = a_s.at()
                diag_tok = None
                if j >= 8 * half:
                    Gq.op(lambda e, bi=bi, rel0=rel0: e.tensor_tensor(
                        aT[bi][:, rel0:rel0 + 128], aT[bi][:, rel0:rel0 + 128],
                        maskt_sb[:], ALU.mult),
                        waits=[a_exp], inc=g_s.tick())
                    diag_tok = g_s.at()
                if pend is not None:
                    emit_av(pend)
                pend = (cks, bi, a_exp, diag_tok, j)
                yield
            emit_av(pend)
            t_av = t_s.at()
            st["t_av"] = t_av

            Aq.op(lambda e: e.activation(rball[64:65, :], pD[64:65, :], AF.Ln),
                  waits=[t_av], inc=a_s.tick())
            Aq.op(lambda e: e.activation(rrow_bf[64:65, :], rball[64:65, :],
                                         AF.Exp, scale=-1.0), inc=a_s.tick())
            a_r = a_s.at()
            for h5 in range(2):
                Tq.op(lambda e, h5=h5: e.matmul(
                    pB[0:64, h5 * 512:h5 * 512 + 512], ones64[64:65, :],
                    rrow_bf[64:65, h5 * 512:h5 * 512 + 512], start=True, stop=True),
                    waits=([a_r, st["pT"][0]] if h5 == 0 else []),
                    inc=(t_s.tick() if h5 == 1 else None))
            t_rb = t_s.at()
            Aq.op(lambda e: e.activation(rball[0:64, :], pB[0:64, :], AF.Copy),
                  waits=[t_rb], inc=a_s.tick())
            v_rb = a_s.at()
            st["pT"][0] = v_rb
            if h % 2 == 0:
                Vq.op(lambda e, h=h, q0=q0: e.tensor_tensor(
                    slotC[0:64, (h // 2) * T + q0:(h // 2) * T + q0 + 1024],
                    pD[0:64, :], rball[0:64, :], ALU.mult),
                    waits=[v_rb], inc=v_s.tick())
                st["ctx"] = v_s.at()
            else:
                Vq.op(lambda e: e.tensor_tensor(
                    ctxstage[:], pD[0:64, :], rball[0:64, :], ALU.mult),
                    waits=[v_rb, ctxstage_guard[0]], inc=v_s.tick())
                st["ctx"] = v_s.at()
                tok = dma(slotC[64:128, (h // 2) * T + q0:(h // 2) * T + q0 + 1024],
                          ctxstage[:], d_cs, waits=[st["ctx"]])
                ctxstage_guard[0] = tok
            if half == 1:
                asm_guard[h % 2] = t_av
                if h + 2 < NH:
                    emit_asm(h + 2, t_av)
            yield

        pairs = [(h, hf) for h in range(NH) for hf in range(2)]
        gA0 = gen_A(*pairs[0])
        gA0_done = [False]

        def pump():
            if not gA0_done[0]:
                try:
                    next(gA0)
                except StopIteration:
                    gA0_done[0] = True
        v_ev = {}
        last_v_read = None
        for vh in range(2):
            w = [setup_d, t_qk if vh == 0 else last_v_read] + list(prevs["out_dma"])
            wtok = dma(wv_st[:].rearrange("p (km f) -> p km f", km=32),
                       wvs_d[vh, :, :, :, :].rearrange("kc mc p f -> p (kc mc) f"),
                       d_wv, waits=w)
            for tt in range(NT):
                gidx = vh * NT + tt
                pt = (pB, pC, pD)[gidx % 3]
                wfirst = [wtok, xn_done]
                if gidx - 3 >= 0:
                    wfirst.append(v_ev[gidx - 3])
                else:
                    wfirst.append(qk_done)
                for kc in range(NCH):
                    Tq.op(lambda e, tt=tt, kc=kc, pt=pt: e.matmul(
                        pt[:, 0:512],
                        slotC[:, kc * T + tt * 128:kc * T + tt * 128 + 128],
                        wv_st[:, kc * 512:kc * 512 + 512],
                        start=(kc == 0), stop=(kc == NCH - 1)),
                        waits=(wfirst if kc == 0 else []))
                Tq.retro_inc(t_s.tick())
                t_grp = t_s.at()
                last_v_read = t_grp
                dstv = vbuf[:, tt * NH * 65:(tt + 1) * NH * 65] \
                    .rearrange("p (h f) -> p h f", h=NH)[:, vh * 8:vh * 8 + 8, 0:64]
                srcv = pt[:, 0:512].rearrange("p (h f) -> p h f", h=8)
                Aq.op(lambda e, dstv=dstv, srcv=srcv: e.activation(dstv, srcv, AF.Copy),
                      waits=[t_grp, setup_g] + prevs["vbuf_free"] + list(prevs["out_dma"]),
                      inc=a_s.tick())
                v_ev[gidx] = a_s.at()
                pump()
                pump()
        v_done = a_s.at()
        t_v = t_s.at()
        st["pT"] = [v_done, v_done]
        st["ctx"] = v_done
        st["aT"] = [st["aT"][0], st["aT"][1]]
        for _ in gA0:
            pass
        gAn = None
        for k, pr in enumerate(pairs):
            gB = gen_B(*pr)
            gAn = gen_A(*pairs[k + 1]) if k + 1 < len(pairs) else None
            nstrips = 8 * (pr[1] + 1)
            if nstrips >= 10:
                sched = [0] * (nstrips - 10) + [1] * 10
            else:
                sched = [1] * nstrips
                for i in range(10 - nstrips):
                    sched[-1 - (i % 2)] += 1
            yi = 0
            for _ in gB:
                nsteps = sched[yi] if yi < len(sched) else 1
                yi += 1
                for _ in range(nsteps):
                    if gAn is not None:
                        try:
                            next(gAn)
                        except StopIteration:
                            gAn = None
            if gAn is not None:
                for _ in gAn:
                    pass
                gAn = None
        t_av_last = st["t_av"]
        v_p3 = v_s.at()
        a_p3 = a_s.at()
        d_p3 = [d_kq[0].at(), d_kq[1].at(), d_cs.at()]
        t_p3 = t_s.at()

        ao_evac = {}
        last_wo_read = [None, None]
        for mc in range(8):
            sbh = mc % 2
            w = [setup_d, last_wo_read[sbh]]
            if mc == 0:
                w += [t_p3]
            wtok = dma(wo_st[:, sbh * 1024:sbh * 1024 + 1024]
                       .rearrange("p (kc f) -> p kc f", kc=8),
                       wos_d[mc, :, :, :].rearrange("kc p f -> p kc f"),
                       d_w[sbh], waits=w)
            pts = (pA, pB) if mc % 2 == 0 else (pC, pD)
            wfirst = [wtok, v_p3, a_p3, d_p3]
            if mc >= 2:
                wfirst.append(ao_evac[mc - 2])
            for kc in range(NCH):
                for sl in range(4):
                    pt = pts[sl // 2]
                    Tq.op(lambda e, mc=mc, kc=kc, sl=sl, pt=pt, sbh=sbh: e.matmul(
                        pt[:, (sl % 2) * 512:(sl % 2) * 512 + 512],
                        wo_st[:, sbh * 1024 + kc * 128:sbh * 1024 + kc * 128 + 128],
                        slotC[:, kc * T + sl * 512:kc * T + sl * 512 + 512],
                        start=(kc == 0), stop=(kc == NCH - 1)),
                        waits=(wfirst if (kc == 0 and sl == 0) else []))
            Tq.retro_inc(t_s.tick())
            t_grp = t_s.at()
            last_wo_read[sbh] = t_grp
            for halfp in range(2):
                Aq.op(lambda e, mc=mc, halfp=halfp, pts=pts: e.activation(
                    slotA[:, mc * T + halfp * 1024:mc * T + halfp * 1024 + 1024],
                    pts[halfp][:], AF.Identity, bias=c3_col[:, mc:mc + 1]),
                    waits=[t_grp, d_p3], inc=a_s.tick())
            ao_evac[mc] = a_s.at()
        ao_done = a_s.at()
        t_wo = t_s.at()

        ln3 = emit_ln(slotA, lambda c: ao_evac[c], 1.0, EPS, [ao_done, v_p3])
        x2_toks = []
        for c in range(NCH):
            w = [cast_toks[-1], t_av_last] + ([x2_toks[c - 2]] if c >= 2 else [ln3["Rb"]])
            xtok = dma(xstage[:, (c % 2) * T:(c % 2) * T + T],
                       x_d[s, c * 128:(c + 1) * 128, :], d_x[c % 2], waits=w)
            Vq.op(lambda e, c=c: e.tensor_tensor(
                tmp_b[:, (c % 2) * T:(c % 2) * T + T],
                slotA[:, c * T:(c + 1) * T], Rb_b[:], ALU.mult),
                waits=[ln3["Rb"]] + ([x2_toks[c - 2]] if c >= 2 else []),
                inc=v_s.tick())
            Vq.op(lambda e, c=c: e.tensor_tensor(
                tmp_b[:, (c % 2) * T:(c % 2) * T + T],
                tmp_b[:, (c % 2) * T:(c % 2) * T + T], Bb_b[:], ALU.add),
                inc=v_s.tick())
            Vq.op(lambda e, c=c: e.tensor_tensor(
                slotB[:, c * T:(c + 1) * T],
                tmp_b[:, (c % 2) * T:(c % 2) * T + T],
                xstage[:, (c % 2) * T:(c % 2) * T + T], ALU.add),
                waits=[xtok, t_wo, d_p3], inc=v_s.tick())
            x2_toks.append(v_s.at())
        x2_done = v_s.at()

        ln4 = emit_ln(slotB, lambda c: x2_toks[c], 1.0, EPS, [x2_done, ao_done])
        xn2_toks = []
        for c in range(NCH):
            Vq.op(lambda e, c=c: e.tensor_tensor(
                tmp_b[:, (c % 2) * T:(c % 2) * T + T],
                slotB[:, c * T:(c + 1) * T], Rb_b[:], ALU.mult),
                waits=[ln4["Rb"]], inc=v_s.tick())
            Vq.op(lambda e, c=c: e.tensor_tensor(
                slotA[:, c * T:(c + 1) * T],
                tmp_b[:, (c % 2) * T:(c % 2) * T + T], Bb_b[:], ALU.add),
                inc=v_s.tick())
            xn2_toks.append(v_s.at())
        xn2_done = v_s.at()

        s_ev = []
        for sl in range(4):
            pt = (pA, pA, pB, pB)[sl]
            off = (sl % 2) * 512
            for kc in range(NCH):
                Tq.op(lambda e, sl=sl, kc=kc, pt=pt, off=off: e.matmul(
                    pt[0:64, off:off + 512],
                    win_sb[:, kc * 64:kc * 64 + 64],
                    slotA[:, kc * T + sl * 512:kc * T + sl * 512 + 512],
                    start=(kc == 0), stop=(kc == NCH - 1)),
                    waits=([xn2_done, ln4["t_end"]] if kc == 0 else []))
            Tq.retro_inc(t_s.tick())
            t_grp = t_s.at()
            if sl % 2 == 1:
                pth = (pA, pB)[sl // 2]
                Aq.op(lambda e, sl=sl, pth=pth: e.activation(
                    s_sb[:, (sl - 1) * 512:(sl + 1) * 512], pth[0:64, :],
                    AF.Identity, bias=bin_sb[:]),
                    waits=[t_grp, d_p3], inc=a_s.tick())
                s_ev.append(a_s.at())
        s_done = a_s.at()
        g_q = []
        for q4 in range(4):
            qs = slice(q4 * 512, q4 * 512 + 512)
            for i in range(16):
                Vq.op(lambda e, i=i, qs=qs: e.tensor_scalar(
                    tfn[0:64, qs], s_sb[0:64, qs], w1_sb[:, i:i + 1],
                    b1_sb[:, i:i + 1], ALU.mult, ALU.add),
                    waits=([s_done] if i == 0 else []), inc=v_s.tick())
                dst = g_sb if i == 0 else u_fn
                Vq.op(lambda e, dst=dst, i=i, qs=qs: e.tensor_scalar(
                    dst[0:64, qs], tfn[0:64, qs], 0.0, w2_sb[:, i:i + 1],
                    ALU.max, ALU.mult), inc=v_s.tick())
                if i > 0:
                    Vq.op(lambda e, qs=qs: e.tensor_tensor(
                        g_sb[0:64, qs], g_sb[0:64, qs], u_fn[0:64, qs], ALU.add),
                        inc=v_s.tick())
            Vq.op(lambda e, qs=qs: e.tensor_scalar(
                gb_sb[0:64, qs], g_sb[0:64, qs], b2_sb[:], None, ALU.add),
                inc=v_s.tick())
            g_q.append(v_s.at())
        g_done = v_s.at()
        fo_ev = {}
        for mc in range(8):
            pts = (pA, pB) if mc % 2 == 0 else (pC, pD)
            for sl in range(4):
                pt = pts[sl // 2]
                w = [g_q[sl]]
                if sl == 0 and mc >= 2:
                    w.append(fo_ev[mc - 2])
                Tq.op(lambda e, mc=mc, sl=sl, pt=pt: e.matmul(
                    pt[:, (sl % 2) * 512:(sl % 2) * 512 + 512],
                    wout_sb[0:64, mc * 128:mc * 128 + 128],
                    gb_sb[:, sl * 512:sl * 512 + 512],
                    start=True, stop=True),
                    waits=w)
            Tq.retro_inc(t_s.tick())
            t_grp = t_s.at()
            for halfp in range(2):
                Aq.op(lambda e, mc=mc, halfp=halfp, pts=pts: e.activation(
                    slotA[:, mc * T + halfp * 1024:mc * T + halfp * 1024 + 1024],
                    pts[halfp][:], AF.Identity, bias=bout_col[:, mc:mc + 1]),
                    waits=[t_grp, xn2_done], inc=a_s.tick())
            fo_ev[mc] = a_s.at()
        fo_done = a_s.at()

        ln5 = emit_ln(slotA, lambda c: fo_ev[c], 1.0, EPS, [fo_done, x2_done])
        out_toks = []
        out_g = []
        for c in range(NCH):
            Vq.op(lambda e, c=c: e.tensor_tensor(
                tmp_b[:, (c % 2) * T:(c % 2) * T + T],
                slotA[:, c * T:(c + 1) * T], Rb_b[:], ALU.mult),
                waits=[ln5["Rb"]] + ([out_g[c - 2]] if c >= 2 else []),
                inc=v_s.tick())
            Vq.op(lambda e, c=c: e.tensor_tensor(
                tmp_b[:, (c % 2) * T:(c % 2) * T + T],
                tmp_b[:, (c % 2) * T:(c % 2) * T + T], Bb_b[:], ALU.add),
                inc=v_s.tick())
            w = [out_toks[c - 2]] if c >= 2 else [xn2_done]
            Vq.op(lambda e, c=c: e.tensor_tensor(
                xstage[:, (c % 2) * T:(c % 2) * T + T],
                tmp_b[:, (c % 2) * T:(c % 2) * T + T],
                slotB[:, c * T:(c + 1) * T], ALU.add),
                waits=w, inc=v_s.tick())
            v_out = v_s.at()
            out_g.append(v_out)
            tok = dma(y_d[s, c * 128:(c + 1) * 128, :],
                      xstage[:, (c % 2) * T:(c % 2) * T + T], d_out, waits=[v_out])
            out_toks.append(tok)
        v_p6 = v_s.at()
        t_ln5 = ln5["t_end"]

        g_p6 = g_s.at()
        prevs = {
            "out_dma": [out_toks[-1]],
            "slotA_free": [v_p6, t_ln5],
            "slotB_free": [v_p6, g_p6],
            "slotC_free": [t_wo],
            "slotA_q_free": [t_p3, d_p3],
            "vbuf_free": [t_av_last],
            "stream_free": [v_p6, fo_done],
            "ln_guards": [v_p6, fo_done],
        }

    nc._dbg = dict(slotA=slotA, slotB=slotB, slotC=slotC, vbuf=vbuf, kt0=kt[0],
                   kt1=kt[1], qt0=qt[0], qt1=qt[1], c2_col=c2_col, c3_col=c3_col,
                   Rb_b=Rb_b, Bb_b=Bb_b, R_row=R_row, B_row=B_row, s_sb=s_sb,
                   g_sb=g_sb, gb_sb=gb_sb, rball=rball, mneg=mneg, aT0=aT[0],
                   aT1=aT[1], win_sb=win_sb, wout_sb=wout_sb, tfn=tfn, u_fn=u_fn, g=g_sb, w1=w1_sb, b1=b1_sb, w2=w2_sb)

    Sq.op(lambda e: None, waits=[d_out.at()])

    with nc.Block() as block:
        @block.tensor
        def _(e):
            run_ops(e, Tq.ops)

        @block.vector
        def _(e):
            run_ops(e, Vq.ops)

        @block.scalar
        def _(e):
            run_ops(e, Aq.ops)

        @block.gpsimd
        def _(e):
            run_ops(e, Gq.ops)

        @block.sync
        def _(e):
            run_ops(e, Sq.ops)

    return nc


def kernel(L0_attn_x, L0_attn_ln_gamma, L0_attn_ln_beta, L0_attn_W_qkv, L0_attn_W_o,
           L0_ffn_Win, L0_ffn_bin_, L0_ffn_W1, L0_ffn_b1, L0_ffn_W2, L0_ffn_b2,
           L0_ffn_Wout, L0_ffn_bout):
    x = np.asarray(L0_attn_x, dtype=np.float32)
    gamma = np.asarray(L0_attn_ln_gamma, dtype=np.float32)
    beta = np.asarray(L0_attn_ln_beta, dtype=np.float32)
    wqkv = np.asarray(L0_attn_W_qkv, dtype=np.float32)
    wo = np.asarray(L0_attn_W_o, dtype=np.float32)
    win = np.asarray(L0_ffn_Win, dtype=np.float32)
    bin_ = np.asarray(L0_ffn_bin_, dtype=np.float32)
    w1 = np.asarray(L0_ffn_W1, dtype=np.float32)
    b1 = np.asarray(L0_ffn_b1, dtype=np.float32)
    w2 = np.asarray(L0_ffn_W2, dtype=np.float32)
    b2 = np.asarray(L0_ffn_b2, dtype=np.float32)
    wout = np.asarray(L0_ffn_Wout, dtype=np.float32)
    bout = np.asarray(L0_ffn_bout, dtype=np.float32)

    idx = []
    for blk in range(3):
        for h in range(NH):
            base = h * 192 + blk * 64
            idx.extend(range(base, base + 64))
    wqkvt = np.ascontiguousarray(wqkv[np.array(idx)].T)
    wot = np.ascontiguousarray(wo.T)
    gcol = np.ascontiguousarray(gamma.reshape(8, 128).T)
    bcol = np.ascontiguousarray(beta.reshape(8, 128).T)
    boutcol = np.ascontiguousarray(bout.reshape(8, 128).T)
    bincol = np.ascontiguousarray(bin_.reshape(64, 1))
    b2col = np.ascontiguousarray(b2.reshape(64, 1))
    ident = np.eye(128, dtype=np.float32)
    kq = np.arange(128)
    maskt = np.ascontiguousarray((kq[:, None] <= kq[None, :]).astype(np.float32))
    maskn = np.ascontiguousarray(
        np.where(kq[None, :] > kq[:, None], np.float32(-1e30), np.float32(0.0)))

    nc = build_program()
    in_maps = []
    for c in range(NCORE):
        xs = np.ascontiguousarray(np.transpose(x[c * NSEQ:(c + 1) * NSEQ], (0, 2, 1)))
        in_maps.append(dict(
            x=xs, wqkvt=wqkvt, wot=wot, win=win, wout=wout,
            gcol=gcol, bcol=bcol, bincol=bincol, w1=w1, b1=b1, w2=w2,
            b2col=b2col, boutcol=boutcol, ident=ident, maskt=maskt, maskn=maskn,
        ))
    res = run_bass_kernel_spmd(nc, in_maps, core_ids=list(range(NCORE)))
    outs = []
    for c in range(NCORE):
        yc = res.results[c]["y"]
        outs.append(np.transpose(yc, (0, 2, 1)))
    return np.ascontiguousarray(np.concatenate(outs, axis=0))


# version 27
# speedup vs baseline: 1.1183x; 1.0204x over previous
import numpy as np
import ml_dtypes
import concourse.bass as bass
import concourse.mybir as mybir
from concourse.bass_utils import run_bass_kernel_spmd

F32 = mybir.dt.float32
BF16 = mybir.dt.bfloat16
AF = mybir.ActivationFunctionType
ALU = mybir.AluOpType
AX = mybir.AxisListType

NCORE = 8
NSEQ = 4
T = 2048
D = 1024
NH = 16
DH = 64
NCH = 8
NT = 16
EPS = 1e-5


class Ctr:
    def __init__(self, nc, name):
        self.h = nc.alloc_semaphore(name)
        self.n = 0

    def tick(self, k=1):
        self.n += k
        return (self.h, k)

    def at(self):
        return (self.h, self.n)


class Eng:
    def __init__(self):
        self.ops = []

    def op(self, fn, waits=(), inc=None):
        ws = []
        for w in waits:
            if w is None:
                continue
            if isinstance(w, list):
                ws.extend(t for t in w if t is not None)
            else:
                ws.append(w)
        self.ops.append((tuple(ws), fn, inc))

    def retro_inc(self, inc):
        w, f, old = self.ops[-1]
        assert old is None
        self.ops[-1] = (w, f, inc)


def run_ops(engine, ops):
    waited = {}
    for waits, fn, inc in ops:
        for sem, val in waits:
            if val > 0 and waited.get(id(sem), -1) < val:
                engine.wait_ge(sem, val)
                waited[id(sem)] = val
        ins = fn(engine)
        if inc is not None:
            sem, k = inc
            ins.then_inc(sem, k)


def build_program(nc=None):
    if nc is None:
        nc = bass.Bass("TRN2", target_bir_lowering=False)

    x_d = nc.dram_tensor("x", [NSEQ, D, T], F32, kind="ExternalInput").ap()
    y_d = nc.dram_tensor("y", [NSEQ, D, T], F32, kind="ExternalOutput").ap()
    wqkvt_d = nc.dram_tensor("wqkvt", [D, 3 * D], F32, kind="ExternalInput").ap()
    wot_d = nc.dram_tensor("wot", [D, D], F32, kind="ExternalInput").ap()
    win_d = nc.dram_tensor("win", [D, 64], F32, kind="ExternalInput").ap()
    wout_d = nc.dram_tensor("wout", [64, D], F32, kind="ExternalInput").ap()
    gcol_d = nc.dram_tensor("gcol", [128, 8], F32, kind="ExternalInput").ap()
    bcol_d = nc.dram_tensor("bcol", [128, 8], F32, kind="ExternalInput").ap()
    bincol_d = nc.dram_tensor("bincol", [64, 1], F32, kind="ExternalInput").ap()
    w1_d = nc.dram_tensor("w1", [64, 16], F32, kind="ExternalInput").ap()
    b1_d = nc.dram_tensor("b1", [64, 16], F32, kind="ExternalInput").ap()
    w2_d = nc.dram_tensor("w2", [64, 16], F32, kind="ExternalInput").ap()
    b2col_d = nc.dram_tensor("b2col", [64, 1], F32, kind="ExternalInput").ap()
    boutcol_d = nc.dram_tensor("boutcol", [128, 8], F32, kind="ExternalInput").ap()
    ident_d = nc.dram_tensor("ident", [128, 128], F32, kind="ExternalInput").ap()
    maskt_d = nc.dram_tensor("maskt", [128, 128], F32, kind="ExternalInput").ap()
    maskn_d = nc.dram_tensor("maskn", [128, 128], F32, kind="ExternalInput").ap()
    wqs_d = nc.dram_tensor("wqs", [24, 8, 128, 128], BF16, kind="Internal").ap()
    wos_d = nc.dram_tensor("wos", [8, 8, 128, 128], BF16, kind="Internal").ap()
    wvs_d = nc.dram_tensor("wvs", [2, 8, 4, 128, 128], BF16, kind="Internal").ap()

    BASE = 18464
    def at(name, shape, dt, off):
        return nc.alloc_sbuf_tensor_at(name, shape, dt, offset=BASE + off).ap()

    slotA = at("slotA", [128, NCH * T], BF16, 0)
    slotB = at("slotB", [128, NCH * T], BF16, 32768)
    slotC = at("slotC", [128, NCH * T], BF16, 65536)
    VOFF = 98304
    vbuf = at("vbuf", [128, NT * NH * 65], BF16, VOFF)
    wstage2 = [at("wstage0", [128, 3 * D], F32, VOFF),
               at("wstage1", [128, 3 * D], F32, VOFF + 12288)]
    wbstage = at("wbstage", [128, 3 * D], BF16, VOFF + 24576)
    xstage = at("xstage", [128, 2 * T], F32, VOFF)
    FOFF = 131584
    s_sb = at("s_sb", [64, T], F32, FOFF)
    tfn = at("tfn", [64, T], F32, FOFF + 8192)
    u_fn = at("u_fn", [64, T], F32, FOFF + 16384)
    g_sb = at("g_sb", [64, T], F32, FOFF + 24576)
    gb_sb = at("gb_sb", [64, T], BF16, FOFF + 32768)
    kt = [at("kt0", [65, T], BF16, FOFF), at("kt1", [65, T], BF16, FOFF + 4096)]
    qt = [at("qt0", [65, T], BF16, FOFF + 8192), at("qt1", [65, T], BF16, FOFF + 12288)]
    aT = [at("aT0", [128, 1024], BF16, FOFF + 16384), at("aT1", [128, 1024], BF16, FOFF + 18432)]
    rball = at("rball", [128, 1024], F32, FOFF + 20480)
    ctxstage = at("ctxstage", [64, 1024], BF16, FOFF + 24576)
    wqk_st = at("wqk_st", [128, 4096], BF16, FOFF + 26624)
    wv_st = at("wv_st", [128, 4096], BF16, FOFF + 26624)
    wo_st = at("wo_st", [128, 2048], BF16, FOFF + 26624)
    tmpblk = at("tmpblk", [128, 128], F32, 209472)
    mneg = at("mneg", [128, 8], BF16, 209984)
    mtmp = at("mtmp", [128, 8], F32, 210016)
    R_row = at("R_row", [1, 2048], BF16, 168448)
    B_row = at("B_row", [1, 2048], BF16, 172544)
    Rb_b = at("Rb_b", [128, T], BF16, 176640)
    Bb_b = at("Bb_b", [128, T], BF16, 180736)
    sq_b = at("sq_b", [128, 2 * T], BF16, 184832)
    tmp_b = at("tmp_b", [128, 2 * T], BF16, 193024)
    CO = 201216
    win_sb = at("win_sb", [128, 8 * 64], BF16, CO)
    wout_sb = at("wout_sb", [64, D], BF16, CO + 1024)
    ident_sb = at("ident_sb", [128, 128], F32, CO + 3072)
    maskt_f = at("maskt_f", [128, 128], F32, CO + 3584)
    maskn_sb = at("maskn_sb", [128, 128], F32, CO + 4096)
    maskt_sb = at("maskt_sb", [128, 128], BF16, CO + 4608)
    onesrow_bf = at("onesrow_bf", [1, 128], BF16, CO + 4864)
    ones64 = at("ones64", [65, 64], BF16, CO + 5152)
    ident_bf = at("ident_bf", [128, 128], BF16, CO + 5920)
    rrow_bf = at("rrow_bf", [65, 1024], BF16, CO + 6208)
    w1_sb = at("w1_sb", [64, 16], F32, CO + 5408)
    b1_sb = at("b1_sb", [64, 16], F32, CO + 5472)
    w2_sb = at("w2_sb", [64, 16], F32, CO + 5536)
    c2_col = at("c2_col", [128, 24], F32, CO + 5600)
    c3_col = at("c3_col", [128, 8], F32, CO + 5696)
    g_col = at("g_col", [128, 8], F32, CO + 5728)
    be_col = at("be_col", [128, 8], F32, CO + 5760)
    bout_col = at("bout_col", [128, 8], F32, CO + 5792)
    bin_sb = at("bin_sb", [64, 1], F32, CO + 5824)
    b2_sb = at("b2_sb", [64, 1], F32, CO + 5856)
    onescol_bf = at("onescol_bf", [128, 1], BF16, CO + 5888)

    pA = nc.alloc_psum_tensor("pA", [128, 1024], F32).ap()
    pB = nc.alloc_psum_tensor("pB", [128, 1024], F32).ap()
    pC = nc.alloc_psum_tensor("pC", [128, 1024], F32).ap()
    pD = nc.alloc_psum_tensor("pD", [128, 1024], F32).ap()
    pAll = [pA, pB, pC, pD]

    Tq, Vq, Aq, Gq, Sq = Eng(), Eng(), Eng(), Eng(), Eng()
    t_s = Ctr(nc, "t_s")
    v_s = Ctr(nc, "v_s")
    a_s = Ctr(nc, "a_s")
    g_s = Ctr(nc, "g_s")
    d_c = Ctr(nc, "d_c")
    d_ws = Ctr(nc, "d_ws")
    d_x = [Ctr(nc, "d_x0"), Ctr(nc, "d_x1")]
    d_w = [Ctr(nc, "d_w0"), Ctr(nc, "d_w1"),
           Ctr(nc, "d_w2"), Ctr(nc, "d_w3")]
    d_wv = Ctr(nc, "d_wv")
    d_kq = [Ctr(nc, "d_kq0"), Ctr(nc, "d_kq1")]
    d_cs = Ctr(nc, "d_cs")
    d_out = Ctr(nc, "d_out")

    def dma(dst, src, ctr, waits=()):
        Sq.op(lambda e, d=dst, s=src: e.dma_start(d, s), waits=waits, inc=ctr.tick(16))
        return ctr.at()

    for dst, src in [
        (w1_sb[:], w1_d), (b1_sb[:], b1_d), (w2_sb[:], w2_d),
        (bin_sb[:], bincol_d), (b2_sb[:], b2col_d), (bout_col[:], boutcol_d),
        (g_col[:], gcol_d), (be_col[:], bcol_d), (ident_sb[:], ident_d),
        (maskt_f[:], maskt_d), (maskn_sb[:], maskn_d),
    ]:
        dma(dst, src, d_c)
    t_consts = d_c.at()

    Gq.op(lambda e: e.memset(onesrow_bf[0:1, :], 1.0), inc=g_s.tick())
    Gq.op(lambda e: e.memset(onescol_bf[:], 1.0), inc=g_s.tick())
    Gq.op(lambda e: e.memset(ones64[64:65, :], 1.0), inc=g_s.tick())
    Gq.op(lambda e: e.tensor_copy(maskt_sb[:], maskt_f[:]), waits=[t_consts], inc=g_s.tick())
    Gq.op(lambda e: e.tensor_copy(ident_bf[:], ident_sb[:]), waits=[t_consts], inc=g_s.tick())
    g_const = g_s.at()

    tok = dma(xstage[:, 0:512], win_d.rearrange("(kc p) m -> p kc m", p=128), d_ws)
    Vq.op(lambda e: e.tensor_copy(win_sb[:], xstage[:, 0:512]), waits=[tok], inc=v_s.tick())
    tok = dma(xstage[0:64, 1024:2048], wout_d, d_ws, waits=[v_s.at()])
    Vq.op(lambda e: e.tensor_copy(wout_sb[:], xstage[0:64, 1024:2048]), waits=[tok], inc=v_s.tick())
    v_wsmall = v_s.at()

    last_store = None
    last_wread_t = {0: None, 1: None}
    last_wread_v = {0: None, 1: None}
    for kc in range(NCH):
        sb_ = kc % 2
        wstage = wstage2[sb_]
        tok = dma(wstage[:], wqkvt_d[kc * 128:(kc + 1) * 128, :], d_ws,
                  waits=[last_wread_t[sb_], last_wread_v[sb_], v_wsmall])
        Vq.op(lambda e, kc=kc, wstage=wstage: e.tensor_scalar(
            wbstage[:], wstage[:], g_col[:, kc:kc + 1], None, ALU.mult),
            waits=[tok, t_consts, last_store], inc=v_s.tick())
        last_wread_v[sb_] = v_s.at()
        for mc in range(24):
            Tq.op(lambda e, kc=kc, mc=mc, wstage=wstage: e.matmul(
                pA[:, mc * 8 + kc:mc * 8 + kc + 1], wstage[:, mc * 128:(mc + 1) * 128],
                be_col[:, kc:kc + 1], start=True, stop=True),
                waits=([tok, t_consts] if mc == 0 else []))
        Tq.retro_inc(t_s.tick())
        last_wread_t[sb_] = t_s.at()
        dma(wqs_d[0:16, kc, :, :].rearrange("mc p f -> p mc f"),
            wbstage[:, 0:2048].rearrange("p (mc f) -> p mc f", mc=16),
            d_out, waits=[last_wread_v[sb_]])
        for vh_ in range(2):
            dma(wvs_d[vh_, kc, :, :, :].rearrange("mc p f -> p mc f"),
                wbstage[:, 2048 + vh_ * 512:2048 + vh_ * 512 + 512]
                .rearrange("p (mc f) -> p mc f", mc=4),
                d_out, waits=[])
        last_store = d_out.at()
    for mc in range(24):
        Vq.op(lambda e, mc=mc: e.reduce_sum(
            c2_col[:, mc:mc + 1], pA[:, mc * 8:mc * 8 + 8], axis=AX.X),
            waits=[last_wread_t[0], last_wread_t[1]], inc=v_s.tick())
    v_c2 = v_s.at()

    for kc in range(NCH):
        sb_ = kc % 2
        wstage = wstage2[sb_]
        tok = dma(wstage[:, 0:1024], wot_d[kc * 128:(kc + 1) * 128, :], d_ws,
                  waits=[last_wread_t[sb_], last_wread_v[sb_]])
        Vq.op(lambda e, wstage=wstage: e.tensor_copy(
            wbstage[:, 0:1024], wstage[:, 0:1024]),
            waits=[tok, v_c2, last_store], inc=v_s.tick())
        last_wread_v[sb_] = v_s.at()
        for mc in range(8):
            Tq.op(lambda e, kc=kc, mc=mc, wstage=wstage: e.matmul(
                pB[:, mc * 8 + kc:mc * 8 + kc + 1], wstage[:, mc * 128:(mc + 1) * 128],
                c2_col[:, 16 + kc:17 + kc], start=True, stop=True),
                waits=([tok, v_c2] if mc == 0 else []))
        Tq.retro_inc(t_s.tick())
        last_wread_t[sb_] = t_s.at()
        last_store = dma(wos_d[:, kc, :, :].rearrange("mc p f -> p mc f"),
                         wbstage[:, 0:1024].rearrange("p (mc f) -> p mc f", mc=8),
                         d_out, waits=[last_wread_v[sb_]])
    for mc in range(8):
        Vq.op(lambda e, mc=mc: e.reduce_sum(
            c3_col[:, mc:mc + 1], pB[:, mc * 8:mc * 8 + 8], axis=AX.X),
            waits=[last_wread_t[0], last_wread_t[1]], inc=v_s.tick())
    vput = vbuf[:].rearrange("p (t h f) -> p t h f", t=NT, h=NH, f=65)
    setup_v = v_s.at()
    setup_t = max(last_wread_t[0], last_wread_t[1], key=lambda t: t[1])
    setup_d = d_out.at()
    setup_g = g_s.at()

    def emit_ln(src_sb, src_tok_fn, eps_mul, eps_add, guards):
        sq_read_t = {}
        sqtoks = {}
        for c in range(NCH):
            w = [src_tok_fn(c)] + guards
            if c - 2 in sq_read_t:
                w.append(sq_read_t[c - 2])
            Vq.op(lambda e, c=c: e.tensor_tensor(
                sq_b[:, (c % 2) * T:(c % 2) * T + T],
                src_sb[:, c * T:(c + 1) * T], src_sb[:, c * T:(c + 1) * T], ALU.mult),
                waits=w, inc=v_s.tick())
            sqtoks[c] = v_s.at()
            for th in range(2):
                psum_s = (pA, pB)[th]
                psum_q = (pC, pD)[th]
                for h5 in range(2):
                    cs = th * 1024 + h5 * 512
                    w = [src_tok_fn(c)] + (guards if c == 0 else [])
                    Tq.op(lambda e, c=c, h5=h5, cs=cs, psum_s=psum_s: e.matmul(
                        psum_s[0:1, h5 * 512:h5 * 512 + 512], onescol_bf[:],
                        src_sb[:, c * T + cs:c * T + cs + 512],
                        start=(c == 0), stop=(c == NCH - 1)), waits=w)
                for h5 in range(2):
                    cs = th * 1024 + h5 * 512
                    Tq.op(lambda e, c=c, h5=h5, cs=cs, psum_q=psum_q: e.matmul(
                        psum_q[0:1, h5 * 512:h5 * 512 + 512], onescol_bf[:],
                        sq_b[:, (c % 2) * T + cs:(c % 2) * T + cs + 512],
                        start=(c == 0), stop=(c == NCH - 1)), waits=[sqtoks[c]])
            Tq.retro_inc(t_s.tick())
            sq_read_t[c] = t_s.at()
        t_stats = t_s.at()
        out = {}
        for th in range(2):
            psum_s = (pA, pB)[th]
            psum_q = (pC, pD)[th]
            seg = slice(th * 1024, th * 1024 + 1024)
            Vq.op(lambda e, psum_s=psum_s, seg=seg: e.tensor_scalar(
                B_row[0:1, seg], psum_s[0:1, 0:1024], 1.0 / D, None, ALU.mult),
                waits=[t_stats] + guards, inc=v_s.tick())
            Vq.op(lambda e, seg=seg: e.tensor_tensor(
                R_row[0:1, seg], B_row[0:1, seg], B_row[0:1, seg],
                ALU.mult), inc=v_s.tick())
            Vq.op(lambda e, psum_q=psum_q, seg=seg: e.scalar_tensor_tensor(
                psum_q[0:1, 0:1024], psum_q[0:1, 0:1024], 1.0 / D,
                R_row[0:1, seg], ALU.mult, ALU.subtract), inc=v_s.tick())
            Vq.op(lambda e, psum_q=psum_q: e.tensor_scalar(
                psum_q[0:1, 0:1024], psum_q[0:1, 0:1024], eps_mul, eps_add,
                ALU.mult, ALU.add), inc=v_s.tick())
            v_rows = v_s.at()
            Aq.op(lambda e, psum_s=psum_s, psum_q=psum_q: e.activation(
                psum_s[0:1, 0:1024], psum_q[0:1, 0:1024], AF.Ln),
                waits=[v_rows] + guards, inc=a_s.tick())
            Aq.op(lambda e, psum_s=psum_s, seg=seg: e.activation(
                R_row[0:1, seg], psum_s[0:1, 0:1024], AF.Exp, scale=-0.5),
                inc=a_s.tick())
            a_R = a_s.at()
            Vq.op(lambda e, seg=seg: e.scalar_tensor_tensor(
                B_row[0:1, seg], B_row[0:1, seg], -1.0,
                R_row[0:1, seg], ALU.mult, ALU.mult),
                waits=[a_R], inc=v_s.tick())
        v_rowsdone = v_s.at()
        bc_evac = {0: None, 1: None}
        for row, dstb in [(R_row, Rb_b), (B_row, Bb_b)]:
            for th in range(2):
                for h5 in range(2):
                    pdst = (pC, pD)[h5]
                    cs = th * 1024 + h5 * 512
                    Tq.op(lambda e, row=row, cs=cs, pdst=pdst: e.matmul(
                        pdst[:, 0:512], onesrow_bf[0:1, :],
                        row[0:1, cs:cs + 512],
                        start=True, stop=True),
                        waits=[v_rowsdone, bc_evac[h5]] + guards, inc=t_s.tick())
                    tb = t_s.at()
                    Vq.op(lambda e, dstb=dstb, cs=cs, pdst=pdst: e.tensor_copy(
                        dstb[:, cs:cs + 512],
                        pdst[:, 0:512]), waits=[tb], inc=v_s.tick())
                    bc_evac[h5] = v_s.at()
        out["Rb"] = v_s.at()
        out["t_end"] = t_s.at()
        return out

    prevs = {
        "out_dma": [setup_d, None],
        "slotA_free": [setup_t],
        "slotB_free": [],
        "slotC_free": [],
        "slotA_q_free": [],
        "vbuf_free": [],
        "stream_free": [],
        "ln_guards": [setup_v],
    }
    asm_guard = [None, None]
    ctxstage_guard = [None]
    x_first_guard = [setup_t, setup_d, setup_v, setup_g]

    for s in range(NSEQ):
        cast_toks = []
        xload = []
        for c in range(NCH):
            w = list(prevs["out_dma"]) + (x_first_guard if s == 0 else [])
            if c >= 2:
                w.append(cast_toks[c - 2])
            tok = dma(xstage[:, (c % 2) * T:(c % 2) * T + T],
                      x_d[s, c * 128:(c + 1) * 128, :], d_x[c % 2], waits=w)
            xload.append(tok)
            Aq.op(lambda e, c=c: e.activation(
                slotA[:, c * T:(c + 1) * T], xstage[:, (c % 2) * T:(c % 2) * T + T],
                AF.Copy),
                waits=[tok] + prevs["slotA_free"], inc=a_s.tick())
            cast_toks.append(a_s.at())
        ones_w = [cast_toks[-1]] + prevs["vbuf_free"] + prevs["stream_free"]
        if s == 0:
            ones_w += [setup_d, setup_t]
        Gq.op(lambda e: e.memset(vput[:, :, :, 64:65], 1.0), waits=ones_w,
              inc=g_s.tick())
        Gq.op(lambda e: e.memset(kt[0][0:1, :], 1.0), inc=g_s.tick())
        Gq.op(lambda e: e.memset(kt[1][0:1, :], 1.0), inc=g_s.tick())
        ones_tok = g_s.at()
        ln1 = emit_ln(slotA, lambda c: cast_toks[c], 1.0 + EPS, EPS * EPS,
                      prevs["ln_guards"])
        xn_toks = []
        for c in range(NCH):
            Vq.op(lambda e, c=c: e.tensor_tensor(
                tmp_b[:, (c % 2) * T:(c % 2) * T + T],
                slotA[:, c * T:(c + 1) * T], Rb_b[:], ALU.mult),
                waits=[ln1["Rb"], cast_toks[c]], inc=v_s.tick())
            Vq.op(lambda e, c=c: e.tensor_tensor(
                slotC[:, c * T:(c + 1) * T],
                tmp_b[:, (c % 2) * T:(c % 2) * T + T], Bb_b[:], ALU.add),
                waits=prevs["slotC_free"], inc=v_s.tick())
            xn_toks.append(v_s.at())
        xn_done = v_s.at()

        qk_evac = {}
        last_st_read = [None, None, None, None]
        first_stream = True
        for th in range(2):
            for mc in range(16):
                g = th * 16 + mc
                sbh = g % 4
                w = [setup_d, last_st_read[sbh]]
                if first_stream:
                    w += prevs["stream_free"]
                    first_stream = False
                wtok = dma(wqk_st[:, sbh * 1024:sbh * 1024 + 1024]
                           .rearrange("p (kc f) -> p kc f", kc=8),
                           wqs_d[mc, :, :, :].rearrange("kc p f -> p kc f"),
                           d_w[sbh], waits=w)
                pt = pAll[g % 4]
                wfirst = [wtok, xn_done, ln1["t_end"]]
                if g - 4 >= 0:
                    wfirst.append(qk_evac[g - 4])
                for kc in range(NCH):
                    for h5 in range(2):
                        Tq.op(lambda e, mc=mc, kc=kc, th=th, h5=h5, pt=pt, sbh=sbh: e.matmul(
                            pt[:, h5 * 512:h5 * 512 + 512],
                            wqk_st[:, sbh * 1024 + kc * 128:sbh * 1024 + kc * 128 + 128],
                            slotC[:, kc * T + th * 1024 + h5 * 512:
                                  kc * T + th * 1024 + h5 * 512 + 512],
                            start=(kc == 0), stop=(kc == NCH - 1)),
                            waits=(wfirst if (kc == 0 and h5 == 0) else []))
                Tq.retro_inc(t_s.tick())
                t_grp = t_s.at()
                last_st_read[sbh] = t_grp
                dst = slotA if mc < 8 else slotB
                dmc = mc if mc < 8 else mc - 8
                wev = [t_grp]
                wev += prevs["slotA_q_free"] if mc < 8 else prevs["slotB_free"]
                Aq.op(lambda e, dst=dst, dmc=dmc, th=th, pt=pt, mc=mc: e.activation(
                    dst[:, dmc * T + th * 1024:dmc * T + th * 1024 + 1024], pt[:],
                    AF.Identity, bias=c2_col[:, mc:mc + 1]),
                    waits=wev, inc=a_s.tick())
                qk_evac[g] = a_s.at()
        qk_done = a_s.at()
        t_qk = t_s.at()


        asm_tok = {}

        def emit_asm(h, extra):
            wk = [qk_done, asm_guard[h % 2], extra]
            dma(kt[h % 2][1:65, :],
                slotB[(h % 2) * 64:(h % 2) * 64 + 64, (h // 2) * T:(h // 2) * T + T],
                d_kq[h % 2], waits=wk)
            dma(qt[h % 2][1:65, :],
                slotA[(h % 2) * 64:(h % 2) * 64 + 64, (h // 2) * T:(h // 2) * T + T],
                d_kq[h % 2], waits=[])
            asm_tok[h] = d_kq[h % 2].at()

        emit_asm(0, None)
        emit_asm(1, None)

        st = {"sprobe_b": [None, None], "probe_par": 0, "mrow": None,
              "aT": [None, None], "pT": [None, None], "ctx": None,
              "qrow": {}, "t_av": None}

        def gen_A(h, half):
            q0 = half * 1024
            mcol_toks = []
            for idx, i in enumerate(range(8 * half, 8 * half + 8)):
                row_w = (i + 1) * 128
                nchunks = 0
                for ck0 in range(0, row_w, 512):
                    ck1 = min(ck0 + 512, row_w)
                    pb_ = st["probe_par"]
                    st["probe_par"] ^= 1
                    w = [qk_done, st["sprobe_b"][pb_], ln1["Rb"]]
                    if pb_ == 1:
                        w.append(st["mrow"])
                    Tq.op(lambda e, h=h, i=i, ck0=ck0, ck1=ck1, pb_=pb_: e.matmul(
                        pA[:, pb_ * 512:pb_ * 512 + ck1 - ck0],
                        slotA[(h % 2) * 64:(h % 2) * 64 + 64,
                              (h // 2) * T + i * 128:(h // 2) * T + i * 128 + 128],
                        slotB[(h % 2) * 64:(h % 2) * 64 + 64,
                              (h // 2) * T + ck0:(h // 2) * T + ck1],
                        start=True, stop=True),
                        waits=w, inc=t_s.tick())
                    tmm = t_s.at()
                    Vq.op(lambda e, nch=nchunks, w=ck1 - ck0, pb_=pb_: e.reduce_max(
                        mtmp[:, nch:nch + 1], pA[:, pb_ * 512:pb_ * 512 + w], axis=AX.X),
                        waits=[tmm], inc=v_s.tick())
                    st["sprobe_b"][pb_] = v_s.at()
                    nchunks += 1
                Vq.op(lambda e, idx=idx, nch=nchunks: e.reduce_max(
                    mneg[:, idx:idx + 1], mtmp[:, 0:nch], axis=AX.X, negate=True),
                    inc=v_s.tick())
                mcol_toks.append(v_s.at())
                yield
            for qtr in range(2):
                for r in range(4):
                    w = [mcol_toks[qtr * 4 + r]]
                    if r == 0:
                        w += [st["mrow"], st["sprobe_b"][1]]
                    Tq.op(lambda e, qtr=qtr, r=r: e.matmul(
                        pA[0:1, 512 + r * 128:512 + r * 128 + 128],
                        mneg[:, qtr * 4 + r:qtr * 4 + r + 1], ident_bf[:],
                        start=True, stop=True),
                        waits=w, inc=(t_s.tick() if r == 3 else None))
                t_mrow = t_s.at()
                st["sprobe_b"][1] = None
                Aq.op(lambda e, h=h, q0=q0, qtr=qtr: e.activation(
                    qt[h % 2][0:1, q0 + qtr * 512:q0 + qtr * 512 + 512],
                    pA[0:1, 512:1024], AF.Copy),
                    waits=[t_mrow, asm_tok[h]], inc=a_s.tick())
                st["mrow"] = a_s.at()
                yield
            st["qrow"][(h, half)] = a_s.at()

        def gen_B(h, half):
            q0 = half * 1024
            qrow_done = st["qrow"][(h, half)]
            strips = list(range(0, 8 * (half + 1)))
            first_j, last_j = strips[0], strips[-1]
            pend = None

            def emit_av(pend):
                cks, bi, a_exp, diag_tok, j = pend
                wv = [a_exp, diag_tok, ones_tok]
                if j == first_j:
                    wv.append(st["ctx"])
                for ci, (c0, c1) in enumerate(cks):
                    lastj_chunk = min(last_j, (q0 + c1 - 1) // 128)
                    Tq.op(lambda e, h=h, j=j, c0=c0, c1=c1, bi=bi,
                          sp=(j == lastj_chunk): e.matmul(
                        pD[0:65, c0:c1],
                        vbuf[:, j * NH * 65 + h * 65:j * NH * 65 + h * 65 + 65],
                        aT[bi][:, c0:c1],
                        start=(j == first_j), stop=sp),
                        waits=(wv if ci == 0 else []),
                        inc=(t_s.tick() if ci == len(cks) - 1 else None))
                st["aT"][bi] = t_s.at()

            for j in strips:
                rel0 = max(j * 128 - q0, 0)
                bi = j % 2
                buf = (pB, pC)[bi]
                cks = []
                c0 = rel0
                while c0 < 1024:
                    c1 = min((c0 // 512 + 1) * 512, 1024)
                    cks.append((c0, c1))
                    c0 = c1
                wv = [qrow_done, asm_tok[h], st["pT"][bi], ones_tok]
                for ci, (c0, c1) in enumerate(cks):
                    Tq.op(lambda e, h=h, j=j, c0=c0, c1=c1, buf=buf, q0=q0: e.matmul(
                        buf[:, c0:c1],
                        kt[h % 2][0:65, j * 128:j * 128 + 128],
                        qt[h % 2][0:65, q0 + c0:q0 + c1],
                        start=True, stop=True),
                        waits=(wv if ci == 0 else []),
                        inc=(t_s.tick() if ci == len(cks) - 1 else None))
                t_st = t_s.at()
                Aq.op(lambda e, bi=bi, rel0=rel0, buf=buf: e.activation(
                    aT[bi][:, rel0:1024], buf[:, rel0:1024], AF.Exp, scale=0.125),
                    waits=[t_st, st["aT"][bi]], inc=a_s.tick())
                st["pT"][bi] = a_s.at()
                a_exp = a_s.at()
                diag_tok = None
                if j >= 8 * half:
                    Gq.op(lambda e, bi=bi, rel0=rel0: e.tensor_tensor(
                        aT[bi][:, rel0:rel0 + 128], aT[bi][:, rel0:rel0 + 128],
                        maskt_sb[:], ALU.mult),
                        waits=[a_exp], inc=g_s.tick())
                    diag_tok = g_s.at()
                if pend is not None:
                    emit_av(pend)
                pend = (cks, bi, a_exp, diag_tok, j)
                yield
            emit_av(pend)
            t_av = t_s.at()
            st["t_av"] = t_av

            Aq.op(lambda e: e.activation(rball[64:65, :], pD[64:65, :], AF.Ln),
                  waits=[t_av], inc=a_s.tick())
            Aq.op(lambda e: e.activation(rrow_bf[64:65, :], rball[64:65, :],
                                         AF.Exp, scale=-1.0), inc=a_s.tick())
            a_r = a_s.at()
            for h5 in range(2):
                Tq.op(lambda e, h5=h5: e.matmul(
                    pB[0:64, h5 * 512:h5 * 512 + 512], ones64[64:65, :],
                    rrow_bf[64:65, h5 * 512:h5 * 512 + 512], start=True, stop=True),
                    waits=([a_r, st["pT"][0]] if h5 == 0 else []),
                    inc=(t_s.tick() if h5 == 1 else None))
            t_rb = t_s.at()
            Aq.op(lambda e: e.activation(rball[0:64, :], pB[0:64, :], AF.Copy),
                  waits=[t_rb], inc=a_s.tick())
            v_rb = a_s.at()
            st["pT"][0] = v_rb
            if h % 2 == 0:
                Vq.op(lambda e, h=h, q0=q0: e.tensor_tensor(
                    slotC[0:64, (h // 2) * T + q0:(h // 2) * T + q0 + 1024],
                    pD[0:64, :], rball[0:64, :], ALU.mult),
                    waits=[v_rb], inc=v_s.tick())
                st["ctx"] = v_s.at()
            else:
                Vq.op(lambda e: e.tensor_tensor(
                    ctxstage[:], pD[0:64, :], rball[0:64, :], ALU.mult),
                    waits=[v_rb, ctxstage_guard[0]], inc=v_s.tick())
                st["ctx"] = v_s.at()
                tok = dma(slotC[64:128, (h // 2) * T + q0:(h // 2) * T + q0 + 1024],
                          ctxstage[:], d_cs, waits=[st["ctx"]])
                ctxstage_guard[0] = tok
            if half == 1:
                asm_guard[h % 2] = t_av
                if h + 2 < NH:
                    emit_asm(h + 2, t_av)
            yield

        pairs = [(h, hf) for h in range(NH) for hf in range(2)]
        gA0 = gen_A(*pairs[0])
        gA0_done = [False]

        def pump():
            if not gA0_done[0]:
                try:
                    next(gA0)
                except StopIteration:
                    gA0_done[0] = True
        v_ev = {}
        last_v_read = None
        for vh in range(2):
            w = [setup_d, t_qk if vh == 0 else last_v_read] + list(prevs["out_dma"])
            wtok = dma(wv_st[:].rearrange("p (km f) -> p km f", km=32),
                       wvs_d[vh, :, :, :, :].rearrange("kc mc p f -> p (kc mc) f"),
                       d_wv, waits=w)
            for tt in range(NT):
                gidx = vh * NT + tt
                pt = (pB, pC, pD)[gidx % 3]
                wfirst = [wtok, xn_done]
                if gidx - 3 >= 0:
                    wfirst.append(v_ev[gidx - 3])
                else:
                    wfirst.append(qk_done)
                for kc in range(NCH):
                    Tq.op(lambda e, tt=tt, kc=kc, pt=pt: e.matmul(
                        pt[:, 0:512],
                        slotC[:, kc * T + tt * 128:kc * T + tt * 128 + 128],
                        wv_st[:, kc * 512:kc * 512 + 512],
                        start=(kc == 0), stop=(kc == NCH - 1)),
                        waits=(wfirst if kc == 0 else []))
                Tq.retro_inc(t_s.tick())
                t_grp = t_s.at()
                last_v_read = t_grp
                dstv = vbuf[:, tt * NH * 65:(tt + 1) * NH * 65] \
                    .rearrange("p (h f) -> p h f", h=NH)[:, vh * 8:vh * 8 + 8, 0:64]
                srcv = pt[:, 0:512].rearrange("p (h f) -> p h f", h=8)
                Aq.op(lambda e, dstv=dstv, srcv=srcv: e.activation(dstv, srcv, AF.Copy),
                      waits=[t_grp, setup_g] + prevs["vbuf_free"] + list(prevs["out_dma"]),
                      inc=a_s.tick())
                v_ev[gidx] = a_s.at()
                pump()
                pump()
        v_done = a_s.at()
        t_v = t_s.at()
        st["pT"] = [v_done, v_done]
        st["ctx"] = v_done
        st["aT"] = [st["aT"][0], st["aT"][1]]
        for _ in gA0:
            pass
        gAn = None
        for k, pr in enumerate(pairs):
            gB = gen_B(*pr)
            gAn = gen_A(*pairs[k + 1]) if k + 1 < len(pairs) else None
            nstrips = 8 * (pr[1] + 1)
            if nstrips >= 10:
                sched = [0] * (nstrips - 10) + [1] * 10
            else:
                sched = [1] * nstrips
                for i in range(10 - nstrips):
                    sched[-1 - (i % 2)] += 1
            yi = 0
            for _ in gB:
                nsteps = sched[yi] if yi < len(sched) else 1
                yi += 1
                for _ in range(nsteps):
                    if gAn is not None:
                        try:
                            next(gAn)
                        except StopIteration:
                            gAn = None
            if gAn is not None:
                for _ in gAn:
                    pass
                gAn = None
        t_av_last = st["t_av"]
        v_p3 = v_s.at()
        a_p3 = a_s.at()
        d_p3 = [d_kq[0].at(), d_kq[1].at(), d_cs.at()]
        t_p3 = t_s.at()

        ao_evac = {}
        last_wo_read = [None, None]
        for mc in range(8):
            sbh = mc % 2
            w = [setup_d, last_wo_read[sbh]]
            if mc == 0:
                w += [t_p3]
            wtok = dma(wo_st[:, sbh * 1024:sbh * 1024 + 1024]
                       .rearrange("p (kc f) -> p kc f", kc=8),
                       wos_d[mc, :, :, :].rearrange("kc p f -> p kc f"),
                       d_w[sbh], waits=w)
            pts = (pA, pB) if mc % 2 == 0 else (pC, pD)
            wfirst = [wtok, v_p3, a_p3, d_p3]
            if mc >= 2:
                wfirst.append(ao_evac[mc - 2])
            for kc in range(NCH):
                for sl in range(4):
                    pt = pts[sl // 2]
                    Tq.op(lambda e, mc=mc, kc=kc, sl=sl, pt=pt, sbh=sbh: e.matmul(
                        pt[:, (sl % 2) * 512:(sl % 2) * 512 + 512],
                        wo_st[:, sbh * 1024 + kc * 128:sbh * 1024 + kc * 128 + 128],
                        slotC[:, kc * T + sl * 512:kc * T + sl * 512 + 512],
                        start=(kc == 0), stop=(kc == NCH - 1)),
                        waits=(wfirst if (kc == 0 and sl == 0) else []))
            Tq.retro_inc(t_s.tick())
            t_grp = t_s.at()
            last_wo_read[sbh] = t_grp
            for halfp in range(2):
                Aq.op(lambda e, mc=mc, halfp=halfp, pts=pts: e.activation(
                    slotA[:, mc * T + halfp * 1024:mc * T + halfp * 1024 + 1024],
                    pts[halfp][:], AF.Identity, bias=c3_col[:, mc:mc + 1]),
                    waits=[t_grp, d_p3], inc=a_s.tick())
            ao_evac[mc] = a_s.at()
        ao_done = a_s.at()
        t_wo = t_s.at()

        ln3 = emit_ln(slotA, lambda c: ao_evac[c], 1.0, EPS, [ao_done, v_p3])
        x2_toks = []
        for c in range(NCH):
            w = [cast_toks[-1], t_av_last] + ([x2_toks[c - 2]] if c >= 2 else [ln3["Rb"]])
            xtok = dma(xstage[:, (c % 2) * T:(c % 2) * T + T],
                       x_d[s, c * 128:(c + 1) * 128, :], d_x[c % 2], waits=w)
            Vq.op(lambda e, c=c: e.tensor_tensor(
                tmp_b[:, (c % 2) * T:(c % 2) * T + T],
                slotA[:, c * T:(c + 1) * T], Rb_b[:], ALU.mult),
                waits=[ln3["Rb"]] + ([x2_toks[c - 2]] if c >= 2 else []),
                inc=v_s.tick())
            Vq.op(lambda e, c=c: e.tensor_tensor(
                tmp_b[:, (c % 2) * T:(c % 2) * T + T],
                tmp_b[:, (c % 2) * T:(c % 2) * T + T], Bb_b[:], ALU.add),
                inc=v_s.tick())
            Vq.op(lambda e, c=c: e.tensor_tensor(
                slotB[:, c * T:(c + 1) * T],
                tmp_b[:, (c % 2) * T:(c % 2) * T + T],
                xstage[:, (c % 2) * T:(c % 2) * T + T], ALU.add),
                waits=[xtok, t_wo, d_p3], inc=v_s.tick())
            x2_toks.append(v_s.at())
        x2_done = v_s.at()

        ln4 = emit_ln(slotB, lambda c: x2_toks[c], 1.0, EPS, [x2_done, ao_done])
        xn2_toks = []
        for c in range(NCH):
            Vq.op(lambda e, c=c: e.tensor_tensor(
                tmp_b[:, (c % 2) * T:(c % 2) * T + T],
                slotB[:, c * T:(c + 1) * T], Rb_b[:], ALU.mult),
                waits=[ln4["Rb"]], inc=v_s.tick())
            Vq.op(lambda e, c=c: e.tensor_tensor(
                slotA[:, c * T:(c + 1) * T],
                tmp_b[:, (c % 2) * T:(c % 2) * T + T], Bb_b[:], ALU.add),
                inc=v_s.tick())
            xn2_toks.append(v_s.at())
        xn2_done = v_s.at()

        s_ev = []
        for sl in range(4):
            pt = (pA, pA, pB, pB)[sl]
            off = (sl % 2) * 512
            for kc in range(NCH):
                Tq.op(lambda e, sl=sl, kc=kc, pt=pt, off=off: e.matmul(
                    pt[0:64, off:off + 512],
                    win_sb[:, kc * 64:kc * 64 + 64],
                    slotA[:, kc * T + sl * 512:kc * T + sl * 512 + 512],
                    start=(kc == 0), stop=(kc == NCH - 1)),
                    waits=([xn2_done, ln4["t_end"]] if kc == 0 else []))
            Tq.retro_inc(t_s.tick())
            t_grp = t_s.at()
            if sl % 2 == 1:
                pth = (pA, pB)[sl // 2]
                Aq.op(lambda e, sl=sl, pth=pth: e.activation(
                    s_sb[:, (sl - 1) * 512:(sl + 1) * 512], pth[0:64, :],
                    AF.Identity, bias=bin_sb[:]),
                    waits=[t_grp, d_p3], inc=a_s.tick())
                s_ev.append(a_s.at())
        s_done = a_s.at()
        g_q = []
        ts_toks = {}
        for q4 in range(4):
            qs = slice(q4 * 512, q4 * 512 + 512)
            for i in range(16):
                tfa = tfn[0:64, (i % 2) * 512:(i % 2) * 512 + 512]
                if i < 2:
                    wa = [s_ev[q4 // 2]]
                    if q4 > 0:
                        wa.append(ts_toks[(q4 - 1, 14 + i)])
                else:
                    wa = [ts_toks[(q4, i - 2)]]
                Aq.op(lambda e, i=i, qs=qs, tfa=tfa: e.activation(
                    tfa, s_sb[0:64, qs], AF.Identity,
                    scale=w1_sb[:, i:i + 1], bias=b1_sb[:, i:i + 1]),
                    waits=wa, inc=a_s.tick())
                a_tf = a_s.at()
                dst = g_sb if i == 0 else u_fn
                Vq.op(lambda e, dst=dst, i=i, qs=qs, tfa=tfa: e.tensor_scalar(
                    dst[0:64, qs], tfa, 0.0, w2_sb[:, i:i + 1],
                    ALU.max, ALU.mult), waits=[a_tf], inc=v_s.tick())
                ts_toks[(q4, i)] = v_s.at()
                if i > 0:
                    Vq.op(lambda e, qs=qs: e.tensor_tensor(
                        g_sb[0:64, qs], g_sb[0:64, qs], u_fn[0:64, qs], ALU.add),
                        inc=v_s.tick())
            Vq.op(lambda e, qs=qs: e.tensor_scalar(
                gb_sb[0:64, qs], g_sb[0:64, qs], b2_sb[:], None, ALU.add),
                inc=v_s.tick())
            g_q.append(v_s.at())
        g_done = v_s.at()
        fo_ev = {}
        for mc in range(8):
            pts = (pA, pB) if mc % 2 == 0 else (pC, pD)
            for sl in range(4):
                pt = pts[sl // 2]
                w = [g_q[sl]]
                if sl == 0 and mc >= 2:
                    w.append(fo_ev[mc - 2])
                Tq.op(lambda e, mc=mc, sl=sl, pt=pt: e.matmul(
                    pt[:, (sl % 2) * 512:(sl % 2) * 512 + 512],
                    wout_sb[0:64, mc * 128:mc * 128 + 128],
                    gb_sb[:, sl * 512:sl * 512 + 512],
                    start=True, stop=True),
                    waits=w)
            Tq.retro_inc(t_s.tick())
            t_grp = t_s.at()
            for halfp in range(2):
                Aq.op(lambda e, mc=mc, halfp=halfp, pts=pts: e.activation(
                    slotA[:, mc * T + halfp * 1024:mc * T + halfp * 1024 + 1024],
                    pts[halfp][:], AF.Identity, bias=bout_col[:, mc:mc + 1]),
                    waits=[t_grp, xn2_done], inc=a_s.tick())
            fo_ev[mc] = a_s.at()
        fo_done = a_s.at()

        ln5 = emit_ln(slotA, lambda c: fo_ev[c], 1.0, EPS, [fo_done, x2_done])
        out_toks = []
        out_g = []
        for c in range(NCH):
            Vq.op(lambda e, c=c: e.tensor_tensor(
                tmp_b[:, (c % 2) * T:(c % 2) * T + T],
                slotA[:, c * T:(c + 1) * T], Rb_b[:], ALU.mult),
                waits=[ln5["Rb"]] + ([out_g[c - 2]] if c >= 2 else []),
                inc=v_s.tick())
            Vq.op(lambda e, c=c: e.tensor_tensor(
                tmp_b[:, (c % 2) * T:(c % 2) * T + T],
                tmp_b[:, (c % 2) * T:(c % 2) * T + T], Bb_b[:], ALU.add),
                inc=v_s.tick())
            w = [out_toks[c - 2]] if c >= 2 else [xn2_done]
            Vq.op(lambda e, c=c: e.tensor_tensor(
                xstage[:, (c % 2) * T:(c % 2) * T + T],
                tmp_b[:, (c % 2) * T:(c % 2) * T + T],
                slotB[:, c * T:(c + 1) * T], ALU.add),
                waits=w, inc=v_s.tick())
            v_out = v_s.at()
            out_g.append(v_out)
            tok = dma(y_d[s, c * 128:(c + 1) * 128, :],
                      xstage[:, (c % 2) * T:(c % 2) * T + T], d_out, waits=[v_out])
            out_toks.append(tok)
        v_p6 = v_s.at()
        t_ln5 = ln5["t_end"]

        g_p6 = g_s.at()
        prevs = {
            "out_dma": [out_toks[-1]],
            "slotA_free": [v_p6, t_ln5],
            "slotB_free": [v_p6, g_p6],
            "slotC_free": [t_wo],
            "slotA_q_free": [t_p3, d_p3],
            "vbuf_free": [t_av_last],
            "stream_free": [v_p6, fo_done],
            "ln_guards": [v_p6, fo_done],
        }

    nc._dbg = dict(slotA=slotA, slotB=slotB, slotC=slotC, vbuf=vbuf, kt0=kt[0],
                   kt1=kt[1], qt0=qt[0], qt1=qt[1], c2_col=c2_col, c3_col=c3_col,
                   Rb_b=Rb_b, Bb_b=Bb_b, R_row=R_row, B_row=B_row, s_sb=s_sb,
                   g_sb=g_sb, gb_sb=gb_sb, rball=rball, mneg=mneg, aT0=aT[0],
                   aT1=aT[1], win_sb=win_sb, wout_sb=wout_sb, tfn=tfn, u_fn=u_fn, g=g_sb, w1=w1_sb, b1=b1_sb, w2=w2_sb)

    Sq.op(lambda e: None, waits=[d_out.at()])

    with nc.Block() as block:
        @block.tensor
        def _(e):
            run_ops(e, Tq.ops)

        @block.vector
        def _(e):
            run_ops(e, Vq.ops)

        @block.scalar
        def _(e):
            run_ops(e, Aq.ops)

        @block.gpsimd
        def _(e):
            run_ops(e, Gq.ops)

        @block.sync
        def _(e):
            run_ops(e, Sq.ops)

    return nc


def kernel(L0_attn_x, L0_attn_ln_gamma, L0_attn_ln_beta, L0_attn_W_qkv, L0_attn_W_o,
           L0_ffn_Win, L0_ffn_bin_, L0_ffn_W1, L0_ffn_b1, L0_ffn_W2, L0_ffn_b2,
           L0_ffn_Wout, L0_ffn_bout):
    x = np.asarray(L0_attn_x, dtype=np.float32)
    gamma = np.asarray(L0_attn_ln_gamma, dtype=np.float32)
    beta = np.asarray(L0_attn_ln_beta, dtype=np.float32)
    wqkv = np.asarray(L0_attn_W_qkv, dtype=np.float32)
    wo = np.asarray(L0_attn_W_o, dtype=np.float32)
    win = np.asarray(L0_ffn_Win, dtype=np.float32)
    bin_ = np.asarray(L0_ffn_bin_, dtype=np.float32)
    w1 = np.asarray(L0_ffn_W1, dtype=np.float32)
    b1 = np.asarray(L0_ffn_b1, dtype=np.float32)
    w2 = np.asarray(L0_ffn_W2, dtype=np.float32)
    b2 = np.asarray(L0_ffn_b2, dtype=np.float32)
    wout = np.asarray(L0_ffn_Wout, dtype=np.float32)
    bout = np.asarray(L0_ffn_bout, dtype=np.float32)

    idx = []
    for blk in range(3):
        for h in range(NH):
            base = h * 192 + blk * 64
            idx.extend(range(base, base + 64))
    wqkvt = np.ascontiguousarray(wqkv[np.array(idx)].T)
    wot = np.ascontiguousarray(wo.T)
    gcol = np.ascontiguousarray(gamma.reshape(8, 128).T)
    bcol = np.ascontiguousarray(beta.reshape(8, 128).T)
    boutcol = np.ascontiguousarray(bout.reshape(8, 128).T)
    bincol = np.ascontiguousarray(bin_.reshape(64, 1))
    b2col = np.ascontiguousarray(b2.reshape(64, 1))
    ident = np.eye(128, dtype=np.float32)
    kq = np.arange(128)
    maskt = np.ascontiguousarray((kq[:, None] <= kq[None, :]).astype(np.float32))
    maskn = np.ascontiguousarray(
        np.where(kq[None, :] > kq[:, None], np.float32(-1e30), np.float32(0.0)))

    nc = build_program()
    in_maps = []
    for c in range(NCORE):
        xs = np.ascontiguousarray(np.transpose(x[c * NSEQ:(c + 1) * NSEQ], (0, 2, 1)))
        in_maps.append(dict(
            x=xs, wqkvt=wqkvt, wot=wot, win=win, wout=wout,
            gcol=gcol, bcol=bcol, bincol=bincol, w1=w1, b1=b1, w2=w2,
            b2col=b2col, boutcol=boutcol, ident=ident, maskt=maskt, maskn=maskn,
        ))
    res = run_bass_kernel_spmd(nc, in_maps, core_ids=list(range(NCORE)))
    outs = []
    for c in range(NCORE):
        yc = res.results[c]["y"]
        outs.append(np.transpose(yc, (0, 2, 1)))
    return np.ascontiguousarray(np.concatenate(outs, axis=0))


# version 29
# speedup vs baseline: 1.1318x; 1.0120x over previous
import numpy as np
import ml_dtypes
import concourse.bass as bass
import concourse.mybir as mybir
from concourse.bass_utils import run_bass_kernel_spmd

F32 = mybir.dt.float32
BF16 = mybir.dt.bfloat16
AF = mybir.ActivationFunctionType
ALU = mybir.AluOpType
AX = mybir.AxisListType

NCORE = 8
NSEQ = 4
T = 2048
D = 1024
NH = 16
DH = 64
NCH = 8
NT = 16
EPS = 1e-5


class Ctr:
    def __init__(self, nc, name):
        self.h = nc.alloc_semaphore(name)
        self.n = 0

    def tick(self, k=1):
        self.n += k
        return (self.h, k)

    def at(self):
        return (self.h, self.n)


class Eng:
    def __init__(self):
        self.ops = []

    def op(self, fn, waits=(), inc=None):
        ws = []
        for w in waits:
            if w is None:
                continue
            if isinstance(w, list):
                ws.extend(t for t in w if t is not None)
            else:
                ws.append(w)
        self.ops.append((tuple(ws), fn, inc))

    def retro_inc(self, inc):
        w, f, old = self.ops[-1]
        assert old is None
        self.ops[-1] = (w, f, inc)


def run_ops(engine, ops):
    waited = {}
    for waits, fn, inc in ops:
        for sem, val in waits:
            if val > 0 and waited.get(id(sem), -1) < val:
                engine.wait_ge(sem, val)
                waited[id(sem)] = val
        ins = fn(engine)
        if inc is not None:
            sem, k = inc
            ins.then_inc(sem, k)


def build_program(nc=None):
    if nc is None:
        nc = bass.Bass("TRN2", target_bir_lowering=False)

    x_d = nc.dram_tensor("x", [NSEQ, D, T], F32, kind="ExternalInput").ap()
    y_d = nc.dram_tensor("y", [NSEQ, D, T], F32, kind="ExternalOutput").ap()
    wqkvt_d = nc.dram_tensor("wqkvt", [D, 3 * D], F32, kind="ExternalInput").ap()
    wot_d = nc.dram_tensor("wot", [D, D], F32, kind="ExternalInput").ap()
    win_d = nc.dram_tensor("win", [D, 64], F32, kind="ExternalInput").ap()
    wout_d = nc.dram_tensor("wout", [64, D], F32, kind="ExternalInput").ap()
    gcol_d = nc.dram_tensor("gcol", [128, 8], F32, kind="ExternalInput").ap()
    bcol_d = nc.dram_tensor("bcol", [128, 8], F32, kind="ExternalInput").ap()
    bincol_d = nc.dram_tensor("bincol", [64, 1], F32, kind="ExternalInput").ap()
    w1_d = nc.dram_tensor("w1", [64, 16], F32, kind="ExternalInput").ap()
    b1_d = nc.dram_tensor("b1", [64, 16], F32, kind="ExternalInput").ap()
    w2_d = nc.dram_tensor("w2", [64, 16], F32, kind="ExternalInput").ap()
    b2col_d = nc.dram_tensor("b2col", [64, 1], F32, kind="ExternalInput").ap()
    boutcol_d = nc.dram_tensor("boutcol", [128, 8], F32, kind="ExternalInput").ap()
    ident_d = nc.dram_tensor("ident", [128, 128], F32, kind="ExternalInput").ap()
    maskt_d = nc.dram_tensor("maskt", [128, 128], F32, kind="ExternalInput").ap()
    maskn_d = nc.dram_tensor("maskn", [128, 128], F32, kind="ExternalInput").ap()
    wqs_d = nc.dram_tensor("wqs", [24, 8, 128, 128], BF16, kind="Internal").ap()
    wos_d = nc.dram_tensor("wos", [8, 8, 128, 128], BF16, kind="Internal").ap()
    wvs_d = nc.dram_tensor("wvs", [2, 8, 4, 128, 128], BF16, kind="Internal").ap()

    BASE = 18464
    def at(name, shape, dt, off):
        return nc.alloc_sbuf_tensor_at(name, shape, dt, offset=BASE + off).ap()

    slotA = at("slotA", [128, NCH * T], BF16, 0)
    slotB = at("slotB", [128, NCH * T], BF16, 32768)
    slotC = at("slotC", [128, NCH * T], BF16, 65536)
    VOFF = 98304
    vbuf = at("vbuf", [128, NT * NH * 65], BF16, VOFF)
    wstage2 = [at("wstage0", [128, 3 * D], F32, VOFF),
               at("wstage1", [128, 3 * D], F32, VOFF + 12288)]
    wbstage = at("wbstage", [128, 3 * D], BF16, VOFF + 24576)
    xstage = at("xstage", [128, 2 * T], F32, VOFF)
    ostage = at("ostage", [128, 2 * T], F32, VOFF + 16384)
    FOFF = 131584
    s_sb = at("s_sb", [64, T], F32, FOFF)
    tfn = at("tfn", [64, T], F32, FOFF + 8192)
    u_fn = at("u_fn", [64, T], F32, FOFF + 16384)
    g_sb = at("g_sb", [64, T], F32, FOFF + 24576)
    gb_sb = at("gb_sb", [64, T], BF16, FOFF + 32768)
    kt = [at("kt0", [65, T], BF16, FOFF), at("kt1", [65, T], BF16, FOFF + 4096)]
    qt = [at("qt0", [65, T], BF16, FOFF + 8192), at("qt1", [65, T], BF16, FOFF + 12288)]
    aT = [at("aT0", [128, 1024], BF16, FOFF + 16384), at("aT1", [128, 1024], BF16, FOFF + 18432)]
    rball = at("rball", [128, 1024], F32, FOFF + 20480)
    ctxstage = at("ctxstage", [64, 1024], BF16, FOFF + 24576)
    wqk_st = at("wqk_st", [128, 4096], BF16, FOFF + 26624)
    wv_st = at("wv_st", [128, 4096], BF16, FOFF + 26624)
    wo_st = at("wo_st", [128, 2048], BF16, FOFF + 26624)
    tmpblk = at("tmpblk", [128, 128], F32, 209472)
    mneg = at("mneg", [128, 8], BF16, 209984)
    mtmp = at("mtmp", [128, 8], F32, 210016)
    R_row = at("R_row", [1, 2048], BF16, 168448)
    B_row = at("B_row", [1, 2048], BF16, 172544)
    Rb_b = at("Rb_b", [128, T], BF16, 176640)
    Bb_b = at("Bb_b", [128, T], BF16, 180736)
    sq_b = at("sq_b", [128, 2 * T], BF16, 184832)
    tmp_b = at("tmp_b", [128, 2 * T], BF16, 193024)
    CO = 201216
    win_sb = at("win_sb", [128, 8 * 64], BF16, CO)
    wout_sb = at("wout_sb", [64, D], BF16, CO + 1024)
    ident_sb = at("ident_sb", [128, 128], F32, CO + 3072)
    maskt_f = at("maskt_f", [128, 128], F32, CO + 3584)
    maskn_sb = at("maskn_sb", [128, 128], F32, CO + 4096)
    maskt_sb = at("maskt_sb", [128, 128], BF16, CO + 4608)
    onesrow_bf = at("onesrow_bf", [1, 128], BF16, CO + 4864)
    ones64 = at("ones64", [65, 64], BF16, CO + 5152)
    ident_bf = at("ident_bf", [128, 128], BF16, CO + 5920)
    rrow_bf = at("rrow_bf", [65, 1024], BF16, CO + 6208)
    w1_sb = at("w1_sb", [64, 16], F32, CO + 5408)
    b1_sb = at("b1_sb", [64, 16], F32, CO + 5472)
    w2_sb = at("w2_sb", [64, 16], F32, CO + 5536)
    c2_col = at("c2_col", [128, 24], F32, CO + 5600)
    c3_col = at("c3_col", [128, 8], F32, CO + 5696)
    g_col = at("g_col", [128, 8], F32, CO + 5728)
    be_col = at("be_col", [128, 8], F32, CO + 5760)
    bout_col = at("bout_col", [128, 8], F32, CO + 5792)
    bin_sb = at("bin_sb", [64, 1], F32, CO + 5824)
    b2_sb = at("b2_sb", [64, 1], F32, CO + 5856)
    onescol_bf = at("onescol_bf", [128, 1], BF16, CO + 5888)

    pA = nc.alloc_psum_tensor("pA", [128, 1024], F32).ap()
    pB = nc.alloc_psum_tensor("pB", [128, 1024], F32).ap()
    pC = nc.alloc_psum_tensor("pC", [128, 1024], F32).ap()
    pD = nc.alloc_psum_tensor("pD", [128, 1024], F32).ap()
    pAll = [pA, pB, pC, pD]

    Tq, Vq, Aq, Gq, Sq = Eng(), Eng(), Eng(), Eng(), Eng()
    t_s = Ctr(nc, "t_s")
    v_s = Ctr(nc, "v_s")
    a_s = Ctr(nc, "a_s")
    g_s = Ctr(nc, "g_s")
    d_c = Ctr(nc, "d_c")
    d_ws = Ctr(nc, "d_ws")
    d_x = [Ctr(nc, "d_x0"), Ctr(nc, "d_x1"),
           Ctr(nc, "d_x2"), Ctr(nc, "d_x3")]
    d_y = [Ctr(nc, "d_y0"), Ctr(nc, "d_y1"),
           Ctr(nc, "d_y2"), Ctr(nc, "d_y3")]
    d_w = [Ctr(nc, "d_w0"), Ctr(nc, "d_w1"),
           Ctr(nc, "d_w2"), Ctr(nc, "d_w3")]
    d_wv = Ctr(nc, "d_wv")
    d_kq = [Ctr(nc, "d_kq0"), Ctr(nc, "d_kq1")]
    d_cs = Ctr(nc, "d_cs")
    d_out = Ctr(nc, "d_out")

    def dma(dst, src, ctr, waits=()):
        Sq.op(lambda e, d=dst, s=src: e.dma_start(d, s), waits=waits, inc=ctr.tick(16))
        return ctr.at()

    for dst, src in [
        (w1_sb[:], w1_d), (b1_sb[:], b1_d), (w2_sb[:], w2_d),
        (bin_sb[:], bincol_d), (b2_sb[:], b2col_d), (bout_col[:], boutcol_d),
        (g_col[:], gcol_d), (be_col[:], bcol_d), (ident_sb[:], ident_d),
        (maskt_f[:], maskt_d), (maskn_sb[:], maskn_d),
    ]:
        dma(dst, src, d_c)
    t_consts = d_c.at()

    Gq.op(lambda e: e.memset(onesrow_bf[0:1, :], 1.0), inc=g_s.tick())
    Gq.op(lambda e: e.memset(onescol_bf[:], 1.0), inc=g_s.tick())
    Gq.op(lambda e: e.memset(ones64[64:65, :], 1.0), inc=g_s.tick())
    Gq.op(lambda e: e.tensor_copy(maskt_sb[:], maskt_f[:]), waits=[t_consts], inc=g_s.tick())
    Gq.op(lambda e: e.tensor_copy(ident_bf[:], ident_sb[:]), waits=[t_consts], inc=g_s.tick())
    g_const = g_s.at()

    tok = dma(xstage[:, 0:512], win_d.rearrange("(kc p) m -> p kc m", p=128), d_ws)
    Vq.op(lambda e: e.tensor_copy(win_sb[:], xstage[:, 0:512]), waits=[tok], inc=v_s.tick())
    tok = dma(xstage[0:64, 1024:2048], wout_d, d_ws, waits=[v_s.at()])
    Vq.op(lambda e: e.tensor_copy(wout_sb[:], xstage[0:64, 1024:2048]), waits=[tok], inc=v_s.tick())
    v_wsmall = v_s.at()

    last_store = None
    last_wread_t = {0: None, 1: None}
    last_wread_v = {0: None, 1: None}
    for kc in range(NCH):
        sb_ = kc % 2
        wstage = wstage2[sb_]
        tok = dma(wstage[:], wqkvt_d[kc * 128:(kc + 1) * 128, :], d_ws,
                  waits=[last_wread_t[sb_], last_wread_v[sb_], v_wsmall])
        Vq.op(lambda e, kc=kc, wstage=wstage: e.tensor_scalar(
            wbstage[:], wstage[:], g_col[:, kc:kc + 1], None, ALU.mult),
            waits=[tok, t_consts, last_store], inc=v_s.tick())
        last_wread_v[sb_] = v_s.at()
        for mc in range(24):
            Tq.op(lambda e, kc=kc, mc=mc, wstage=wstage: e.matmul(
                pA[:, mc * 8 + kc:mc * 8 + kc + 1], wstage[:, mc * 128:(mc + 1) * 128],
                be_col[:, kc:kc + 1], start=True, stop=True),
                waits=([tok, t_consts] if mc == 0 else []))
        Tq.retro_inc(t_s.tick())
        last_wread_t[sb_] = t_s.at()
        dma(wqs_d[0:16, kc, :, :].rearrange("mc p f -> p mc f"),
            wbstage[:, 0:2048].rearrange("p (mc f) -> p mc f", mc=16),
            d_out, waits=[last_wread_v[sb_]])
        for vh_ in range(2):
            dma(wvs_d[vh_, kc, :, :, :].rearrange("mc p f -> p mc f"),
                wbstage[:, 2048 + vh_ * 512:2048 + vh_ * 512 + 512]
                .rearrange("p (mc f) -> p mc f", mc=4),
                d_out, waits=[])
        last_store = d_out.at()
    for mc in range(24):
        Vq.op(lambda e, mc=mc: e.reduce_sum(
            c2_col[:, mc:mc + 1], pA[:, mc * 8:mc * 8 + 8], axis=AX.X),
            waits=[last_wread_t[0], last_wread_t[1]], inc=v_s.tick())
    v_c2 = v_s.at()

    for kc in range(NCH):
        sb_ = kc % 2
        wstage = wstage2[sb_]
        tok = dma(wstage[:, 0:1024], wot_d[kc * 128:(kc + 1) * 128, :], d_ws,
                  waits=[last_wread_t[sb_], last_wread_v[sb_]])
        Vq.op(lambda e, wstage=wstage: e.tensor_copy(
            wbstage[:, 0:1024], wstage[:, 0:1024]),
            waits=[tok, v_c2, last_store], inc=v_s.tick())
        last_wread_v[sb_] = v_s.at()
        for mc in range(8):
            Tq.op(lambda e, kc=kc, mc=mc, wstage=wstage: e.matmul(
                pB[:, mc * 8 + kc:mc * 8 + kc + 1], wstage[:, mc * 128:(mc + 1) * 128],
                c2_col[:, 16 + kc:17 + kc], start=True, stop=True),
                waits=([tok, v_c2] if mc == 0 else []))
        Tq.retro_inc(t_s.tick())
        last_wread_t[sb_] = t_s.at()
        last_store = dma(wos_d[:, kc, :, :].rearrange("mc p f -> p mc f"),
                         wbstage[:, 0:1024].rearrange("p (mc f) -> p mc f", mc=8),
                         d_out, waits=[last_wread_v[sb_]])
    for mc in range(8):
        Vq.op(lambda e, mc=mc: e.reduce_sum(
            c3_col[:, mc:mc + 1], pB[:, mc * 8:mc * 8 + 8], axis=AX.X),
            waits=[last_wread_t[0], last_wread_t[1]], inc=v_s.tick())
    vput = vbuf[:].rearrange("p (t h f) -> p t h f", t=NT, h=NH, f=65)
    setup_v = v_s.at()
    setup_t = max(last_wread_t[0], last_wread_t[1], key=lambda t: t[1])
    setup_d = d_out.at()
    setup_g = g_s.at()

    def emit_ln(src_sb, src_tok_fn, eps_mul, eps_add, guards):
        sq_read_t = {}
        sqtoks = {}
        for c in range(NCH):
            w = [src_tok_fn(c)] + guards
            if c - 2 in sq_read_t:
                w.append(sq_read_t[c - 2])
            Vq.op(lambda e, c=c: e.tensor_tensor(
                sq_b[:, (c % 2) * T:(c % 2) * T + T],
                src_sb[:, c * T:(c + 1) * T], src_sb[:, c * T:(c + 1) * T], ALU.mult),
                waits=w, inc=v_s.tick())
            sqtoks[c] = v_s.at()
            for th in range(2):
                psum_s = (pA, pB)[th]
                psum_q = (pC, pD)[th]
                for h5 in range(2):
                    cs = th * 1024 + h5 * 512
                    w = [src_tok_fn(c)] + (guards if c == 0 else [])
                    Tq.op(lambda e, c=c, h5=h5, cs=cs, psum_s=psum_s: e.matmul(
                        psum_s[0:1, h5 * 512:h5 * 512 + 512], onescol_bf[:],
                        src_sb[:, c * T + cs:c * T + cs + 512],
                        start=(c == 0), stop=(c == NCH - 1)), waits=w)
                for h5 in range(2):
                    cs = th * 1024 + h5 * 512
                    Tq.op(lambda e, c=c, h5=h5, cs=cs, psum_q=psum_q: e.matmul(
                        psum_q[0:1, h5 * 512:h5 * 512 + 512], onescol_bf[:],
                        sq_b[:, (c % 2) * T + cs:(c % 2) * T + cs + 512],
                        start=(c == 0), stop=(c == NCH - 1)), waits=[sqtoks[c]])
            Tq.retro_inc(t_s.tick())
            sq_read_t[c] = t_s.at()
        t_stats = t_s.at()
        out = {}
        for th in range(2):
            psum_s = (pA, pB)[th]
            psum_q = (pC, pD)[th]
            seg = slice(th * 1024, th * 1024 + 1024)
            Vq.op(lambda e, psum_s=psum_s, seg=seg: e.tensor_scalar(
                B_row[0:1, seg], psum_s[0:1, 0:1024], 1.0 / D, None, ALU.mult),
                waits=[t_stats] + guards, inc=v_s.tick())
            Vq.op(lambda e, seg=seg: e.tensor_tensor(
                R_row[0:1, seg], B_row[0:1, seg], B_row[0:1, seg],
                ALU.mult), inc=v_s.tick())
            Vq.op(lambda e, psum_q=psum_q, seg=seg: e.scalar_tensor_tensor(
                psum_q[0:1, 0:1024], psum_q[0:1, 0:1024], 1.0 / D,
                R_row[0:1, seg], ALU.mult, ALU.subtract), inc=v_s.tick())
            Vq.op(lambda e, psum_q=psum_q: e.tensor_scalar(
                psum_q[0:1, 0:1024], psum_q[0:1, 0:1024], eps_mul, eps_add,
                ALU.mult, ALU.add), inc=v_s.tick())
            v_rows = v_s.at()
            Aq.op(lambda e, psum_s=psum_s, psum_q=psum_q: e.activation(
                psum_s[0:1, 0:1024], psum_q[0:1, 0:1024], AF.Ln),
                waits=[v_rows] + guards, inc=a_s.tick())
            Aq.op(lambda e, psum_s=psum_s, seg=seg: e.activation(
                R_row[0:1, seg], psum_s[0:1, 0:1024], AF.Exp, scale=-0.5),
                inc=a_s.tick())
            a_R = a_s.at()
            Vq.op(lambda e, seg=seg: e.scalar_tensor_tensor(
                B_row[0:1, seg], B_row[0:1, seg], -1.0,
                R_row[0:1, seg], ALU.mult, ALU.mult),
                waits=[a_R], inc=v_s.tick())
        v_rowsdone = v_s.at()
        bc_evac = {0: None, 1: None}
        for row, dstb in [(R_row, Rb_b), (B_row, Bb_b)]:
            for th in range(2):
                for h5 in range(2):
                    pdst = (pC, pD)[h5]
                    cs = th * 1024 + h5 * 512
                    Tq.op(lambda e, row=row, cs=cs, pdst=pdst: e.matmul(
                        pdst[:, 0:512], onesrow_bf[0:1, :],
                        row[0:1, cs:cs + 512],
                        start=True, stop=True),
                        waits=[v_rowsdone, bc_evac[h5]] + guards, inc=t_s.tick())
                    tb = t_s.at()
                    Vq.op(lambda e, dstb=dstb, cs=cs, pdst=pdst: e.tensor_copy(
                        dstb[:, cs:cs + 512],
                        pdst[:, 0:512]), waits=[tb], inc=v_s.tick())
                    bc_evac[h5] = v_s.at()
        out["Rb"] = v_s.at()
        out["t_end"] = t_s.at()
        return out

    prevs = {
        "out_dma": [setup_d, None],
        "slotA_free": [setup_t],
        "slotB_free": [],
        "slotC_free": [],
        "slotA_q_free": [],
        "vbuf_free": [],
        "stream_free": [],
        "ln_guards": [setup_v],
        "xin_free": [],
        "xload_early": [],
    }
    asm_guard = [None, None]
    ctxstage_guard = [None]
    x_first_guard = [setup_t, setup_d, setup_v, setup_g]

    for s in range(NSEQ):
        cast_toks = []
        xload = []
        for c in range(NCH):
            slab = c % 4
            stg = xstage if slab < 2 else ostage
            so = (slab % 2) * T
            if c < 2 and prevs.get("xload_early"):
                tok = prevs["xload_early"][c]
            else:
                if not prevs["xin_free"]:
                    w = list(x_first_guard) + (list(prevs["out_dma"])
                                               if c in (2, 3) else [])
                elif c < 2:
                    w = [prevs["xin_free"][c]]
                elif c in (2, 3):
                    w = list(prevs["out_dma"])
                else:
                    w = []
                if c >= 4:
                    w.append(cast_toks[c - 4])
                tok = dma(stg[:, so:so + T],
                          x_d[s, c * 128:(c + 1) * 128, :], d_x[slab], waits=w)
            xload.append(tok)
            saf = prevs["slotA_free"]
            gate = [saf[c]] if len(saf) == NCH else list(saf)
            Aq.op(lambda e, c=c, stg=stg, so=so: e.activation(
                slotA[:, c * T:(c + 1) * T], stg[:, so:so + T],
                AF.Copy),
                waits=[tok] + gate, inc=a_s.tick())
            cast_toks.append(a_s.at())
        ones_w = [cast_toks[-1]] + prevs["vbuf_free"] + prevs["stream_free"] \
            + list(prevs["out_dma"])
        if s == 0:
            ones_w += [setup_d, setup_t]
        Gq.op(lambda e: e.memset(vput[:, :, :, 64:65], 1.0), waits=ones_w,
              inc=g_s.tick())
        Gq.op(lambda e: e.memset(kt[0][0:1, :], 1.0), inc=g_s.tick())
        Gq.op(lambda e: e.memset(kt[1][0:1, :], 1.0), inc=g_s.tick())
        ones_tok = g_s.at()
        ln1 = emit_ln(slotA, lambda c: cast_toks[c], 1.0 + EPS, EPS * EPS,
                      prevs["ln_guards"])
        xn_toks = []
        for c in range(NCH):
            Vq.op(lambda e, c=c: e.tensor_tensor(
                tmp_b[:, (c % 2) * T:(c % 2) * T + T],
                slotA[:, c * T:(c + 1) * T], Rb_b[:], ALU.mult),
                waits=[ln1["Rb"], cast_toks[c]], inc=v_s.tick())
            Vq.op(lambda e, c=c: e.tensor_tensor(
                slotC[:, c * T:(c + 1) * T],
                tmp_b[:, (c % 2) * T:(c % 2) * T + T], Bb_b[:], ALU.add),
                waits=prevs["slotC_free"], inc=v_s.tick())
            xn_toks.append(v_s.at())
        xn_done = v_s.at()

        qk_evac = {}
        last_st_read = [None, None, None, None]
        first_stream = True
        for th in range(2):
            for mc in range(16):
                g = th * 16 + mc
                sbh = g % 4
                w = [setup_d, last_st_read[sbh]]
                if first_stream:
                    w += prevs["stream_free"]
                    first_stream = False
                wtok = dma(wqk_st[:, sbh * 1024:sbh * 1024 + 1024]
                           .rearrange("p (kc f) -> p kc f", kc=8),
                           wqs_d[mc, :, :, :].rearrange("kc p f -> p kc f"),
                           d_w[sbh], waits=w)
                pt = pAll[g % 4]
                wfirst = [wtok, ln1["t_end"]]
                if g - 4 >= 0:
                    wfirst.append(qk_evac[g - 4])
                for kc in range(NCH):
                    for h5 in range(2):
                        wkc = [xn_toks[kc]] if h5 == 0 else []
                        Tq.op(lambda e, mc=mc, kc=kc, th=th, h5=h5, pt=pt, sbh=sbh: e.matmul(
                            pt[:, h5 * 512:h5 * 512 + 512],
                            wqk_st[:, sbh * 1024 + kc * 128:sbh * 1024 + kc * 128 + 128],
                            slotC[:, kc * T + th * 1024 + h5 * 512:
                                  kc * T + th * 1024 + h5 * 512 + 512],
                            start=(kc == 0), stop=(kc == NCH - 1)),
                            waits=(wfirst + wkc if (kc == 0 and h5 == 0) else wkc))
                Tq.retro_inc(t_s.tick())
                t_grp = t_s.at()
                last_st_read[sbh] = t_grp
                dst = slotA if mc < 8 else slotB
                dmc = mc if mc < 8 else mc - 8
                wev = [t_grp]
                wev += prevs["slotA_q_free"] if mc < 8 else prevs["slotB_free"]
                Aq.op(lambda e, dst=dst, dmc=dmc, th=th, pt=pt, mc=mc: e.activation(
                    dst[:, dmc * T + th * 1024:dmc * T + th * 1024 + 1024], pt[:],
                    AF.Identity, bias=c2_col[:, mc:mc + 1]),
                    waits=wev, inc=a_s.tick())
                qk_evac[g] = a_s.at()
        qk_done = a_s.at()
        t_qk = t_s.at()


        asm_tok = {}

        def emit_asm(h, extra):
            wk = [qk_done, asm_guard[h % 2], extra]
            dma(kt[h % 2][1:65, :],
                slotB[(h % 2) * 64:(h % 2) * 64 + 64, (h // 2) * T:(h // 2) * T + T],
                d_kq[h % 2], waits=wk)
            dma(qt[h % 2][1:65, :],
                slotA[(h % 2) * 64:(h % 2) * 64 + 64, (h // 2) * T:(h // 2) * T + T],
                d_kq[h % 2], waits=[])
            asm_tok[h] = d_kq[h % 2].at()

        emit_asm(0, None)
        emit_asm(1, None)

        st = {"sprobe_b": [None, None], "probe_par": 0, "mrow": None,
              "aT": [None, None], "pT": [None, None], "ctx": None,
              "qrow": {}, "t_av": None}

        def gen_A(h, half):
            q0 = half * 1024
            mcol_toks = []
            for idx, i in enumerate(range(8 * half, 8 * half + 8)):
                row_w = (i + 1) * 128
                nchunks = 0
                for ck0 in range(0, row_w, 512):
                    ck1 = min(ck0 + 512, row_w)
                    pb_ = st["probe_par"]
                    st["probe_par"] ^= 1
                    w = [qk_done, st["sprobe_b"][pb_], ln1["Rb"]]
                    if pb_ == 1:
                        w.append(st["mrow"])
                    Tq.op(lambda e, h=h, i=i, ck0=ck0, ck1=ck1, pb_=pb_: e.matmul(
                        pA[:, pb_ * 512:pb_ * 512 + ck1 - ck0],
                        slotA[(h % 2) * 64:(h % 2) * 64 + 64,
                              (h // 2) * T + i * 128:(h // 2) * T + i * 128 + 128],
                        slotB[(h % 2) * 64:(h % 2) * 64 + 64,
                              (h // 2) * T + ck0:(h // 2) * T + ck1],
                        start=True, stop=True),
                        waits=w, inc=t_s.tick())
                    tmm = t_s.at()
                    Vq.op(lambda e, nch=nchunks, w=ck1 - ck0, pb_=pb_: e.reduce_max(
                        mtmp[:, nch:nch + 1], pA[:, pb_ * 512:pb_ * 512 + w], axis=AX.X),
                        waits=[tmm], inc=v_s.tick())
                    st["sprobe_b"][pb_] = v_s.at()
                    nchunks += 1
                Vq.op(lambda e, idx=idx, nch=nchunks: e.reduce_max(
                    mneg[:, idx:idx + 1], mtmp[:, 0:nch], axis=AX.X, negate=True),
                    inc=v_s.tick())
                mcol_toks.append(v_s.at())
                yield
            for qtr in range(2):
                for r in range(4):
                    w = [mcol_toks[qtr * 4 + r]]
                    if r == 0:
                        w += [st["mrow"], st["sprobe_b"][1]]
                    Tq.op(lambda e, qtr=qtr, r=r: e.matmul(
                        pA[0:1, 512 + r * 128:512 + r * 128 + 128],
                        mneg[:, qtr * 4 + r:qtr * 4 + r + 1], ident_bf[:],
                        start=True, stop=True),
                        waits=w, inc=(t_s.tick() if r == 3 else None))
                t_mrow = t_s.at()
                st["sprobe_b"][1] = None
                Aq.op(lambda e, h=h, q0=q0, qtr=qtr: e.activation(
                    qt[h % 2][0:1, q0 + qtr * 512:q0 + qtr * 512 + 512],
                    pA[0:1, 512:1024], AF.Copy),
                    waits=[t_mrow, asm_tok[h]], inc=a_s.tick())
                st["mrow"] = a_s.at()
                yield
            st["qrow"][(h, half)] = a_s.at()

        def gen_B(h, half):
            q0 = half * 1024
            qrow_done = st["qrow"][(h, half)]
            strips = list(range(0, 8 * (half + 1)))
            first_j, last_j = strips[0], strips[-1]
            pend = None

            def emit_av(pend):
                cks, bi, a_exp, diag_tok, j = pend
                wv = [a_exp, diag_tok, ones_tok]
                if j == first_j:
                    wv.append(st["ctx"])
                for ci, (c0, c1) in enumerate(cks):
                    lastj_chunk = min(last_j, (q0 + c1 - 1) // 128)
                    Tq.op(lambda e, h=h, j=j, c0=c0, c1=c1, bi=bi,
                          sp=(j == lastj_chunk): e.matmul(
                        pD[0:65, c0:c1],
                        vbuf[:, j * NH * 65 + h * 65:j * NH * 65 + h * 65 + 65],
                        aT[bi][:, c0:c1],
                        start=(j == first_j), stop=sp),
                        waits=(wv if ci == 0 else []),
                        inc=(t_s.tick() if ci == len(cks) - 1 else None))
                st["aT"][bi] = t_s.at()

            for j in strips:
                rel0 = max(j * 128 - q0, 0)
                bi = j % 2
                buf = (pB, pC)[bi]
                cks = []
                c0 = rel0
                while c0 < 1024:
                    c1 = min((c0 // 512 + 1) * 512, 1024)
                    cks.append((c0, c1))
                    c0 = c1
                wv = [qrow_done, asm_tok[h], st["pT"][bi], ones_tok]
                for ci, (c0, c1) in enumerate(cks):
                    Tq.op(lambda e, h=h, j=j, c0=c0, c1=c1, buf=buf, q0=q0: e.matmul(
                        buf[:, c0:c1],
                        kt[h % 2][0:65, j * 128:j * 128 + 128],
                        qt[h % 2][0:65, q0 + c0:q0 + c1],
                        start=True, stop=True),
                        waits=(wv if ci == 0 else []),
                        inc=(t_s.tick() if ci == len(cks) - 1 else None))
                t_st = t_s.at()
                Aq.op(lambda e, bi=bi, rel0=rel0, buf=buf: e.activation(
                    aT[bi][:, rel0:1024], buf[:, rel0:1024], AF.Exp, scale=0.125),
                    waits=[t_st, st["aT"][bi]], inc=a_s.tick())
                st["pT"][bi] = a_s.at()
                a_exp = a_s.at()
                diag_tok = None
                if j >= 8 * half:
                    Gq.op(lambda e, bi=bi, rel0=rel0: e.tensor_tensor(
                        aT[bi][:, rel0:rel0 + 128], aT[bi][:, rel0:rel0 + 128],
                        maskt_sb[:], ALU.mult),
                        waits=[a_exp], inc=g_s.tick())
                    diag_tok = g_s.at()
                if pend is not None:
                    emit_av(pend)
                pend = (cks, bi, a_exp, diag_tok, j)
                yield
            emit_av(pend)
            t_av = t_s.at()
            st["t_av"] = t_av

            Aq.op(lambda e: e.activation(rball[64:65, :], pD[64:65, :], AF.Ln),
                  waits=[t_av], inc=a_s.tick())
            Aq.op(lambda e: e.activation(rrow_bf[64:65, :], rball[64:65, :],
                                         AF.Exp, scale=-1.0), inc=a_s.tick())
            a_r = a_s.at()
            for h5 in range(2):
                Tq.op(lambda e, h5=h5: e.matmul(
                    pB[0:64, h5 * 512:h5 * 512 + 512], ones64[64:65, :],
                    rrow_bf[64:65, h5 * 512:h5 * 512 + 512], start=True, stop=True),
                    waits=([a_r, st["pT"][0]] if h5 == 0 else []),
                    inc=(t_s.tick() if h5 == 1 else None))
            t_rb = t_s.at()
            Aq.op(lambda e: e.activation(rball[0:64, :], pB[0:64, :], AF.Copy),
                  waits=[t_rb], inc=a_s.tick())
            v_rb = a_s.at()
            st["pT"][0] = v_rb
            if h % 2 == 0:
                Vq.op(lambda e, h=h, q0=q0: e.tensor_tensor(
                    slotC[0:64, (h // 2) * T + q0:(h // 2) * T + q0 + 1024],
                    pD[0:64, :], rball[0:64, :], ALU.mult),
                    waits=[v_rb], inc=v_s.tick())
                st["ctx"] = v_s.at()
            else:
                Vq.op(lambda e: e.tensor_tensor(
                    ctxstage[:], pD[0:64, :], rball[0:64, :], ALU.mult),
                    waits=[v_rb, ctxstage_guard[0]], inc=v_s.tick())
                st["ctx"] = v_s.at()
                tok = dma(slotC[64:128, (h // 2) * T + q0:(h // 2) * T + q0 + 1024],
                          ctxstage[:], d_cs, waits=[st["ctx"]])
                ctxstage_guard[0] = tok
            if half == 1:
                asm_guard[h % 2] = t_av
                if h + 2 < NH:
                    emit_asm(h + 2, t_av)
            yield

        pairs = [(h, hf) for h in range(NH) for hf in range(2)]
        gA0 = gen_A(*pairs[0])
        gA0_done = [False]

        def pump():
            if not gA0_done[0]:
                try:
                    next(gA0)
                except StopIteration:
                    gA0_done[0] = True
        v_ev = {}
        last_v_read = None
        for vh in range(2):
            w = [setup_d, t_qk if vh == 0 else last_v_read] + list(prevs["out_dma"])
            wtok = dma(wv_st[:].rearrange("p (km f) -> p km f", km=32),
                       wvs_d[vh, :, :, :, :].rearrange("kc mc p f -> p (kc mc) f"),
                       d_wv, waits=w)
            for tt in range(NT):
                gidx = vh * NT + tt
                pt = (pB, pC, pD)[gidx % 3]
                wfirst = [wtok, xn_done]
                if gidx - 3 >= 0:
                    wfirst.append(v_ev[gidx - 3])
                else:
                    wfirst.append(qk_done)
                for kc in range(NCH):
                    Tq.op(lambda e, tt=tt, kc=kc, pt=pt: e.matmul(
                        pt[:, 0:512],
                        slotC[:, kc * T + tt * 128:kc * T + tt * 128 + 128],
                        wv_st[:, kc * 512:kc * 512 + 512],
                        start=(kc == 0), stop=(kc == NCH - 1)),
                        waits=(wfirst if kc == 0 else []))
                Tq.retro_inc(t_s.tick())
                t_grp = t_s.at()
                last_v_read = t_grp
                dstv = vbuf[:, tt * NH * 65:(tt + 1) * NH * 65] \
                    .rearrange("p (h f) -> p h f", h=NH)[:, vh * 8:vh * 8 + 8, 0:64]
                srcv = pt[:, 0:512].rearrange("p (h f) -> p h f", h=8)
                Aq.op(lambda e, dstv=dstv, srcv=srcv: e.activation(dstv, srcv, AF.Copy),
                      waits=[t_grp, setup_g] + prevs["vbuf_free"] + list(prevs["out_dma"]),
                      inc=a_s.tick())
                v_ev[gidx] = a_s.at()
                pump()
                pump()
        v_done = a_s.at()
        t_v = t_s.at()
        st["pT"] = [v_done, v_done]
        st["ctx"] = v_done
        st["aT"] = [st["aT"][0], st["aT"][1]]
        for _ in gA0:
            pass
        gAn = None
        for k, pr in enumerate(pairs):
            gB = gen_B(*pr)
            gAn = gen_A(*pairs[k + 1]) if k + 1 < len(pairs) else None
            nstrips = 8 * (pr[1] + 1)
            if nstrips >= 10:
                sched = [0] * (nstrips - 10) + [1] * 10
            else:
                sched = [1] * nstrips
                for i in range(10 - nstrips):
                    sched[-1 - (i % 2)] += 1
            yi = 0
            for _ in gB:
                nsteps = sched[yi] if yi < len(sched) else 1
                yi += 1
                for _ in range(nsteps):
                    if gAn is not None:
                        try:
                            next(gAn)
                        except StopIteration:
                            gAn = None
            if gAn is not None:
                for _ in gAn:
                    pass
                gAn = None
        t_av_last = st["t_av"]
        v_p3 = v_s.at()
        a_p3 = a_s.at()
        d_p3 = [d_kq[0].at(), d_kq[1].at(), d_cs.at()]
        t_p3 = t_s.at()

        ao_evac = {}
        last_wo_read = [None, None]
        for mc in range(8):
            sbh = mc % 2
            w = [setup_d, last_wo_read[sbh]]
            if mc == 0:
                w += [t_p3]
            wtok = dma(wo_st[:, sbh * 1024:sbh * 1024 + 1024]
                       .rearrange("p (kc f) -> p kc f", kc=8),
                       wos_d[mc, :, :, :].rearrange("kc p f -> p kc f"),
                       d_w[sbh], waits=w)
            pts = (pA, pB) if mc % 2 == 0 else (pC, pD)
            wfirst = [wtok, v_p3, a_p3, d_p3]
            if mc >= 2:
                wfirst.append(ao_evac[mc - 2])
            for kc in range(NCH):
                for sl in range(4):
                    pt = pts[sl // 2]
                    Tq.op(lambda e, mc=mc, kc=kc, sl=sl, pt=pt, sbh=sbh: e.matmul(
                        pt[:, (sl % 2) * 512:(sl % 2) * 512 + 512],
                        wo_st[:, sbh * 1024 + kc * 128:sbh * 1024 + kc * 128 + 128],
                        slotC[:, kc * T + sl * 512:kc * T + sl * 512 + 512],
                        start=(kc == 0), stop=(kc == NCH - 1)),
                        waits=(wfirst if (kc == 0 and sl == 0) else []))
            Tq.retro_inc(t_s.tick())
            t_grp = t_s.at()
            last_wo_read[sbh] = t_grp
            for halfp in range(2):
                Aq.op(lambda e, mc=mc, halfp=halfp, pts=pts: e.activation(
                    slotA[:, mc * T + halfp * 1024:mc * T + halfp * 1024 + 1024],
                    pts[halfp][:], AF.Identity, bias=c3_col[:, mc:mc + 1]),
                    waits=[t_grp, d_p3], inc=a_s.tick())
            ao_evac[mc] = a_s.at()
        ao_done = a_s.at()
        t_wo = t_s.at()

        ln3 = emit_ln(slotA, lambda c: ao_evac[c], 1.0, EPS, [ao_done, v_p3])
        x2_toks = []
        for c in range(NCH):
            slab = c % 4
            stg = xstage if slab < 2 else ostage
            so = (slab % 2) * T
            w = [cast_toks[-1], t_av_last] + ([x2_toks[c - 4]] if c >= 4 else [])
            xtok = dma(stg[:, so:so + T],
                       x_d[s, c * 128:(c + 1) * 128, :], d_x[slab], waits=w)
            Vq.op(lambda e, c=c: e.tensor_tensor(
                tmp_b[:, (c % 2) * T:(c % 2) * T + T],
                slotA[:, c * T:(c + 1) * T], Rb_b[:], ALU.mult),
                waits=[ln3["Rb"]] + ([x2_toks[c - 2]] if c >= 2 else []),
                inc=v_s.tick())
            Vq.op(lambda e, c=c: e.tensor_tensor(
                tmp_b[:, (c % 2) * T:(c % 2) * T + T],
                tmp_b[:, (c % 2) * T:(c % 2) * T + T], Bb_b[:], ALU.add),
                inc=v_s.tick())
            Vq.op(lambda e, c=c, stg=stg, so=so: e.tensor_tensor(
                slotB[:, c * T:(c + 1) * T],
                tmp_b[:, (c % 2) * T:(c % 2) * T + T],
                stg[:, so:so + T], ALU.add),
                waits=[xtok, t_wo, d_p3], inc=v_s.tick())
            x2_toks.append(v_s.at())
        x2_done = v_s.at()

        ln4 = emit_ln(slotB, lambda c: x2_toks[c], 1.0, EPS, [x2_done, ao_done])
        xn2_toks = []
        for c in range(NCH):
            Vq.op(lambda e, c=c: e.tensor_tensor(
                tmp_b[:, (c % 2) * T:(c % 2) * T + T],
                slotB[:, c * T:(c + 1) * T], Rb_b[:], ALU.mult),
                waits=[ln4["Rb"]], inc=v_s.tick())
            Vq.op(lambda e, c=c: e.tensor_tensor(
                slotA[:, c * T:(c + 1) * T],
                tmp_b[:, (c % 2) * T:(c % 2) * T + T], Bb_b[:], ALU.add),
                inc=v_s.tick())
            xn2_toks.append(v_s.at())
        xn2_done = v_s.at()

        s_ev = []
        for sl in range(4):
            pt = (pA, pA, pB, pB)[sl]
            off = (sl % 2) * 512
            for kc in range(NCH):
                wkc = [xn2_toks[kc]] + ([ln4["t_end"]] if kc == 0 else [])
                Tq.op(lambda e, sl=sl, kc=kc, pt=pt, off=off: e.matmul(
                    pt[0:64, off:off + 512],
                    win_sb[:, kc * 64:kc * 64 + 64],
                    slotA[:, kc * T + sl * 512:kc * T + sl * 512 + 512],
                    start=(kc == 0), stop=(kc == NCH - 1)),
                    waits=wkc)
            Tq.retro_inc(t_s.tick())
            t_grp = t_s.at()
            if sl % 2 == 1:
                pth = (pA, pB)[sl // 2]
                Aq.op(lambda e, sl=sl, pth=pth: e.activation(
                    s_sb[:, (sl - 1) * 512:(sl + 1) * 512], pth[0:64, :],
                    AF.Identity, bias=bin_sb[:]),
                    waits=[t_grp, d_p3], inc=a_s.tick())
                s_ev.append(a_s.at())
        s_done = a_s.at()
        g_q = []
        ts_toks = {}
        for q4 in range(4):
            qs = slice(q4 * 512, q4 * 512 + 512)
            for i in range(16):
                tfa = tfn[0:64, (i % 2) * 512:(i % 2) * 512 + 512]
                if i < 2:
                    wa = [s_ev[q4 // 2]]
                    if q4 > 0:
                        wa.append(ts_toks[(q4 - 1, 14 + i)])
                else:
                    wa = [ts_toks[(q4, i - 2)]]
                Aq.op(lambda e, i=i, qs=qs, tfa=tfa: e.activation(
                    tfa, s_sb[0:64, qs], AF.Identity,
                    scale=w1_sb[:, i:i + 1], bias=b1_sb[:, i:i + 1]),
                    waits=wa, inc=a_s.tick())
                a_tf = a_s.at()
                dst = g_sb if i == 0 else u_fn
                Vq.op(lambda e, dst=dst, i=i, qs=qs, tfa=tfa: e.tensor_scalar(
                    dst[0:64, qs], tfa, 0.0, w2_sb[:, i:i + 1],
                    ALU.max, ALU.mult), waits=[a_tf], inc=v_s.tick())
                ts_toks[(q4, i)] = v_s.at()
                if i > 0:
                    Vq.op(lambda e, qs=qs: e.tensor_tensor(
                        g_sb[0:64, qs], g_sb[0:64, qs], u_fn[0:64, qs], ALU.add),
                        inc=v_s.tick())
            Vq.op(lambda e, qs=qs: e.tensor_scalar(
                gb_sb[0:64, qs], g_sb[0:64, qs], b2_sb[:], None, ALU.add),
                inc=v_s.tick())
            g_q.append(v_s.at())
        g_done = v_s.at()
        fo_ev = {}
        for mc in range(8):
            pts = (pA, pB) if mc % 2 == 0 else (pC, pD)
            for sl in range(4):
                pt = pts[sl // 2]
                w = [g_q[sl]]
                if sl == 0 and mc >= 2:
                    w.append(fo_ev[mc - 2])
                Tq.op(lambda e, mc=mc, sl=sl, pt=pt: e.matmul(
                    pt[:, (sl % 2) * 512:(sl % 2) * 512 + 512],
                    wout_sb[0:64, mc * 128:mc * 128 + 128],
                    gb_sb[:, sl * 512:sl * 512 + 512],
                    start=True, stop=True),
                    waits=w)
            Tq.retro_inc(t_s.tick())
            t_grp = t_s.at()
            for halfp in range(2):
                Aq.op(lambda e, mc=mc, halfp=halfp, pts=pts: e.activation(
                    slotA[:, mc * T + halfp * 1024:mc * T + halfp * 1024 + 1024],
                    pts[halfp][:], AF.Identity, bias=bout_col[:, mc:mc + 1]),
                    waits=[t_grp, xn2_done], inc=a_s.tick())
            fo_ev[mc] = a_s.at()
        fo_done = a_s.at()

        ln5 = emit_ln(slotA, lambda c: fo_ev[c], 1.0, EPS, [fo_done, x2_done])
        xload_early = []
        if s + 1 < NSEQ:
            for ce in range(2):
                xload_early.append(dma(
                    xstage[:, ce * T:ce * T + T],
                    x_d[s + 1, ce * 128:(ce + 1) * 128, :], d_x[ce],
                    waits=[x2_toks[4 + ce]]))
        out_toks = []
        out_g = []
        ln5_tt1 = []
        for c in range(NCH):
            Vq.op(lambda e, c=c: e.tensor_tensor(
                tmp_b[:, (c % 2) * T:(c % 2) * T + T],
                slotA[:, c * T:(c + 1) * T], Rb_b[:], ALU.mult),
                waits=[ln5["Rb"]] + ([out_g[c - 2]] if c >= 2 else []),
                inc=v_s.tick())
            ln5_tt1.append(v_s.at())
            Vq.op(lambda e, c=c: e.tensor_tensor(
                tmp_b[:, (c % 2) * T:(c % 2) * T + T],
                tmp_b[:, (c % 2) * T:(c % 2) * T + T], Bb_b[:], ALU.add),
                inc=v_s.tick())
            w = [out_toks[c - 2]] if c >= 2 else [x2_toks[6 + c]]
            Vq.op(lambda e, c=c: e.tensor_tensor(
                ostage[:, (c % 2) * T:(c % 2) * T + T],
                tmp_b[:, (c % 2) * T:(c % 2) * T + T],
                slotB[:, c * T:(c + 1) * T], ALU.add),
                waits=w, inc=v_s.tick())
            v_out = v_s.at()
            out_g.append(v_out)
            tok = dma(y_d[s, c * 128:(c + 1) * 128, :],
                      ostage[:, (c % 2) * T:(c % 2) * T + T], d_y[c % 4], waits=[v_out])
            out_toks.append(tok)
        v_p6 = v_s.at()
        t_ln5 = ln5["t_end"]

        g_p6 = g_s.at()
        prevs = {
            "out_dma": out_toks[-4:],
            "slotA_free": ln5_tt1,
            "slotB_free": [v_p6, g_p6],
            "slotC_free": [t_wo],
            "slotA_q_free": [t_p3, d_p3],
            "vbuf_free": [t_av_last],
            "stream_free": [v_p6, fo_done],
            "ln_guards": [ln5["Rb"], fo_done],
            "xin_free": [x2_toks[4], x2_toks[5]],
            "xload_early": xload_early,
        }

    nc._dbg = dict(slotA=slotA, slotB=slotB, slotC=slotC, vbuf=vbuf, kt0=kt[0],
                   kt1=kt[1], qt0=qt[0], qt1=qt[1], c2_col=c2_col, c3_col=c3_col,
                   Rb_b=Rb_b, Bb_b=Bb_b, R_row=R_row, B_row=B_row, s_sb=s_sb,
                   g_sb=g_sb, gb_sb=gb_sb, rball=rball, mneg=mneg, aT0=aT[0],
                   aT1=aT[1], win_sb=win_sb, wout_sb=wout_sb, tfn=tfn, u_fn=u_fn, g=g_sb, w1=w1_sb, b1=b1_sb, w2=w2_sb)

    Sq.op(lambda e: None, waits=[d.at() for d in d_y])

    with nc.Block() as block:
        @block.tensor
        def _(e):
            run_ops(e, Tq.ops)

        @block.vector
        def _(e):
            run_ops(e, Vq.ops)

        @block.scalar
        def _(e):
            run_ops(e, Aq.ops)

        @block.gpsimd
        def _(e):
            run_ops(e, Gq.ops)

        @block.sync
        def _(e):
            run_ops(e, Sq.ops)

    return nc


def kernel(L0_attn_x, L0_attn_ln_gamma, L0_attn_ln_beta, L0_attn_W_qkv, L0_attn_W_o,
           L0_ffn_Win, L0_ffn_bin_, L0_ffn_W1, L0_ffn_b1, L0_ffn_W2, L0_ffn_b2,
           L0_ffn_Wout, L0_ffn_bout):
    x = np.asarray(L0_attn_x, dtype=np.float32)
    gamma = np.asarray(L0_attn_ln_gamma, dtype=np.float32)
    beta = np.asarray(L0_attn_ln_beta, dtype=np.float32)
    wqkv = np.asarray(L0_attn_W_qkv, dtype=np.float32)
    wo = np.asarray(L0_attn_W_o, dtype=np.float32)
    win = np.asarray(L0_ffn_Win, dtype=np.float32)
    bin_ = np.asarray(L0_ffn_bin_, dtype=np.float32)
    w1 = np.asarray(L0_ffn_W1, dtype=np.float32)
    b1 = np.asarray(L0_ffn_b1, dtype=np.float32)
    w2 = np.asarray(L0_ffn_W2, dtype=np.float32)
    b2 = np.asarray(L0_ffn_b2, dtype=np.float32)
    wout = np.asarray(L0_ffn_Wout, dtype=np.float32)
    bout = np.asarray(L0_ffn_bout, dtype=np.float32)

    idx = []
    for blk in range(3):
        for h in range(NH):
            base = h * 192 + blk * 64
            idx.extend(range(base, base + 64))
    wqkvt = np.ascontiguousarray(wqkv[np.array(idx)].T)
    wot = np.ascontiguousarray(wo.T)
    gcol = np.ascontiguousarray(gamma.reshape(8, 128).T)
    bcol = np.ascontiguousarray(beta.reshape(8, 128).T)
    boutcol = np.ascontiguousarray(bout.reshape(8, 128).T)
    bincol = np.ascontiguousarray(bin_.reshape(64, 1))
    b2col = np.ascontiguousarray(b2.reshape(64, 1))
    ident = np.eye(128, dtype=np.float32)
    kq = np.arange(128)
    maskt = np.ascontiguousarray((kq[:, None] <= kq[None, :]).astype(np.float32))
    maskn = np.ascontiguousarray(
        np.where(kq[None, :] > kq[:, None], np.float32(-1e30), np.float32(0.0)))

    nc = build_program()
    in_maps = []
    for c in range(NCORE):
        xs = np.ascontiguousarray(np.transpose(x[c * NSEQ:(c + 1) * NSEQ], (0, 2, 1)))
        in_maps.append(dict(
            x=xs, wqkvt=wqkvt, wot=wot, win=win, wout=wout,
            gcol=gcol, bcol=bcol, bincol=bincol, w1=w1, b1=b1, w2=w2,
            b2col=b2col, boutcol=boutcol, ident=ident, maskt=maskt, maskn=maskn,
        ))
    res = run_bass_kernel_spmd(nc, in_maps, core_ids=list(range(NCORE)))
    outs = []
    for c in range(NCORE):
        yc = res.results[c]["y"]
        outs.append(np.transpose(yc, (0, 2, 1)))
    return np.ascontiguousarray(np.concatenate(outs, axis=0))


# version 31
# speedup vs baseline: 1.1434x; 1.0103x over previous
import numpy as np
import ml_dtypes
import concourse.bass as bass
import concourse.mybir as mybir
from concourse.bass_utils import run_bass_kernel_spmd

F32 = mybir.dt.float32
BF16 = mybir.dt.bfloat16
AF = mybir.ActivationFunctionType
ALU = mybir.AluOpType
AX = mybir.AxisListType

NCORE = 8
NSEQ = 4
T = 2048
D = 1024
NH = 16
DH = 64
NCH = 8
NT = 16
EPS = 1e-5


class Ctr:
    def __init__(self, nc, name):
        self.h = nc.alloc_semaphore(name)
        self.n = 0

    def tick(self, k=1):
        self.n += k
        return (self.h, k)

    def at(self):
        return (self.h, self.n)


class Eng:
    def __init__(self):
        self.ops = []

    def op(self, fn, waits=(), inc=None):
        ws = []
        for w in waits:
            if w is None:
                continue
            if isinstance(w, list):
                ws.extend(t for t in w if t is not None)
            else:
                ws.append(w)
        self.ops.append((tuple(ws), fn, inc))

    def retro_inc(self, inc):
        w, f, old = self.ops[-1]
        assert old is None
        self.ops[-1] = (w, f, inc)


def run_ops(engine, ops):
    waited = {}
    for waits, fn, inc in ops:
        for sem, val in waits:
            if val > 0 and waited.get(id(sem), -1) < val:
                engine.wait_ge(sem, val)
                waited[id(sem)] = val
        ins = fn(engine)
        if inc is not None:
            sem, k = inc
            ins.then_inc(sem, k)


def build_program(nc=None):
    if nc is None:
        nc = bass.Bass("TRN2", target_bir_lowering=False)

    x_d = nc.dram_tensor("x", [NSEQ, D, T], F32, kind="ExternalInput").ap()
    y_d = nc.dram_tensor("y", [NSEQ, D, T], F32, kind="ExternalOutput").ap()
    wqkvt_d = nc.dram_tensor("wqkvt", [D, 3 * D], F32, kind="ExternalInput").ap()
    wot_d = nc.dram_tensor("wot", [D, D], F32, kind="ExternalInput").ap()
    win_d = nc.dram_tensor("win", [D, 64], F32, kind="ExternalInput").ap()
    wout_d = nc.dram_tensor("wout", [64, D], F32, kind="ExternalInput").ap()
    gcol_d = nc.dram_tensor("gcol", [128, 8], F32, kind="ExternalInput").ap()
    bcol_d = nc.dram_tensor("bcol", [128, 8], F32, kind="ExternalInput").ap()
    bincol_d = nc.dram_tensor("bincol", [64, 1], F32, kind="ExternalInput").ap()
    w1_d = nc.dram_tensor("w1", [64, 16], F32, kind="ExternalInput").ap()
    b1_d = nc.dram_tensor("b1", [64, 16], F32, kind="ExternalInput").ap()
    w2_d = nc.dram_tensor("w2", [64, 16], F32, kind="ExternalInput").ap()
    b2col_d = nc.dram_tensor("b2col", [64, 1], F32, kind="ExternalInput").ap()
    boutcol_d = nc.dram_tensor("boutcol", [128, 8], F32, kind="ExternalInput").ap()
    ident_d = nc.dram_tensor("ident", [128, 128], F32, kind="ExternalInput").ap()
    maskt_d = nc.dram_tensor("maskt", [128, 128], F32, kind="ExternalInput").ap()
    maskn_d = nc.dram_tensor("maskn", [128, 128], F32, kind="ExternalInput").ap()
    wqs_d = nc.dram_tensor("wqs", [24, 8, 128, 128], BF16, kind="Internal").ap()
    wos_d = nc.dram_tensor("wos", [8, 8, 128, 128], BF16, kind="Internal").ap()
    wvs_d = nc.dram_tensor("wvs", [2, 8, 4, 128, 128], BF16, kind="Internal").ap()

    BASE = 18464
    def at(name, shape, dt, off):
        return nc.alloc_sbuf_tensor_at(name, shape, dt, offset=BASE + off).ap()

    slotA = at("slotA", [128, NCH * T], BF16, 0)
    slotB = at("slotB", [128, NCH * T], BF16, 32768)
    slotC = at("slotC", [128, NCH * T], BF16, 65536)
    VOFF = 98304
    vbuf = at("vbuf", [128, NT * NH * 65], BF16, VOFF)
    wstage2 = [at("wstage0", [128, 3 * D], F32, VOFF),
               at("wstage1", [128, 3 * D], F32, VOFF + 12288)]
    wbstage = at("wbstage", [128, 3 * D], BF16, VOFF + 24576)
    xstage = at("xstage", [128, 2 * T], F32, VOFF)
    ostage = at("ostage", [128, 2 * T], F32, VOFF + 16384)
    FOFF = 131584
    s_sb = at("s_sb", [64, T], F32, FOFF)
    tfn = at("tfn", [64, T], F32, FOFF + 8192)
    u_fn = at("u_fn", [64, T], F32, FOFF + 16384)
    g_sb = at("g_sb", [64, T], F32, FOFF + 24576)
    gb_sb = at("gb_sb", [64, T], BF16, FOFF + 32768)
    kt = [at("kt0", [65, T], BF16, FOFF), at("kt1", [65, T], BF16, FOFF + 4096)]
    qt = [at("qt0", [65, T], BF16, FOFF + 8192), at("qt1", [65, T], BF16, FOFF + 12288)]
    aT = [at("aT0", [128, 1024], BF16, FOFF + 16384), at("aT1", [128, 1024], BF16, FOFF + 18432)]
    rball = at("rball", [128, 1024], F32, FOFF + 20480)
    ctxstage = at("ctxstage", [64, 1024], BF16, FOFF + 24576)
    wqk_st = at("wqk_st", [128, 4096], BF16, FOFF + 26624)
    wv_st = at("wv_st", [128, 4096], BF16, FOFF + 26624)
    wo_st = at("wo_st", [128, 2048], BF16, FOFF + 26624)
    tmpblk = at("tmpblk", [128, 128], F32, 209472)
    mneg = at("mneg", [128, 8], BF16, 209984)
    mtmp = at("mtmp", [128, 8], F32, 210016)
    R_row = at("R_row", [1, 2048], BF16, 168448)
    B_row = at("B_row", [1, 2048], BF16, 172544)
    Rb_b = at("Rb_b", [128, T], BF16, 176640)
    Bb_b = at("Bb_b", [128, T], BF16, 180736)
    sq_b = at("sq_b", [128, 2 * T], BF16, 184832)
    tmp_b = at("tmp_b", [128, 2 * T], BF16, 193024)
    CO = 201216
    win_sb = at("win_sb", [128, 8 * 64], BF16, CO)
    wout_sb = at("wout_sb", [64, D], BF16, CO + 1024)
    ident_sb = at("ident_sb", [128, 128], F32, CO + 3072)
    maskt_f = at("maskt_f", [128, 128], F32, CO + 3584)
    maskn_sb = at("maskn_sb", [128, 128], F32, CO + 4096)
    maskt_sb = at("maskt_sb", [128, 128], BF16, CO + 4608)
    onesrow_bf = at("onesrow_bf", [1, 128], BF16, CO + 4864)
    ones64 = at("ones64", [65, 64], BF16, CO + 5152)
    ident_bf = at("ident_bf", [128, 128], BF16, CO + 5920)
    rrow_bf = at("rrow_bf", [65, 1024], BF16, CO + 6208)
    w1_sb = at("w1_sb", [64, 16], F32, CO + 5408)
    b1_sb = at("b1_sb", [64, 16], F32, CO + 5472)
    w2_sb = at("w2_sb", [64, 16], F32, CO + 5536)
    c2_col = at("c2_col", [128, 24], F32, CO + 5600)
    c3_col = at("c3_col", [128, 8], F32, CO + 5696)
    g_col = at("g_col", [128, 8], F32, CO + 5728)
    be_col = at("be_col", [128, 8], F32, CO + 5760)
    bout_col = at("bout_col", [128, 8], F32, CO + 5792)
    bin_sb = at("bin_sb", [64, 1], F32, CO + 5824)
    b2_sb = at("b2_sb", [64, 1], F32, CO + 5856)
    onescol_bf = at("onescol_bf", [128, 1], BF16, CO + 5888)

    pA = nc.alloc_psum_tensor("pA", [128, 1024], F32).ap()
    pB = nc.alloc_psum_tensor("pB", [128, 1024], F32).ap()
    pC = nc.alloc_psum_tensor("pC", [128, 1024], F32).ap()
    pD = nc.alloc_psum_tensor("pD", [128, 1024], F32).ap()
    pAll = [pA, pB, pC, pD]

    Tq, Vq, Aq, Gq, Sq = Eng(), Eng(), Eng(), Eng(), Eng()
    t_s = Ctr(nc, "t_s")
    v_s = Ctr(nc, "v_s")
    a_s = Ctr(nc, "a_s")
    g_s = Ctr(nc, "g_s")
    d_c = Ctr(nc, "d_c")
    d_ws = Ctr(nc, "d_ws")
    d_x = [Ctr(nc, "d_x0"), Ctr(nc, "d_x1"),
           Ctr(nc, "d_x2"), Ctr(nc, "d_x3")]
    d_y = [Ctr(nc, "d_y0"), Ctr(nc, "d_y1"),
           Ctr(nc, "d_y2"), Ctr(nc, "d_y3")]
    d_w = [Ctr(nc, "d_w0"), Ctr(nc, "d_w1"),
           Ctr(nc, "d_w2"), Ctr(nc, "d_w3")]
    d_wv = Ctr(nc, "d_wv")
    d_kq = [Ctr(nc, "d_kq0"), Ctr(nc, "d_kq1")]
    d_cs = Ctr(nc, "d_cs")
    d_out = Ctr(nc, "d_out")

    def dma(dst, src, ctr, waits=()):
        Sq.op(lambda e, d=dst, s=src: e.dma_start(d, s), waits=waits, inc=ctr.tick(16))
        return ctr.at()

    for dst, src in [
        (w1_sb[:], w1_d), (b1_sb[:], b1_d), (w2_sb[:], w2_d),
        (bin_sb[:], bincol_d), (b2_sb[:], b2col_d), (bout_col[:], boutcol_d),
        (g_col[:], gcol_d), (be_col[:], bcol_d), (ident_sb[:], ident_d),
        (maskt_f[:], maskt_d), (maskn_sb[:], maskn_d),
    ]:
        dma(dst, src, d_c)
    t_consts = d_c.at()

    Gq.op(lambda e: e.memset(onesrow_bf[0:1, :], 1.0), inc=g_s.tick())
    Gq.op(lambda e: e.memset(onescol_bf[:], 1.0), inc=g_s.tick())
    Gq.op(lambda e: e.memset(ones64[64:65, :], 1.0), inc=g_s.tick())
    Gq.op(lambda e: e.tensor_copy(maskt_sb[:], maskt_f[:]), waits=[t_consts], inc=g_s.tick())
    Gq.op(lambda e: e.tensor_copy(ident_bf[:], ident_sb[:]), waits=[t_consts], inc=g_s.tick())
    g_const = g_s.at()

    tok = dma(xstage[:, 0:512], win_d.rearrange("(kc p) m -> p kc m", p=128), d_ws)
    Vq.op(lambda e: e.tensor_copy(win_sb[:], xstage[:, 0:512]), waits=[tok], inc=v_s.tick())
    tok = dma(xstage[0:64, 1024:2048], wout_d, d_ws, waits=[v_s.at()])
    Vq.op(lambda e: e.tensor_copy(wout_sb[:], xstage[0:64, 1024:2048]), waits=[tok], inc=v_s.tick())
    v_wsmall = v_s.at()

    last_store = None
    last_wread_t = {0: None, 1: None}
    last_wread_v = {0: None, 1: None}
    for kc in range(NCH):
        sb_ = kc % 2
        wstage = wstage2[sb_]
        tok = dma(wstage[:], wqkvt_d[kc * 128:(kc + 1) * 128, :], d_ws,
                  waits=[last_wread_t[sb_], last_wread_v[sb_], v_wsmall])
        Vq.op(lambda e, kc=kc, wstage=wstage: e.tensor_scalar(
            wbstage[:], wstage[:], g_col[:, kc:kc + 1], None, ALU.mult),
            waits=[tok, t_consts, last_store], inc=v_s.tick())
        last_wread_v[sb_] = v_s.at()
        for mc in range(24):
            Tq.op(lambda e, kc=kc, mc=mc, wstage=wstage: e.matmul(
                pA[:, mc * 8 + kc:mc * 8 + kc + 1], wstage[:, mc * 128:(mc + 1) * 128],
                be_col[:, kc:kc + 1], start=True, stop=True),
                waits=([tok, t_consts] if mc == 0 else []))
        Tq.retro_inc(t_s.tick())
        last_wread_t[sb_] = t_s.at()
        dma(wqs_d[0:16, kc, :, :].rearrange("mc p f -> p mc f"),
            wbstage[:, 0:2048].rearrange("p (mc f) -> p mc f", mc=16),
            d_out, waits=[last_wread_v[sb_]])
        for vh_ in range(2):
            dma(wvs_d[vh_, kc, :, :, :].rearrange("mc p f -> p mc f"),
                wbstage[:, 2048 + vh_ * 512:2048 + vh_ * 512 + 512]
                .rearrange("p (mc f) -> p mc f", mc=4),
                d_out, waits=[])
        last_store = d_out.at()
    for mc in range(24):
        Vq.op(lambda e, mc=mc: e.reduce_sum(
            c2_col[:, mc:mc + 1], pA[:, mc * 8:mc * 8 + 8], axis=AX.X),
            waits=[last_wread_t[0], last_wread_t[1]], inc=v_s.tick())
    v_c2 = v_s.at()

    for kc in range(NCH):
        sb_ = kc % 2
        wstage = wstage2[sb_]
        tok = dma(wstage[:, 0:1024], wot_d[kc * 128:(kc + 1) * 128, :], d_ws,
                  waits=[last_wread_t[sb_], last_wread_v[sb_]])
        Vq.op(lambda e, wstage=wstage: e.tensor_copy(
            wbstage[:, 0:1024], wstage[:, 0:1024]),
            waits=[tok, v_c2, last_store], inc=v_s.tick())
        last_wread_v[sb_] = v_s.at()
        for mc in range(8):
            Tq.op(lambda e, kc=kc, mc=mc, wstage=wstage: e.matmul(
                pB[:, mc * 8 + kc:mc * 8 + kc + 1], wstage[:, mc * 128:(mc + 1) * 128],
                c2_col[:, 16 + kc:17 + kc], start=True, stop=True),
                waits=([tok, v_c2] if mc == 0 else []))
        Tq.retro_inc(t_s.tick())
        last_wread_t[sb_] = t_s.at()
        last_store = dma(wos_d[:, kc, :, :].rearrange("mc p f -> p mc f"),
                         wbstage[:, 0:1024].rearrange("p (mc f) -> p mc f", mc=8),
                         d_out, waits=[last_wread_v[sb_]])
    for mc in range(8):
        Vq.op(lambda e, mc=mc: e.reduce_sum(
            c3_col[:, mc:mc + 1], pB[:, mc * 8:mc * 8 + 8], axis=AX.X),
            waits=[last_wread_t[0], last_wread_t[1]], inc=v_s.tick())
    vput = vbuf[:].rearrange("p (t h f) -> p t h f", t=NT, h=NH, f=65)
    setup_v = v_s.at()
    setup_t = max(last_wread_t[0], last_wread_t[1], key=lambda t: t[1])
    setup_d = d_out.at()
    setup_g = g_s.at()

    def emit_ln(src_sb, src_tok_fn, eps_mul, eps_add, guards, rb_war=()):
        sq_read_t = {}
        sqtoks = {}
        for c in range(NCH):
            w = [src_tok_fn(c)] + guards
            if c - 2 in sq_read_t:
                w.append(sq_read_t[c - 2])
            Vq.op(lambda e, c=c: e.tensor_tensor(
                sq_b[:, (c % 2) * T:(c % 2) * T + T],
                src_sb[:, c * T:(c + 1) * T], src_sb[:, c * T:(c + 1) * T], ALU.mult),
                waits=w, inc=v_s.tick())
            sqtoks[c] = v_s.at()
            for th in range(2):
                psum_s = (pA, pB)[th]
                psum_q = (pC, pD)[th]
                for h5 in range(2):
                    cs = th * 1024 + h5 * 512
                    w = [src_tok_fn(c)] + (guards if c == 0 else [])
                    Tq.op(lambda e, c=c, h5=h5, cs=cs, psum_s=psum_s: e.matmul(
                        psum_s[0:1, h5 * 512:h5 * 512 + 512], onescol_bf[:],
                        src_sb[:, c * T + cs:c * T + cs + 512],
                        start=(c == 0), stop=(c == NCH - 1)), waits=w)
                for h5 in range(2):
                    cs = th * 1024 + h5 * 512
                    Tq.op(lambda e, c=c, h5=h5, cs=cs, psum_q=psum_q: e.matmul(
                        psum_q[0:1, h5 * 512:h5 * 512 + 512], onescol_bf[:],
                        sq_b[:, (c % 2) * T + cs:(c % 2) * T + cs + 512],
                        start=(c == 0), stop=(c == NCH - 1)), waits=[sqtoks[c]])
            Tq.retro_inc(t_s.tick())
            sq_read_t[c] = t_s.at()
        t_stats = t_s.at()
        out = {}
        for th in range(2):
            psum_s = (pA, pB)[th]
            psum_q = (pC, pD)[th]
            seg = slice(th * 1024, th * 1024 + 1024)
            Vq.op(lambda e, psum_s=psum_s, seg=seg: e.tensor_scalar(
                B_row[0:1, seg], psum_s[0:1, 0:1024], 1.0 / D, None, ALU.mult),
                waits=[t_stats] + guards, inc=v_s.tick())
            Vq.op(lambda e, seg=seg: e.tensor_tensor(
                R_row[0:1, seg], B_row[0:1, seg], B_row[0:1, seg],
                ALU.mult), inc=v_s.tick())
            Vq.op(lambda e, psum_q=psum_q, seg=seg: e.scalar_tensor_tensor(
                psum_q[0:1, 0:1024], psum_q[0:1, 0:1024], 1.0 / D,
                R_row[0:1, seg], ALU.mult, ALU.subtract), inc=v_s.tick())
            Vq.op(lambda e, psum_q=psum_q: e.tensor_scalar(
                psum_q[0:1, 0:1024], psum_q[0:1, 0:1024], eps_mul, eps_add,
                ALU.mult, ALU.add), inc=v_s.tick())
            v_rows = v_s.at()
            Aq.op(lambda e, psum_s=psum_s, psum_q=psum_q: e.activation(
                psum_s[0:1, 0:1024], psum_q[0:1, 0:1024], AF.Ln),
                waits=[v_rows] + guards, inc=a_s.tick())
            Aq.op(lambda e, psum_s=psum_s, seg=seg: e.activation(
                R_row[0:1, seg], psum_s[0:1, 0:1024], AF.Exp, scale=-0.5),
                inc=a_s.tick())
            a_R = a_s.at()
            Vq.op(lambda e, seg=seg: e.scalar_tensor_tensor(
                B_row[0:1, seg], B_row[0:1, seg], -1.0,
                R_row[0:1, seg], ALU.mult, ALU.mult),
                waits=[a_R], inc=v_s.tick())
            b_tok = v_s.at()
            bank_ev = [None, None]
            for row, dstb, gate in [(R_row, Rb_b, a_R), (B_row, Bb_b, b_tok)]:
                for h5 in range(2):
                    cs = th * 1024 + h5 * 512
                    Tq.op(lambda e, row=row, cs=cs, psum_s=psum_s, h5=h5: e.matmul(
                        psum_s[:, h5 * 512:h5 * 512 + 512], onesrow_bf[0:1, :],
                        row[0:1, cs:cs + 512],
                        start=True, stop=True),
                        waits=[gate, bank_ev[h5]] + guards + list(rb_war),
                        inc=t_s.tick())
                    tb = t_s.at()
                    Aq.op(lambda e, dstb=dstb, cs=cs, psum_s=psum_s, h5=h5: e.activation(
                        dstb[:, cs:cs + 512], psum_s[:, h5 * 512:h5 * 512 + 512],
                        AF.Copy), waits=[tb], inc=a_s.tick())
                    bank_ev[h5] = a_s.at()
            out["rb_th%d" % th] = a_s.at()
        out["Rb"] = a_s.at()
        out["t_end"] = t_s.at()
        return out

    prevs = {
        "out_dma": [setup_d, None],
        "slotA_free": [setup_t],
        "slotB_free": [],
        "slotC_free": [],
        "slotA_q_free": [],
        "vbuf_free": [],
        "stream_free": [],
        "ln_guards": [setup_v],
        "xin_free": [],
        "rb_war": [setup_v],
        "xload_early": [],
    }
    asm_guard = [None, None]
    ctxstage_guard = [None]
    x_first_guard = [setup_t, setup_d, setup_v, setup_g]

    for s in range(NSEQ):
        cast_toks = []
        xload = []
        for c in range(NCH):
            slab = c % 4
            stg = xstage if slab < 2 else ostage
            so = (slab % 2) * T
            if c < 2 and prevs.get("xload_early"):
                tok = prevs["xload_early"][c]
            else:
                if not prevs["xin_free"]:
                    w = list(x_first_guard) + (list(prevs["out_dma"])
                                               if c in (2, 3) else [])
                elif c < 2:
                    w = [prevs["xin_free"][c]]
                elif c in (2, 3):
                    w = list(prevs["out_dma"])
                else:
                    w = []
                if c >= 4:
                    w.append(cast_toks[c - 4])
                tok = dma(stg[:, so:so + T],
                          x_d[s, c * 128:(c + 1) * 128, :], d_x[slab], waits=w)
            xload.append(tok)
            saf = prevs["slotA_free"]
            gate = [saf[c]] if len(saf) == NCH else list(saf)
            Aq.op(lambda e, c=c, stg=stg, so=so: e.activation(
                slotA[:, c * T:(c + 1) * T], stg[:, so:so + T],
                AF.Copy),
                waits=[tok] + gate, inc=a_s.tick())
            cast_toks.append(a_s.at())
        ones_w = [cast_toks[-1]] + prevs["vbuf_free"] + prevs["stream_free"] \
            + list(prevs["out_dma"])
        if s == 0:
            ones_w += [setup_d, setup_t]
        Gq.op(lambda e: e.memset(vput[:, :, :, 64:65], 1.0), waits=ones_w,
              inc=g_s.tick())
        Gq.op(lambda e: e.memset(kt[0][0:1, :], 1.0), inc=g_s.tick())
        Gq.op(lambda e: e.memset(kt[1][0:1, :], 1.0), inc=g_s.tick())
        ones_tok = g_s.at()
        ln1 = emit_ln(slotA, lambda c: cast_toks[c], 1.0 + EPS, EPS * EPS,
                      prevs["ln_guards"], rb_war=prevs["rb_war"])
        xn_toks = []
        for c in range(NCH):
            Vq.op(lambda e, c=c: e.tensor_tensor(
                tmp_b[:, (c % 2) * T:(c % 2) * T + T],
                slotA[:, c * T:(c + 1) * T], Rb_b[:], ALU.mult),
                waits=[ln1["Rb"], cast_toks[c]], inc=v_s.tick())
            Vq.op(lambda e, c=c: e.tensor_tensor(
                slotC[:, c * T:(c + 1) * T],
                tmp_b[:, (c % 2) * T:(c % 2) * T + T], Bb_b[:], ALU.add),
                waits=prevs["slotC_free"], inc=v_s.tick())
            xn_toks.append(v_s.at())
        xn_done = v_s.at()

        qk_evac = {}
        last_st_read = [None, None, None, None]
        first_stream = True
        for th in range(2):
            for mc in range(16):
                g = th * 16 + mc
                sbh = g % 4
                w = [setup_d, last_st_read[sbh]]
                if first_stream:
                    w += prevs["stream_free"]
                    first_stream = False
                wtok = dma(wqk_st[:, sbh * 1024:sbh * 1024 + 1024]
                           .rearrange("p (kc f) -> p kc f", kc=8),
                           wqs_d[mc, :, :, :].rearrange("kc p f -> p kc f"),
                           d_w[sbh], waits=w)
                pt = pAll[g % 4]
                wfirst = [wtok, ln1["t_end"]]
                if g - 4 >= 0:
                    wfirst.append(qk_evac[g - 4])
                for kc in range(NCH):
                    for h5 in range(2):
                        wkc = [xn_toks[kc]] if h5 == 0 else []
                        Tq.op(lambda e, mc=mc, kc=kc, th=th, h5=h5, pt=pt, sbh=sbh: e.matmul(
                            pt[:, h5 * 512:h5 * 512 + 512],
                            wqk_st[:, sbh * 1024 + kc * 128:sbh * 1024 + kc * 128 + 128],
                            slotC[:, kc * T + th * 1024 + h5 * 512:
                                  kc * T + th * 1024 + h5 * 512 + 512],
                            start=(kc == 0), stop=(kc == NCH - 1)),
                            waits=(wfirst + wkc if (kc == 0 and h5 == 0) else wkc))
                Tq.retro_inc(t_s.tick())
                t_grp = t_s.at()
                last_st_read[sbh] = t_grp
                dst = slotA if mc < 8 else slotB
                dmc = mc if mc < 8 else mc - 8
                wev = [t_grp]
                wev += prevs["slotA_q_free"] if mc < 8 else prevs["slotB_free"]
                Aq.op(lambda e, dst=dst, dmc=dmc, th=th, pt=pt, mc=mc: e.activation(
                    dst[:, dmc * T + th * 1024:dmc * T + th * 1024 + 1024], pt[:],
                    AF.Identity, bias=c2_col[:, mc:mc + 1]),
                    waits=wev, inc=a_s.tick())
                qk_evac[g] = a_s.at()
        qk_done = a_s.at()
        t_qk = t_s.at()


        asm_tok = {}

        def emit_asm(h, extra):
            wk = [qk_done, asm_guard[h % 2], extra]
            dma(kt[h % 2][1:65, :],
                slotB[(h % 2) * 64:(h % 2) * 64 + 64, (h // 2) * T:(h // 2) * T + T],
                d_kq[h % 2], waits=wk)
            dma(qt[h % 2][1:65, :],
                slotA[(h % 2) * 64:(h % 2) * 64 + 64, (h // 2) * T:(h // 2) * T + T],
                d_kq[h % 2], waits=[])
            asm_tok[h] = d_kq[h % 2].at()

        emit_asm(0, None)
        emit_asm(1, None)

        st = {"sprobe_b": [None, None], "probe_par": 0, "mrow": None,
              "aT": [None, None], "pT": [None, None], "ctx": None,
              "qrow": {}, "t_av": None}

        def gen_A(h, half):
            q0 = half * 1024
            mcol_toks = []
            for idx, i in enumerate(range(8 * half, 8 * half + 8)):
                row_w = (i + 1) * 128
                nchunks = 0
                for ck0 in range(0, row_w, 512):
                    ck1 = min(ck0 + 512, row_w)
                    pb_ = st["probe_par"]
                    st["probe_par"] ^= 1
                    w = [qk_done, st["sprobe_b"][pb_], ln1["Rb"]]
                    if pb_ == 1:
                        w.append(st["mrow"])
                    Tq.op(lambda e, h=h, i=i, ck0=ck0, ck1=ck1, pb_=pb_: e.matmul(
                        pA[:, pb_ * 512:pb_ * 512 + ck1 - ck0],
                        slotA[(h % 2) * 64:(h % 2) * 64 + 64,
                              (h // 2) * T + i * 128:(h // 2) * T + i * 128 + 128],
                        slotB[(h % 2) * 64:(h % 2) * 64 + 64,
                              (h // 2) * T + ck0:(h // 2) * T + ck1],
                        start=True, stop=True),
                        waits=w, inc=t_s.tick())
                    tmm = t_s.at()
                    Vq.op(lambda e, nch=nchunks, w=ck1 - ck0, pb_=pb_: e.reduce_max(
                        mtmp[:, nch:nch + 1], pA[:, pb_ * 512:pb_ * 512 + w], axis=AX.X),
                        waits=[tmm], inc=v_s.tick())
                    st["sprobe_b"][pb_] = v_s.at()
                    nchunks += 1
                Vq.op(lambda e, idx=idx, nch=nchunks: e.reduce_max(
                    mneg[:, idx:idx + 1], mtmp[:, 0:nch], axis=AX.X, negate=True),
                    inc=v_s.tick())
                mcol_toks.append(v_s.at())
                yield
            for qtr in range(2):
                for r in range(4):
                    w = [mcol_toks[qtr * 4 + r]]
                    if r == 0:
                        w += [st["mrow"], st["sprobe_b"][1]]
                    Tq.op(lambda e, qtr=qtr, r=r: e.matmul(
                        pA[0:1, 512 + r * 128:512 + r * 128 + 128],
                        mneg[:, qtr * 4 + r:qtr * 4 + r + 1], ident_bf[:],
                        start=True, stop=True),
                        waits=w, inc=(t_s.tick() if r == 3 else None))
                t_mrow = t_s.at()
                st["sprobe_b"][1] = None
                Aq.op(lambda e, h=h, q0=q0, qtr=qtr: e.activation(
                    qt[h % 2][0:1, q0 + qtr * 512:q0 + qtr * 512 + 512],
                    pA[0:1, 512:1024], AF.Copy),
                    waits=[t_mrow, asm_tok[h]], inc=a_s.tick())
                st["mrow"] = a_s.at()
                yield
            st["qrow"][(h, half)] = a_s.at()

        def gen_B(h, half):
            q0 = half * 1024
            qrow_done = st["qrow"][(h, half)]
            strips = list(range(0, 8 * (half + 1)))
            first_j, last_j = strips[0], strips[-1]
            pend = None

            def emit_av(pend):
                cks, bi, a_exp, diag_tok, j = pend
                wv = [a_exp, diag_tok, ones_tok]
                if j == first_j:
                    wv.append(st["ctx"])
                for ci, (c0, c1) in enumerate(cks):
                    lastj_chunk = min(last_j, (q0 + c1 - 1) // 128)
                    Tq.op(lambda e, h=h, j=j, c0=c0, c1=c1, bi=bi,
                          sp=(j == lastj_chunk): e.matmul(
                        pD[0:65, c0:c1],
                        vbuf[:, j * NH * 65 + h * 65:j * NH * 65 + h * 65 + 65],
                        aT[bi][:, c0:c1],
                        start=(j == first_j), stop=sp),
                        waits=(wv if ci == 0 else []),
                        inc=(t_s.tick() if ci == len(cks) - 1 else None))
                st["aT"][bi] = t_s.at()

            for j in strips:
                rel0 = max(j * 128 - q0, 0)
                bi = j % 2
                buf = (pB, pC)[bi]
                cks = []
                c0 = rel0
                while c0 < 1024:
                    c1 = min((c0 // 512 + 1) * 512, 1024)
                    cks.append((c0, c1))
                    c0 = c1
                wv = [qrow_done, asm_tok[h], st["pT"][bi], ones_tok]
                for ci, (c0, c1) in enumerate(cks):
                    Tq.op(lambda e, h=h, j=j, c0=c0, c1=c1, buf=buf, q0=q0: e.matmul(
                        buf[:, c0:c1],
                        kt[h % 2][0:65, j * 128:j * 128 + 128],
                        qt[h % 2][0:65, q0 + c0:q0 + c1],
                        start=True, stop=True),
                        waits=(wv if ci == 0 else []),
                        inc=(t_s.tick() if ci == len(cks) - 1 else None))
                t_st = t_s.at()
                Aq.op(lambda e, bi=bi, rel0=rel0, buf=buf: e.activation(
                    aT[bi][:, rel0:1024], buf[:, rel0:1024], AF.Exp, scale=0.125),
                    waits=[t_st, st["aT"][bi]], inc=a_s.tick())
                st["pT"][bi] = a_s.at()
                a_exp = a_s.at()
                diag_tok = None
                if j >= 8 * half:
                    Gq.op(lambda e, bi=bi, rel0=rel0: e.tensor_tensor(
                        aT[bi][:, rel0:rel0 + 128], aT[bi][:, rel0:rel0 + 128],
                        maskt_sb[:], ALU.mult),
                        waits=[a_exp], inc=g_s.tick())
                    diag_tok = g_s.at()
                if pend is not None:
                    emit_av(pend)
                pend = (cks, bi, a_exp, diag_tok, j)
                yield
            emit_av(pend)
            t_av = t_s.at()
            st["t_av"] = t_av

            Aq.op(lambda e: e.activation(rball[64:65, :], pD[64:65, :], AF.Ln),
                  waits=[t_av], inc=a_s.tick())
            Aq.op(lambda e: e.activation(rrow_bf[64:65, :], rball[64:65, :],
                                         AF.Exp, scale=-1.0), inc=a_s.tick())
            a_r = a_s.at()
            for h5 in range(2):
                Tq.op(lambda e, h5=h5: e.matmul(
                    pB[0:64, h5 * 512:h5 * 512 + 512], ones64[64:65, :],
                    rrow_bf[64:65, h5 * 512:h5 * 512 + 512], start=True, stop=True),
                    waits=([a_r, st["pT"][0]] if h5 == 0 else []),
                    inc=(t_s.tick() if h5 == 1 else None))
            t_rb = t_s.at()
            Aq.op(lambda e: e.activation(rball[0:64, :], pB[0:64, :], AF.Copy),
                  waits=[t_rb], inc=a_s.tick())
            v_rb = a_s.at()
            st["pT"][0] = v_rb
            if h % 2 == 0:
                Vq.op(lambda e, h=h, q0=q0: e.tensor_tensor(
                    slotC[0:64, (h // 2) * T + q0:(h // 2) * T + q0 + 1024],
                    pD[0:64, :], rball[0:64, :], ALU.mult),
                    waits=[v_rb], inc=v_s.tick())
                st["ctx"] = v_s.at()
            else:
                Vq.op(lambda e: e.tensor_tensor(
                    ctxstage[:], pD[0:64, :], rball[0:64, :], ALU.mult),
                    waits=[v_rb, ctxstage_guard[0]], inc=v_s.tick())
                st["ctx"] = v_s.at()
                tok = dma(slotC[64:128, (h // 2) * T + q0:(h // 2) * T + q0 + 1024],
                          ctxstage[:], d_cs, waits=[st["ctx"]])
                ctxstage_guard[0] = tok
            if half == 1:
                asm_guard[h % 2] = t_av
                if h + 2 < NH:
                    emit_asm(h + 2, t_av)
            yield

        pairs = [(h, hf) for h in range(NH) for hf in range(2)]
        gA0 = gen_A(*pairs[0])
        gA0_done = [False]

        def pump():
            if not gA0_done[0]:
                try:
                    next(gA0)
                except StopIteration:
                    gA0_done[0] = True
        v_ev = {}
        last_v_read = None
        for vh in range(2):
            w = [setup_d, t_qk if vh == 0 else last_v_read] + list(prevs["out_dma"])
            wtok = dma(wv_st[:].rearrange("p (km f) -> p km f", km=32),
                       wvs_d[vh, :, :, :, :].rearrange("kc mc p f -> p (kc mc) f"),
                       d_wv, waits=w)
            for tt in range(NT):
                gidx = vh * NT + tt
                pt = (pB, pC, pD)[gidx % 3]
                wfirst = [wtok, xn_done]
                if gidx - 3 >= 0:
                    wfirst.append(v_ev[gidx - 3])
                else:
                    wfirst.append(qk_done)
                for kc in range(NCH):
                    Tq.op(lambda e, tt=tt, kc=kc, pt=pt: e.matmul(
                        pt[:, 0:512],
                        slotC[:, kc * T + tt * 128:kc * T + tt * 128 + 128],
                        wv_st[:, kc * 512:kc * 512 + 512],
                        start=(kc == 0), stop=(kc == NCH - 1)),
                        waits=(wfirst if kc == 0 else []))
                Tq.retro_inc(t_s.tick())
                t_grp = t_s.at()
                last_v_read = t_grp
                dstv = vbuf[:, tt * NH * 65:(tt + 1) * NH * 65] \
                    .rearrange("p (h f) -> p h f", h=NH)[:, vh * 8:vh * 8 + 8, 0:64]
                srcv = pt[:, 0:512].rearrange("p (h f) -> p h f", h=8)
                Aq.op(lambda e, dstv=dstv, srcv=srcv: e.activation(dstv, srcv, AF.Copy),
                      waits=[t_grp, setup_g] + prevs["vbuf_free"] + list(prevs["out_dma"]),
                      inc=a_s.tick())
                v_ev[gidx] = a_s.at()
                pump()
                pump()
        v_done = a_s.at()
        t_v = t_s.at()
        st["pT"] = [v_done, v_done]
        st["ctx"] = v_done
        st["aT"] = [st["aT"][0], st["aT"][1]]
        for _ in gA0:
            pass
        gAn = None
        for k, pr in enumerate(pairs):
            gB = gen_B(*pr)
            gAn = gen_A(*pairs[k + 1]) if k + 1 < len(pairs) else None
            nstrips = 8 * (pr[1] + 1)
            if nstrips >= 10:
                sched = [0] * (nstrips - 10) + [1] * 10
            else:
                sched = [1] * nstrips
                for i in range(10 - nstrips):
                    sched[-1 - (i % 2)] += 1
            yi = 0
            for _ in gB:
                nsteps = sched[yi] if yi < len(sched) else 1
                yi += 1
                for _ in range(nsteps):
                    if gAn is not None:
                        try:
                            next(gAn)
                        except StopIteration:
                            gAn = None
            if gAn is not None:
                for _ in gAn:
                    pass
                gAn = None
        t_av_last = st["t_av"]
        v_p3 = v_s.at()
        a_p3 = a_s.at()
        d_p3 = [d_kq[0].at(), d_kq[1].at(), d_cs.at()]
        t_p3 = t_s.at()

        ao_evac = {}
        last_wo_read = [None, None]
        for mc in range(8):
            sbh = mc % 2
            w = [setup_d, last_wo_read[sbh]]
            if mc == 0:
                w += [t_p3]
            wtok = dma(wo_st[:, sbh * 1024:sbh * 1024 + 1024]
                       .rearrange("p (kc f) -> p kc f", kc=8),
                       wos_d[mc, :, :, :].rearrange("kc p f -> p kc f"),
                       d_w[sbh], waits=w)
            pts = (pA, pB) if mc % 2 == 0 else (pC, pD)
            wfirst = [wtok, v_p3, a_p3, d_p3]
            if mc >= 2:
                wfirst.append(ao_evac[mc - 2])
            for kc in range(NCH):
                for sl in range(4):
                    pt = pts[sl // 2]
                    Tq.op(lambda e, mc=mc, kc=kc, sl=sl, pt=pt, sbh=sbh: e.matmul(
                        pt[:, (sl % 2) * 512:(sl % 2) * 512 + 512],
                        wo_st[:, sbh * 1024 + kc * 128:sbh * 1024 + kc * 128 + 128],
                        slotC[:, kc * T + sl * 512:kc * T + sl * 512 + 512],
                        start=(kc == 0), stop=(kc == NCH - 1)),
                        waits=(wfirst if (kc == 0 and sl == 0) else []))
            Tq.retro_inc(t_s.tick())
            t_grp = t_s.at()
            last_wo_read[sbh] = t_grp
            for halfp in range(2):
                Aq.op(lambda e, mc=mc, halfp=halfp, pts=pts: e.activation(
                    slotA[:, mc * T + halfp * 1024:mc * T + halfp * 1024 + 1024],
                    pts[halfp][:], AF.Identity, bias=c3_col[:, mc:mc + 1]),
                    waits=[t_grp, d_p3], inc=a_s.tick())
            ao_evac[mc] = a_s.at()
        ao_done = a_s.at()
        t_wo = t_s.at()

        ln3 = emit_ln(slotA, lambda c: ao_evac[c], 1.0, EPS, [ao_done, v_p3],
                      rb_war=[xn_done])
        x2_toks = []
        for c in range(NCH):
            slab = c % 4
            stg = xstage if slab < 2 else ostage
            so = (slab % 2) * T
            w = [cast_toks[-1], t_av_last] + ([x2_toks[c - 4]] if c >= 4 else [])
            xtok = dma(stg[:, so:so + T],
                       x_d[s, c * 128:(c + 1) * 128, :], d_x[slab], waits=w)
            Vq.op(lambda e, c=c: e.tensor_tensor(
                tmp_b[:, (c % 2) * T:(c % 2) * T + T],
                slotA[:, c * T:(c + 1) * T], Rb_b[:], ALU.mult),
                waits=[ln3["Rb"]] + ([x2_toks[c - 2]] if c >= 2 else []),
                inc=v_s.tick())
            Vq.op(lambda e, c=c: e.tensor_tensor(
                tmp_b[:, (c % 2) * T:(c % 2) * T + T],
                tmp_b[:, (c % 2) * T:(c % 2) * T + T], Bb_b[:], ALU.add),
                inc=v_s.tick())
            Vq.op(lambda e, c=c, stg=stg, so=so: e.tensor_tensor(
                slotB[:, c * T:(c + 1) * T],
                tmp_b[:, (c % 2) * T:(c % 2) * T + T],
                stg[:, so:so + T], ALU.add),
                waits=[xtok, t_wo, d_p3], inc=v_s.tick())
            x2_toks.append(v_s.at())
        x2_done = v_s.at()

        ln4 = emit_ln(slotB, lambda c: x2_toks[c], 1.0, EPS, [x2_done, ao_done],
                      rb_war=[x2_done])
        xn2_toks = []
        for c in range(NCH):
            Vq.op(lambda e, c=c: e.tensor_tensor(
                tmp_b[:, (c % 2) * T:(c % 2) * T + T],
                slotB[:, c * T:(c + 1) * T], Rb_b[:], ALU.mult),
                waits=[ln4["Rb"]], inc=v_s.tick())
            Vq.op(lambda e, c=c: e.tensor_tensor(
                slotA[:, c * T:(c + 1) * T],
                tmp_b[:, (c % 2) * T:(c % 2) * T + T], Bb_b[:], ALU.add),
                inc=v_s.tick())
            xn2_toks.append(v_s.at())
        xn2_done = v_s.at()

        s_ev = []
        for sl in range(4):
            pt = (pA, pA, pB, pB)[sl]
            off = (sl % 2) * 512
            for kc in range(NCH):
                wkc = [xn2_toks[kc]] + ([ln4["t_end"]] if kc == 0 else [])
                Tq.op(lambda e, sl=sl, kc=kc, pt=pt, off=off: e.matmul(
                    pt[0:64, off:off + 512],
                    win_sb[:, kc * 64:kc * 64 + 64],
                    slotA[:, kc * T + sl * 512:kc * T + sl * 512 + 512],
                    start=(kc == 0), stop=(kc == NCH - 1)),
                    waits=wkc)
            Tq.retro_inc(t_s.tick())
            t_grp = t_s.at()
            if sl % 2 == 1:
                pth = (pA, pB)[sl // 2]
                Aq.op(lambda e, sl=sl, pth=pth: e.activation(
                    s_sb[:, (sl - 1) * 512:(sl + 1) * 512], pth[0:64, :],
                    AF.Identity, bias=bin_sb[:]),
                    waits=[t_grp, d_p3], inc=a_s.tick())
                s_ev.append(a_s.at())
        s_done = a_s.at()
        g_q = []
        ts_toks = {}
        for q4 in range(4):
            qs = slice(q4 * 512, q4 * 512 + 512)
            for i in range(16):
                tfa = tfn[0:64, (i % 2) * 512:(i % 2) * 512 + 512]
                if i < 2:
                    wa = [s_ev[q4 // 2]]
                    if q4 > 0:
                        wa.append(ts_toks[(q4 - 1, 14 + i)])
                else:
                    wa = [ts_toks[(q4, i - 2)]]
                Aq.op(lambda e, i=i, qs=qs, tfa=tfa: e.activation(
                    tfa, s_sb[0:64, qs], AF.Identity,
                    scale=w1_sb[:, i:i + 1], bias=b1_sb[:, i:i + 1]),
                    waits=wa, inc=a_s.tick())
                a_tf = a_s.at()
                dst = g_sb if i == 0 else u_fn
                Vq.op(lambda e, dst=dst, i=i, qs=qs, tfa=tfa: e.tensor_scalar(
                    dst[0:64, qs], tfa, 0.0, w2_sb[:, i:i + 1],
                    ALU.max, ALU.mult), waits=[a_tf], inc=v_s.tick())
                ts_toks[(q4, i)] = v_s.at()
                if i > 0:
                    Vq.op(lambda e, qs=qs: e.tensor_tensor(
                        g_sb[0:64, qs], g_sb[0:64, qs], u_fn[0:64, qs], ALU.add),
                        inc=v_s.tick())
            Vq.op(lambda e, qs=qs: e.tensor_scalar(
                gb_sb[0:64, qs], g_sb[0:64, qs], b2_sb[:], None, ALU.add),
                inc=v_s.tick())
            g_q.append(v_s.at())
        g_done = v_s.at()
        fo_ev = {}
        for mc in range(8):
            pts = (pA, pB) if mc % 2 == 0 else (pC, pD)
            for sl in range(4):
                pt = pts[sl // 2]
                w = [g_q[sl]]
                if sl == 0 and mc >= 2:
                    w.append(fo_ev[mc - 2])
                Tq.op(lambda e, mc=mc, sl=sl, pt=pt: e.matmul(
                    pt[:, (sl % 2) * 512:(sl % 2) * 512 + 512],
                    wout_sb[0:64, mc * 128:mc * 128 + 128],
                    gb_sb[:, sl * 512:sl * 512 + 512],
                    start=True, stop=True),
                    waits=w)
            Tq.retro_inc(t_s.tick())
            t_grp = t_s.at()
            for halfp in range(2):
                Aq.op(lambda e, mc=mc, halfp=halfp, pts=pts: e.activation(
                    slotA[:, mc * T + halfp * 1024:mc * T + halfp * 1024 + 1024],
                    pts[halfp][:], AF.Identity, bias=bout_col[:, mc:mc + 1]),
                    waits=[t_grp, xn2_done], inc=a_s.tick())
            fo_ev[mc] = a_s.at()
        fo_done = a_s.at()

        ln5 = emit_ln(slotA, lambda c: fo_ev[c], 1.0, EPS, [fo_done, x2_done],
                      rb_war=[xn2_done])
        xload_early = []
        if s + 1 < NSEQ:
            for ce in range(2):
                xload_early.append(dma(
                    xstage[:, ce * T:ce * T + T],
                    x_d[s + 1, ce * 128:(ce + 1) * 128, :], d_x[ce],
                    waits=[x2_toks[4 + ce]]))
        out_toks = []
        out_g = []
        ln5_tt1 = []
        for c in range(NCH):
            Vq.op(lambda e, c=c: e.tensor_tensor(
                tmp_b[:, (c % 2) * T:(c % 2) * T + T],
                slotA[:, c * T:(c + 1) * T], Rb_b[:], ALU.mult),
                waits=[ln5["Rb"]] + ([out_g[c - 2]] if c >= 2 else []),
                inc=v_s.tick())
            ln5_tt1.append(v_s.at())
            Vq.op(lambda e, c=c: e.tensor_tensor(
                tmp_b[:, (c % 2) * T:(c % 2) * T + T],
                tmp_b[:, (c % 2) * T:(c % 2) * T + T], Bb_b[:], ALU.add),
                inc=v_s.tick())
            w = [out_toks[c - 2]] if c >= 2 else [x2_toks[6 + c]]
            Vq.op(lambda e, c=c: e.tensor_tensor(
                ostage[:, (c % 2) * T:(c % 2) * T + T],
                tmp_b[:, (c % 2) * T:(c % 2) * T + T],
                slotB[:, c * T:(c + 1) * T], ALU.add),
                waits=w, inc=v_s.tick())
            v_out = v_s.at()
            out_g.append(v_out)
            tok = dma(y_d[s, c * 128:(c + 1) * 128, :],
                      ostage[:, (c % 2) * T:(c % 2) * T + T], d_y[c % 4], waits=[v_out])
            out_toks.append(tok)
        v_p6 = v_s.at()
        t_ln5 = ln5["t_end"]

        g_p6 = g_s.at()
        prevs = {
            "out_dma": out_toks[-4:],
            "slotA_free": ln5_tt1,
            "slotB_free": [v_p6, g_p6],
            "slotC_free": [t_wo],
            "slotA_q_free": [t_p3, d_p3],
            "vbuf_free": [t_av_last],
            "stream_free": [v_p6, fo_done],
            "ln_guards": [ln5["Rb"], fo_done],
            "xin_free": [x2_toks[4], x2_toks[5]],
            "rb_war": [v_p6],
            "xload_early": xload_early,
        }

    nc._dbg = dict(slotA=slotA, slotB=slotB, slotC=slotC, vbuf=vbuf, kt0=kt[0],
                   kt1=kt[1], qt0=qt[0], qt1=qt[1], c2_col=c2_col, c3_col=c3_col,
                   Rb_b=Rb_b, Bb_b=Bb_b, R_row=R_row, B_row=B_row, s_sb=s_sb,
                   g_sb=g_sb, gb_sb=gb_sb, rball=rball, mneg=mneg, aT0=aT[0],
                   aT1=aT[1], win_sb=win_sb, wout_sb=wout_sb, tfn=tfn, u_fn=u_fn, g=g_sb, w1=w1_sb, b1=b1_sb, w2=w2_sb)

    Sq.op(lambda e: None, waits=[d.at() for d in d_y])

    with nc.Block() as block:
        @block.tensor
        def _(e):
            run_ops(e, Tq.ops)

        @block.vector
        def _(e):
            run_ops(e, Vq.ops)

        @block.scalar
        def _(e):
            run_ops(e, Aq.ops)

        @block.gpsimd
        def _(e):
            run_ops(e, Gq.ops)

        @block.sync
        def _(e):
            run_ops(e, Sq.ops)

    return nc


def kernel(L0_attn_x, L0_attn_ln_gamma, L0_attn_ln_beta, L0_attn_W_qkv, L0_attn_W_o,
           L0_ffn_Win, L0_ffn_bin_, L0_ffn_W1, L0_ffn_b1, L0_ffn_W2, L0_ffn_b2,
           L0_ffn_Wout, L0_ffn_bout):
    x = np.asarray(L0_attn_x, dtype=np.float32)
    gamma = np.asarray(L0_attn_ln_gamma, dtype=np.float32)
    beta = np.asarray(L0_attn_ln_beta, dtype=np.float32)
    wqkv = np.asarray(L0_attn_W_qkv, dtype=np.float32)
    wo = np.asarray(L0_attn_W_o, dtype=np.float32)
    win = np.asarray(L0_ffn_Win, dtype=np.float32)
    bin_ = np.asarray(L0_ffn_bin_, dtype=np.float32)
    w1 = np.asarray(L0_ffn_W1, dtype=np.float32)
    b1 = np.asarray(L0_ffn_b1, dtype=np.float32)
    w2 = np.asarray(L0_ffn_W2, dtype=np.float32)
    b2 = np.asarray(L0_ffn_b2, dtype=np.float32)
    wout = np.asarray(L0_ffn_Wout, dtype=np.float32)
    bout = np.asarray(L0_ffn_bout, dtype=np.float32)

    idx = []
    for blk in range(3):
        for h in range(NH):
            base = h * 192 + blk * 64
            idx.extend(range(base, base + 64))
    wqkvt = np.ascontiguousarray(wqkv[np.array(idx)].T)
    wot = np.ascontiguousarray(wo.T)
    gcol = np.ascontiguousarray(gamma.reshape(8, 128).T)
    bcol = np.ascontiguousarray(beta.reshape(8, 128).T)
    boutcol = np.ascontiguousarray(bout.reshape(8, 128).T)
    bincol = np.ascontiguousarray(bin_.reshape(64, 1))
    b2col = np.ascontiguousarray(b2.reshape(64, 1))
    ident = np.eye(128, dtype=np.float32)
    kq = np.arange(128)
    maskt = np.ascontiguousarray((kq[:, None] <= kq[None, :]).astype(np.float32))
    maskn = np.ascontiguousarray(
        np.where(kq[None, :] > kq[:, None], np.float32(-1e30), np.float32(0.0)))

    nc = build_program()
    in_maps = []
    for c in range(NCORE):
        xs = np.ascontiguousarray(np.transpose(x[c * NSEQ:(c + 1) * NSEQ], (0, 2, 1)))
        in_maps.append(dict(
            x=xs, wqkvt=wqkvt, wot=wot, win=win, wout=wout,
            gcol=gcol, bcol=bcol, bincol=bincol, w1=w1, b1=b1, w2=w2,
            b2col=b2col, boutcol=boutcol, ident=ident, maskt=maskt, maskn=maskn,
        ))
    res = run_bass_kernel_spmd(nc, in_maps, core_ids=list(range(NCORE)))
    outs = []
    for c in range(NCORE):
        yc = res.results[c]["y"]
        outs.append(np.transpose(yc, (0, 2, 1)))
    return np.ascontiguousarray(np.concatenate(outs, axis=0))


# version 32
# speedup vs baseline: 1.1492x; 1.0050x over previous
import numpy as np
import ml_dtypes
import concourse.bass as bass
import concourse.mybir as mybir
from concourse.bass_utils import run_bass_kernel_spmd

F32 = mybir.dt.float32
BF16 = mybir.dt.bfloat16
AF = mybir.ActivationFunctionType
ALU = mybir.AluOpType
AX = mybir.AxisListType

NCORE = 8
NSEQ = 4
T = 2048
D = 1024
NH = 16
DH = 64
NCH = 8
NT = 16
EPS = 1e-5


class Ctr:
    def __init__(self, nc, name):
        self.h = nc.alloc_semaphore(name)
        self.n = 0

    def tick(self, k=1):
        self.n += k
        return (self.h, k)

    def at(self):
        return (self.h, self.n)


class Eng:
    def __init__(self):
        self.ops = []

    def op(self, fn, waits=(), inc=None):
        ws = []
        for w in waits:
            if w is None:
                continue
            if isinstance(w, list):
                ws.extend(t for t in w if t is not None)
            else:
                ws.append(w)
        self.ops.append((tuple(ws), fn, inc))

    def retro_inc(self, inc):
        w, f, old = self.ops[-1]
        assert old is None
        self.ops[-1] = (w, f, inc)


def run_ops(engine, ops):
    waited = {}
    for waits, fn, inc in ops:
        for sem, val in waits:
            if val > 0 and waited.get(id(sem), -1) < val:
                engine.wait_ge(sem, val)
                waited[id(sem)] = val
        ins = fn(engine)
        if inc is not None:
            sem, k = inc
            ins.then_inc(sem, k)


def build_program(nc=None):
    if nc is None:
        nc = bass.Bass("TRN2", target_bir_lowering=False)

    x_d = nc.dram_tensor("x", [NSEQ, D, T], F32, kind="ExternalInput").ap()
    y_d = nc.dram_tensor("y", [NSEQ, D, T], F32, kind="ExternalOutput").ap()
    wqkvt_d = nc.dram_tensor("wqkvt", [D, 3 * D], F32, kind="ExternalInput").ap()
    wot_d = nc.dram_tensor("wot", [D, D], F32, kind="ExternalInput").ap()
    win_d = nc.dram_tensor("win", [D, 64], F32, kind="ExternalInput").ap()
    wout_d = nc.dram_tensor("wout", [64, D], F32, kind="ExternalInput").ap()
    gcol_d = nc.dram_tensor("gcol", [128, 8], F32, kind="ExternalInput").ap()
    bcol_d = nc.dram_tensor("bcol", [128, 8], F32, kind="ExternalInput").ap()
    bincol_d = nc.dram_tensor("bincol", [64, 1], F32, kind="ExternalInput").ap()
    w1_d = nc.dram_tensor("w1", [64, 16], F32, kind="ExternalInput").ap()
    b1_d = nc.dram_tensor("b1", [64, 16], F32, kind="ExternalInput").ap()
    w2_d = nc.dram_tensor("w2", [64, 16], F32, kind="ExternalInput").ap()
    b2col_d = nc.dram_tensor("b2col", [64, 1], F32, kind="ExternalInput").ap()
    boutcol_d = nc.dram_tensor("boutcol", [128, 8], F32, kind="ExternalInput").ap()
    ident_d = nc.dram_tensor("ident", [128, 128], F32, kind="ExternalInput").ap()
    maskt_d = nc.dram_tensor("maskt", [128, 128], F32, kind="ExternalInput").ap()
    maskn_d = nc.dram_tensor("maskn", [128, 128], F32, kind="ExternalInput").ap()
    wqs_d = nc.dram_tensor("wqs", [24, 8, 128, 128], BF16, kind="Internal").ap()
    wos_d = nc.dram_tensor("wos", [8, 8, 128, 128], BF16, kind="Internal").ap()
    wvs_d = nc.dram_tensor("wvs", [2, 8, 4, 128, 128], BF16, kind="Internal").ap()

    BASE = 18464
    def at(name, shape, dt, off):
        return nc.alloc_sbuf_tensor_at(name, shape, dt, offset=BASE + off).ap()

    slotA = at("slotA", [128, NCH * T], BF16, 0)
    slotB = at("slotB", [128, NCH * T], BF16, 32768)
    slotC = at("slotC", [128, NCH * T], BF16, 65536)
    VOFF = 98304
    vbuf = at("vbuf", [128, NT * NH * 65], BF16, VOFF)
    wstage2 = [at("wstage0", [128, 3 * D], F32, VOFF),
               at("wstage1", [128, 3 * D], F32, VOFF + 12288)]
    wbstage = at("wbstage", [128, 3 * D], BF16, VOFF + 24576)
    xstage = at("xstage", [128, 2 * T], F32, VOFF)
    ostage = at("ostage", [128, 2 * T], F32, VOFF + 16384)
    FOFF = 131584
    s_sb = at("s_sb", [64, T], F32, FOFF)
    tfn = at("tfn", [64, T], F32, FOFF + 8192)
    u_fn = at("u_fn", [64, T], F32, FOFF + 16384)
    g_sb = at("g_sb", [64, T], F32, FOFF + 24576)
    gb_sb = at("gb_sb", [64, T], BF16, FOFF + 32768)
    kt = [at("kt0", [65, T], BF16, FOFF), at("kt1", [65, T], BF16, FOFF + 4096)]
    qt = [at("qt0", [65, T], BF16, FOFF + 8192), at("qt1", [65, T], BF16, FOFF + 12288)]
    aT = [at("aT0", [128, 1024], BF16, FOFF + 16384), at("aT1", [128, 1024], BF16, FOFF + 18432)]
    rball = at("rball", [128, 1024], F32, FOFF + 20480)
    ctxstage = at("ctxstage", [64, 1024], BF16, FOFF + 24576)
    wqk_st = at("wqk_st", [128, 4096], BF16, FOFF + 26624)
    wv_st = at("wv_st", [128, 4096], BF16, FOFF + 26624)
    wo_st = at("wo_st", [128, 2048], BF16, FOFF + 26624)
    tmpblk = at("tmpblk", [128, 128], F32, 209472)
    mneg = at("mneg", [128, 8], BF16, 209984)
    mtmp = at("mtmp", [128, 8], F32, 210016)
    R_row = at("R_row", [1, 2048], BF16, 168448)
    B_row = at("B_row", [1, 2048], BF16, 172544)
    Rb_b = at("Rb_b", [128, T], BF16, 176640)
    Bb_b = at("Bb_b", [128, T], BF16, 180736)
    sq_b = at("sq_b", [128, 2 * T], BF16, 184832)
    tmp_b = at("tmp_b", [128, 2 * T], BF16, 193024)
    CO = 201216
    win_sb = at("win_sb", [128, 8 * 64], BF16, CO)
    wout_sb = at("wout_sb", [64, D], BF16, CO + 1024)
    ident_sb = at("ident_sb", [128, 128], F32, CO + 3072)
    maskt_f = at("maskt_f", [128, 128], F32, CO + 3584)
    maskn_sb = at("maskn_sb", [128, 128], F32, CO + 4096)
    maskt_sb = at("maskt_sb", [128, 128], BF16, CO + 4608)
    onesrow_bf = at("onesrow_bf", [1, 128], BF16, CO + 4864)
    ones64 = at("ones64", [65, 64], BF16, CO + 5152)
    ident_bf = at("ident_bf", [128, 128], BF16, CO + 5920)
    rrow_bf = at("rrow_bf", [65, 1024], BF16, CO + 6208)
    w1_sb = at("w1_sb", [64, 16], F32, CO + 5408)
    b1_sb = at("b1_sb", [64, 16], F32, CO + 5472)
    w2_sb = at("w2_sb", [64, 16], F32, CO + 5536)
    c2_col = at("c2_col", [128, 24], F32, CO + 5600)
    c3_col = at("c3_col", [128, 8], F32, CO + 5696)
    g_col = at("g_col", [128, 8], F32, CO + 5728)
    be_col = at("be_col", [128, 8], F32, CO + 5760)
    bout_col = at("bout_col", [128, 8], F32, CO + 5792)
    bin_sb = at("bin_sb", [64, 1], F32, CO + 5824)
    b2_sb = at("b2_sb", [64, 1], F32, CO + 5856)
    onescol_bf = at("onescol_bf", [128, 1], BF16, CO + 5888)

    pA = nc.alloc_psum_tensor("pA", [128, 1024], F32).ap()
    pB = nc.alloc_psum_tensor("pB", [128, 1024], F32).ap()
    pC = nc.alloc_psum_tensor("pC", [128, 1024], F32).ap()
    pD = nc.alloc_psum_tensor("pD", [128, 1024], F32).ap()
    pAll = [pA, pB, pC, pD]

    Tq, Vq, Aq, Gq, Sq = Eng(), Eng(), Eng(), Eng(), Eng()
    t_s = Ctr(nc, "t_s")
    v_s = Ctr(nc, "v_s")
    a_s = Ctr(nc, "a_s")
    g_s = Ctr(nc, "g_s")
    d_c = Ctr(nc, "d_c")
    d_ws = Ctr(nc, "d_ws")
    d_x = [Ctr(nc, "d_x0"), Ctr(nc, "d_x1"),
           Ctr(nc, "d_x2"), Ctr(nc, "d_x3")]
    d_y = [Ctr(nc, "d_y0"), Ctr(nc, "d_y1"),
           Ctr(nc, "d_y2"), Ctr(nc, "d_y3")]
    d_w = [Ctr(nc, "d_w0"), Ctr(nc, "d_w1"),
           Ctr(nc, "d_w2"), Ctr(nc, "d_w3")]
    d_wv = Ctr(nc, "d_wv")
    d_kq = [Ctr(nc, "d_kq0"), Ctr(nc, "d_kq1")]
    d_cs = Ctr(nc, "d_cs")
    d_out = Ctr(nc, "d_out")

    def dma(dst, src, ctr, waits=()):
        Sq.op(lambda e, d=dst, s=src: e.dma_start(d, s), waits=waits, inc=ctr.tick(16))
        return ctr.at()

    for dst, src in [
        (w1_sb[:], w1_d), (b1_sb[:], b1_d), (w2_sb[:], w2_d),
        (bin_sb[:], bincol_d), (b2_sb[:], b2col_d), (bout_col[:], boutcol_d),
        (g_col[:], gcol_d), (be_col[:], bcol_d), (ident_sb[:], ident_d),
        (maskt_f[:], maskt_d), (maskn_sb[:], maskn_d),
    ]:
        dma(dst, src, d_c)
    t_consts = d_c.at()

    Gq.op(lambda e: e.memset(onesrow_bf[0:1, :], 1.0), inc=g_s.tick())
    Gq.op(lambda e: e.memset(onescol_bf[:], 1.0), inc=g_s.tick())
    Gq.op(lambda e: e.memset(ones64[64:65, :], 1.0), inc=g_s.tick())
    Gq.op(lambda e: e.tensor_copy(maskt_sb[:], maskt_f[:]), waits=[t_consts], inc=g_s.tick())
    Gq.op(lambda e: e.tensor_copy(ident_bf[:], ident_sb[:]), waits=[t_consts], inc=g_s.tick())
    g_const = g_s.at()

    tok = dma(xstage[:, 0:512], win_d.rearrange("(kc p) m -> p kc m", p=128), d_ws)
    Vq.op(lambda e: e.tensor_copy(win_sb[:], xstage[:, 0:512]), waits=[tok], inc=v_s.tick())
    tok = dma(xstage[0:64, 1024:2048], wout_d, d_ws, waits=[v_s.at()])
    Vq.op(lambda e: e.tensor_copy(wout_sb[:], xstage[0:64, 1024:2048]), waits=[tok], inc=v_s.tick())
    v_wsmall = v_s.at()

    last_store = None
    last_wread_t = {0: None, 1: None}
    last_wread_v = {0: None, 1: None}
    for kc in range(NCH):
        sb_ = kc % 2
        wstage = wstage2[sb_]
        tok = dma(wstage[:], wqkvt_d[kc * 128:(kc + 1) * 128, :], d_ws,
                  waits=[last_wread_t[sb_], last_wread_v[sb_], v_wsmall])
        Vq.op(lambda e, kc=kc, wstage=wstage: e.tensor_scalar(
            wbstage[:], wstage[:], g_col[:, kc:kc + 1], None, ALU.mult),
            waits=[tok, t_consts, last_store], inc=v_s.tick())
        last_wread_v[sb_] = v_s.at()
        for mc in range(24):
            Tq.op(lambda e, kc=kc, mc=mc, wstage=wstage: e.matmul(
                pA[:, mc * 8 + kc:mc * 8 + kc + 1], wstage[:, mc * 128:(mc + 1) * 128],
                be_col[:, kc:kc + 1], start=True, stop=True),
                waits=([tok, t_consts] if mc == 0 else []))
        Tq.retro_inc(t_s.tick())
        last_wread_t[sb_] = t_s.at()
        dma(wqs_d[0:16, kc, :, :].rearrange("mc p f -> p mc f"),
            wbstage[:, 0:2048].rearrange("p (mc f) -> p mc f", mc=16),
            d_out, waits=[last_wread_v[sb_]])
        for vh_ in range(2):
            dma(wvs_d[vh_, kc, :, :, :].rearrange("mc p f -> p mc f"),
                wbstage[:, 2048 + vh_ * 512:2048 + vh_ * 512 + 512]
                .rearrange("p (mc f) -> p mc f", mc=4),
                d_out, waits=[])
        last_store = d_out.at()
    for mc in range(24):
        Vq.op(lambda e, mc=mc: e.reduce_sum(
            c2_col[:, mc:mc + 1], pA[:, mc * 8:mc * 8 + 8], axis=AX.X),
            waits=[last_wread_t[0], last_wread_t[1]], inc=v_s.tick())
    v_c2 = v_s.at()

    for kc in range(NCH):
        sb_ = kc % 2
        wstage = wstage2[sb_]
        tok = dma(wstage[:, 0:1024], wot_d[kc * 128:(kc + 1) * 128, :], d_ws,
                  waits=[last_wread_t[sb_], last_wread_v[sb_]])
        Vq.op(lambda e, wstage=wstage: e.tensor_copy(
            wbstage[:, 0:1024], wstage[:, 0:1024]),
            waits=[tok, v_c2, last_store], inc=v_s.tick())
        last_wread_v[sb_] = v_s.at()
        for mc in range(8):
            Tq.op(lambda e, kc=kc, mc=mc, wstage=wstage: e.matmul(
                pB[:, mc * 8 + kc:mc * 8 + kc + 1], wstage[:, mc * 128:(mc + 1) * 128],
                c2_col[:, 16 + kc:17 + kc], start=True, stop=True),
                waits=([tok, v_c2] if mc == 0 else []))
        Tq.retro_inc(t_s.tick())
        last_wread_t[sb_] = t_s.at()
        last_store = dma(wos_d[:, kc, :, :].rearrange("mc p f -> p mc f"),
                         wbstage[:, 0:1024].rearrange("p (mc f) -> p mc f", mc=8),
                         d_out, waits=[last_wread_v[sb_]])
    for mc in range(8):
        Vq.op(lambda e, mc=mc: e.reduce_sum(
            c3_col[:, mc:mc + 1], pB[:, mc * 8:mc * 8 + 8], axis=AX.X),
            waits=[last_wread_t[0], last_wread_t[1]], inc=v_s.tick())
    vput = vbuf[:].rearrange("p (t h f) -> p t h f", t=NT, h=NH, f=65)
    setup_v = v_s.at()
    setup_t = max(last_wread_t[0], last_wread_t[1], key=lambda t: t[1])
    setup_d = d_out.at()
    setup_g = g_s.at()

    def emit_ln(src_sb, src_tok_fn, eps_mul, eps_add, guards, rb_war=()):
        sq_read_t = {}
        sqtoks = {}
        for c in range(NCH):
            w = [src_tok_fn(c)] + guards
            if c - 2 in sq_read_t:
                w.append(sq_read_t[c - 2])
            Vq.op(lambda e, c=c: e.tensor_tensor(
                sq_b[:, (c % 2) * T:(c % 2) * T + T],
                src_sb[:, c * T:(c + 1) * T], src_sb[:, c * T:(c + 1) * T], ALU.mult),
                waits=w, inc=v_s.tick())
            sqtoks[c] = v_s.at()
            for th in range(2):
                psum_s = (pA, pB)[th]
                psum_q = (pC, pD)[th]
                for h5 in range(2):
                    cs = th * 1024 + h5 * 512
                    w = [src_tok_fn(c)] + (guards if c == 0 else [])
                    Tq.op(lambda e, c=c, h5=h5, cs=cs, psum_s=psum_s: e.matmul(
                        psum_s[0:1, h5 * 512:h5 * 512 + 512], onescol_bf[:],
                        src_sb[:, c * T + cs:c * T + cs + 512],
                        start=(c == 0), stop=(c == NCH - 1)), waits=w)
                for h5 in range(2):
                    cs = th * 1024 + h5 * 512
                    Tq.op(lambda e, c=c, h5=h5, cs=cs, psum_q=psum_q: e.matmul(
                        psum_q[0:1, h5 * 512:h5 * 512 + 512], onescol_bf[:],
                        sq_b[:, (c % 2) * T + cs:(c % 2) * T + cs + 512],
                        start=(c == 0), stop=(c == NCH - 1)), waits=[sqtoks[c]])
            Tq.retro_inc(t_s.tick())
            sq_read_t[c] = t_s.at()
        t_stats = t_s.at()
        out = {}
        for th in range(2):
            psum_s = (pA, pB)[th]
            psum_q = (pC, pD)[th]
            seg = slice(th * 1024, th * 1024 + 1024)
            Vq.op(lambda e, psum_s=psum_s, seg=seg: e.tensor_scalar(
                B_row[0:1, seg], psum_s[0:1, 0:1024], 1.0 / D, None, ALU.mult),
                waits=[t_stats] + guards, inc=v_s.tick())
            Vq.op(lambda e, seg=seg: e.tensor_tensor(
                R_row[0:1, seg], B_row[0:1, seg], B_row[0:1, seg],
                ALU.mult), inc=v_s.tick())
            Vq.op(lambda e, psum_q=psum_q, seg=seg: e.scalar_tensor_tensor(
                psum_q[0:1, 0:1024], psum_q[0:1, 0:1024], 1.0 / D,
                R_row[0:1, seg], ALU.mult, ALU.subtract), inc=v_s.tick())
            Vq.op(lambda e, psum_q=psum_q: e.tensor_scalar(
                psum_q[0:1, 0:1024], psum_q[0:1, 0:1024], eps_mul, eps_add,
                ALU.mult, ALU.add), inc=v_s.tick())
            v_rows = v_s.at()
            Aq.op(lambda e, psum_s=psum_s, psum_q=psum_q: e.activation(
                psum_s[0:1, 0:1024], psum_q[0:1, 0:1024], AF.Ln),
                waits=[v_rows] + guards, inc=a_s.tick())
            Aq.op(lambda e, psum_s=psum_s, seg=seg: e.activation(
                R_row[0:1, seg], psum_s[0:1, 0:1024], AF.Exp, scale=-0.5),
                inc=a_s.tick())
            a_R = a_s.at()
            Vq.op(lambda e, seg=seg: e.scalar_tensor_tensor(
                B_row[0:1, seg], B_row[0:1, seg], -1.0,
                R_row[0:1, seg], ALU.mult, ALU.mult),
                waits=[a_R], inc=v_s.tick())
            b_tok = v_s.at()
            bank_ev = [None, None]
            for row, dstb, gate in [(R_row, Rb_b, a_R), (B_row, Bb_b, b_tok)]:
                for h5 in range(2):
                    cs = th * 1024 + h5 * 512
                    Tq.op(lambda e, row=row, cs=cs, psum_s=psum_s, h5=h5: e.matmul(
                        psum_s[:, h5 * 512:h5 * 512 + 512], onesrow_bf[0:1, :],
                        row[0:1, cs:cs + 512],
                        start=True, stop=True),
                        waits=[gate, bank_ev[h5]] + guards + list(rb_war),
                        inc=t_s.tick())
                    tb = t_s.at()
                    Aq.op(lambda e, dstb=dstb, cs=cs, psum_s=psum_s, h5=h5: e.activation(
                        dstb[:, cs:cs + 512], psum_s[:, h5 * 512:h5 * 512 + 512],
                        AF.Copy), waits=[tb], inc=a_s.tick())
                    bank_ev[h5] = a_s.at()
            out["rb_th%d" % th] = a_s.at()
        out["Rb"] = a_s.at()
        out["t_end"] = t_s.at()
        return out

    prevs = {
        "out_dma": [setup_d, None],
        "slotA_free": [setup_t],
        "slotB_free": [],
        "slotC_free": [],
        "slotA_q_free": [],
        "vbuf_free": [],
        "stream_free": [],
        "ln_guards": [setup_v],
        "xin_free": [],
        "rb_war": [setup_v],
        "xload_early": [],
    }
    asm_guard = [None, None]
    ctxstage_guard = [None]
    x_first_guard = [setup_t, setup_d, setup_v, setup_g]

    for s in range(NSEQ):
        cast_toks = []
        xload = []
        for c in range(NCH):
            slab = c % 4
            stg = xstage if slab < 2 else ostage
            so = (slab % 2) * T
            if c < 2 and prevs.get("xload_early"):
                tok = prevs["xload_early"][c]
            else:
                if not prevs["xin_free"]:
                    w = list(x_first_guard) + (list(prevs["out_dma"])
                                               if c in (2, 3) else [])
                elif c < 2:
                    w = [prevs["xin_free"][c]]
                elif c in (2, 3):
                    w = list(prevs["out_dma"])
                else:
                    w = []
                if c >= 4:
                    w.append(cast_toks[c - 4])
                tok = dma(stg[:, so:so + T],
                          x_d[s, c * 128:(c + 1) * 128, :], d_x[slab], waits=w)
            xload.append(tok)
            saf = prevs["slotA_free"]
            gate = [saf[c]] if len(saf) == NCH else list(saf)
            Aq.op(lambda e, c=c, stg=stg, so=so: e.activation(
                slotA[:, c * T:(c + 1) * T], stg[:, so:so + T],
                AF.Copy),
                waits=[tok] + gate, inc=a_s.tick())
            cast_toks.append(a_s.at())
        ones_w = [cast_toks[-1]] + prevs["vbuf_free"] + prevs["stream_free"] \
            + list(prevs["out_dma"])
        if s == 0:
            ones_w += [setup_d, setup_t]
        Gq.op(lambda e: e.memset(vput[:, :, :, 64:65], 1.0), waits=ones_w,
              inc=g_s.tick())
        Gq.op(lambda e: e.memset(kt[0][0:1, :], 1.0), inc=g_s.tick())
        Gq.op(lambda e: e.memset(kt[1][0:1, :], 1.0), inc=g_s.tick())
        ones_tok = g_s.at()
        ln1 = emit_ln(slotA, lambda c: cast_toks[c], 1.0 + EPS, EPS * EPS,
                      prevs["ln_guards"], rb_war=prevs["rb_war"])
        xn_toks = []
        for c in range(NCH):
            for th in range(2):
                hs = slice(th * 1024, th * 1024 + 1024)
                Vq.op(lambda e, c=c, th=th, hs=hs: e.tensor_tensor(
                    tmp_b[:, (c % 2) * T + th * 1024:(c % 2) * T + th * 1024 + 1024],
                    slotA[:, c * T + th * 1024:c * T + th * 1024 + 1024],
                    Rb_b[:, hs], ALU.mult),
                    waits=[ln1["rb_th%d" % th], cast_toks[c]], inc=v_s.tick())
                Vq.op(lambda e, c=c, th=th, hs=hs: e.tensor_tensor(
                    slotC[:, c * T + th * 1024:c * T + th * 1024 + 1024],
                    tmp_b[:, (c % 2) * T + th * 1024:(c % 2) * T + th * 1024 + 1024],
                    Bb_b[:, hs], ALU.add),
                    waits=prevs["slotC_free"], inc=v_s.tick())
            xn_toks.append(v_s.at())
        xn_done = v_s.at()

        qk_evac = {}
        last_st_read = [None, None, None, None]
        first_stream = True
        for th in range(2):
            for mc in range(16):
                g = th * 16 + mc
                sbh = g % 4
                w = [setup_d, last_st_read[sbh]]
                if first_stream:
                    w += prevs["stream_free"]
                    first_stream = False
                wtok = dma(wqk_st[:, sbh * 1024:sbh * 1024 + 1024]
                           .rearrange("p (kc f) -> p kc f", kc=8),
                           wqs_d[mc, :, :, :].rearrange("kc p f -> p kc f"),
                           d_w[sbh], waits=w)
                pt = pAll[g % 4]
                wfirst = [wtok, ln1["t_end"]]
                if g - 4 >= 0:
                    wfirst.append(qk_evac[g - 4])
                for kc in range(NCH):
                    for h5 in range(2):
                        wkc = [xn_toks[kc]] if h5 == 0 else []
                        Tq.op(lambda e, mc=mc, kc=kc, th=th, h5=h5, pt=pt, sbh=sbh: e.matmul(
                            pt[:, h5 * 512:h5 * 512 + 512],
                            wqk_st[:, sbh * 1024 + kc * 128:sbh * 1024 + kc * 128 + 128],
                            slotC[:, kc * T + th * 1024 + h5 * 512:
                                  kc * T + th * 1024 + h5 * 512 + 512],
                            start=(kc == 0), stop=(kc == NCH - 1)),
                            waits=(wfirst + wkc if (kc == 0 and h5 == 0) else wkc))
                Tq.retro_inc(t_s.tick())
                t_grp = t_s.at()
                last_st_read[sbh] = t_grp
                dst = slotA if mc < 8 else slotB
                dmc = mc if mc < 8 else mc - 8
                wev = [t_grp]
                wev += prevs["slotA_q_free"] if mc < 8 else prevs["slotB_free"]
                Aq.op(lambda e, dst=dst, dmc=dmc, th=th, pt=pt, mc=mc: e.activation(
                    dst[:, dmc * T + th * 1024:dmc * T + th * 1024 + 1024], pt[:],
                    AF.Identity, bias=c2_col[:, mc:mc + 1]),
                    waits=wev, inc=a_s.tick())
                qk_evac[g] = a_s.at()
        qk_done = a_s.at()
        t_qk = t_s.at()


        asm_tok = {}

        def emit_asm(h, extra):
            wk = [qk_done, asm_guard[h % 2], extra]
            dma(kt[h % 2][1:65, :],
                slotB[(h % 2) * 64:(h % 2) * 64 + 64, (h // 2) * T:(h // 2) * T + T],
                d_kq[h % 2], waits=wk)
            dma(qt[h % 2][1:65, :],
                slotA[(h % 2) * 64:(h % 2) * 64 + 64, (h // 2) * T:(h // 2) * T + T],
                d_kq[h % 2], waits=[])
            asm_tok[h] = d_kq[h % 2].at()

        emit_asm(0, None)
        emit_asm(1, None)

        st = {"sprobe_b": [None, None], "probe_par": 0, "mrow": None,
              "aT": [None, None], "pT": [None, None], "ctx": None,
              "qrow": {}, "t_av": None}

        def gen_A(h, half):
            q0 = half * 1024
            mcol_toks = []
            for idx, i in enumerate(range(8 * half, 8 * half + 8)):
                row_w = (i + 1) * 128
                nchunks = 0
                for ck0 in range(0, row_w, 512):
                    ck1 = min(ck0 + 512, row_w)
                    pb_ = st["probe_par"]
                    st["probe_par"] ^= 1
                    w = [qk_done, st["sprobe_b"][pb_], ln1["Rb"]]
                    if pb_ == 1:
                        w.append(st["mrow"])
                    Tq.op(lambda e, h=h, i=i, ck0=ck0, ck1=ck1, pb_=pb_: e.matmul(
                        pA[:, pb_ * 512:pb_ * 512 + ck1 - ck0],
                        slotA[(h % 2) * 64:(h % 2) * 64 + 64,
                              (h // 2) * T + i * 128:(h // 2) * T + i * 128 + 128],
                        slotB[(h % 2) * 64:(h % 2) * 64 + 64,
                              (h // 2) * T + ck0:(h // 2) * T + ck1],
                        start=True, stop=True),
                        waits=w, inc=t_s.tick())
                    tmm = t_s.at()
                    Vq.op(lambda e, nch=nchunks, w=ck1 - ck0, pb_=pb_: e.reduce_max(
                        mtmp[:, nch:nch + 1], pA[:, pb_ * 512:pb_ * 512 + w], axis=AX.X),
                        waits=[tmm], inc=v_s.tick())
                    st["sprobe_b"][pb_] = v_s.at()
                    nchunks += 1
                Vq.op(lambda e, idx=idx, nch=nchunks: e.reduce_max(
                    mneg[:, idx:idx + 1], mtmp[:, 0:nch], axis=AX.X, negate=True),
                    inc=v_s.tick())
                mcol_toks.append(v_s.at())
                yield
            for qtr in range(2):
                for r in range(4):
                    w = [mcol_toks[qtr * 4 + r]]
                    if r == 0:
                        w += [st["mrow"], st["sprobe_b"][1]]
                    Tq.op(lambda e, qtr=qtr, r=r: e.matmul(
                        pA[0:1, 512 + r * 128:512 + r * 128 + 128],
                        mneg[:, qtr * 4 + r:qtr * 4 + r + 1], ident_bf[:],
                        start=True, stop=True),
                        waits=w, inc=(t_s.tick() if r == 3 else None))
                t_mrow = t_s.at()
                st["sprobe_b"][1] = None
                Aq.op(lambda e, h=h, q0=q0, qtr=qtr: e.activation(
                    qt[h % 2][0:1, q0 + qtr * 512:q0 + qtr * 512 + 512],
                    pA[0:1, 512:1024], AF.Copy),
                    waits=[t_mrow, asm_tok[h]], inc=a_s.tick())
                st["mrow"] = a_s.at()
                yield
            st["qrow"][(h, half)] = a_s.at()

        def gen_B(h, half):
            q0 = half * 1024
            qrow_done = st["qrow"][(h, half)]
            strips = list(range(0, 8 * (half + 1)))
            first_j, last_j = strips[0], strips[-1]
            pend = None

            def emit_av(pend):
                cks, bi, a_exp, diag_tok, j = pend
                wv = [a_exp, diag_tok, ones_tok]
                if j == first_j:
                    wv.append(st["ctx"])
                for ci, (c0, c1) in enumerate(cks):
                    lastj_chunk = min(last_j, (q0 + c1 - 1) // 128)
                    Tq.op(lambda e, h=h, j=j, c0=c0, c1=c1, bi=bi,
                          sp=(j == lastj_chunk): e.matmul(
                        pD[0:65, c0:c1],
                        vbuf[:, j * NH * 65 + h * 65:j * NH * 65 + h * 65 + 65],
                        aT[bi][:, c0:c1],
                        start=(j == first_j), stop=sp),
                        waits=(wv if ci == 0 else []),
                        inc=(t_s.tick() if ci == len(cks) - 1 else None))
                st["aT"][bi] = t_s.at()

            for j in strips:
                rel0 = max(j * 128 - q0, 0)
                bi = j % 2
                buf = (pB, pC)[bi]
                cks = []
                c0 = rel0
                while c0 < 1024:
                    c1 = min((c0 // 512 + 1) * 512, 1024)
                    cks.append((c0, c1))
                    c0 = c1
                wv = [qrow_done, asm_tok[h], st["pT"][bi], ones_tok]
                for ci, (c0, c1) in enumerate(cks):
                    Tq.op(lambda e, h=h, j=j, c0=c0, c1=c1, buf=buf, q0=q0: e.matmul(
                        buf[:, c0:c1],
                        kt[h % 2][0:65, j * 128:j * 128 + 128],
                        qt[h % 2][0:65, q0 + c0:q0 + c1],
                        start=True, stop=True),
                        waits=(wv if ci == 0 else []),
                        inc=(t_s.tick() if ci == len(cks) - 1 else None))
                t_st = t_s.at()
                Aq.op(lambda e, bi=bi, rel0=rel0, buf=buf: e.activation(
                    aT[bi][:, rel0:1024], buf[:, rel0:1024], AF.Exp, scale=0.125),
                    waits=[t_st, st["aT"][bi]], inc=a_s.tick())
                st["pT"][bi] = a_s.at()
                a_exp = a_s.at()
                diag_tok = None
                if j >= 8 * half:
                    Gq.op(lambda e, bi=bi, rel0=rel0: e.tensor_tensor(
                        aT[bi][:, rel0:rel0 + 128], aT[bi][:, rel0:rel0 + 128],
                        maskt_sb[:], ALU.mult),
                        waits=[a_exp], inc=g_s.tick())
                    diag_tok = g_s.at()
                if pend is not None:
                    emit_av(pend)
                pend = (cks, bi, a_exp, diag_tok, j)
                yield
            emit_av(pend)
            t_av = t_s.at()
            st["t_av"] = t_av

            Aq.op(lambda e: e.activation(rball[64:65, :], pD[64:65, :], AF.Ln),
                  waits=[t_av], inc=a_s.tick())
            Aq.op(lambda e: e.activation(rrow_bf[64:65, :], rball[64:65, :],
                                         AF.Exp, scale=-1.0), inc=a_s.tick())
            a_r = a_s.at()
            for h5 in range(2):
                Tq.op(lambda e, h5=h5: e.matmul(
                    pB[0:64, h5 * 512:h5 * 512 + 512], ones64[64:65, :],
                    rrow_bf[64:65, h5 * 512:h5 * 512 + 512], start=True, stop=True),
                    waits=([a_r, st["pT"][0]] if h5 == 0 else []),
                    inc=(t_s.tick() if h5 == 1 else None))
            t_rb = t_s.at()
            Aq.op(lambda e: e.activation(rball[0:64, :], pB[0:64, :], AF.Copy),
                  waits=[t_rb], inc=a_s.tick())
            v_rb = a_s.at()
            st["pT"][0] = v_rb
            if h % 2 == 0:
                Vq.op(lambda e, h=h, q0=q0: e.tensor_tensor(
                    slotC[0:64, (h // 2) * T + q0:(h // 2) * T + q0 + 1024],
                    pD[0:64, :], rball[0:64, :], ALU.mult),
                    waits=[v_rb], inc=v_s.tick())
                st["ctx"] = v_s.at()
            else:
                Vq.op(lambda e: e.tensor_tensor(
                    ctxstage[:], pD[0:64, :], rball[0:64, :], ALU.mult),
                    waits=[v_rb, ctxstage_guard[0]], inc=v_s.tick())
                st["ctx"] = v_s.at()
                tok = dma(slotC[64:128, (h // 2) * T + q0:(h // 2) * T + q0 + 1024],
                          ctxstage[:], d_cs, waits=[st["ctx"]])
                ctxstage_guard[0] = tok
            if half == 1:
                asm_guard[h % 2] = t_av
                if h + 2 < NH:
                    emit_asm(h + 2, t_av)
            yield

        pairs = [(h, hf) for h in range(NH) for hf in range(2)]
        gA0 = gen_A(*pairs[0])
        gA0_done = [False]

        def pump():
            if not gA0_done[0]:
                try:
                    next(gA0)
                except StopIteration:
                    gA0_done[0] = True
        v_ev = {}
        last_v_read = None
        for vh in range(2):
            w = [setup_d, t_qk if vh == 0 else last_v_read] + list(prevs["out_dma"])
            wtok = dma(wv_st[:].rearrange("p (km f) -> p km f", km=32),
                       wvs_d[vh, :, :, :, :].rearrange("kc mc p f -> p (kc mc) f"),
                       d_wv, waits=w)
            for tt in range(NT):
                gidx = vh * NT + tt
                pt = (pB, pC, pD)[gidx % 3]
                wfirst = [wtok, xn_done]
                if gidx - 3 >= 0:
                    wfirst.append(v_ev[gidx - 3])
                else:
                    wfirst.append(qk_done)
                for kc in range(NCH):
                    Tq.op(lambda e, tt=tt, kc=kc, pt=pt: e.matmul(
                        pt[:, 0:512],
                        slotC[:, kc * T + tt * 128:kc * T + tt * 128 + 128],
                        wv_st[:, kc * 512:kc * 512 + 512],
                        start=(kc == 0), stop=(kc == NCH - 1)),
                        waits=(wfirst if kc == 0 else []))
                Tq.retro_inc(t_s.tick())
                t_grp = t_s.at()
                last_v_read = t_grp
                dstv = vbuf[:, tt * NH * 65:(tt + 1) * NH * 65] \
                    .rearrange("p (h f) -> p h f", h=NH)[:, vh * 8:vh * 8 + 8, 0:64]
                srcv = pt[:, 0:512].rearrange("p (h f) -> p h f", h=8)
                Aq.op(lambda e, dstv=dstv, srcv=srcv: e.activation(dstv, srcv, AF.Copy),
                      waits=[t_grp, setup_g] + prevs["vbuf_free"] + list(prevs["out_dma"]),
                      inc=a_s.tick())
                v_ev[gidx] = a_s.at()
                pump()
                pump()
        v_done = a_s.at()
        t_v = t_s.at()
        st["pT"] = [v_done, v_done]
        st["ctx"] = v_done
        st["aT"] = [st["aT"][0], st["aT"][1]]
        for _ in gA0:
            pass
        gAn = None
        for k, pr in enumerate(pairs):
            gB = gen_B(*pr)
            gAn = gen_A(*pairs[k + 1]) if k + 1 < len(pairs) else None
            nstrips = 8 * (pr[1] + 1)
            if nstrips >= 10:
                sched = [0] * (nstrips - 10) + [1] * 10
            else:
                sched = [1] * nstrips
                for i in range(10 - nstrips):
                    sched[-1 - (i % 2)] += 1
            yi = 0
            for _ in gB:
                nsteps = sched[yi] if yi < len(sched) else 1
                yi += 1
                for _ in range(nsteps):
                    if gAn is not None:
                        try:
                            next(gAn)
                        except StopIteration:
                            gAn = None
            if gAn is not None:
                for _ in gAn:
                    pass
                gAn = None
        t_av_last = st["t_av"]
        v_p3 = v_s.at()
        a_p3 = a_s.at()
        d_p3 = [d_kq[0].at(), d_kq[1].at(), d_cs.at()]
        t_p3 = t_s.at()

        ao_evac = {}
        last_wo_read = [None, None]
        for mc in range(8):
            sbh = mc % 2
            w = [setup_d, last_wo_read[sbh]]
            if mc == 0:
                w += [t_p3]
            wtok = dma(wo_st[:, sbh * 1024:sbh * 1024 + 1024]
                       .rearrange("p (kc f) -> p kc f", kc=8),
                       wos_d[mc, :, :, :].rearrange("kc p f -> p kc f"),
                       d_w[sbh], waits=w)
            pts = (pA, pB) if mc % 2 == 0 else (pC, pD)
            wfirst = [wtok, v_p3, a_p3, d_p3]
            if mc >= 2:
                wfirst.append(ao_evac[mc - 2])
            for kc in range(NCH):
                for sl in range(4):
                    pt = pts[sl // 2]
                    Tq.op(lambda e, mc=mc, kc=kc, sl=sl, pt=pt, sbh=sbh: e.matmul(
                        pt[:, (sl % 2) * 512:(sl % 2) * 512 + 512],
                        wo_st[:, sbh * 1024 + kc * 128:sbh * 1024 + kc * 128 + 128],
                        slotC[:, kc * T + sl * 512:kc * T + sl * 512 + 512],
                        start=(kc == 0), stop=(kc == NCH - 1)),
                        waits=(wfirst if (kc == 0 and sl == 0) else []))
            Tq.retro_inc(t_s.tick())
            t_grp = t_s.at()
            last_wo_read[sbh] = t_grp
            for halfp in range(2):
                Aq.op(lambda e, mc=mc, halfp=halfp, pts=pts: e.activation(
                    slotA[:, mc * T + halfp * 1024:mc * T + halfp * 1024 + 1024],
                    pts[halfp][:], AF.Identity, bias=c3_col[:, mc:mc + 1]),
                    waits=[t_grp, d_p3], inc=a_s.tick())
            ao_evac[mc] = a_s.at()
        ao_done = a_s.at()
        t_wo = t_s.at()

        ln3 = emit_ln(slotA, lambda c: ao_evac[c], 1.0, EPS, [ao_done, v_p3],
                      rb_war=[xn_done])
        x2_toks = []
        for c in range(NCH):
            slab = c % 4
            stg = xstage if slab < 2 else ostage
            so = (slab % 2) * T
            w = [cast_toks[-1], t_av_last] + ([x2_toks[c - 4]] if c >= 4 else [])
            xtok = dma(stg[:, so:so + T],
                       x_d[s, c * 128:(c + 1) * 128, :], d_x[slab], waits=w)
            for th in range(2):
                to = (c % 2) * T + th * 1024
                Vq.op(lambda e, c=c, th=th, to=to: e.tensor_tensor(
                    tmp_b[:, to:to + 1024],
                    slotA[:, c * T + th * 1024:c * T + th * 1024 + 1024],
                    Rb_b[:, th * 1024:th * 1024 + 1024], ALU.mult),
                    waits=[ln3["rb_th%d" % th]] + ([x2_toks[c - 2]] if c >= 2 else []),
                    inc=v_s.tick())
                Vq.op(lambda e, to=to, th=th: e.tensor_tensor(
                    tmp_b[:, to:to + 1024], tmp_b[:, to:to + 1024],
                    Bb_b[:, th * 1024:th * 1024 + 1024], ALU.add),
                    inc=v_s.tick())
                Vq.op(lambda e, c=c, th=th, to=to, stg=stg, so=so: e.tensor_tensor(
                    slotB[:, c * T + th * 1024:c * T + th * 1024 + 1024],
                    tmp_b[:, to:to + 1024],
                    stg[:, so + th * 1024:so + th * 1024 + 1024], ALU.add),
                    waits=[xtok, t_wo, d_p3], inc=v_s.tick())
            x2_toks.append(v_s.at())
        x2_done = v_s.at()

        ln4 = emit_ln(slotB, lambda c: x2_toks[c], 1.0, EPS, [x2_done, ao_done],
                      rb_war=[x2_done])
        xn2_toks = []
        for c in range(NCH):
            for th in range(2):
                to = (c % 2) * T + th * 1024
                Vq.op(lambda e, c=c, th=th, to=to: e.tensor_tensor(
                    tmp_b[:, to:to + 1024],
                    slotB[:, c * T + th * 1024:c * T + th * 1024 + 1024],
                    Rb_b[:, th * 1024:th * 1024 + 1024], ALU.mult),
                    waits=[ln4["rb_th%d" % th]], inc=v_s.tick())
                Vq.op(lambda e, c=c, th=th, to=to: e.tensor_tensor(
                    slotA[:, c * T + th * 1024:c * T + th * 1024 + 1024],
                    tmp_b[:, to:to + 1024],
                    Bb_b[:, th * 1024:th * 1024 + 1024], ALU.add),
                    inc=v_s.tick())
            xn2_toks.append(v_s.at())
        xn2_done = v_s.at()

        s_ev = []
        for sl in range(4):
            pt = (pA, pA, pB, pB)[sl]
            off = (sl % 2) * 512
            for kc in range(NCH):
                wkc = [xn2_toks[kc]] + ([ln4["t_end"]] if kc == 0 else [])
                Tq.op(lambda e, sl=sl, kc=kc, pt=pt, off=off: e.matmul(
                    pt[0:64, off:off + 512],
                    win_sb[:, kc * 64:kc * 64 + 64],
                    slotA[:, kc * T + sl * 512:kc * T + sl * 512 + 512],
                    start=(kc == 0), stop=(kc == NCH - 1)),
                    waits=wkc)
            Tq.retro_inc(t_s.tick())
            t_grp = t_s.at()
            if sl % 2 == 1:
                pth = (pA, pB)[sl // 2]
                Aq.op(lambda e, sl=sl, pth=pth: e.activation(
                    s_sb[:, (sl - 1) * 512:(sl + 1) * 512], pth[0:64, :],
                    AF.Identity, bias=bin_sb[:]),
                    waits=[t_grp, d_p3], inc=a_s.tick())
                s_ev.append(a_s.at())
        s_done = a_s.at()
        g_q = []
        ts_toks = {}
        for q4 in range(4):
            qs = slice(q4 * 512, q4 * 512 + 512)
            for i in range(16):
                tfa = tfn[0:64, (i % 2) * 512:(i % 2) * 512 + 512]
                if i < 2:
                    wa = [s_ev[q4 // 2]]
                    if q4 > 0:
                        wa.append(ts_toks[(q4 - 1, 14 + i)])
                else:
                    wa = [ts_toks[(q4, i - 2)]]
                Aq.op(lambda e, i=i, qs=qs, tfa=tfa: e.activation(
                    tfa, s_sb[0:64, qs], AF.Identity,
                    scale=w1_sb[:, i:i + 1], bias=b1_sb[:, i:i + 1]),
                    waits=wa, inc=a_s.tick())
                a_tf = a_s.at()
                dst = g_sb if i == 0 else u_fn
                Vq.op(lambda e, dst=dst, i=i, qs=qs, tfa=tfa: e.tensor_scalar(
                    dst[0:64, qs], tfa, 0.0, w2_sb[:, i:i + 1],
                    ALU.max, ALU.mult), waits=[a_tf], inc=v_s.tick())
                ts_toks[(q4, i)] = v_s.at()
                if i > 0:
                    Vq.op(lambda e, qs=qs: e.tensor_tensor(
                        g_sb[0:64, qs], g_sb[0:64, qs], u_fn[0:64, qs], ALU.add),
                        inc=v_s.tick())
            Vq.op(lambda e, qs=qs: e.tensor_scalar(
                gb_sb[0:64, qs], g_sb[0:64, qs], b2_sb[:], None, ALU.add),
                inc=v_s.tick())
            g_q.append(v_s.at())
        g_done = v_s.at()
        fo_ev = {}
        for mc in range(8):
            pts = (pA, pB) if mc % 2 == 0 else (pC, pD)
            for sl in range(4):
                pt = pts[sl // 2]
                w = [g_q[sl]]
                if sl == 0 and mc >= 2:
                    w.append(fo_ev[mc - 2])
                Tq.op(lambda e, mc=mc, sl=sl, pt=pt: e.matmul(
                    pt[:, (sl % 2) * 512:(sl % 2) * 512 + 512],
                    wout_sb[0:64, mc * 128:mc * 128 + 128],
                    gb_sb[:, sl * 512:sl * 512 + 512],
                    start=True, stop=True),
                    waits=w)
            Tq.retro_inc(t_s.tick())
            t_grp = t_s.at()
            for halfp in range(2):
                Aq.op(lambda e, mc=mc, halfp=halfp, pts=pts: e.activation(
                    slotA[:, mc * T + halfp * 1024:mc * T + halfp * 1024 + 1024],
                    pts[halfp][:], AF.Identity, bias=bout_col[:, mc:mc + 1]),
                    waits=[t_grp, xn2_done], inc=a_s.tick())
            fo_ev[mc] = a_s.at()
        fo_done = a_s.at()

        ln5 = emit_ln(slotA, lambda c: fo_ev[c], 1.0, EPS, [fo_done, x2_done],
                      rb_war=[xn2_done])
        xload_early = []
        if s + 1 < NSEQ:
            for ce in range(2):
                xload_early.append(dma(
                    xstage[:, ce * T:ce * T + T],
                    x_d[s + 1, ce * 128:(ce + 1) * 128, :], d_x[ce],
                    waits=[x2_toks[4 + ce]]))
        out_toks = []
        out_g = []
        ln5_tt1 = []
        for c in range(NCH):
            for th in range(2):
                to = (c % 2) * T + th * 1024
                Vq.op(lambda e, c=c, th=th, to=to: e.tensor_tensor(
                    tmp_b[:, to:to + 1024],
                    slotA[:, c * T + th * 1024:c * T + th * 1024 + 1024],
                    Rb_b[:, th * 1024:th * 1024 + 1024], ALU.mult),
                    waits=[ln5["rb_th%d" % th]] + ([out_g[c - 2]] if c >= 2 else []),
                    inc=v_s.tick())
                Vq.op(lambda e, c=c, th=th, to=to: e.tensor_tensor(
                    tmp_b[:, to:to + 1024], tmp_b[:, to:to + 1024],
                    Bb_b[:, th * 1024:th * 1024 + 1024], ALU.add),
                    inc=v_s.tick())
                w = [out_toks[c - 2]] if c >= 2 else [x2_toks[6 + c]]
                Vq.op(lambda e, c=c, th=th, to=to: e.tensor_tensor(
                    ostage[:, to:to + 1024],
                    tmp_b[:, to:to + 1024],
                    slotB[:, c * T + th * 1024:c * T + th * 1024 + 1024], ALU.add),
                    waits=w, inc=v_s.tick())
            ln5_tt1.append(v_s.at())
            v_out = v_s.at()
            out_g.append(v_out)
            tok = dma(y_d[s, c * 128:(c + 1) * 128, :],
                      ostage[:, (c % 2) * T:(c % 2) * T + T], d_y[c % 4], waits=[v_out])
            out_toks.append(tok)
        v_p6 = v_s.at()
        t_ln5 = ln5["t_end"]

        g_p6 = g_s.at()
        prevs = {
            "out_dma": out_toks[-4:],
            "slotA_free": ln5_tt1,
            "slotB_free": [v_p6, g_p6],
            "slotC_free": [t_wo],
            "slotA_q_free": [t_p3, d_p3],
            "vbuf_free": [t_av_last],
            "stream_free": [v_p6, fo_done],
            "ln_guards": [ln5["Rb"], fo_done],
            "xin_free": [x2_toks[4], x2_toks[5]],
            "rb_war": [v_p6],
            "xload_early": xload_early,
        }

    nc._dbg = dict(slotA=slotA, slotB=slotB, slotC=slotC, vbuf=vbuf, kt0=kt[0],
                   kt1=kt[1], qt0=qt[0], qt1=qt[1], c2_col=c2_col, c3_col=c3_col,
                   Rb_b=Rb_b, Bb_b=Bb_b, R_row=R_row, B_row=B_row, s_sb=s_sb,
                   g_sb=g_sb, gb_sb=gb_sb, rball=rball, mneg=mneg, aT0=aT[0],
                   aT1=aT[1], win_sb=win_sb, wout_sb=wout_sb, tfn=tfn, u_fn=u_fn, g=g_sb, w1=w1_sb, b1=b1_sb, w2=w2_sb)

    Sq.op(lambda e: None, waits=[d.at() for d in d_y])

    with nc.Block() as block:
        @block.tensor
        def _(e):
            run_ops(e, Tq.ops)

        @block.vector
        def _(e):
            run_ops(e, Vq.ops)

        @block.scalar
        def _(e):
            run_ops(e, Aq.ops)

        @block.gpsimd
        def _(e):
            run_ops(e, Gq.ops)

        @block.sync
        def _(e):
            run_ops(e, Sq.ops)

    return nc


def kernel(L0_attn_x, L0_attn_ln_gamma, L0_attn_ln_beta, L0_attn_W_qkv, L0_attn_W_o,
           L0_ffn_Win, L0_ffn_bin_, L0_ffn_W1, L0_ffn_b1, L0_ffn_W2, L0_ffn_b2,
           L0_ffn_Wout, L0_ffn_bout):
    x = np.asarray(L0_attn_x, dtype=np.float32)
    gamma = np.asarray(L0_attn_ln_gamma, dtype=np.float32)
    beta = np.asarray(L0_attn_ln_beta, dtype=np.float32)
    wqkv = np.asarray(L0_attn_W_qkv, dtype=np.float32)
    wo = np.asarray(L0_attn_W_o, dtype=np.float32)
    win = np.asarray(L0_ffn_Win, dtype=np.float32)
    bin_ = np.asarray(L0_ffn_bin_, dtype=np.float32)
    w1 = np.asarray(L0_ffn_W1, dtype=np.float32)
    b1 = np.asarray(L0_ffn_b1, dtype=np.float32)
    w2 = np.asarray(L0_ffn_W2, dtype=np.float32)
    b2 = np.asarray(L0_ffn_b2, dtype=np.float32)
    wout = np.asarray(L0_ffn_Wout, dtype=np.float32)
    bout = np.asarray(L0_ffn_bout, dtype=np.float32)

    idx = []
    for blk in range(3):
        for h in range(NH):
            base = h * 192 + blk * 64
            idx.extend(range(base, base + 64))
    wqkvt = np.ascontiguousarray(wqkv[np.array(idx)].T)
    wot = np.ascontiguousarray(wo.T)
    gcol = np.ascontiguousarray(gamma.reshape(8, 128).T)
    bcol = np.ascontiguousarray(beta.reshape(8, 128).T)
    boutcol = np.ascontiguousarray(bout.reshape(8, 128).T)
    bincol = np.ascontiguousarray(bin_.reshape(64, 1))
    b2col = np.ascontiguousarray(b2.reshape(64, 1))
    ident = np.eye(128, dtype=np.float32)
    kq = np.arange(128)
    maskt = np.ascontiguousarray((kq[:, None] <= kq[None, :]).astype(np.float32))
    maskn = np.ascontiguousarray(
        np.where(kq[None, :] > kq[:, None], np.float32(-1e30), np.float32(0.0)))

    nc = build_program()
    in_maps = []
    for c in range(NCORE):
        xs = np.ascontiguousarray(np.transpose(x[c * NSEQ:(c + 1) * NSEQ], (0, 2, 1)))
        in_maps.append(dict(
            x=xs, wqkvt=wqkvt, wot=wot, win=win, wout=wout,
            gcol=gcol, bcol=bcol, bincol=bincol, w1=w1, b1=b1, w2=w2,
            b2col=b2col, boutcol=boutcol, ident=ident, maskt=maskt, maskn=maskn,
        ))
    res = run_bass_kernel_spmd(nc, in_maps, core_ids=list(range(NCORE)))
    outs = []
    for c in range(NCORE):
        yc = res.results[c]["y"]
        outs.append(np.transpose(yc, (0, 2, 1)))
    return np.ascontiguousarray(np.concatenate(outs, axis=0))


# version 35
# speedup vs baseline: 1.1496x; 1.0003x over previous
import numpy as np
import ml_dtypes
import concourse.bass as bass
import concourse.mybir as mybir
from concourse.bass_utils import run_bass_kernel_spmd

F32 = mybir.dt.float32
BF16 = mybir.dt.bfloat16
AF = mybir.ActivationFunctionType
ALU = mybir.AluOpType
AX = mybir.AxisListType

NCORE = 8
NSEQ = 4
T = 2048
D = 1024
NH = 16
DH = 64
NCH = 8
NT = 16
EPS = 1e-5


class Ctr:
    def __init__(self, nc, name):
        self.h = nc.alloc_semaphore(name)
        self.n = 0

    def tick(self, k=1):
        self.n += k
        return (self.h, k)

    def at(self):
        return (self.h, self.n)


class Eng:
    def __init__(self):
        self.ops = []

    def op(self, fn, waits=(), inc=None):
        ws = []
        for w in waits:
            if w is None:
                continue
            if isinstance(w, list):
                ws.extend(t for t in w if t is not None)
            else:
                ws.append(w)
        self.ops.append((tuple(ws), fn, inc))

    def retro_inc(self, inc):
        w, f, old = self.ops[-1]
        assert old is None
        self.ops[-1] = (w, f, inc)


def run_ops(engine, ops):
    waited = {}
    for waits, fn, inc in ops:
        for sem, val in waits:
            if val > 0 and waited.get(id(sem), -1) < val:
                engine.wait_ge(sem, val)
                waited[id(sem)] = val
        ins = fn(engine)
        if inc is not None:
            sem, k = inc
            ins.then_inc(sem, k)


def build_program(nc=None):
    if nc is None:
        nc = bass.Bass("TRN2", target_bir_lowering=False)

    x_d = nc.dram_tensor("x", [NSEQ, D, T], F32, kind="ExternalInput").ap()
    y_d = nc.dram_tensor("y", [NSEQ, D, T], F32, kind="ExternalOutput").ap()
    wqkvt_d = nc.dram_tensor("wqkvt", [D, 3 * D], F32, kind="ExternalInput").ap()
    wot_d = nc.dram_tensor("wot", [D, D], F32, kind="ExternalInput").ap()
    win_d = nc.dram_tensor("win", [D, 64], F32, kind="ExternalInput").ap()
    wout_d = nc.dram_tensor("wout", [64, D], F32, kind="ExternalInput").ap()
    gcol_d = nc.dram_tensor("gcol", [128, 8], F32, kind="ExternalInput").ap()
    bcol_d = nc.dram_tensor("bcol", [128, 8], F32, kind="ExternalInput").ap()
    bincol_d = nc.dram_tensor("bincol", [64, 1], F32, kind="ExternalInput").ap()
    w1_d = nc.dram_tensor("w1", [64, 16], F32, kind="ExternalInput").ap()
    b1_d = nc.dram_tensor("b1", [64, 16], F32, kind="ExternalInput").ap()
    w2_d = nc.dram_tensor("w2", [64, 16], F32, kind="ExternalInput").ap()
    b2col_d = nc.dram_tensor("b2col", [64, 1], F32, kind="ExternalInput").ap()
    boutcol_d = nc.dram_tensor("boutcol", [128, 8], F32, kind="ExternalInput").ap()
    ident_d = nc.dram_tensor("ident", [128, 128], F32, kind="ExternalInput").ap()
    maskt_d = nc.dram_tensor("maskt", [128, 128], F32, kind="ExternalInput").ap()
    maskn_d = nc.dram_tensor("maskn", [128, 128], F32, kind="ExternalInput").ap()
    wqs_d = nc.dram_tensor("wqs", [24, 8, 128, 128], BF16, kind="Internal").ap()
    wos_d = nc.dram_tensor("wos", [8, 8, 128, 128], BF16, kind="Internal").ap()
    wvs_d = nc.dram_tensor("wvs", [2, 8, 4, 128, 128], BF16, kind="Internal").ap()

    BASE = 18464
    def at(name, shape, dt, off):
        return nc.alloc_sbuf_tensor_at(name, shape, dt, offset=BASE + off).ap()

    slotA = at("slotA", [128, NCH * T], BF16, 0)
    slotB = at("slotB", [128, NCH * T], BF16, 32768)
    slotC = at("slotC", [128, NCH * T], BF16, 65536)
    VOFF = 98304
    vbuf = at("vbuf", [128, NT * NH * 65], BF16, VOFF)
    wstage2 = [at("wstage0", [128, 3 * D], F32, VOFF),
               at("wstage1", [128, 3 * D], F32, VOFF + 12288)]
    wbstage = at("wbstage", [128, 3 * D], BF16, VOFF + 24576)
    xstage = at("xstage", [128, 2 * T], F32, VOFF)
    ostage = at("ostage", [128, 2 * T], F32, VOFF + 16384)
    FOFF = 131584
    s_sb = at("s_sb", [64, T], F32, FOFF)
    tfn = at("tfn", [64, T], F32, FOFF + 8192)
    u_fn = at("u_fn", [64, T], F32, FOFF + 16384)
    g_sb = at("g_sb", [64, T], F32, FOFF + 24576)
    gb_sb = at("gb_sb", [64, T], BF16, FOFF + 32768)
    kt = [at("kt0", [65, T], BF16, FOFF), at("kt1", [65, T], BF16, FOFF + 4096)]
    qt = [at("qt0", [65, T], BF16, FOFF + 8192), at("qt1", [65, T], BF16, FOFF + 12288)]
    aT = [at("aT0", [128, 1024], BF16, FOFF + 16384), at("aT1", [128, 1024], BF16, FOFF + 18432)]
    rball = at("rball", [128, 1024], F32, FOFF + 20480)
    ctxstage = at("ctxstage", [64, 1024], BF16, FOFF + 24576)
    wqk_st = at("wqk_st", [128, 4096], BF16, FOFF + 26624)
    wv_st = at("wv_st", [128, 4096], BF16, FOFF + 26624)
    wo_st = at("wo_st", [128, 2048], BF16, FOFF + 26624)
    xbf = at("xbf", [128, 4096], BF16, FOFF + 8192)
    tmpblk = at("tmpblk", [128, 128], F32, 209472)
    mneg = at("mneg", [128, 8], BF16, 209984)
    mtmp = at("mtmp", [128, 8], F32, 210016)
    R_row = at("R_row", [1, 2048], BF16, 168448)
    B_row = at("B_row", [1, 2048], BF16, 172544)
    Rb_b = at("Rb_b", [128, T], BF16, 176640)
    Bb_b = at("Bb_b", [128, T], BF16, 180736)
    sq_b = at("sq_b", [128, 2 * T], BF16, 184832)
    tmp_b = at("tmp_b", [128, 2 * T], BF16, 193024)
    CO = 201216
    win_sb = at("win_sb", [128, 8 * 64], BF16, CO)
    wout_sb = at("wout_sb", [64, D], BF16, CO + 1024)
    ident_sb = at("ident_sb", [128, 128], F32, CO + 3072)
    maskt_f = at("maskt_f", [128, 128], F32, CO + 3584)
    maskn_sb = at("maskn_sb", [128, 128], F32, CO + 4096)
    maskt_sb = at("maskt_sb", [128, 128], BF16, CO + 4608)
    onesrow_bf = at("onesrow_bf", [1, 128], BF16, CO + 4864)
    ones64 = at("ones64", [65, 64], BF16, CO + 5152)
    ident_bf = at("ident_bf", [128, 128], BF16, CO + 5920)
    rrow_bf = at("rrow_bf", [65, 1024], BF16, CO + 6208)
    w1_sb = at("w1_sb", [64, 16], F32, CO + 5408)
    b1_sb = at("b1_sb", [64, 16], F32, CO + 5472)
    w2_sb = at("w2_sb", [64, 16], F32, CO + 5536)
    c2_col = at("c2_col", [128, 24], F32, CO + 5600)
    c3_col = at("c3_col", [128, 8], F32, CO + 5696)
    g_col = at("g_col", [128, 8], F32, CO + 5728)
    be_col = at("be_col", [128, 8], F32, CO + 5760)
    bout_col = at("bout_col", [128, 8], F32, CO + 5792)
    bin_sb = at("bin_sb", [64, 1], F32, CO + 5824)
    b2_sb = at("b2_sb", [64, 1], F32, CO + 5856)
    onescol_bf = at("onescol_bf", [128, 1], BF16, CO + 5888)

    pA = nc.alloc_psum_tensor("pA", [128, 1024], F32).ap()
    pB = nc.alloc_psum_tensor("pB", [128, 1024], F32).ap()
    pC = nc.alloc_psum_tensor("pC", [128, 1024], F32).ap()
    pD = nc.alloc_psum_tensor("pD", [128, 1024], F32).ap()
    pAll = [pA, pB, pC, pD]

    Tq, Vq, Aq, Gq, Sq = Eng(), Eng(), Eng(), Eng(), Eng()
    t_s = Ctr(nc, "t_s")
    v_s = Ctr(nc, "v_s")
    a_s = Ctr(nc, "a_s")
    g_s = Ctr(nc, "g_s")
    d_c = Ctr(nc, "d_c")
    d_ws = Ctr(nc, "d_ws")
    d_x = [Ctr(nc, "d_x0"), Ctr(nc, "d_x1"),
           Ctr(nc, "d_x2"), Ctr(nc, "d_x3")]
    d_y = [Ctr(nc, "d_y0"), Ctr(nc, "d_y1"),
           Ctr(nc, "d_y2"), Ctr(nc, "d_y3")]
    d_w = [Ctr(nc, "d_w0"), Ctr(nc, "d_w1"),
           Ctr(nc, "d_w2"), Ctr(nc, "d_w3")]
    d_wv = Ctr(nc, "d_wv")
    d_kq = [Ctr(nc, "d_kq0"), Ctr(nc, "d_kq1")]
    d_cs = Ctr(nc, "d_cs")
    d_out = Ctr(nc, "d_out")

    def dma(dst, src, ctr, waits=()):
        Sq.op(lambda e, d=dst, s=src: e.dma_start(d, s), waits=waits, inc=ctr.tick(16))
        return ctr.at()

    for dst, src in [
        (w1_sb[:], w1_d), (b1_sb[:], b1_d), (w2_sb[:], w2_d),
        (bin_sb[:], bincol_d), (b2_sb[:], b2col_d), (bout_col[:], boutcol_d),
        (g_col[:], gcol_d), (be_col[:], bcol_d), (ident_sb[:], ident_d),
        (maskt_f[:], maskt_d), (maskn_sb[:], maskn_d),
    ]:
        dma(dst, src, d_c)
    t_consts = d_c.at()

    Gq.op(lambda e: e.memset(onesrow_bf[0:1, :], 1.0), inc=g_s.tick())
    Gq.op(lambda e: e.memset(onescol_bf[:], 1.0), inc=g_s.tick())
    Gq.op(lambda e: e.memset(ones64[64:65, :], 1.0), inc=g_s.tick())
    Gq.op(lambda e: e.tensor_copy(maskt_sb[:], maskt_f[:]), waits=[t_consts], inc=g_s.tick())
    Gq.op(lambda e: e.tensor_copy(ident_bf[:], ident_sb[:]), waits=[t_consts], inc=g_s.tick())
    g_const = g_s.at()

    tok = dma(xstage[:, 0:512], win_d.rearrange("(kc p) m -> p kc m", p=128), d_ws)
    Vq.op(lambda e: e.tensor_copy(win_sb[:], xstage[:, 0:512]), waits=[tok], inc=v_s.tick())
    tok = dma(xstage[0:64, 1024:2048], wout_d, d_ws, waits=[v_s.at()])
    Vq.op(lambda e: e.tensor_copy(wout_sb[:], xstage[0:64, 1024:2048]), waits=[tok], inc=v_s.tick())
    v_wsmall = v_s.at()

    last_store = None
    last_wread_t = {0: None, 1: None}
    last_wread_v = {0: None, 1: None}
    for kc in range(NCH):
        sb_ = kc % 2
        wstage = wstage2[sb_]
        tok = dma(wstage[:], wqkvt_d[kc * 128:(kc + 1) * 128, :], d_ws,
                  waits=[last_wread_t[sb_], last_wread_v[sb_], v_wsmall])
        Vq.op(lambda e, kc=kc, wstage=wstage: e.tensor_scalar(
            wbstage[:], wstage[:], g_col[:, kc:kc + 1], None, ALU.mult),
            waits=[tok, t_consts, last_store], inc=v_s.tick())
        last_wread_v[sb_] = v_s.at()
        for mc in range(24):
            Tq.op(lambda e, kc=kc, mc=mc, wstage=wstage: e.matmul(
                pA[:, mc * 8 + kc:mc * 8 + kc + 1], wstage[:, mc * 128:(mc + 1) * 128],
                be_col[:, kc:kc + 1], start=True, stop=True),
                waits=([tok, t_consts] if mc == 0 else []))
        Tq.retro_inc(t_s.tick())
        last_wread_t[sb_] = t_s.at()
        dma(wqs_d[0:16, kc, :, :].rearrange("mc p f -> p mc f"),
            wbstage[:, 0:2048].rearrange("p (mc f) -> p mc f", mc=16),
            d_out, waits=[last_wread_v[sb_]])
        for vh_ in range(2):
            dma(wvs_d[vh_, kc, :, :, :].rearrange("mc p f -> p mc f"),
                wbstage[:, 2048 + vh_ * 512:2048 + vh_ * 512 + 512]
                .rearrange("p (mc f) -> p mc f", mc=4),
                d_out, waits=[])
        last_store = d_out.at()
    for mc in range(24):
        Vq.op(lambda e, mc=mc: e.reduce_sum(
            c2_col[:, mc:mc + 1], pA[:, mc * 8:mc * 8 + 8], axis=AX.X),
            waits=[last_wread_t[0], last_wread_t[1]], inc=v_s.tick())
    v_c2 = v_s.at()

    for kc in range(NCH):
        sb_ = kc % 2
        wstage = wstage2[sb_]
        tok = dma(wstage[:, 0:1024], wot_d[kc * 128:(kc + 1) * 128, :], d_ws,
                  waits=[last_wread_t[sb_], last_wread_v[sb_]])
        Vq.op(lambda e, wstage=wstage: e.tensor_copy(
            wbstage[:, 0:1024], wstage[:, 0:1024]),
            waits=[tok, v_c2, last_store], inc=v_s.tick())
        last_wread_v[sb_] = v_s.at()
        for mc in range(8):
            Tq.op(lambda e, kc=kc, mc=mc, wstage=wstage: e.matmul(
                pB[:, mc * 8 + kc:mc * 8 + kc + 1], wstage[:, mc * 128:(mc + 1) * 128],
                c2_col[:, 16 + kc:17 + kc], start=True, stop=True),
                waits=([tok, v_c2] if mc == 0 else []))
        Tq.retro_inc(t_s.tick())
        last_wread_t[sb_] = t_s.at()
        last_store = dma(wos_d[:, kc, :, :].rearrange("mc p f -> p mc f"),
                         wbstage[:, 0:1024].rearrange("p (mc f) -> p mc f", mc=8),
                         d_out, waits=[last_wread_v[sb_]])
    for mc in range(8):
        Vq.op(lambda e, mc=mc: e.reduce_sum(
            c3_col[:, mc:mc + 1], pB[:, mc * 8:mc * 8 + 8], axis=AX.X),
            waits=[last_wread_t[0], last_wread_t[1]], inc=v_s.tick())
    vput = vbuf[:].rearrange("p (t h f) -> p t h f", t=NT, h=NH, f=65)
    setup_v = v_s.at()
    setup_t = max(last_wread_t[0], last_wread_t[1], key=lambda t: t[1])
    setup_d = d_out.at()
    setup_g = g_s.at()

    def emit_ln(src_sb, src_tok_fn, eps_mul, eps_add, guards, rb_war=()):
        sq_read_t = {}
        sqtoks = {}
        for c in range(NCH):
            w = [src_tok_fn(c)] + guards
            if c - 2 in sq_read_t:
                w.append(sq_read_t[c - 2])
            Vq.op(lambda e, c=c: e.tensor_tensor(
                sq_b[:, (c % 2) * T:(c % 2) * T + T],
                src_sb[:, c * T:(c + 1) * T], src_sb[:, c * T:(c + 1) * T], ALU.mult),
                waits=w, inc=v_s.tick())
            sqtoks[c] = v_s.at()
            for th in range(2):
                psum_s = (pA, pB)[th]
                psum_q = (pC, pD)[th]
                for h5 in range(2):
                    cs = th * 1024 + h5 * 512
                    w = [src_tok_fn(c)] + (guards if c == 0 else [])
                    Tq.op(lambda e, c=c, h5=h5, cs=cs, psum_s=psum_s: e.matmul(
                        psum_s[0:1, h5 * 512:h5 * 512 + 512], onescol_bf[:],
                        src_sb[:, c * T + cs:c * T + cs + 512],
                        start=(c == 0), stop=(c == NCH - 1)), waits=w)
                for h5 in range(2):
                    cs = th * 1024 + h5 * 512
                    Tq.op(lambda e, c=c, h5=h5, cs=cs, psum_q=psum_q: e.matmul(
                        psum_q[0:1, h5 * 512:h5 * 512 + 512], onescol_bf[:],
                        sq_b[:, (c % 2) * T + cs:(c % 2) * T + cs + 512],
                        start=(c == 0), stop=(c == NCH - 1)), waits=[sqtoks[c]])
            Tq.retro_inc(t_s.tick())
            sq_read_t[c] = t_s.at()
        t_stats = t_s.at()
        out = {}
        for th in range(2):
            psum_s = (pA, pB)[th]
            psum_q = (pC, pD)[th]
            seg = slice(th * 1024, th * 1024 + 1024)
            Vq.op(lambda e, psum_s=psum_s, seg=seg: e.tensor_scalar(
                B_row[0:1, seg], psum_s[0:1, 0:1024], 1.0 / D, None, ALU.mult),
                waits=[t_stats] + guards, inc=v_s.tick())
            Vq.op(lambda e, seg=seg: e.tensor_tensor(
                R_row[0:1, seg], B_row[0:1, seg], B_row[0:1, seg],
                ALU.mult), inc=v_s.tick())
            Vq.op(lambda e, psum_q=psum_q, seg=seg: e.scalar_tensor_tensor(
                psum_q[0:1, 0:1024], psum_q[0:1, 0:1024], 1.0 / D,
                R_row[0:1, seg], ALU.mult, ALU.subtract), inc=v_s.tick())
            Vq.op(lambda e, psum_q=psum_q: e.tensor_scalar(
                psum_q[0:1, 0:1024], psum_q[0:1, 0:1024], eps_mul, eps_add,
                ALU.mult, ALU.add), inc=v_s.tick())
            v_rows = v_s.at()
            Aq.op(lambda e, psum_s=psum_s, psum_q=psum_q: e.activation(
                psum_s[0:1, 0:1024], psum_q[0:1, 0:1024], AF.Ln),
                waits=[v_rows] + guards, inc=a_s.tick())
            Aq.op(lambda e, psum_s=psum_s, seg=seg: e.activation(
                R_row[0:1, seg], psum_s[0:1, 0:1024], AF.Exp, scale=-0.5),
                inc=a_s.tick())
            a_R = a_s.at()
            Vq.op(lambda e, seg=seg: e.scalar_tensor_tensor(
                B_row[0:1, seg], B_row[0:1, seg], -1.0,
                R_row[0:1, seg], ALU.mult, ALU.mult),
                waits=[a_R], inc=v_s.tick())
            b_tok = v_s.at()
            bank_ev = [None, None]
            for row, dstb, gate in [(R_row, Rb_b, a_R), (B_row, Bb_b, b_tok)]:
                for h5 in range(2):
                    cs = th * 1024 + h5 * 512
                    Tq.op(lambda e, row=row, cs=cs, psum_s=psum_s, h5=h5: e.matmul(
                        psum_s[:, h5 * 512:h5 * 512 + 512], onesrow_bf[0:1, :],
                        row[0:1, cs:cs + 512],
                        start=True, stop=True),
                        waits=[gate, bank_ev[h5]] + guards + list(rb_war),
                        inc=t_s.tick())
                    tb = t_s.at()
                    Aq.op(lambda e, dstb=dstb, cs=cs, psum_s=psum_s, h5=h5: e.activation(
                        dstb[:, cs:cs + 512], psum_s[:, h5 * 512:h5 * 512 + 512],
                        AF.Copy), waits=[tb], inc=a_s.tick())
                    bank_ev[h5] = a_s.at()
            out["rb_th%d" % th] = a_s.at()
        out["Rb"] = a_s.at()
        out["t_end"] = t_s.at()
        return out

    prevs = {
        "out_dma": [setup_d, None],
        "slotA_free": [setup_t],
        "slotB_free": [],
        "slotC_free": [],
        "slotA_q_free": [],
        "vbuf_free": [],
        "stream_free": [],
        "ln_guards": [setup_v],
        "xin_free": [],
        "rb_war": [setup_v],
        "xload_early": [],
    }
    asm_guard = [None, None]
    ctxstage_guard = [None]
    x_first_guard = [setup_t, setup_d, setup_v, setup_g]

    for s in range(NSEQ):
        cast_toks = []
        xload = []
        for c in range(NCH):
            slab = c % 4
            stg = xstage if slab < 2 else ostage
            so = (slab % 2) * T
            if c < 2 and prevs.get("xload_early"):
                tok = prevs["xload_early"][c]
            else:
                if not prevs["xin_free"]:
                    w = list(x_first_guard) + (list(prevs["out_dma"])
                                               if c in (2, 3) else [])
                elif c < 2:
                    w = [prevs["xin_free"][c]]
                elif c in (2, 3):
                    w = list(prevs["out_dma"])
                else:
                    w = []
                if c >= 4:
                    w.append(cast_toks[c - 4])
                tok = dma(stg[:, so:so + T],
                          x_d[s, c * 128:(c + 1) * 128, :], d_x[slab], waits=w)
            xload.append(tok)
            saf = prevs["slotA_free"]
            gate = [saf[c]] if len(saf) == NCH else list(saf)
            Aq.op(lambda e, c=c, stg=stg, so=so: e.activation(
                slotA[:, c * T:(c + 1) * T], stg[:, so:so + T],
                AF.Copy),
                waits=[tok] + gate, inc=a_s.tick())
            cast_toks.append(a_s.at())
        ones_w = [cast_toks[-1]] + prevs["vbuf_free"] + prevs["stream_free"] \
            + list(prevs["out_dma"])
        if s == 0:
            ones_w += [setup_d, setup_t]
        Gq.op(lambda e: e.memset(vput[:, :, :, 64:65], 1.0), waits=ones_w,
              inc=g_s.tick())
        Gq.op(lambda e: e.memset(kt[0][0:1, :], 1.0), inc=g_s.tick())
        Gq.op(lambda e: e.memset(kt[1][0:1, :], 1.0), inc=g_s.tick())
        ones_tok = g_s.at()
        ln1 = emit_ln(slotA, lambda c: cast_toks[c], 1.0 + EPS, EPS * EPS,
                      prevs["ln_guards"], rb_war=prevs["rb_war"])
        xn_toks = []
        for c in range(NCH):
            for th in range(2):
                hs = slice(th * 1024, th * 1024 + 1024)
                Vq.op(lambda e, c=c, th=th, hs=hs: e.tensor_tensor(
                    tmp_b[:, (c % 2) * T + th * 1024:(c % 2) * T + th * 1024 + 1024],
                    slotA[:, c * T + th * 1024:c * T + th * 1024 + 1024],
                    Rb_b[:, hs], ALU.mult),
                    waits=[ln1["rb_th%d" % th], cast_toks[c]], inc=v_s.tick())
                Vq.op(lambda e, c=c, th=th, hs=hs: e.tensor_tensor(
                    slotC[:, c * T + th * 1024:c * T + th * 1024 + 1024],
                    tmp_b[:, (c % 2) * T + th * 1024:(c % 2) * T + th * 1024 + 1024],
                    Bb_b[:, hs], ALU.add),
                    waits=prevs["slotC_free"], inc=v_s.tick())
            xn_toks.append(v_s.at())
        xn_done = v_s.at()

        qk_evac = {}
        last_st_read = [None, None, None, None]
        first_stream = True
        for th in range(2):
            for mc in range(16):
                g = th * 16 + mc
                sbh = g % 4
                w = [setup_d, last_st_read[sbh]]
                if first_stream:
                    w += prevs["stream_free"]
                    first_stream = False
                wtok = dma(wqk_st[:, sbh * 1024:sbh * 1024 + 1024]
                           .rearrange("p (kc f) -> p kc f", kc=8),
                           wqs_d[mc, :, :, :].rearrange("kc p f -> p kc f"),
                           d_w[sbh], waits=w)
                pt = pAll[g % 4]
                wfirst = [wtok, ln1["t_end"]]
                if g - 4 >= 0:
                    wfirst.append(qk_evac[g - 4])
                for kc in range(NCH):
                    for h5 in range(2):
                        wkc = [xn_toks[kc]] if h5 == 0 else []
                        Tq.op(lambda e, mc=mc, kc=kc, th=th, h5=h5, pt=pt, sbh=sbh: e.matmul(
                            pt[:, h5 * 512:h5 * 512 + 512],
                            wqk_st[:, sbh * 1024 + kc * 128:sbh * 1024 + kc * 128 + 128],
                            slotC[:, kc * T + th * 1024 + h5 * 512:
                                  kc * T + th * 1024 + h5 * 512 + 512],
                            start=(kc == 0), stop=(kc == NCH - 1)),
                            waits=(wfirst + wkc if (kc == 0 and h5 == 0) else wkc))
                Tq.retro_inc(t_s.tick())
                t_grp = t_s.at()
                last_st_read[sbh] = t_grp
                dst = slotA if mc < 8 else slotB
                dmc = mc if mc < 8 else mc - 8
                wev = [t_grp]
                wev += prevs["slotA_q_free"] if mc < 8 else prevs["slotB_free"]
                Aq.op(lambda e, dst=dst, dmc=dmc, th=th, pt=pt, mc=mc: e.activation(
                    dst[:, dmc * T + th * 1024:dmc * T + th * 1024 + 1024], pt[:],
                    AF.Identity, bias=c2_col[:, mc:mc + 1]),
                    waits=wev, inc=a_s.tick())
                qk_evac[g] = a_s.at()
        qk_done = a_s.at()
        t_qk = t_s.at()


        asm_tok = {}

        def emit_asm(h, extra):
            wk = [qk_done, asm_guard[h % 2], extra]
            dma(kt[h % 2][1:65, :],
                slotB[(h % 2) * 64:(h % 2) * 64 + 64, (h // 2) * T:(h // 2) * T + T],
                d_kq[h % 2], waits=wk)
            dma(qt[h % 2][1:65, :],
                slotA[(h % 2) * 64:(h % 2) * 64 + 64, (h // 2) * T:(h // 2) * T + T],
                d_kq[h % 2], waits=[])
            asm_tok[h] = d_kq[h % 2].at()

        emit_asm(0, None)
        emit_asm(1, None)

        st = {"sprobe_b": [None, None], "probe_par": 0, "mrow": None,
              "aT": [None, None], "pT": [None, None], "ctx": None,
              "qrow": {}, "t_av": None}

        def gen_A(h, half):
            q0 = half * 1024
            mcol_toks = []
            for idx, i in enumerate(range(8 * half, 8 * half + 8)):
                row_w = (i + 1) * 128
                nchunks = 0
                for ck0 in range(0, row_w, 512):
                    ck1 = min(ck0 + 512, row_w)
                    pb_ = st["probe_par"]
                    st["probe_par"] ^= 1
                    w = [qk_done, st["sprobe_b"][pb_], ln1["Rb"]]
                    if pb_ == 1:
                        w.append(st["mrow"])
                    Tq.op(lambda e, h=h, i=i, ck0=ck0, ck1=ck1, pb_=pb_: e.matmul(
                        pA[:, pb_ * 512:pb_ * 512 + ck1 - ck0],
                        slotA[(h % 2) * 64:(h % 2) * 64 + 64,
                              (h // 2) * T + i * 128:(h // 2) * T + i * 128 + 128],
                        slotB[(h % 2) * 64:(h % 2) * 64 + 64,
                              (h // 2) * T + ck0:(h // 2) * T + ck1],
                        start=True, stop=True),
                        waits=w, inc=t_s.tick())
                    tmm = t_s.at()
                    Vq.op(lambda e, nch=nchunks, w=ck1 - ck0, pb_=pb_: e.reduce_max(
                        mtmp[:, nch:nch + 1], pA[:, pb_ * 512:pb_ * 512 + w], axis=AX.X),
                        waits=[tmm], inc=v_s.tick())
                    st["sprobe_b"][pb_] = v_s.at()
                    nchunks += 1
                Vq.op(lambda e, idx=idx, nch=nchunks: e.reduce_max(
                    mneg[:, idx:idx + 1], mtmp[:, 0:nch], axis=AX.X, negate=True),
                    inc=v_s.tick())
                mcol_toks.append(v_s.at())
                yield
            for qtr in range(2):
                for r in range(4):
                    w = [mcol_toks[qtr * 4 + r]]
                    if r == 0:
                        w += [st["mrow"], st["sprobe_b"][1]]
                    Tq.op(lambda e, qtr=qtr, r=r: e.matmul(
                        pA[0:1, 512 + r * 128:512 + r * 128 + 128],
                        mneg[:, qtr * 4 + r:qtr * 4 + r + 1], ident_bf[:],
                        start=True, stop=True),
                        waits=w, inc=(t_s.tick() if r == 3 else None))
                t_mrow = t_s.at()
                st["sprobe_b"][1] = None
                Aq.op(lambda e, h=h, q0=q0, qtr=qtr: e.activation(
                    qt[h % 2][0:1, q0 + qtr * 512:q0 + qtr * 512 + 512],
                    pA[0:1, 512:1024], AF.Copy),
                    waits=[t_mrow, asm_tok[h]], inc=a_s.tick())
                st["mrow"] = a_s.at()
                yield
            st["qrow"][(h, half)] = a_s.at()

        def gen_B(h, half):
            q0 = half * 1024
            qrow_done = st["qrow"][(h, half)]
            strips = list(range(0, 8 * (half + 1)))
            first_j, last_j = strips[0], strips[-1]
            pend = None

            def emit_av(pend):
                cks, bi, a_exp, diag_tok, j = pend
                wv = [a_exp, diag_tok, ones_tok]
                if j == first_j:
                    wv.append(st["ctx"])
                for ci, (c0, c1) in enumerate(cks):
                    lastj_chunk = min(last_j, (q0 + c1 - 1) // 128)
                    Tq.op(lambda e, h=h, j=j, c0=c0, c1=c1, bi=bi,
                          sp=(j == lastj_chunk): e.matmul(
                        pD[0:65, c0:c1],
                        vbuf[:, j * NH * 65 + h * 65:j * NH * 65 + h * 65 + 65],
                        aT[bi][:, c0:c1],
                        start=(j == first_j), stop=sp),
                        waits=(wv if ci == 0 else []),
                        inc=(t_s.tick() if ci == len(cks) - 1 else None))
                st["aT"][bi] = t_s.at()

            for j in strips:
                rel0 = max(j * 128 - q0, 0)
                bi = j % 2
                buf = (pB, pC)[bi]
                cks = []
                c0 = rel0
                while c0 < 1024:
                    c1 = min((c0 // 512 + 1) * 512, 1024)
                    cks.append((c0, c1))
                    c0 = c1
                wv = [qrow_done, asm_tok[h], st["pT"][bi], ones_tok]
                for ci, (c0, c1) in enumerate(cks):
                    Tq.op(lambda e, h=h, j=j, c0=c0, c1=c1, buf=buf, q0=q0: e.matmul(
                        buf[:, c0:c1],
                        kt[h % 2][0:65, j * 128:j * 128 + 128],
                        qt[h % 2][0:65, q0 + c0:q0 + c1],
                        start=True, stop=True),
                        waits=(wv if ci == 0 else []),
                        inc=(t_s.tick() if ci == len(cks) - 1 else None))
                t_st = t_s.at()
                Aq.op(lambda e, bi=bi, rel0=rel0, buf=buf: e.activation(
                    aT[bi][:, rel0:1024], buf[:, rel0:1024], AF.Exp, scale=0.125),
                    waits=[t_st, st["aT"][bi]], inc=a_s.tick())
                st["pT"][bi] = a_s.at()
                a_exp = a_s.at()
                diag_tok = None
                if j >= 8 * half:
                    Gq.op(lambda e, bi=bi, rel0=rel0: e.tensor_tensor(
                        aT[bi][:, rel0:rel0 + 128], aT[bi][:, rel0:rel0 + 128],
                        maskt_sb[:], ALU.mult),
                        waits=[a_exp], inc=g_s.tick())
                    diag_tok = g_s.at()
                if pend is not None:
                    emit_av(pend)
                pend = (cks, bi, a_exp, diag_tok, j)
                yield
            emit_av(pend)
            t_av = t_s.at()
            st["t_av"] = t_av

            Aq.op(lambda e: e.activation(rball[64:65, :], pD[64:65, :], AF.Ln),
                  waits=[t_av], inc=a_s.tick())
            Aq.op(lambda e: e.activation(rrow_bf[64:65, :], rball[64:65, :],
                                         AF.Exp, scale=-1.0), inc=a_s.tick())
            a_r = a_s.at()
            for h5 in range(2):
                Tq.op(lambda e, h5=h5: e.matmul(
                    pB[0:64, h5 * 512:h5 * 512 + 512], ones64[64:65, :],
                    rrow_bf[64:65, h5 * 512:h5 * 512 + 512], start=True, stop=True),
                    waits=([a_r, st["pT"][0]] if h5 == 0 else []),
                    inc=(t_s.tick() if h5 == 1 else None))
            t_rb = t_s.at()
            Aq.op(lambda e: e.activation(rball[0:64, :], pB[0:64, :], AF.Copy),
                  waits=[t_rb], inc=a_s.tick())
            v_rb = a_s.at()
            st["pT"][0] = v_rb
            if h % 2 == 0:
                Vq.op(lambda e, h=h, q0=q0: e.tensor_tensor(
                    slotC[0:64, (h // 2) * T + q0:(h // 2) * T + q0 + 1024],
                    pD[0:64, :], rball[0:64, :], ALU.mult),
                    waits=[v_rb], inc=v_s.tick())
                st["ctx"] = v_s.at()
            else:
                Vq.op(lambda e: e.tensor_tensor(
                    ctxstage[:], pD[0:64, :], rball[0:64, :], ALU.mult),
                    waits=[v_rb, ctxstage_guard[0]], inc=v_s.tick())
                st["ctx"] = v_s.at()
                tok = dma(slotC[64:128, (h // 2) * T + q0:(h // 2) * T + q0 + 1024],
                          ctxstage[:], d_cs, waits=[st["ctx"]])
                ctxstage_guard[0] = tok
            if half == 1:
                asm_guard[h % 2] = t_av
                if h + 2 < NH:
                    emit_asm(h + 2, t_av)
            yield

        pairs = [(h, hf) for h in range(NH) for hf in range(2)]
        gA0 = gen_A(*pairs[0])
        gA0_done = [False]

        def pump():
            if not gA0_done[0]:
                try:
                    next(gA0)
                except StopIteration:
                    gA0_done[0] = True
        v_ev = {}
        last_v_read = None
        for vh in range(2):
            w = [setup_d, t_qk if vh == 0 else last_v_read] + list(prevs["out_dma"])
            wtok = dma(wv_st[:].rearrange("p (km f) -> p km f", km=32),
                       wvs_d[vh, :, :, :, :].rearrange("kc mc p f -> p (kc mc) f"),
                       d_wv, waits=w)
            for tt in range(NT):
                gidx = vh * NT + tt
                pt = (pB, pC, pD)[gidx % 3]
                wfirst = [wtok, xn_done]
                if gidx - 3 >= 0:
                    wfirst.append(v_ev[gidx - 3])
                else:
                    wfirst.append(qk_done)
                for kc in range(NCH):
                    Tq.op(lambda e, tt=tt, kc=kc, pt=pt: e.matmul(
                        pt[:, 0:512],
                        slotC[:, kc * T + tt * 128:kc * T + tt * 128 + 128],
                        wv_st[:, kc * 512:kc * 512 + 512],
                        start=(kc == 0), stop=(kc == NCH - 1)),
                        waits=(wfirst if kc == 0 else []))
                Tq.retro_inc(t_s.tick())
                t_grp = t_s.at()
                last_v_read = t_grp
                dstv = vbuf[:, tt * NH * 65:(tt + 1) * NH * 65] \
                    .rearrange("p (h f) -> p h f", h=NH)[:, vh * 8:vh * 8 + 8, 0:64]
                srcv = pt[:, 0:512].rearrange("p (h f) -> p h f", h=8)
                Aq.op(lambda e, dstv=dstv, srcv=srcv: e.activation(dstv, srcv, AF.Copy),
                      waits=[t_grp, setup_g] + prevs["vbuf_free"] + list(prevs["out_dma"]),
                      inc=a_s.tick())
                v_ev[gidx] = a_s.at()
                pump()
                pump()
        v_done = a_s.at()
        t_v = t_s.at()
        st["pT"] = [v_done, v_done]
        st["ctx"] = v_done
        st["aT"] = [st["aT"][0], st["aT"][1]]
        for _ in gA0:
            pass
        gAn = None
        for k, pr in enumerate(pairs):
            gB = gen_B(*pr)
            gAn = gen_A(*pairs[k + 1]) if k + 1 < len(pairs) else None
            nstrips = 8 * (pr[1] + 1)
            if nstrips >= 10:
                sched = [0] * (nstrips - 10) + [1] * 10
            else:
                sched = [1] * nstrips
                for i in range(10 - nstrips):
                    sched[-1 - (i % 2)] += 1
            yi = 0
            for _ in gB:
                nsteps = sched[yi] if yi < len(sched) else 1
                yi += 1
                for _ in range(nsteps):
                    if gAn is not None:
                        try:
                            next(gAn)
                        except StopIteration:
                            gAn = None
            if gAn is not None:
                for _ in gAn:
                    pass
                gAn = None
        t_av_last = st["t_av"]
        v_p3 = v_s.at()
        a_p3 = a_s.at()
        d_p3 = [d_kq[0].at(), d_kq[1].at(), d_cs.at()]
        t_p3 = t_s.at()

        ao_evac = {}
        last_wo_read = [None, None]
        for mc in range(8):
            sbh = mc % 2
            w = [setup_d, last_wo_read[sbh]]
            if mc == 0:
                w += [t_p3]
            wtok = dma(wo_st[:, sbh * 1024:sbh * 1024 + 1024]
                       .rearrange("p (kc f) -> p kc f", kc=8),
                       wos_d[mc, :, :, :].rearrange("kc p f -> p kc f"),
                       d_w[sbh], waits=w)
            pts = (pA, pB) if mc % 2 == 0 else (pC, pD)
            wfirst = [wtok, v_p3, a_p3, d_p3]
            if mc >= 2:
                wfirst.append(ao_evac[mc - 2])
            for kc in range(NCH):
                for sl in range(4):
                    pt = pts[sl // 2]
                    Tq.op(lambda e, mc=mc, kc=kc, sl=sl, pt=pt, sbh=sbh: e.matmul(
                        pt[:, (sl % 2) * 512:(sl % 2) * 512 + 512],
                        wo_st[:, sbh * 1024 + kc * 128:sbh * 1024 + kc * 128 + 128],
                        slotC[:, kc * T + sl * 512:kc * T + sl * 512 + 512],
                        start=(kc == 0), stop=(kc == NCH - 1)),
                        waits=(wfirst if (kc == 0 and sl == 0) else []))
            Tq.retro_inc(t_s.tick())
            t_grp = t_s.at()
            last_wo_read[sbh] = t_grp
            for halfp in range(2):
                Aq.op(lambda e, mc=mc, halfp=halfp, pts=pts: e.activation(
                    slotA[:, mc * T + halfp * 1024:mc * T + halfp * 1024 + 1024],
                    pts[halfp][:], AF.Identity, bias=c3_col[:, mc:mc + 1]),
                    waits=[t_grp, d_p3], inc=a_s.tick())
            ao_evac[mc] = a_s.at()
        ao_done = a_s.at()
        t_wo = t_s.at()

        ln3 = emit_ln(slotA, lambda c: ao_evac[c], 1.0, EPS, [ao_done, v_p3],
                      rb_war=[xn_done])
        x2_toks = []
        for c in range(NCH):
            slab = c % 4
            stg = xstage if slab < 2 else ostage
            so = (slab % 2) * T
            w = [cast_toks[-1], t_av_last] + ([x2_toks[c - 4]] if c >= 4 else [])
            xtok = dma(stg[:, so:so + T],
                       x_d[s, c * 128:(c + 1) * 128, :], d_x[slab], waits=w)
            xo = (c % 2) * T
            Aq.op(lambda e, c=c, stg=stg, so=so, xo=xo: e.activation(
                xbf[:, xo:xo + T], stg[:, so:so + T], AF.Copy),
                waits=[xtok, t_wo, d_p3] + ([x2_toks[c - 2]] if c >= 2 else []),
                inc=a_s.tick())
            xc_tok = a_s.at()
            for th in range(2):
                to = (c % 2) * T + th * 1024
                Vq.op(lambda e, c=c, th=th, to=to: e.tensor_tensor(
                    tmp_b[:, to:to + 1024],
                    slotA[:, c * T + th * 1024:c * T + th * 1024 + 1024],
                    Rb_b[:, th * 1024:th * 1024 + 1024], ALU.mult),
                    waits=[ln3["rb_th%d" % th]] + ([x2_toks[c - 2]] if c >= 2 else []),
                    inc=v_s.tick())
                Vq.op(lambda e, to=to, th=th: e.tensor_tensor(
                    tmp_b[:, to:to + 1024], tmp_b[:, to:to + 1024],
                    Bb_b[:, th * 1024:th * 1024 + 1024], ALU.add),
                    inc=v_s.tick())
                Vq.op(lambda e, c=c, th=th, to=to: e.tensor_tensor(
                    slotB[:, c * T + th * 1024:c * T + th * 1024 + 1024],
                    tmp_b[:, to:to + 1024],
                    xbf[:, (c % 2) * T + th * 1024:(c % 2) * T + th * 1024 + 1024],
                    ALU.add),
                    waits=[xc_tok], inc=v_s.tick())
            x2_toks.append(v_s.at())
        x2_done = v_s.at()

        ln4 = emit_ln(slotB, lambda c: x2_toks[c], 1.0, EPS, [x2_done, ao_done],
                      rb_war=[x2_done])
        xn2_toks = []
        for c in range(NCH):
            for th in range(2):
                to = (c % 2) * T + th * 1024
                Vq.op(lambda e, c=c, th=th, to=to: e.tensor_tensor(
                    tmp_b[:, to:to + 1024],
                    slotB[:, c * T + th * 1024:c * T + th * 1024 + 1024],
                    Rb_b[:, th * 1024:th * 1024 + 1024], ALU.mult),
                    waits=[ln4["rb_th%d" % th]], inc=v_s.tick())
                Vq.op(lambda e, c=c, th=th, to=to: e.tensor_tensor(
                    slotA[:, c * T + th * 1024:c * T + th * 1024 + 1024],
                    tmp_b[:, to:to + 1024],
                    Bb_b[:, th * 1024:th * 1024 + 1024], ALU.add),
                    inc=v_s.tick())
            xn2_toks.append(v_s.at())
        xn2_done = v_s.at()

        s_ev = []
        for sl in range(4):
            pt = (pA, pA, pB, pB)[sl]
            off = (sl % 2) * 512
            for kc in range(NCH):
                wkc = [xn2_toks[kc]] + ([ln4["t_end"]] if kc == 0 else [])
                Tq.op(lambda e, sl=sl, kc=kc, pt=pt, off=off: e.matmul(
                    pt[0:64, off:off + 512],
                    win_sb[:, kc * 64:kc * 64 + 64],
                    slotA[:, kc * T + sl * 512:kc * T + sl * 512 + 512],
                    start=(kc == 0), stop=(kc == NCH - 1)),
                    waits=wkc)
            Tq.retro_inc(t_s.tick())
            t_grp = t_s.at()
            if sl % 2 == 1:
                pth = (pA, pB)[sl // 2]
                Aq.op(lambda e, sl=sl, pth=pth: e.activation(
                    s_sb[:, (sl - 1) * 512:(sl + 1) * 512], pth[0:64, :],
                    AF.Identity, bias=bin_sb[:]),
                    waits=[t_grp, d_p3], inc=a_s.tick())
                s_ev.append(a_s.at())
        s_done = a_s.at()
        g_q = []
        ts_toks = {}
        for q4 in range(4):
            qs = slice(q4 * 512, q4 * 512 + 512)
            for i in range(16):
                tfa = tfn[0:64, (i % 2) * 512:(i % 2) * 512 + 512]
                if i < 2:
                    wa = [s_ev[q4 // 2]]
                    if q4 > 0:
                        wa.append(ts_toks[(q4 - 1, 14 + i)])
                else:
                    wa = [ts_toks[(q4, i - 2)]]
                Aq.op(lambda e, i=i, qs=qs, tfa=tfa: e.activation(
                    tfa, s_sb[0:64, qs], AF.Identity,
                    scale=w1_sb[:, i:i + 1], bias=b1_sb[:, i:i + 1]),
                    waits=wa, inc=a_s.tick())
                a_tf = a_s.at()
                dst = g_sb if i == 0 else u_fn
                Vq.op(lambda e, dst=dst, i=i, qs=qs, tfa=tfa: e.tensor_scalar(
                    dst[0:64, qs], tfa, 0.0, w2_sb[:, i:i + 1],
                    ALU.max, ALU.mult), waits=[a_tf], inc=v_s.tick())
                ts_toks[(q4, i)] = v_s.at()
                if i > 0:
                    Vq.op(lambda e, qs=qs: e.tensor_tensor(
                        g_sb[0:64, qs], g_sb[0:64, qs], u_fn[0:64, qs], ALU.add),
                        inc=v_s.tick())
            Vq.op(lambda e, qs=qs: e.tensor_scalar(
                gb_sb[0:64, qs], g_sb[0:64, qs], b2_sb[:], None, ALU.add),
                inc=v_s.tick())
            g_q.append(v_s.at())
        g_done = v_s.at()
        fo_ev = {}
        for mc in range(8):
            pts = (pA, pB) if mc % 2 == 0 else (pC, pD)
            for sl in range(4):
                pt = pts[sl // 2]
                w = [g_q[sl]]
                if sl == 0 and mc >= 2:
                    w.append(fo_ev[mc - 2])
                Tq.op(lambda e, mc=mc, sl=sl, pt=pt: e.matmul(
                    pt[:, (sl % 2) * 512:(sl % 2) * 512 + 512],
                    wout_sb[0:64, mc * 128:mc * 128 + 128],
                    gb_sb[:, sl * 512:sl * 512 + 512],
                    start=True, stop=True),
                    waits=w)
            Tq.retro_inc(t_s.tick())
            t_grp = t_s.at()
            for halfp in range(2):
                Aq.op(lambda e, mc=mc, halfp=halfp, pts=pts: e.activation(
                    slotA[:, mc * T + halfp * 1024:mc * T + halfp * 1024 + 1024],
                    pts[halfp][:], AF.Identity, bias=bout_col[:, mc:mc + 1]),
                    waits=[t_grp, xn2_done], inc=a_s.tick())
            fo_ev[mc] = a_s.at()
        fo_done = a_s.at()

        ln5 = emit_ln(slotA, lambda c: fo_ev[c], 1.0, EPS, [fo_done, x2_done],
                      rb_war=[xn2_done])
        xload_early = []
        if s + 1 < NSEQ:
            for ce in range(2):
                xload_early.append(dma(
                    xstage[:, ce * T:ce * T + T],
                    x_d[s + 1, ce * 128:(ce + 1) * 128, :], d_x[ce],
                    waits=[x2_toks[4 + ce]]))
        out_toks = []
        out_g = []
        ln5_tt1 = []
        for c in range(NCH):
            for th in range(2):
                to = (c % 2) * T + th * 1024
                Vq.op(lambda e, c=c, th=th, to=to: e.tensor_tensor(
                    tmp_b[:, to:to + 1024],
                    slotA[:, c * T + th * 1024:c * T + th * 1024 + 1024],
                    Rb_b[:, th * 1024:th * 1024 + 1024], ALU.mult),
                    waits=[ln5["rb_th%d" % th]] + ([out_g[c - 2]] if c >= 2 else []),
                    inc=v_s.tick())
                Vq.op(lambda e, c=c, th=th, to=to: e.tensor_tensor(
                    tmp_b[:, to:to + 1024], tmp_b[:, to:to + 1024],
                    Bb_b[:, th * 1024:th * 1024 + 1024], ALU.add),
                    inc=v_s.tick())
                w = [out_toks[c - 2]] if c >= 2 else [x2_toks[6 + c]]
                Vq.op(lambda e, c=c, th=th, to=to: e.tensor_tensor(
                    ostage[:, to:to + 1024],
                    tmp_b[:, to:to + 1024],
                    slotB[:, c * T + th * 1024:c * T + th * 1024 + 1024], ALU.add),
                    waits=w, inc=v_s.tick())
            ln5_tt1.append(v_s.at())
            v_out = v_s.at()
            out_g.append(v_out)
            tok = dma(y_d[s, c * 128:(c + 1) * 128, :],
                      ostage[:, (c % 2) * T:(c % 2) * T + T], d_y[c % 4], waits=[v_out])
            out_toks.append(tok)
        v_p6 = v_s.at()
        t_ln5 = ln5["t_end"]

        g_p6 = g_s.at()
        prevs = {
            "out_dma": out_toks[-4:],
            "slotA_free": ln5_tt1,
            "slotB_free": [v_p6, g_p6],
            "slotC_free": [t_wo],
            "slotA_q_free": [t_p3, d_p3],
            "vbuf_free": [t_av_last],
            "stream_free": [v_p6, fo_done],
            "ln_guards": [ln5["Rb"], fo_done],
            "xin_free": [x2_toks[4], x2_toks[5]],
            "rb_war": [v_p6],
            "xload_early": xload_early,
        }

    nc._dbg = dict(slotA=slotA, slotB=slotB, slotC=slotC, vbuf=vbuf, kt0=kt[0],
                   kt1=kt[1], qt0=qt[0], qt1=qt[1], c2_col=c2_col, c3_col=c3_col,
                   Rb_b=Rb_b, Bb_b=Bb_b, R_row=R_row, B_row=B_row, s_sb=s_sb,
                   g_sb=g_sb, gb_sb=gb_sb, rball=rball, mneg=mneg, aT0=aT[0],
                   aT1=aT[1], win_sb=win_sb, wout_sb=wout_sb, tfn=tfn, u_fn=u_fn, g=g_sb, w1=w1_sb, b1=b1_sb, w2=w2_sb)

    Sq.op(lambda e: None, waits=[d.at() for d in d_y])

    with nc.Block() as block:
        @block.tensor
        def _(e):
            run_ops(e, Tq.ops)

        @block.vector
        def _(e):
            run_ops(e, Vq.ops)

        @block.scalar
        def _(e):
            run_ops(e, Aq.ops)

        @block.gpsimd
        def _(e):
            run_ops(e, Gq.ops)

        @block.sync
        def _(e):
            run_ops(e, Sq.ops)

    return nc


def kernel(L0_attn_x, L0_attn_ln_gamma, L0_attn_ln_beta, L0_attn_W_qkv, L0_attn_W_o,
           L0_ffn_Win, L0_ffn_bin_, L0_ffn_W1, L0_ffn_b1, L0_ffn_W2, L0_ffn_b2,
           L0_ffn_Wout, L0_ffn_bout):
    x = np.asarray(L0_attn_x, dtype=np.float32)
    gamma = np.asarray(L0_attn_ln_gamma, dtype=np.float32)
    beta = np.asarray(L0_attn_ln_beta, dtype=np.float32)
    wqkv = np.asarray(L0_attn_W_qkv, dtype=np.float32)
    wo = np.asarray(L0_attn_W_o, dtype=np.float32)
    win = np.asarray(L0_ffn_Win, dtype=np.float32)
    bin_ = np.asarray(L0_ffn_bin_, dtype=np.float32)
    w1 = np.asarray(L0_ffn_W1, dtype=np.float32)
    b1 = np.asarray(L0_ffn_b1, dtype=np.float32)
    w2 = np.asarray(L0_ffn_W2, dtype=np.float32)
    b2 = np.asarray(L0_ffn_b2, dtype=np.float32)
    wout = np.asarray(L0_ffn_Wout, dtype=np.float32)
    bout = np.asarray(L0_ffn_bout, dtype=np.float32)

    idx = []
    for blk in range(3):
        for h in range(NH):
            base = h * 192 + blk * 64
            idx.extend(range(base, base + 64))
    wqkvt = np.ascontiguousarray(wqkv[np.array(idx)].T)
    wot = np.ascontiguousarray(wo.T)
    gcol = np.ascontiguousarray(gamma.reshape(8, 128).T)
    bcol = np.ascontiguousarray(beta.reshape(8, 128).T)
    boutcol = np.ascontiguousarray(bout.reshape(8, 128).T)
    bincol = np.ascontiguousarray(bin_.reshape(64, 1))
    b2col = np.ascontiguousarray(b2.reshape(64, 1))
    ident = np.eye(128, dtype=np.float32)
    kq = np.arange(128)
    maskt = np.ascontiguousarray((kq[:, None] <= kq[None, :]).astype(np.float32))
    maskn = np.ascontiguousarray(
        np.where(kq[None, :] > kq[:, None], np.float32(-1e30), np.float32(0.0)))

    nc = build_program()
    in_maps = []
    for c in range(NCORE):
        xs = np.ascontiguousarray(np.transpose(x[c * NSEQ:(c + 1) * NSEQ], (0, 2, 1)))
        in_maps.append(dict(
            x=xs, wqkvt=wqkvt, wot=wot, win=win, wout=wout,
            gcol=gcol, bcol=bcol, bincol=bincol, w1=w1, b1=b1, w2=w2,
            b2col=b2col, boutcol=boutcol, ident=ident, maskt=maskt, maskn=maskn,
        ))
    res = run_bass_kernel_spmd(nc, in_maps, core_ids=list(range(NCORE)))
    outs = []
    for c in range(NCORE):
        yc = res.results[c]["y"]
        outs.append(np.transpose(yc, (0, 2, 1)))
    return np.ascontiguousarray(np.concatenate(outs, axis=0))


# version 37
# speedup vs baseline: 1.1680x; 1.0160x over previous
import numpy as np
import ml_dtypes
import concourse.bass as bass
import concourse.mybir as mybir
from concourse.bass_utils import run_bass_kernel_spmd

F32 = mybir.dt.float32
BF16 = mybir.dt.bfloat16
AF = mybir.ActivationFunctionType
ALU = mybir.AluOpType
AX = mybir.AxisListType

NCORE = 8
NSEQ = 4
T = 2048
D = 1024
NH = 16
DH = 64
NCH = 8
NT = 16
EPS = 1e-5


class Ctr:
    def __init__(self, nc, name):
        self.h = nc.alloc_semaphore(name)
        self.n = 0

    def tick(self, k=1):
        self.n += k
        return (self.h, k)

    def at(self):
        return (self.h, self.n)


class Eng:
    def __init__(self):
        self.ops = []

    def op(self, fn, waits=(), inc=None):
        ws = []
        for w in waits:
            if w is None:
                continue
            if isinstance(w, list):
                ws.extend(t for t in w if t is not None)
            else:
                ws.append(w)
        self.ops.append((tuple(ws), fn, inc))

    def retro_inc(self, inc):
        w, f, old = self.ops[-1]
        assert old is None
        self.ops[-1] = (w, f, inc)


def run_ops(engine, ops):
    waited = {}
    for waits, fn, inc in ops:
        for sem, val in waits:
            if val > 0 and waited.get(id(sem), -1) < val:
                engine.wait_ge(sem, val)
                waited[id(sem)] = val
        ins = fn(engine)
        if inc is not None:
            sem, k = inc
            ins.then_inc(sem, k)


def build_program(nc=None):
    if nc is None:
        nc = bass.Bass("TRN2", target_bir_lowering=False)

    x_d = nc.dram_tensor("x", [NSEQ, D, T], F32, kind="ExternalInput").ap()
    y_d = nc.dram_tensor("y", [NSEQ, D, T], F32, kind="ExternalOutput").ap()
    wqkvt_d = nc.dram_tensor("wqkvt", [D, 3 * D], F32, kind="ExternalInput").ap()
    wot_d = nc.dram_tensor("wot", [D, D], F32, kind="ExternalInput").ap()
    win_d = nc.dram_tensor("win", [D, 64], F32, kind="ExternalInput").ap()
    wout_d = nc.dram_tensor("wout", [64, D], F32, kind="ExternalInput").ap()
    gcol_d = nc.dram_tensor("gcol", [128, 8], F32, kind="ExternalInput").ap()
    bcol_d = nc.dram_tensor("bcol", [128, 8], F32, kind="ExternalInput").ap()
    bincol_d = nc.dram_tensor("bincol", [64, 1], F32, kind="ExternalInput").ap()
    w1_d = nc.dram_tensor("w1", [64, 16], F32, kind="ExternalInput").ap()
    b1_d = nc.dram_tensor("b1", [64, 16], F32, kind="ExternalInput").ap()
    w2_d = nc.dram_tensor("w2", [64, 16], F32, kind="ExternalInput").ap()
    b2col_d = nc.dram_tensor("b2col", [64, 1], F32, kind="ExternalInput").ap()
    boutcol_d = nc.dram_tensor("boutcol", [128, 8], F32, kind="ExternalInput").ap()
    ident_d = nc.dram_tensor("ident", [128, 128], F32, kind="ExternalInput").ap()
    maskt_d = nc.dram_tensor("maskt", [128, 128], F32, kind="ExternalInput").ap()
    maskn_d = nc.dram_tensor("maskn", [128, 128], F32, kind="ExternalInput").ap()
    wqs_d = nc.dram_tensor("wqs", [24, 8, 128, 128], BF16, kind="Internal").ap()
    wos_d = nc.dram_tensor("wos", [8, 8, 128, 128], BF16, kind="Internal").ap()
    wvs_d = nc.dram_tensor("wvs", [2, 8, 4, 128, 128], BF16, kind="Internal").ap()

    BASE = 18464
    def at(name, shape, dt, off):
        return nc.alloc_sbuf_tensor_at(name, shape, dt, offset=BASE + off).ap()

    slotA = at("slotA", [128, NCH * T], BF16, 0)
    slotB = at("slotB", [128, NCH * T], BF16, 32768)
    slotC = at("slotC", [128, NCH * T], BF16, 65536)
    VOFF = 98304
    vbuf = at("vbuf", [128, NT * NH * 65], BF16, VOFF)
    wstage2 = [at("wstage0", [128, 3 * D], F32, VOFF),
               at("wstage1", [128, 3 * D], F32, VOFF + 12288)]
    wbstage = at("wbstage", [128, 3 * D], BF16, VOFF + 24576)
    xstage = at("xstage", [128, 2 * T], F32, VOFF)
    ostage = at("ostage", [128, 2 * T], F32, VOFF + 16384)
    FOFF = 131584
    s_sb = at("s_sb", [64, T], F32, FOFF)
    tfn = at("tfn", [64, T], F32, FOFF + 8192)
    u_fn = at("u_fn", [64, T], F32, FOFF + 16384)
    g_sb = at("g_sb", [64, T], F32, FOFF + 24576)
    gb_sb = at("gb_sb", [64, T], BF16, FOFF + 32768)
    kt = [at("kt0", [65, T], BF16, FOFF), at("kt1", [65, T], BF16, FOFF + 4096)]
    qt = [at("qt0", [65, T], BF16, FOFF + 8192), at("qt1", [65, T], BF16, FOFF + 12288)]
    aT = [at("aT0", [128, 1024], BF16, FOFF + 16384), at("aT1", [128, 1024], BF16, FOFF + 18432)]
    rball = at("rball", [128, 1024], F32, FOFF + 20480)
    ctxstage = at("ctxstage", [64, 1024], BF16, FOFF + 24576)
    wqk_st = at("wqk_st", [128, 4096], BF16, FOFF + 26624)
    wv_st = at("wv_st", [128, 4096], BF16, FOFF + 26624)
    wo_st = at("wo_st", [128, 2048], BF16, FOFF + 26624)
    xbf = at("xbf", [128, 4096], BF16, FOFF + 8192)
    tmpblk = at("tmpblk", [128, 128], F32, 209472)
    mneg = at("mneg", [128, 8], BF16, 209984)
    mtmp = at("mtmp", [128, 8], F32, 210016)
    R_row = at("R_row", [1, 2048], BF16, 168448)
    B_row = at("B_row", [1, 2048], BF16, 172544)
    Rb_b = at("Rb_b", [128, T], BF16, 176640)
    Bb_b = at("Bb_b", [128, T], BF16, 180736)
    sq_b = at("sq_b", [128, 2 * T], BF16, 184832)
    tmp_b = at("tmp_b", [128, 2 * T], BF16, 193024)
    CO = 201216
    win_sb = at("win_sb", [128, 8 * 64], BF16, CO)
    wout_sb = at("wout_sb", [64, D], BF16, CO + 1024)
    ident_sb = at("ident_sb", [128, 128], F32, CO + 3072)
    maskt_f = at("maskt_f", [128, 128], F32, CO + 3584)
    maskn_sb = at("maskn_sb", [128, 128], F32, CO + 4096)
    maskt_sb = at("maskt_sb", [128, 128], BF16, CO + 4608)
    onesrow_bf = at("onesrow_bf", [1, 128], BF16, CO + 4864)
    ones64 = at("ones64", [65, 64], BF16, CO + 5152)
    ident_bf = at("ident_bf", [128, 128], BF16, CO + 5920)
    rrow_bf = at("rrow_bf", [65, 1024], BF16, CO + 6208)
    w1_sb = at("w1_sb", [64, 16], F32, CO + 5408)
    b1_sb = at("b1_sb", [64, 16], F32, CO + 5472)
    w2_sb = at("w2_sb", [64, 16], F32, CO + 5536)
    c2_col = at("c2_col", [128, 24], F32, CO + 5600)
    c3_col = at("c3_col", [128, 8], F32, CO + 5696)
    g_col = at("g_col", [128, 8], F32, CO + 5728)
    be_col = at("be_col", [128, 8], F32, CO + 5760)
    bout_col = at("bout_col", [128, 8], F32, CO + 5792)
    bin_sb = at("bin_sb", [64, 1], F32, CO + 5824)
    b2_sb = at("b2_sb", [64, 1], F32, CO + 5856)
    onescol_bf = at("onescol_bf", [128, 1], BF16, CO + 5888)

    pA = nc.alloc_psum_tensor("pA", [128, 1024], F32).ap()
    pB = nc.alloc_psum_tensor("pB", [128, 1024], F32).ap()
    pC = nc.alloc_psum_tensor("pC", [128, 1024], F32).ap()
    pD = nc.alloc_psum_tensor("pD", [128, 1024], F32).ap()
    pAll = [pA, pB, pC, pD]

    Tq, Vq, Aq, Gq, Sq = Eng(), Eng(), Eng(), Eng(), Eng()
    t_s = Ctr(nc, "t_s")
    v_s = Ctr(nc, "v_s")
    a_s = Ctr(nc, "a_s")
    g_s = Ctr(nc, "g_s")
    d_c = Ctr(nc, "d_c")
    d_ws = Ctr(nc, "d_ws")
    d_x = [Ctr(nc, "d_x0"), Ctr(nc, "d_x1"),
           Ctr(nc, "d_x2"), Ctr(nc, "d_x3")]
    d_y = [Ctr(nc, "d_y0"), Ctr(nc, "d_y1"),
           Ctr(nc, "d_y2"), Ctr(nc, "d_y3")]
    d_w = [Ctr(nc, "d_w0"), Ctr(nc, "d_w1"),
           Ctr(nc, "d_w2"), Ctr(nc, "d_w3")]
    d_wv = Ctr(nc, "d_wv")
    d_kq = [Ctr(nc, "d_kq0"), Ctr(nc, "d_kq1")]
    d_cs = Ctr(nc, "d_cs")
    d_out = Ctr(nc, "d_out")

    def dma(dst, src, ctr, waits=()):
        Sq.op(lambda e, d=dst, s=src: e.dma_start(d, s), waits=waits, inc=ctr.tick(16))
        return ctr.at()

    for dst, src in [
        (w1_sb[:], w1_d), (b1_sb[:], b1_d), (w2_sb[:], w2_d),
        (bin_sb[:], bincol_d), (b2_sb[:], b2col_d), (bout_col[:], boutcol_d),
        (g_col[:], gcol_d), (be_col[:], bcol_d), (ident_sb[:], ident_d),
        (maskt_f[:], maskt_d), (maskn_sb[:], maskn_d),
    ]:
        dma(dst, src, d_c)
    t_consts = d_c.at()

    Gq.op(lambda e: e.memset(onesrow_bf[0:1, :], 1.0), inc=g_s.tick())
    Gq.op(lambda e: e.memset(onescol_bf[:], 1.0), inc=g_s.tick())
    Gq.op(lambda e: e.memset(ones64[64:65, :], 1.0), inc=g_s.tick())
    Gq.op(lambda e: e.tensor_copy(maskt_sb[:], maskt_f[:]), waits=[t_consts], inc=g_s.tick())
    Gq.op(lambda e: e.tensor_copy(ident_bf[:], ident_sb[:]), waits=[t_consts], inc=g_s.tick())
    g_const = g_s.at()

    tok = dma(xstage[:, 0:512], win_d.rearrange("(kc p) m -> p kc m", p=128), d_ws)
    Vq.op(lambda e: e.tensor_copy(win_sb[:], xstage[:, 0:512]), waits=[tok], inc=v_s.tick())
    tok = dma(xstage[0:64, 1024:2048], wout_d, d_ws, waits=[v_s.at()])
    Vq.op(lambda e: e.tensor_copy(wout_sb[:], xstage[0:64, 1024:2048]), waits=[tok], inc=v_s.tick())
    v_wsmall = v_s.at()

    last_store = None
    last_wread_t = {0: None, 1: None}
    last_wread_v = {0: None, 1: None}
    for kc in range(NCH):
        sb_ = kc % 2
        wstage = wstage2[sb_]
        tok = dma(wstage[:], wqkvt_d[kc * 128:(kc + 1) * 128, :], d_ws,
                  waits=[last_wread_t[sb_], last_wread_v[sb_], v_wsmall])
        Vq.op(lambda e, kc=kc, wstage=wstage: e.tensor_scalar(
            wbstage[:], wstage[:], g_col[:, kc:kc + 1], None, ALU.mult),
            waits=[tok, t_consts, last_store], inc=v_s.tick())
        last_wread_v[sb_] = v_s.at()
        for mc in range(24):
            Tq.op(lambda e, kc=kc, mc=mc, wstage=wstage: e.matmul(
                pA[:, mc * 8 + kc:mc * 8 + kc + 1], wstage[:, mc * 128:(mc + 1) * 128],
                be_col[:, kc:kc + 1], start=True, stop=True),
                waits=([tok, t_consts] if mc == 0 else []))
        Tq.retro_inc(t_s.tick())
        last_wread_t[sb_] = t_s.at()
        dma(wqs_d[0:16, kc, :, :].rearrange("mc p f -> p mc f"),
            wbstage[:, 0:2048].rearrange("p (mc f) -> p mc f", mc=16),
            d_out, waits=[last_wread_v[sb_]])
        for vh_ in range(2):
            dma(wvs_d[vh_, kc, :, :, :].rearrange("mc p f -> p mc f"),
                wbstage[:, 2048 + vh_ * 512:2048 + vh_ * 512 + 512]
                .rearrange("p (mc f) -> p mc f", mc=4),
                d_out, waits=[])
        last_store = d_out.at()
    for mc in range(24):
        Vq.op(lambda e, mc=mc: e.reduce_sum(
            c2_col[:, mc:mc + 1], pA[:, mc * 8:mc * 8 + 8], axis=AX.X),
            waits=[last_wread_t[0], last_wread_t[1]], inc=v_s.tick())
    v_c2 = v_s.at()

    for kc in range(NCH):
        sb_ = kc % 2
        wstage = wstage2[sb_]
        tok = dma(wstage[:, 0:1024], wot_d[kc * 128:(kc + 1) * 128, :], d_ws,
                  waits=[last_wread_t[sb_], last_wread_v[sb_]])
        Vq.op(lambda e, wstage=wstage: e.tensor_copy(
            wbstage[:, 0:1024], wstage[:, 0:1024]),
            waits=[tok, v_c2, last_store], inc=v_s.tick())
        last_wread_v[sb_] = v_s.at()
        for mc in range(8):
            Tq.op(lambda e, kc=kc, mc=mc, wstage=wstage: e.matmul(
                pB[:, mc * 8 + kc:mc * 8 + kc + 1], wstage[:, mc * 128:(mc + 1) * 128],
                c2_col[:, 16 + kc:17 + kc], start=True, stop=True),
                waits=([tok, v_c2] if mc == 0 else []))
        Tq.retro_inc(t_s.tick())
        last_wread_t[sb_] = t_s.at()
        last_store = dma(wos_d[:, kc, :, :].rearrange("mc p f -> p mc f"),
                         wbstage[:, 0:1024].rearrange("p (mc f) -> p mc f", mc=8),
                         d_out, waits=[last_wread_v[sb_]])
    for mc in range(8):
        Vq.op(lambda e, mc=mc: e.reduce_sum(
            c3_col[:, mc:mc + 1], pB[:, mc * 8:mc * 8 + 8], axis=AX.X),
            waits=[last_wread_t[0], last_wread_t[1]], inc=v_s.tick())
    vput = vbuf[:].rearrange("p (t h f) -> p t h f", t=NT, h=NH, f=65)
    setup_v = v_s.at()
    setup_t = max(last_wread_t[0], last_wread_t[1], key=lambda t: t[1])
    setup_d = d_out.at()
    setup_g = g_s.at()

    def emit_ln(src_sb, src_tok_fn, eps_mul, eps_add, guards, rb_war=()):
        sq_read_t = {}
        sqtoks = {}
        for c in range(NCH):
            w = [src_tok_fn(c)] + guards
            if c - 2 in sq_read_t:
                w.append(sq_read_t[c - 2])
            Vq.op(lambda e, c=c: e.tensor_tensor(
                sq_b[:, (c % 2) * T:(c % 2) * T + T],
                src_sb[:, c * T:(c + 1) * T], src_sb[:, c * T:(c + 1) * T], ALU.mult),
                waits=w, inc=v_s.tick())
            sqtoks[c] = v_s.at()
            for th in range(2):
                psum_s = (pA, pB)[th]
                psum_q = (pC, pD)[th]
                for h5 in range(2):
                    cs = th * 1024 + h5 * 512
                    w = [src_tok_fn(c)] + (guards if c == 0 else [])
                    Tq.op(lambda e, c=c, h5=h5, cs=cs, psum_s=psum_s: e.matmul(
                        psum_s[0:1, h5 * 512:h5 * 512 + 512], onescol_bf[:],
                        src_sb[:, c * T + cs:c * T + cs + 512],
                        start=(c == 0), stop=(c == NCH - 1)), waits=w)
                for h5 in range(2):
                    cs = th * 1024 + h5 * 512
                    Tq.op(lambda e, c=c, h5=h5, cs=cs, psum_q=psum_q: e.matmul(
                        psum_q[0:1, h5 * 512:h5 * 512 + 512], onescol_bf[:],
                        sq_b[:, (c % 2) * T + cs:(c % 2) * T + cs + 512],
                        start=(c == 0), stop=(c == NCH - 1)), waits=[sqtoks[c]])
            Tq.retro_inc(t_s.tick())
            sq_read_t[c] = t_s.at()
        t_stats = t_s.at()
        out = {}
        for th in range(2):
            psum_s = (pA, pB)[th]
            psum_q = (pC, pD)[th]
            seg = slice(th * 1024, th * 1024 + 1024)
            Vq.op(lambda e, psum_s=psum_s, seg=seg: e.tensor_scalar(
                B_row[0:1, seg], psum_s[0:1, 0:1024], 1.0 / D, None, ALU.mult),
                waits=[t_stats] + guards, inc=v_s.tick())
            Vq.op(lambda e, seg=seg: e.tensor_tensor(
                R_row[0:1, seg], B_row[0:1, seg], B_row[0:1, seg],
                ALU.mult), inc=v_s.tick())
            Vq.op(lambda e, psum_q=psum_q, seg=seg: e.scalar_tensor_tensor(
                psum_q[0:1, 0:1024], psum_q[0:1, 0:1024], 1.0 / D,
                R_row[0:1, seg], ALU.mult, ALU.subtract), inc=v_s.tick())
            Vq.op(lambda e, psum_q=psum_q: e.tensor_scalar(
                psum_q[0:1, 0:1024], psum_q[0:1, 0:1024], eps_mul, eps_add,
                ALU.mult, ALU.add), inc=v_s.tick())
            v_rows = v_s.at()
            Aq.op(lambda e, psum_s=psum_s, psum_q=psum_q: e.activation(
                psum_s[0:1, 0:1024], psum_q[0:1, 0:1024], AF.Ln),
                waits=[v_rows] + guards, inc=a_s.tick())
            Aq.op(lambda e, psum_s=psum_s, seg=seg: e.activation(
                R_row[0:1, seg], psum_s[0:1, 0:1024], AF.Exp, scale=-0.5),
                inc=a_s.tick())
            a_R = a_s.at()
            Vq.op(lambda e, seg=seg: e.scalar_tensor_tensor(
                B_row[0:1, seg], B_row[0:1, seg], -1.0,
                R_row[0:1, seg], ALU.mult, ALU.mult),
                waits=[a_R], inc=v_s.tick())
            b_tok = v_s.at()
            bank_ev = [None, None]
            for row, dstb, gate in [(R_row, Rb_b, a_R), (B_row, Bb_b, b_tok)]:
                for h5 in range(2):
                    cs = th * 1024 + h5 * 512
                    Tq.op(lambda e, row=row, cs=cs, psum_s=psum_s, h5=h5: e.matmul(
                        psum_s[:, h5 * 512:h5 * 512 + 512], onesrow_bf[0:1, :],
                        row[0:1, cs:cs + 512],
                        start=True, stop=True),
                        waits=[gate, bank_ev[h5]] + guards + list(rb_war),
                        inc=t_s.tick())
                    tb = t_s.at()
                    Aq.op(lambda e, dstb=dstb, cs=cs, psum_s=psum_s, h5=h5: e.activation(
                        dstb[:, cs:cs + 512], psum_s[:, h5 * 512:h5 * 512 + 512],
                        AF.Copy), waits=[tb], inc=a_s.tick())
                    bank_ev[h5] = a_s.at()
            out["rb_th%d" % th] = a_s.at()
        out["Rb"] = a_s.at()
        out["t_end"] = t_s.at()
        return out

    prevs = {
        "out_dma": [setup_d, None],
        "slotA_free": [setup_t],
        "slotB_free": [],
        "slotC_free": [],
        "slotA_q_free": [],
        "vbuf_free": [],
        "stream_free": [],
        "ln_guards": [setup_v],
        "xin_free": [],
        "rb_war": [setup_v],
        "xload_early": [],
    }
    asm_guard = [None, None]
    ctxstage_guard = [None]
    x_first_guard = [setup_t, setup_d, setup_v, setup_g]

    for s in range(NSEQ):
        cast_toks = []
        xload = []
        for c in range(NCH):
            slab = c % 4
            stg = xstage if slab < 2 else ostage
            so = (slab % 2) * T
            if c < 2 and prevs.get("xload_early"):
                tok = prevs["xload_early"][c]
            else:
                if not prevs["xin_free"]:
                    w = list(x_first_guard) + (list(prevs["out_dma"])
                                               if c in (2, 3) else [])
                elif c < 2:
                    w = [prevs["xin_free"][c]]
                elif c in (2, 3):
                    w = list(prevs["out_dma"])
                else:
                    w = []
                if c >= 4:
                    w.append(cast_toks[c - 4])
                tok = dma(stg[:, so:so + T],
                          x_d[s, c * 128:(c + 1) * 128, :], d_x[slab], waits=w)
            xload.append(tok)
            saf = prevs["slotA_free"]
            gate = [saf[c]] if len(saf) == NCH else list(saf)
            Aq.op(lambda e, c=c, stg=stg, so=so: e.activation(
                slotA[:, c * T:(c + 1) * T], stg[:, so:so + T],
                AF.Copy),
                waits=[tok] + gate, inc=a_s.tick())
            cast_toks.append(a_s.at())
        ones_w = [cast_toks[-1]] + prevs["vbuf_free"] + prevs["stream_free"] \
            + list(prevs["out_dma"])
        if s == 0:
            ones_w += [setup_d, setup_t]
        Gq.op(lambda e: e.memset(vput[:, :, :, 64:65], 1.0), waits=ones_w,
              inc=g_s.tick())
        Gq.op(lambda e: e.memset(kt[0][0:1, :], 1.0), inc=g_s.tick())
        Gq.op(lambda e: e.memset(kt[1][0:1, :], 1.0), inc=g_s.tick())
        ones_tok = g_s.at()
        ln1 = emit_ln(slotA, lambda c: cast_toks[c], 1.0 + EPS, EPS * EPS,
                      prevs["ln_guards"], rb_war=prevs["rb_war"])
        xn_toks = []
        for c in range(NCH):
            for th in range(2):
                hs = slice(th * 1024, th * 1024 + 1024)
                Vq.op(lambda e, c=c, th=th, hs=hs: e.tensor_tensor(
                    tmp_b[:, (c % 2) * T + th * 1024:(c % 2) * T + th * 1024 + 1024],
                    slotA[:, c * T + th * 1024:c * T + th * 1024 + 1024],
                    Rb_b[:, hs], ALU.mult),
                    waits=[ln1["rb_th%d" % th], cast_toks[c]], inc=v_s.tick())
                Vq.op(lambda e, c=c, th=th, hs=hs: e.tensor_tensor(
                    slotC[:, c * T + th * 1024:c * T + th * 1024 + 1024],
                    tmp_b[:, (c % 2) * T + th * 1024:(c % 2) * T + th * 1024 + 1024],
                    Bb_b[:, hs], ALU.add),
                    waits=prevs["slotC_free"], inc=v_s.tick())
            xn_toks.append(v_s.at())
        xn_done = v_s.at()

        qk_evac = {}
        last_st_read = [None, None, None, None]
        first_stream = True
        for th in range(2):
            for mc in range(16):
                g = th * 16 + mc
                sbh = g % 4
                w = [setup_d, last_st_read[sbh]]
                if first_stream:
                    w += prevs["stream_free"]
                    first_stream = False
                wtok = dma(wqk_st[:, sbh * 1024:sbh * 1024 + 1024]
                           .rearrange("p (kc f) -> p kc f", kc=8),
                           wqs_d[mc, :, :, :].rearrange("kc p f -> p kc f"),
                           d_w[sbh], waits=w)
                pt = pAll[g % 4]
                wfirst = [wtok, ln1["t_end"]]
                if g - 4 >= 0:
                    wfirst.append(qk_evac[g - 4])
                for kc in range(NCH):
                    for h5 in range(2):
                        wkc = [xn_toks[kc]] if h5 == 0 else []
                        Tq.op(lambda e, mc=mc, kc=kc, th=th, h5=h5, pt=pt, sbh=sbh: e.matmul(
                            pt[:, h5 * 512:h5 * 512 + 512],
                            wqk_st[:, sbh * 1024 + kc * 128:sbh * 1024 + kc * 128 + 128],
                            slotC[:, kc * T + th * 1024 + h5 * 512:
                                  kc * T + th * 1024 + h5 * 512 + 512],
                            start=(kc == 0), stop=(kc == NCH - 1)),
                            waits=(wfirst + wkc if (kc == 0 and h5 == 0) else wkc))
                Tq.retro_inc(t_s.tick())
                t_grp = t_s.at()
                last_st_read[sbh] = t_grp
                dst = slotA if mc < 8 else slotB
                dmc = mc if mc < 8 else mc - 8
                wev = [t_grp]
                wev += prevs["slotA_q_free"] if mc < 8 else prevs["slotB_free"]
                Aq.op(lambda e, dst=dst, dmc=dmc, th=th, pt=pt, mc=mc: e.activation(
                    dst[:, dmc * T + th * 1024:dmc * T + th * 1024 + 1024], pt[:],
                    AF.Identity, bias=c2_col[:, mc:mc + 1]),
                    waits=wev, inc=a_s.tick())
                qk_evac[g] = a_s.at()
        qk_done = a_s.at()
        t_qk = t_s.at()


        asm_tok = {}

        def emit_asm(h, extra):
            wk = [qk_done, asm_guard[h % 2], extra]
            dma(kt[h % 2][1:65, :],
                slotB[(h % 2) * 64:(h % 2) * 64 + 64, (h // 2) * T:(h // 2) * T + T],
                d_kq[h % 2], waits=wk)
            dma(qt[h % 2][1:65, :],
                slotA[(h % 2) * 64:(h % 2) * 64 + 64, (h // 2) * T:(h // 2) * T + T],
                d_kq[h % 2], waits=[])
            asm_tok[h] = d_kq[h % 2].at()

        emit_asm(0, None)
        emit_asm(1, None)

        st = {"sprobe_b": [None, None], "probe_par": 0, "mrow": None,
              "aT": [None, None], "pT": [None, None], "ctx": None,
              "qrow": {}, "t_av": None}

        def gen_A(h, half):
            q0 = half * 1024
            mcol_toks = []
            for idx, i in enumerate(range(8 * half, 8 * half + 8)):
                row_w = (i + 1) * 128
                nchunks = 0
                for ck0 in range(0, row_w, 512):
                    ck1 = min(ck0 + 512, row_w)
                    pb_ = st["probe_par"]
                    st["probe_par"] ^= 1
                    w = [qk_done, st["sprobe_b"][pb_], ln1["Rb"]]
                    if pb_ == 1:
                        w.append(st["mrow"])
                    Tq.op(lambda e, h=h, i=i, ck0=ck0, ck1=ck1, pb_=pb_: e.matmul(
                        pA[:, pb_ * 512:pb_ * 512 + ck1 - ck0],
                        slotA[(h % 2) * 64:(h % 2) * 64 + 64,
                              (h // 2) * T + i * 128:(h // 2) * T + i * 128 + 128],
                        slotB[(h % 2) * 64:(h % 2) * 64 + 64,
                              (h // 2) * T + ck0:(h // 2) * T + ck1],
                        start=True, stop=True),
                        waits=w, inc=t_s.tick())
                    tmm = t_s.at()
                    Vq.op(lambda e, nch=nchunks, w=ck1 - ck0, pb_=pb_: e.reduce_max(
                        mtmp[:, nch:nch + 1], pA[:, pb_ * 512:pb_ * 512 + w], axis=AX.X),
                        waits=[tmm], inc=v_s.tick())
                    st["sprobe_b"][pb_] = v_s.at()
                    nchunks += 1
                Vq.op(lambda e, idx=idx, nch=nchunks: e.reduce_max(
                    mneg[:, idx:idx + 1], mtmp[:, 0:nch], axis=AX.X, negate=True),
                    inc=v_s.tick())
                mcol_toks.append(v_s.at())
                yield
            for qtr in range(2):
                for r in range(4):
                    w = [mcol_toks[qtr * 4 + r]]
                    if r == 0:
                        w += [st["mrow"], st["sprobe_b"][1]]
                    Tq.op(lambda e, qtr=qtr, r=r: e.matmul(
                        pA[0:1, 512 + r * 128:512 + r * 128 + 128],
                        mneg[:, qtr * 4 + r:qtr * 4 + r + 1], ident_bf[:],
                        start=True, stop=True),
                        waits=w, inc=(t_s.tick() if r == 3 else None))
                t_mrow = t_s.at()
                st["sprobe_b"][1] = None
                Aq.op(lambda e, h=h, q0=q0, qtr=qtr: e.activation(
                    qt[h % 2][0:1, q0 + qtr * 512:q0 + qtr * 512 + 512],
                    pA[0:1, 512:1024], AF.Copy),
                    waits=[t_mrow, asm_tok[h]], inc=a_s.tick())
                st["mrow"] = a_s.at()
                yield
            st["qrow"][(h, half)] = a_s.at()

        def gen_B(h, half):
            q0 = half * 1024
            qrow_done = st["qrow"][(h, half)]
            strips = list(range(0, 8 * (half + 1)))
            first_j, last_j = strips[0], strips[-1]
            pend = None
            av_ck = {}
            a_rk = {}

            def emit_av(pend):
                cks, bi, a_exp, diag_tok, j = pend
                wv = [a_exp, diag_tok, ones_tok]
                if j == first_j:
                    wv.append(st["ctx"])
                for ci, (c0, c1) in enumerate(cks):
                    lastj_chunk = min(last_j, (q0 + c1 - 1) // 128)
                    sp = (j == lastj_chunk)
                    Tq.op(lambda e, h=h, j=j, c0=c0, c1=c1, bi=bi, sp=sp: e.matmul(
                        pD[0:65, c0:c1],
                        vbuf[:, j * NH * 65 + h * 65:j * NH * 65 + h * 65 + 65],
                        aT[bi][:, c0:c1],
                        start=(j == first_j), stop=sp),
                        waits=(wv if ci == 0 else []),
                        inc=(t_s.tick() if (sp or ci == len(cks) - 1) else None))
                    if sp:
                        av_ck[c0 // 512] = t_s.at()
                st["aT"][bi] = t_s.at()

            def tail_rows(kk):
                Aq.op(lambda e, kk=kk: e.activation(
                    rball[64:65, kk * 512:kk * 512 + 512],
                    pD[64:65, kk * 512:kk * 512 + 512], AF.Ln),
                    waits=[av_ck[kk]], inc=a_s.tick())
                Aq.op(lambda e, kk=kk: e.activation(
                    rrow_bf[64:65, kk * 512:kk * 512 + 512],
                    rball[64:65, kk * 512:kk * 512 + 512], AF.Exp, scale=-1.0),
                    inc=a_s.tick())
                a_rk[kk] = a_s.at()

            def tail_bcast(kk):
                Tq.op(lambda e, kk=kk: e.matmul(
                    pB[0:64, kk * 512:kk * 512 + 512], ones64[64:65, :],
                    rrow_bf[64:65, kk * 512:kk * 512 + 512],
                    start=True, stop=True),
                    waits=[a_rk[kk], st["pT"][0]], inc=t_s.tick())
                t_rbk = t_s.at()
                Aq.op(lambda e, kk=kk: e.activation(
                    rball[0:64, kk * 512:kk * 512 + 512],
                    pB[0:64, kk * 512:kk * 512 + 512], AF.Copy),
                    waits=[t_rbk], inc=a_s.tick())

            closer0 = min(last_j, (q0 + 511) // 128)
            for j in strips:
                rel0 = max(j * 128 - q0, 0)
                bi = j % 2
                buf = (pB, pC)[bi]
                cks = []
                c0 = rel0
                while c0 < 1024:
                    c1 = min((c0 // 512 + 1) * 512, 1024)
                    cks.append((c0, c1))
                    c0 = c1
                wv = [qrow_done, asm_tok[h], st["pT"][bi], ones_tok]
                for ci, (c0, c1) in enumerate(cks):
                    Tq.op(lambda e, h=h, j=j, c0=c0, c1=c1, buf=buf, q0=q0: e.matmul(
                        buf[:, c0:c1],
                        kt[h % 2][0:65, j * 128:j * 128 + 128],
                        qt[h % 2][0:65, q0 + c0:q0 + c1],
                        start=True, stop=True),
                        waits=(wv if ci == 0 else []),
                        inc=(t_s.tick() if ci == len(cks) - 1 else None))
                t_st = t_s.at()
                Aq.op(lambda e, bi=bi, rel0=rel0, buf=buf: e.activation(
                    aT[bi][:, rel0:1024], buf[:, rel0:1024], AF.Exp, scale=0.125),
                    waits=[t_st, st["aT"][bi]], inc=a_s.tick())
                st["pT"][bi] = a_s.at()
                a_exp = a_s.at()
                diag_tok = None
                if j >= 8 * half:
                    Gq.op(lambda e, bi=bi, rel0=rel0: e.tensor_tensor(
                        aT[bi][:, rel0:rel0 + 128], aT[bi][:, rel0:rel0 + 128],
                        maskt_sb[:], ALU.mult),
                        waits=[a_exp], inc=g_s.tick())
                    diag_tok = g_s.at()
                if pend is not None:
                    emit_av(pend)
                if j == closer0 + 1:
                    tail_rows(0)
                if j == closer0 + 2:
                    tail_bcast(0)
                pend = (cks, bi, a_exp, diag_tok, j)
                yield
            emit_av(pend)
            t_av = t_s.at()
            st["t_av"] = t_av

            if closer0 + 1 > last_j:
                tail_rows(0)
            if closer0 + 2 > last_j:
                tail_bcast(0)
            tail_rows(1)
            tail_bcast(1)
            v_rb = a_s.at()
            st["pT"][0] = v_rb
            if h % 2 == 0:
                Vq.op(lambda e, h=h, q0=q0: e.tensor_tensor(
                    slotC[0:64, (h // 2) * T + q0:(h // 2) * T + q0 + 1024],
                    pD[0:64, :], rball[0:64, :], ALU.mult),
                    waits=[v_rb], inc=v_s.tick())
                st["ctx"] = v_s.at()
            else:
                Vq.op(lambda e: e.tensor_tensor(
                    ctxstage[:], pD[0:64, :], rball[0:64, :], ALU.mult),
                    waits=[v_rb, ctxstage_guard[0]], inc=v_s.tick())
                st["ctx"] = v_s.at()
                tok = dma(slotC[64:128, (h // 2) * T + q0:(h // 2) * T + q0 + 1024],
                          ctxstage[:], d_cs, waits=[st["ctx"]])
                ctxstage_guard[0] = tok
            if half == 1:
                asm_guard[h % 2] = t_av
                if h + 2 < NH:
                    emit_asm(h + 2, t_av)
            yield

        pairs = [(h, hf) for h in range(NH) for hf in range(2)]
        gA0 = gen_A(*pairs[0])
        gA0_done = [False]

        def pump():
            if not gA0_done[0]:
                try:
                    next(gA0)
                except StopIteration:
                    gA0_done[0] = True
        v_ev = {}
        last_v_read = None
        for vh in range(2):
            w = [setup_d, t_qk if vh == 0 else last_v_read] + list(prevs["out_dma"])
            wtok = dma(wv_st[:].rearrange("p (km f) -> p km f", km=32),
                       wvs_d[vh, :, :, :, :].rearrange("kc mc p f -> p (kc mc) f"),
                       d_wv, waits=w)
            for tt in range(NT):
                gidx = vh * NT + tt
                pt = (pB, pC, pD)[gidx % 3]
                wfirst = [wtok, xn_done]
                if gidx - 3 >= 0:
                    wfirst.append(v_ev[gidx - 3])
                else:
                    wfirst.append(qk_done)
                for kc in range(NCH):
                    Tq.op(lambda e, tt=tt, kc=kc, pt=pt: e.matmul(
                        pt[:, 0:512],
                        slotC[:, kc * T + tt * 128:kc * T + tt * 128 + 128],
                        wv_st[:, kc * 512:kc * 512 + 512],
                        start=(kc == 0), stop=(kc == NCH - 1)),
                        waits=(wfirst if kc == 0 else []))
                Tq.retro_inc(t_s.tick())
                t_grp = t_s.at()
                last_v_read = t_grp
                dstv = vbuf[:, tt * NH * 65:(tt + 1) * NH * 65] \
                    .rearrange("p (h f) -> p h f", h=NH)[:, vh * 8:vh * 8 + 8, 0:64]
                srcv = pt[:, 0:512].rearrange("p (h f) -> p h f", h=8)
                Aq.op(lambda e, dstv=dstv, srcv=srcv: e.activation(dstv, srcv, AF.Copy),
                      waits=[t_grp, setup_g] + prevs["vbuf_free"] + list(prevs["out_dma"]),
                      inc=a_s.tick())
                v_ev[gidx] = a_s.at()
                pump()
                pump()
        v_done = a_s.at()
        t_v = t_s.at()
        st["pT"] = [v_done, v_done]
        st["ctx"] = v_done
        st["aT"] = [st["aT"][0], st["aT"][1]]
        for _ in gA0:
            pass
        gAn = None
        for k, pr in enumerate(pairs):
            gB = gen_B(*pr)
            gAn = gen_A(*pairs[k + 1]) if k + 1 < len(pairs) else None
            nstrips = 8 * (pr[1] + 1)
            if nstrips >= 10:
                sched = [0] * (nstrips - 10) + [1] * 10
            else:
                sched = [1] * nstrips
                for i in range(10 - nstrips):
                    sched[-1 - (i % 2)] += 1
            yi = 0
            for _ in gB:
                nsteps = sched[yi] if yi < len(sched) else 1
                yi += 1
                for _ in range(nsteps):
                    if gAn is not None:
                        try:
                            next(gAn)
                        except StopIteration:
                            gAn = None
            if gAn is not None:
                for _ in gAn:
                    pass
                gAn = None
        t_av_last = st["t_av"]
        v_p3 = v_s.at()
        a_p3 = a_s.at()
        d_p3 = [d_kq[0].at(), d_kq[1].at(), d_cs.at()]
        t_p3 = t_s.at()

        ao_evac = {}
        last_wo_read = [None, None]
        for mc in range(8):
            sbh = mc % 2
            w = [setup_d, last_wo_read[sbh]]
            if mc == 0:
                w += [t_p3]
            wtok = dma(wo_st[:, sbh * 1024:sbh * 1024 + 1024]
                       .rearrange("p (kc f) -> p kc f", kc=8),
                       wos_d[mc, :, :, :].rearrange("kc p f -> p kc f"),
                       d_w[sbh], waits=w)
            pts = (pA, pB) if mc % 2 == 0 else (pC, pD)
            wfirst = [wtok, v_p3, a_p3, d_p3]
            if mc >= 2:
                wfirst.append(ao_evac[mc - 2])
            for kc in range(NCH):
                for sl in range(4):
                    pt = pts[sl // 2]
                    Tq.op(lambda e, mc=mc, kc=kc, sl=sl, pt=pt, sbh=sbh: e.matmul(
                        pt[:, (sl % 2) * 512:(sl % 2) * 512 + 512],
                        wo_st[:, sbh * 1024 + kc * 128:sbh * 1024 + kc * 128 + 128],
                        slotC[:, kc * T + sl * 512:kc * T + sl * 512 + 512],
                        start=(kc == 0), stop=(kc == NCH - 1)),
                        waits=(wfirst if (kc == 0 and sl == 0) else []))
            Tq.retro_inc(t_s.tick())
            t_grp = t_s.at()
            last_wo_read[sbh] = t_grp
            for halfp in range(2):
                Aq.op(lambda e, mc=mc, halfp=halfp, pts=pts: e.activation(
                    slotA[:, mc * T + halfp * 1024:mc * T + halfp * 1024 + 1024],
                    pts[halfp][:], AF.Identity, bias=c3_col[:, mc:mc + 1]),
                    waits=[t_grp, d_p3], inc=a_s.tick())
            ao_evac[mc] = a_s.at()
        ao_done = a_s.at()
        t_wo = t_s.at()

        ln3 = emit_ln(slotA, lambda c: ao_evac[c], 1.0, EPS, [ao_done, v_p3],
                      rb_war=[xn_done])
        x2_toks = []
        for c in range(NCH):
            slab = c % 4
            stg = xstage if slab < 2 else ostage
            so = (slab % 2) * T
            w = [cast_toks[-1], t_av_last] + ([x2_toks[c - 4]] if c >= 4 else [])
            xtok = dma(stg[:, so:so + T],
                       x_d[s, c * 128:(c + 1) * 128, :], d_x[slab], waits=w)
            xo = (c % 2) * T
            Aq.op(lambda e, c=c, stg=stg, so=so, xo=xo: e.activation(
                xbf[:, xo:xo + T], stg[:, so:so + T], AF.Copy),
                waits=[xtok, t_wo, d_p3] + ([x2_toks[c - 2]] if c >= 2 else []),
                inc=a_s.tick())
            xc_tok = a_s.at()
            for th in range(2):
                to = (c % 2) * T + th * 1024
                Vq.op(lambda e, c=c, th=th, to=to: e.tensor_tensor(
                    tmp_b[:, to:to + 1024],
                    slotA[:, c * T + th * 1024:c * T + th * 1024 + 1024],
                    Rb_b[:, th * 1024:th * 1024 + 1024], ALU.mult),
                    waits=[ln3["rb_th%d" % th]] + ([x2_toks[c - 2]] if c >= 2 else []),
                    inc=v_s.tick())
                Vq.op(lambda e, to=to, th=th: e.tensor_tensor(
                    tmp_b[:, to:to + 1024], tmp_b[:, to:to + 1024],
                    Bb_b[:, th * 1024:th * 1024 + 1024], ALU.add),
                    inc=v_s.tick())
                Vq.op(lambda e, c=c, th=th, to=to: e.tensor_tensor(
                    slotB[:, c * T + th * 1024:c * T + th * 1024 + 1024],
                    tmp_b[:, to:to + 1024],
                    xbf[:, (c % 2) * T + th * 1024:(c % 2) * T + th * 1024 + 1024],
                    ALU.add),
                    waits=[xc_tok], inc=v_s.tick())
            x2_toks.append(v_s.at())
        x2_done = v_s.at()

        ln4 = emit_ln(slotB, lambda c: x2_toks[c], 1.0, EPS, [x2_done, ao_done],
                      rb_war=[x2_done])
        xn2_toks = []
        for c in range(NCH):
            for th in range(2):
                to = (c % 2) * T + th * 1024
                Vq.op(lambda e, c=c, th=th, to=to: e.tensor_tensor(
                    tmp_b[:, to:to + 1024],
                    slotB[:, c * T + th * 1024:c * T + th * 1024 + 1024],
                    Rb_b[:, th * 1024:th * 1024 + 1024], ALU.mult),
                    waits=[ln4["rb_th%d" % th]], inc=v_s.tick())
                Vq.op(lambda e, c=c, th=th, to=to: e.tensor_tensor(
                    slotA[:, c * T + th * 1024:c * T + th * 1024 + 1024],
                    tmp_b[:, to:to + 1024],
                    Bb_b[:, th * 1024:th * 1024 + 1024], ALU.add),
                    inc=v_s.tick())
            xn2_toks.append(v_s.at())
        xn2_done = v_s.at()

        s_ev = []
        for sl in range(4):
            pt = (pA, pA, pB, pB)[sl]
            off = (sl % 2) * 512
            for kc in range(NCH):
                wkc = [xn2_toks[kc]] + ([ln4["t_end"]] if kc == 0 else [])
                Tq.op(lambda e, sl=sl, kc=kc, pt=pt, off=off: e.matmul(
                    pt[0:64, off:off + 512],
                    win_sb[:, kc * 64:kc * 64 + 64],
                    slotA[:, kc * T + sl * 512:kc * T + sl * 512 + 512],
                    start=(kc == 0), stop=(kc == NCH - 1)),
                    waits=wkc)
            Tq.retro_inc(t_s.tick())
            t_grp = t_s.at()
            if sl % 2 == 1:
                pth = (pA, pB)[sl // 2]
                Aq.op(lambda e, sl=sl, pth=pth: e.activation(
                    s_sb[:, (sl - 1) * 512:(sl + 1) * 512], pth[0:64, :],
                    AF.Identity, bias=bin_sb[:]),
                    waits=[t_grp, d_p3], inc=a_s.tick())
                s_ev.append(a_s.at())
        s_done = a_s.at()
        g_q = []
        ts_toks = {}
        for q4 in range(4):
            qs = slice(q4 * 512, q4 * 512 + 512)
            for i in range(16):
                tfa = tfn[0:64, (i % 2) * 512:(i % 2) * 512 + 512]
                if i < 2:
                    wa = [s_ev[q4 // 2]]
                    if q4 > 0:
                        wa.append(ts_toks[(q4 - 1, 14 + i)])
                else:
                    wa = [ts_toks[(q4, i - 2)]]
                Aq.op(lambda e, i=i, qs=qs, tfa=tfa: e.activation(
                    tfa, s_sb[0:64, qs], AF.Identity,
                    scale=w1_sb[:, i:i + 1], bias=b1_sb[:, i:i + 1]),
                    waits=wa, inc=a_s.tick())
                a_tf = a_s.at()
                dst = g_sb if i == 0 else u_fn
                Vq.op(lambda e, dst=dst, i=i, qs=qs, tfa=tfa: e.tensor_scalar(
                    dst[0:64, qs], tfa, 0.0, w2_sb[:, i:i + 1],
                    ALU.max, ALU.mult), waits=[a_tf], inc=v_s.tick())
                ts_toks[(q4, i)] = v_s.at()
                if i > 0:
                    Vq.op(lambda e, qs=qs: e.tensor_tensor(
                        g_sb[0:64, qs], g_sb[0:64, qs], u_fn[0:64, qs], ALU.add),
                        inc=v_s.tick())
            Vq.op(lambda e, qs=qs: e.tensor_scalar(
                gb_sb[0:64, qs], g_sb[0:64, qs], b2_sb[:], None, ALU.add),
                inc=v_s.tick())
            g_q.append(v_s.at())
        g_done = v_s.at()
        fo_ev = {}
        for mc in range(8):
            pts = (pA, pB) if mc % 2 == 0 else (pC, pD)
            for sl in range(4):
                pt = pts[sl // 2]
                w = [g_q[sl]]
                if sl == 0 and mc >= 2:
                    w.append(fo_ev[mc - 2])
                Tq.op(lambda e, mc=mc, sl=sl, pt=pt: e.matmul(
                    pt[:, (sl % 2) * 512:(sl % 2) * 512 + 512],
                    wout_sb[0:64, mc * 128:mc * 128 + 128],
                    gb_sb[:, sl * 512:sl * 512 + 512],
                    start=True, stop=True),
                    waits=w)
            Tq.retro_inc(t_s.tick())
            t_grp = t_s.at()
            for halfp in range(2):
                Aq.op(lambda e, mc=mc, halfp=halfp, pts=pts: e.activation(
                    slotA[:, mc * T + halfp * 1024:mc * T + halfp * 1024 + 1024],
                    pts[halfp][:], AF.Identity, bias=bout_col[:, mc:mc + 1]),
                    waits=[t_grp, xn2_done], inc=a_s.tick())
            fo_ev[mc] = a_s.at()
        fo_done = a_s.at()

        ln5 = emit_ln(slotA, lambda c: fo_ev[c], 1.0, EPS, [fo_done, x2_done],
                      rb_war=[xn2_done])
        xload_early = []
        if s + 1 < NSEQ:
            for ce in range(2):
                xload_early.append(dma(
                    xstage[:, ce * T:ce * T + T],
                    x_d[s + 1, ce * 128:(ce + 1) * 128, :], d_x[ce],
                    waits=[x2_toks[4 + ce]]))
        out_toks = []
        out_g = []
        ln5_tt1 = []
        for c in range(NCH):
            for th in range(2):
                to = (c % 2) * T + th * 1024
                Vq.op(lambda e, c=c, th=th, to=to: e.tensor_tensor(
                    tmp_b[:, to:to + 1024],
                    slotA[:, c * T + th * 1024:c * T + th * 1024 + 1024],
                    Rb_b[:, th * 1024:th * 1024 + 1024], ALU.mult),
                    waits=[ln5["rb_th%d" % th]] + ([out_g[c - 2]] if c >= 2 else []),
                    inc=v_s.tick())
                Vq.op(lambda e, c=c, th=th, to=to: e.tensor_tensor(
                    tmp_b[:, to:to + 1024], tmp_b[:, to:to + 1024],
                    Bb_b[:, th * 1024:th * 1024 + 1024], ALU.add),
                    inc=v_s.tick())
                w = [out_toks[c - 2]] if c >= 2 else [x2_toks[6 + c]]
                Vq.op(lambda e, c=c, th=th, to=to: e.tensor_tensor(
                    ostage[:, to:to + 1024],
                    tmp_b[:, to:to + 1024],
                    slotB[:, c * T + th * 1024:c * T + th * 1024 + 1024], ALU.add),
                    waits=w, inc=v_s.tick())
            ln5_tt1.append(v_s.at())
            v_out = v_s.at()
            out_g.append(v_out)
            tok = dma(y_d[s, c * 128:(c + 1) * 128, :],
                      ostage[:, (c % 2) * T:(c % 2) * T + T], d_y[c % 4], waits=[v_out])
            out_toks.append(tok)
        v_p6 = v_s.at()
        t_ln5 = ln5["t_end"]

        g_p6 = g_s.at()
        prevs = {
            "out_dma": out_toks[-4:],
            "slotA_free": ln5_tt1,
            "slotB_free": [v_p6, g_p6],
            "slotC_free": [t_wo],
            "slotA_q_free": [t_p3, d_p3],
            "vbuf_free": [t_av_last],
            "stream_free": [v_p6, fo_done],
            "ln_guards": [ln5["Rb"], fo_done],
            "xin_free": [x2_toks[4], x2_toks[5]],
            "rb_war": [v_p6],
            "xload_early": xload_early,
        }

    nc._dbg = dict(slotA=slotA, slotB=slotB, slotC=slotC, vbuf=vbuf, kt0=kt[0],
                   kt1=kt[1], qt0=qt[0], qt1=qt[1], c2_col=c2_col, c3_col=c3_col,
                   Rb_b=Rb_b, Bb_b=Bb_b, R_row=R_row, B_row=B_row, s_sb=s_sb,
                   g_sb=g_sb, gb_sb=gb_sb, rball=rball, mneg=mneg, aT0=aT[0],
                   aT1=aT[1], win_sb=win_sb, wout_sb=wout_sb, tfn=tfn, u_fn=u_fn, g=g_sb, w1=w1_sb, b1=b1_sb, w2=w2_sb)

    Sq.op(lambda e: None, waits=[d.at() for d in d_y])

    with nc.Block() as block:
        @block.tensor
        def _(e):
            run_ops(e, Tq.ops)

        @block.vector
        def _(e):
            run_ops(e, Vq.ops)

        @block.scalar
        def _(e):
            run_ops(e, Aq.ops)

        @block.gpsimd
        def _(e):
            run_ops(e, Gq.ops)

        @block.sync
        def _(e):
            run_ops(e, Sq.ops)

    return nc


def kernel(L0_attn_x, L0_attn_ln_gamma, L0_attn_ln_beta, L0_attn_W_qkv, L0_attn_W_o,
           L0_ffn_Win, L0_ffn_bin_, L0_ffn_W1, L0_ffn_b1, L0_ffn_W2, L0_ffn_b2,
           L0_ffn_Wout, L0_ffn_bout):
    x = np.asarray(L0_attn_x, dtype=np.float32)
    gamma = np.asarray(L0_attn_ln_gamma, dtype=np.float32)
    beta = np.asarray(L0_attn_ln_beta, dtype=np.float32)
    wqkv = np.asarray(L0_attn_W_qkv, dtype=np.float32)
    wo = np.asarray(L0_attn_W_o, dtype=np.float32)
    win = np.asarray(L0_ffn_Win, dtype=np.float32)
    bin_ = np.asarray(L0_ffn_bin_, dtype=np.float32)
    w1 = np.asarray(L0_ffn_W1, dtype=np.float32)
    b1 = np.asarray(L0_ffn_b1, dtype=np.float32)
    w2 = np.asarray(L0_ffn_W2, dtype=np.float32)
    b2 = np.asarray(L0_ffn_b2, dtype=np.float32)
    wout = np.asarray(L0_ffn_Wout, dtype=np.float32)
    bout = np.asarray(L0_ffn_bout, dtype=np.float32)

    idx = []
    for blk in range(3):
        for h in range(NH):
            base = h * 192 + blk * 64
            idx.extend(range(base, base + 64))
    wqkvt = np.ascontiguousarray(wqkv[np.array(idx)].T)
    wot = np.ascontiguousarray(wo.T)
    gcol = np.ascontiguousarray(gamma.reshape(8, 128).T)
    bcol = np.ascontiguousarray(beta.reshape(8, 128).T)
    boutcol = np.ascontiguousarray(bout.reshape(8, 128).T)
    bincol = np.ascontiguousarray(bin_.reshape(64, 1))
    b2col = np.ascontiguousarray(b2.reshape(64, 1))
    ident = np.eye(128, dtype=np.float32)
    kq = np.arange(128)
    maskt = np.ascontiguousarray((kq[:, None] <= kq[None, :]).astype(np.float32))
    maskn = np.ascontiguousarray(
        np.where(kq[None, :] > kq[:, None], np.float32(-1e30), np.float32(0.0)))

    nc = build_program()
    in_maps = []
    for c in range(NCORE):
        xs = np.ascontiguousarray(np.transpose(x[c * NSEQ:(c + 1) * NSEQ], (0, 2, 1)))
        in_maps.append(dict(
            x=xs, wqkvt=wqkvt, wot=wot, win=win, wout=wout,
            gcol=gcol, bcol=bcol, bincol=bincol, w1=w1, b1=b1, w2=w2,
            b2col=b2col, boutcol=boutcol, ident=ident, maskt=maskt, maskn=maskn,
        ))
    res = run_bass_kernel_spmd(nc, in_maps, core_ids=list(range(NCORE)))
    outs = []
    for c in range(NCORE):
        yc = res.results[c]["y"]
        outs.append(np.transpose(yc, (0, 2, 1)))
    return np.ascontiguousarray(np.concatenate(outs, axis=0))
